# version 02
# baseline (speedup 1.0000x reference)
import numpy as np
import ml_dtypes
from contextlib import ExitStack
import concourse.bass as bass
import concourse.mybir as mybir
from concourse.bass_utils import run_bass_kernel_spmd

BF = ml_dtypes.bfloat16
F32 = mybir.dt.float32
BF16 = mybir.dt.bfloat16
AF = mybir.ActivationFunctionType
ALU = mybir.AluOpType
AX = mybir.AxisListType

NCORE = 8
B = 4
T = 2048
D = 1024
NC = 8
TT = 512
NTT = T // TT
NEG = -30000.0
BETA = -1.0e9
HIDP = 2816
NHC = 22
PH14 = 4
INNER14 = 4
DBG14 = False



class K:

    NPOOL = 8

    def __init__(self, nc):
        self.nc = nc
        self.eng = {"t": nc.tensor, "v": nc.vector, "s": nc.scalar,
                    "p": nc.gpsimd, "q": nc.sync}
        self.prog = {k: nc.alloc_semaphore("prog_" + k) for k in "tvs"}
        self.cnt = {k: 0 for k in "tvs"}
        self.dpool = {k: [nc.alloc_semaphore(f"dma_{k}{i}")
                          for i in range(self.NPOOL)] for k in "qsp"}
        self.dqi = {k: 0 for k in "qsp"}
        self.seen = {k: {} for k in "tvspq"}

    def wait(self, ekey, deps):
        e = self.eng[ekey]
        seen = self.seen[ekey]
        for d in deps:
            if d is None:
                continue
            sem, val = d
            if val <= 0 or val <= seen.get(sem.num, 0):
                continue
            e.wait_ge(sem, val)
            seen[sem.num] = val

    def dma(self, qkey, out, in_, deps=(), **kw):
        slot = self.dqi[qkey] % self.NPOOL
        n_use = self.dqi[qkey] // self.NPOOL
        sem = self.dpool[qkey][slot]
        self.wait(qkey, list(deps) + [(sem, 16 * n_use)])
        i = self.eng[qkey].dma_start(out=out, in_=in_, **kw)
        i.then_inc(sem, 16)
        self.dqi[qkey] += 1
        return (sem, 16 * (n_use + 1))

    def do(self, ekey, method, *args, deps=(), **kw):
        self.wait(ekey, deps)
        i = getattr(self.eng[ekey], method)(*args, **kw)
        self.cnt[ekey] += 1
        i.then_inc(self.prog[ekey], 1)
        return (self.prog[ekey], self.cnt[ekey])

    def mm(self, out, lhsT, rhs, start, stop, deps=(), mark=None):
        if mark is None:
            mark = stop
        self.wait("t", deps)
        i = self.nc.tensor.matmul(out, lhsT, rhs, start=start, stop=stop)
        if mark:
            self.cnt["t"] += 1
            i.then_inc(self.prog["t"], 1)
            return (self.prog["t"], self.cnt["t"])
        return None

    def tip(self):
        toks = [(self.prog[k], self.cnt[k]) for k in "tvs"]
        for q in "qsp":
            n = self.dqi[q]
            for slot in range(self.NPOOL):
                uses = (n - slot + self.NPOOL - 1) // self.NPOOL
                if uses > 0:
                    toks.append((self.dpool[q][slot], 16 * uses))
        return toks

    def fence_all(self, deps):
        for ekey in "tvspq":
            self.wait(ekey, deps)


def cast_weights(k, dst_sb, src_ap, deps, qkey="q", stage_f32=None, tag=""):
    n = dst_sb.shape[1]
    ns = stage_f32.shape[1]
    half = ns // 2
    ld_prev = {}
    tok = None
    if len(src_ap.shape) == 3:
        C, m = src_ap.shape[1], src_ap.shape[2]
        assert C * m == n, (src_ap.shape, n)
        i = 0
        if m > half:
            for c0 in range(C):
                off = 0
                while off < m:
                    mm_ = min(half, m - off)
                    buf = i % 2
                    tl = k.dma(qkey, stage_f32[:, buf * half:buf * half + mm_],
                               src_ap[:, c0, off:off + mm_],
                               deps=list(deps) + [ld_prev.get(buf)])
                    tok = k.do("v", "tensor_copy",
                               dst_sb[:, c0 * m + off:c0 * m + off + mm_],
                               stage_f32[:, buf * half:buf * half + mm_],
                               deps=[tl])
                    ld_prev[buf] = tok
                    off += mm_
                    i += 1
            return tok
        per = max(1, half // m)
        c0 = 0
        while c0 < C:
            cc = min(per, C - c0)
            buf = i % 2
            tl = k.dma(qkey, stage_f32[:, buf * half:buf * half + cc * m]
                       .rearrange("p (c m) -> p c m", c=cc),
                       src_ap[:, c0:c0 + cc, :],
                       deps=list(deps) + [ld_prev.get(buf)])
            tok = k.do("v", "tensor_copy", dst_sb[:, c0 * m:(c0 + cc) * m],
                       stage_f32[:, buf * half:buf * half + cc * m],
                       deps=[tl])
            ld_prev[buf] = tok
            c0 += cc
            i += 1
        return tok
    off = 0
    i = 0
    while off < n:
        m = min(half, n - off)
        buf = i % 2
        tl = k.dma(qkey, stage_f32[:, buf * half:buf * half + m],
                   src_ap[:, off:off + m],
                   deps=list(deps) + [ld_prev.get(buf)])
        tok = k.do("v", "tensor_copy", dst_sb[:, off:off + m],
                   stage_f32[:, buf * half:buf * half + m],
                   deps=[tl])
        ld_prev[buf] = tok
        off += m
        i += 1
    return tok



WEIGHTS = {
    "l0a_wqa": [NC, 128, 64],
    "l0a_wqb": [64, 1024],
    "l0a_wkva": [NC, 128, 32],
    "l0a_wkvbK": [32, 256],
    "l0a_wkvbV": [32, 256],
    "l0a_wout": [NC, 128, 1024],
    "l0a_bout": [128, NC],
    "l0f_wqkv": [2, NC, 128, 1536],
    "l0f_qs": [2, 64],
    "l0f_ks": [2, 64],
    "l0f_wo": [2, 4, 128, 1024],
    "l0f_lng": [2, NC, 128],
    "l0f_lnb": [2, NC, 128],
    "l0f_w1": [2, NC, 128, 2 * HIDP],
    "l0f_b1": [2, 128, 2 * NHC],
    "l0f_w2": [2, NHC, 128, 1024],
    "l0f_b2": [2, 128, NC],
    "l1a_wq": [NC, 128, 1024],
    "l1a_wk": [NC, 128, 1024],
    "l1a_wv": [NC, 128, 1024],
    "l1a_wo": [NC, 128, 1024],
    "l1a_bq": [128, NC],
    "l1a_bk": [128, NC],
    "l1a_bo": [128, NC],
    "l1a_bvrow": [1, 1024],
    "l1a_wqr": [64, 16],
    "l1a_wkr": [64, 16],
    "l1f_gw": [NC, 128, 8],
    "l1f_gbrow": [1, 8],
    "l1f_w1t": [8, NC, 128, 4096],
    "l1f_b1": [8, 128, 32],
    "l1f_w2t": [8, 32, 128, 1024],
    "l1f_b2m": [NC, 8, 128],
}

CONSTS = {
    "c_id": ([128, 128], BF16),
    "c_p128": ([128, 128], BF16),
    "c_prot": ([128, 128], BF16),
    "c_cd": ([128, T], F32),
    "c_sd": ([128, T], F32),
    "c_xqc": ([128, T], F32),
    "c_xqs": ([128, T], F32),
    "c_xkc": ([128, T], F32),
    "c_xks": ([128, T], F32),
    "c_cm": ([4, 128, TT], BF16),
    "c_mtm": ([4, 128, 1024], BF16),
    "c_mkm": ([8, 128, TT], BF16),
}

SCRATCH = {
    "xn1": ([B, NC, 128, T], BF16),
    "xfa": ([B, NC, 128, T], F32),
    "xfb": ([B, NC, 128, T], F32),
    "qaug": ([B, 16, 65, T], BF16),
    "kaug": ([B, 4, 65, T], BF16),
    "vt0": ([B, 4, T, 64], BF16),
    "o0": ([B, NC, 128, T], BF16),
    "x1": ([B, NC, 128, T], F32),
    "qf": ([B, 4, 128, T], BF16),
    "kf": ([B, 4, 128, T], BF16),
    "vt1": ([B, 8, T, 64], BF16),
    "of": ([B, 4, 128, T], BF16),
    "mgl": ([B, NHC, 128, T], BF16),
    "x2": ([B, NC, 128, T], F32),
    "xn2": ([B, NC, 128, T], BF16),
    "q2aug": ([B, 16, 17, T], BF16),
    "k2aug": ([B, 16, 17, T], BF16),
    "vt2": ([B, 16, T, 64], BF16),
    "o2": ([B, NC, 128, T], BF16),
    "mbeta": ([8, B * T], F32),
    "dbg14": ([4, 1024], F32),
    "m8": ([8, B * T], F32),
}

INPUTS = {"xin": ([B, NC, 128, T], F32)}
OUTPUTS = {"out": ([B, NC, 128, T], F32)}



def _fm(x):
    b, t, d = x.shape
    return np.ascontiguousarray(
        x.reshape(b, t, d // 128, 128).transpose(0, 2, 3, 1))


def host_weights(inp):
    w = {}
    w["l0a_wqa"] = inp["L0_attn_Wqa"].reshape(NC, 128, 64)
    w["l0a_wqb"] = inp["L0_attn_Wqb"]
    w["l0a_wkva"] = inp["L0_attn_Wkva"].reshape(NC, 128, 32)
    kvb = np.asarray(inp["L0_attn_Wkvb"]).reshape(32, 4, 64, 2)
    w["l0a_wkvbK"] = kvb[..., 0].reshape(32, 256)
    w["l0a_wkvbV"] = kvb[..., 1].reshape(32, 256)
    w["l0a_wout"] = inp["L0_attn_Wout"].reshape(NC, 128, 1024)
    w["l0a_bout"] = np.ascontiguousarray(np.asarray(inp["L0_attn_bout"]).reshape(NC, 128).T)
    w["l0f_wqkv"] = inp["L0_ffn_wqkv"].reshape(2, NC, 128, 1536)
    w["l0f_qs"] = inp["L0_ffn_q_scale"]
    w["l0f_ks"] = inp["L0_ffn_k_scale"]
    w["l0f_wo"] = inp["L0_ffn_wo"].reshape(2, 4, 128, 1024)
    w["l0f_lng"] = inp["L0_ffn_ln_g"].reshape(2, NC, 128)
    w["l0f_lnb"] = inp["L0_ffn_ln_b"].reshape(2, NC, 128)
    w1 = np.asarray(inp["L0_ffn_w1"])
    w1p = np.zeros((2, 1024, 2 * HIDP), np.float32)
    w1p[:, :, :2730] = w1[:, :, :2730]
    w1p[:, :, HIDP:HIDP + 2730] = w1[:, :, 2730:]
    w["l0f_w1"] = w1p.reshape(2, NC, 128, 2 * HIDP)
    b1 = np.asarray(inp["L0_ffn_b1"])
    b1p = np.zeros((2, 2 * HIDP), np.float32)
    b1p[:, :2730] = b1[:, :2730]
    b1p[:, HIDP:HIDP + 2730] = b1[:, 2730:]
    w["l0f_b1"] = np.ascontiguousarray(b1p.reshape(2, 2 * NHC, 128).transpose(0, 2, 1))
    w2 = np.asarray(inp["L0_ffn_w2"])
    w2p = np.zeros((2, HIDP, 1024), np.float32)
    w2p[:, :2730] = w2
    w["l0f_w2"] = w2p.reshape(2, NHC, 128, 1024)
    w["l0f_b2"] = np.ascontiguousarray(np.asarray(inp["L0_ffn_b2"]).reshape(2, NC, 128).transpose(0, 2, 1))
    for nm, key in (("wq", "WQ"), ("wk", "WK"), ("wv", "WV"), ("wo", "WO")):
        w["l1a_" + nm] = np.asarray(inp["L1_attn_" + key]).reshape(NC, 128, 1024)
    for nm, key in (("bq", "bQ"), ("bk", "bK"), ("bo", "bO")):
        w["l1a_" + nm] = np.ascontiguousarray(
            np.asarray(inp["L1_attn_" + key]).reshape(NC, 128).T)
    w["l1a_bvrow"] = np.asarray(inp["L1_attn_bV"]).reshape(1, 1024)
    w["l1a_wqr"] = inp["L1_attn_WQr"]
    w["l1a_wkr"] = inp["L1_attn_WKr"]
    w["l1f_gw"] = np.ascontiguousarray(np.asarray(inp["L1_ffn_gate_w"]).T
                                       ).reshape(NC, 128, 8)
    w["l1f_gbrow"] = np.asarray(inp["L1_ffn_gate_b"]).reshape(1, 8)
    w["l1f_w1t"] = np.ascontiguousarray(
        np.asarray(inp["L1_ffn_w1"]).transpose(0, 2, 1)).reshape(8, NC, 128, 4096)
    w["l1f_b1"] = np.ascontiguousarray(np.asarray(inp["L1_ffn_b1"]).reshape(8, 32, 128).transpose(0, 2, 1))
    w["l1f_w2t"] = np.ascontiguousarray(
        np.asarray(inp["L1_ffn_w2"]).transpose(0, 2, 1)).reshape(8, 32, 128, 1024)
    w["l1f_b2m"] = np.ascontiguousarray(np.asarray(inp["L1_ffn_b2"]).reshape(8, 8, 128).transpose(1, 0, 2))
    return {k: np.ascontiguousarray(np.asarray(v), dtype=np.float32)
            for k, v in w.items()}


def host_consts():
    c = {}
    c["c_id"] = np.eye(128, dtype=np.float32).astype(BF)
    p = np.zeros((128, 128), np.float32)
    for i in range(64):
        p[2 * i, 2 * i + 1] = -1.0
        p[2 * i + 1, 2 * i] = 1.0
    c["c_p128"] = p.T.astype(BF)
    pr = np.zeros((128, 128), np.float32)
    for h in range(2):
        o = 64 * h
        for i in range(32):
            pr[o + i, o + 32 + i] = -1.0
            pr[o + 32 + i, o + i] = 1.0
    c["c_prot"] = pr.T.astype(BF)
    d = 64
    inv = 1.0 / 10000.0 ** (np.arange(0, d, 2, dtype=np.float64) / d)
    t = np.arange(T, dtype=np.float64)
    fr = inv[:, None] * t[None, :]
    cd = np.repeat(np.cos(fr), 2, axis=0)
    sd = np.repeat(np.sin(fr), 2, axis=0)
    c["c_cd"] = np.tile(cd, (2, 1)).astype(np.float32)
    c["c_sd"] = np.tile(sd, (2, 1)).astype(np.float32)
    half = np.arange(0, d, 2, dtype=np.float64)
    invf = 1.0 / 10000.0 ** (half / d)
    fr2 = invf[:, None] * t[None, :]
    fr2 = np.concatenate([fr2, fr2], axis=0)
    base = (half + 0.4 * d) / (1.4 * d)
    power = (t - T // 2) / 256.0
    scale = base[:, None] ** power[None, :]
    scale = np.concatenate([scale, scale], axis=0)
    cos, sin = np.cos(fr2), np.sin(fr2)
    c["c_xqc"] = np.tile(cos * scale, (2, 1)).astype(np.float32)
    c["c_xqs"] = np.tile(sin * scale, (2, 1)).astype(np.float32)
    c["c_xkc"] = np.tile(cos / scale, (2, 1)).astype(np.float32)
    c["c_xks"] = np.tile(sin / scale, (2, 1)).astype(np.float32)
    kk = np.arange(128)[:, None]
    qq = np.arange(TT)[None, :]
    cm = np.zeros((4, 128, TT), np.float32)
    for m in range(4):
        cm[m] = np.where(m * 128 + kk <= qq, 0.0, NEG)
    c["c_cm"] = cm.astype(BF)
    jj = np.arange(1024)[None, :]
    mtm = np.zeros((4, 128, 1024), np.float32)
    for qc in range(4):
        qrel = qc * 128 + np.arange(128)[:, None]
        mtm[qc] = np.where((jj >= qrel) & (jj <= qrel + 512), 0.0, NEG)
    c["c_mtm"] = mtm.astype(BF)
    mkm = np.zeros((8, 128, TT), np.float32)
    for kt in range(8):
        j = kt * 128 + np.arange(128)[:, None]
        mkm[kt] = np.where((j >= qq) & (j <= qq + 512), 0.0, NEG)
    c["c_mkm"] = mkm.astype(BF)
    return c


def host_core_inputs(inp, core):
    x = np.asarray(inp["L0_attn_x"], np.float32)[4 * core:4 * core + 4]
    return {"xin": _fm(x)}



def emit_ln(k, ap, src, dst_bf, dst_f32, deps):
    nc = k.nc
    with ExitStack() as ctx:
        xt = ctx.enter_context(nc.sbuf_tensor("ln_xt", [128, 2, NC * TT], F32))
        xb = ctx.enter_context(nc.sbuf_tensor("ln_xb", [128, 2, NC * TT], BF16))
        sq = ctx.enter_context(nc.sbuf_tensor("ln_sq", [128, 2, NC * TT], BF16))
        ones = ctx.enter_context(nc.sbuf_tensor("ln_o", [128, 1], BF16))
        o1 = ctx.enter_context(nc.sbuf_tensor("ln_o1", [1, 128], F32))
        rows = ctx.enter_context(nc.sbuf_tensor("ln_r", [1, 2, 6, TT], F32))
        xnf = ctx.enter_context(nc.sbuf_tensor("ln_xf", [128, 2, NC * TT], F32))
        xnb = ctx.enter_context(nc.sbuf_tensor("ln_xB", [128, 2, NC * TT], BF16))
        eps = ctx.enter_context(nc.sbuf_tensor("ln_eps", [1, 1], F32))
        ps = ctx.enter_context(nc.psum_tensor("ln_ps", [128, 8, TT], F32))
        t0tok = k.do("v", "memset", ones[:], 1.0, deps=deps)
        t1tok = k.do("v", "memset", o1[:], 1.0, deps=deps)
        tep = k.do("v", "memset", eps[:], 1e-5, deps=deps)
        joins = {}
        for i in range(B * NTT):
            b, tt = divmod(i, NTT)
            u = i % 2
            war = joins.get(i - 2, list(deps) + [t0tok, t1tok])
            t0 = tt * TT
            X = xt[:, u, :]
            XB = xb[:, u, :]
            SQ = sq[:, u, :]
            XF = xnf[:, u, :]
            XO = xnb[:, u, :]
            R = rows[:, u, :, :]
            s0, s1, pA, pB = 4 * u, 4 * u + 1, 4 * u + 2, 4 * u + 3
            tl = k.dma("q", X.rearrange("p (c t) -> p c t", c=NC),
                       src[b, :, :, t0:t0 + TT].rearrange("c p t -> p c t"),
                       deps=war)
            tc = k.do("v", "tensor_copy", XB, X, deps=war + [tl])
            tq = k.do("s", "activation", SQ, XB, AF.Square, deps=war + [tc])
            for c in range(NC):
                ts0 = k.mm(ps[0:1, s0, :], ones[:], XB[:, c * TT:(c + 1) * TT],
                           start=(c == 0), stop=(c == NC - 1), deps=war + [tc])
            for c in range(NC):
                ts1 = k.mm(ps[0:1, s1, :], ones[:], SQ[:, c * TT:(c + 1) * TT],
                           start=(c == 0), stop=(c == NC - 1), deps=[tq])
            tm = k.do("v", "tensor_scalar_mul", R[:, 0, :], ps[0:1, s0, :],
                      1.0 / D, deps=war + [ts0])
            te = k.do("v", "tensor_scalar_mul", R[:, 1, :], ps[0:1, s1, :],
                      1.0 / D, deps=[ts1])
            tmm = k.do("v", "tensor_mul", R[:, 2, :], R[:, 0, :], R[:, 0, :],
                       deps=[tm])
            tv = k.do("v", "tensor_sub", R[:, 3, :], R[:, 1, :], R[:, 2, :],
                      deps=[te, tmm])
            tsd = k.do("s", "activation", R[:, 4, :], R[:, 3, :], AF.Sqrt,
                       bias=eps[:], deps=[tv, tep])
            tr = k.do("v", "reciprocal", R[:, 5, :], R[:, 4, :], deps=[tsd])
            tnm = k.do("v", "tensor_mul", R[:, 2, :], R[:, 0, :], R[:, 5, :],
                       deps=[tr])
            tnm2 = k.do("v", "tensor_scalar_mul", R[:, 2, :], R[:, 2, :], -1.0,
                        deps=[tnm])
            ta = k.mm(ps[:, pA, :], o1[:], R[:, 5, :], True, True,
                      deps=war + [tr])
            tb = k.mm(ps[:, pB, :], o1[:], R[:, 2, :], True, True, deps=[tnm2])
            for c in range(NC):
                sl = slice(c * TT, (c + 1) * TT)
                tt1 = k.do("v", "tensor_mul", XF[:, sl], X[:, sl], ps[:, pA, :],
                           deps=[ta])
                tt2 = k.do("v", "tensor_add", XF[:, sl], XF[:, sl], ps[:, pB, :],
                           deps=[tb, tt1])
                tt3 = k.do("v", "tensor_copy", XO[:, sl], XF[:, sl], deps=[tt2])
                k.dma("p", dst_bf[b, c, :, t0:t0 + TT], XO[:, sl], deps=[tt3])
                if dst_f32 is not None:
                    k.dma("p", dst_f32[b, c, :, t0:t0 + TT], XF[:, sl],
                          deps=[tt3])
            joins[i] = k.tip()
    return k.tip()


def emit_add(k, ap, a, bsrc, dst, deps):
    nc = k.nc
    with ExitStack() as ctx:
        xa = ctx.enter_context(nc.sbuf_tensor("ad_a", [128, 2, NC * TT], F32))
        xb = ctx.enter_context(nc.sbuf_tensor("ad_b", [128, 2, NC * TT], F32))
        xo = ctx.enter_context(nc.sbuf_tensor("ad_o", [128, 2, NC * TT], F32))
        joins = {}
        for i in range(B * NTT):
            b, tt = divmod(i, NTT)
            u = i % 2
            war = joins.get(i - 2, list(deps))
            t0 = tt * TT
            ta = k.dma("q", xa[:, u, :].rearrange("p (c t) -> p c t", c=NC),
                       a[b, :, :, t0:t0 + TT].rearrange("c p t -> p c t"),
                       deps=war)
            tb = k.dma("q", xb[:, u, :].rearrange("p (c t) -> p c t", c=NC),
                       bsrc[b, :, :, t0:t0 + TT].rearrange("c p t -> p c t"),
                       deps=war)
            to = k.do("v", "tensor_add", xo[:, u, :], xa[:, u, :], xb[:, u, :],
                      deps=war + [ta, tb])
            k.dma("p", dst[b, :, :, t0:t0 + TT].rearrange("c p t -> p c t"),
                  xo[:, u, :].rearrange("p (c t) -> p c t", c=NC), deps=[to])
            joins[i] = k.tip()
    return k.tip()


def _fmslice(t4, b, t0, n=TT):
    return t4[b, :, :, t0:t0 + n].rearrange("c p t -> p c t")


def emit_l0a_qkv(k, ap, deps):
    nc = k.nc
    xn1, qaug, kaug, vt0 = ap["xn1"], ap["qaug"], ap["kaug"], ap["vt0"]
    with ExitStack() as ctx:
        wqa = ctx.enter_context(nc.sbuf_tensor("aq_wqa", [128, NC * 64], BF16))
        wqb = ctx.enter_context(nc.sbuf_tensor("aq_wqb", [64, 1024], BF16))
        wka = ctx.enter_context(nc.sbuf_tensor("aq_wka", [128, NC * 32], BF16))
        wkbK = ctx.enter_context(nc.sbuf_tensor("aq_wkbK", [32, 256], BF16))
        wkbV = ctx.enter_context(nc.sbuf_tensor("aq_wkbV", [32, 256], BF16))
        stg = ctx.enter_context(nc.sbuf_tensor("aq_stg", [128, 2048], F32))
        cd = ctx.enter_context(nc.sbuf_tensor("aq_cd", [128, T], F32))
        sd = ctx.enter_context(nc.sbuf_tensor("aq_sd", [128, T], F32))
        pP = ctx.enter_context(nc.sbuf_tensor("aq_p", [128, 128], BF16))
        hh2 = ctx.enter_context(nc.sbuf_tensor("aq_hh2", [128, 2], BF16))
        onesr = ctx.enter_context(nc.sbuf_tensor("aq_or", [1, T], BF16))
        mk02 = ctx.enter_context(nc.sbuf_tensor("aq_mk", [2, 4], F32))
        mkst = ctx.enter_context(nc.sbuf_tensor("aq_mks", [2, 14], F32))
        xnb = ctx.enter_context(nc.sbuf_tensor("aq_xnb", [128, 2, NC * TT], BF16))
        t1 = ctx.enter_context(nc.sbuf_tensor("aq_t1", [64, T], BF16))
        t2 = ctx.enter_context(nc.sbuf_tensor("aq_t2", [32, T], BF16))
        rb = ctx.enter_context(nc.sbuf_tensor("aq_rb", [128, 2, TT], BF16))
        sqv = ctx.enter_context(nc.sbuf_tensor("aq_sq", [128, 2, TT], BF16))
        ta_f = ctx.enter_context(nc.sbuf_tensor("aq_taf", [128, 2, TT], F32))
        tb_f = ctx.enter_context(nc.sbuf_tensor("aq_tbf", [128, 2, TT], F32))
        ro = ctx.enter_context(nc.sbuf_tensor("aq_ro", [128, 2, TT], BF16))
        nr = ctx.enter_context(nc.sbuf_tensor("aq_nr", [2, 2, TT], F32))
        crw = ctx.enter_context(nc.sbuf_tensor("aq_crw", [2, 2, TT], BF16))
        vb = ctx.enter_context(nc.sbuf_tensor("aq_vb", [128, 2, 256], BF16))
        ps = ctx.enter_context(nc.psum_tensor("aq_ps", [128, 8, TT], F32))

        tw = cast_weights(k, wqa[:], ap["l0a_wqa"].rearrange("c p n -> p c n"),
                          deps, stage_f32=stg[:])
        tw = cast_weights(k, wqb[:], ap["l0a_wqb"][:, :], [tw],
                          stage_f32=stg[0:64, :])
        tw = cast_weights(k, wka[:], ap["l0a_wkva"].rearrange("c p n -> p c n"),
                          [tw], stage_f32=stg[:])
        tw = cast_weights(k, wkbK[:], ap["l0a_wkvbK"][:, :], [tw],
                          stage_f32=stg[0:32, :])
        tw = cast_weights(k, wkbV[:], ap["l0a_wkvbV"][:, :], [tw],
                          stage_f32=stg[0:32, :])
        tcd = k.dma("q", cd[:], ap["c_cd"][:, :], deps=deps)
        tsd = k.dma("q", sd[:], ap["c_sd"][:, :], deps=deps)
        tpp = k.dma("q", pP[:], ap["c_p128"][:, :], deps=deps)
        th0 = k.do("v", "memset", hh2[:], 0.0, deps=deps)
        th1 = k.do("v", "memset", hh2[0:64, 0:1], 1.0, deps=[th0])
        th2 = k.do("v", "memset", hh2[64:128, 1:2], 1.0, deps=[th1])
        tor = k.do("v", "memset", onesr[:], 1.0, deps=deps)
        pre = [tw, tcd, tsd, tpp, th2, tor]

        def rope_block(bank, u, tt, tbl_c, tbl_s, dep_mm, war):
            tqb = k.do("s", "activation", rb[:, u, :], ps[:, bank, :],
                       AF.Copy, deps=war + [dep_mm])
            tp = k.mm(ps[:, 6 + u, :], pP[:], rb[:, u, :], True, True,
                      deps=war + [tqb, tpp])
            t_a = k.do("v", "tensor_mul", ta_f[:, u, :], ps[:, 6 + u, :],
                       tbl_s[:, tt * TT:(tt + 1) * TT],
                       deps=war + [tp])
            t_b = k.do("v", "tensor_mul", tb_f[:, u, :], ps[:, bank, :],
                       tbl_c[:, tt * TT:(tt + 1) * TT],
                       deps=war + [dep_mm, tqb])
            t_o = k.do("v", "tensor_add", ro[:, u, :], ta_f[:, u, :],
                       tb_f[:, u, :], deps=[t_a, t_b])
            return t_o

        bj = list(deps) + pre
        for b in range(B):
            ph1 = []
            for tt in range(NTT):
                u = tt % 2
                war = ph1[-2][2] if tt >= 2 else bj
                tl = k.dma("q", xnb[:, u, :].rearrange("p (c t) -> p c t", c=NC),
                           _fmslice(xn1, b, tt * TT), deps=war)
                for c in range(NC):
                    m1 = k.mm(ps[0:64, u, :], wqa[:, c * 64:(c + 1) * 64],
                              xnb[:, u, c * TT:(c + 1) * TT],
                              start=(c == 0), stop=(c == NC - 1),
                              deps=war + [tl, tw])
                for c in range(NC):
                    m2 = k.mm(ps[0:32, 2 + u, :], wka[:, c * 32:(c + 1) * 32],
                              xnb[:, u, c * TT:(c + 1) * TT],
                              start=(c == 0), stop=(c == NC - 1), deps=[tl])
                c1 = k.do("v", "tensor_copy", t1[:, tt * TT:(tt + 1) * TT],
                          ps[0:64, u, :], deps=war + [m1])
                c2 = k.do("v", "tensor_copy", t2[:, tt * TT:(tt + 1) * TT],
                          ps[0:32, 2 + u, :], deps=[m2])
                ph1.append((c1, c2, k.tip()))
            tph1 = [x for tpl in ph1 for x in tpl[:2]]
            ph2 = []
            for tc in range(T // 128):
                u = tc % 2
                war = ph2[-2] if tc >= 2 else []
                tv = k.mm(ps[:, u, 0:256], t2[:, tc * 128:(tc + 1) * 128],
                          wkbV[:], True, True, deps=tph1 + war)
                tvc = k.do("v", "tensor_copy", vb[:, u, :], ps[:, u, 0:256],
                           deps=war + [tv])
                dt = [k.dma("p", vt0[b, kvh, tc * 128:(tc + 1) * 128, :],
                            vb[:, u, kvh * 64:(kvh + 1) * 64], deps=[tvc])
                      for kvh in range(4)]
                ph2.append([tvc] + dt)
            ph2end = k.tip()
            mk_toks = []
            ph3 = []
            for j2 in range(2):
                for tt in range(NTT):
                    i3 = j2 * NTT + tt
                    u = i3 % 2
                    war = ph3[-2] if i3 >= 2 else ph2end
                    tk = k.mm(ps[:, 2 + u, :], wkbK[:, j2 * 128:(j2 + 1) * 128],
                              t2[:, tt * TT:(tt + 1) * TT], True, True,
                              deps=tph1 + war)
                    tsq = k.do("s", "activation", sqv[:, u, :], ps[:, 2 + u, :],
                               AF.Square, deps=war + [tk])
                    tn = k.mm(ps[0:2, 4 + u, :], hh2[:], sqv[:, u, :], True,
                              True, deps=war + [tsq, th2])
                    tmx = k.do("v", "reduce_max", mkst[:, i3:i3 + 1],
                               ps[0:2, 4 + u, :], axis=AX.X, deps=war + [tn])
                    mk_toks.append(tmx)
                    t_o = rope_block(2 + u, u, tt, cd, sd, tk, war)
                    d1 = k.dma("p", kaug[b, 2 * j2, 0:64, tt * TT:(tt + 1) * TT],
                               ro[0:64, u, :], deps=[t_o])
                    d2 = k.dma("p", kaug[b, 2 * j2 + 1, 0:64,
                                    tt * TT:(tt + 1) * TT],
                               ro[64:128, u, :], deps=[t_o])
                    ph3.append([tmx, t_o, d1, d2])
            for kvh in range(4):
                k.dma("p", kaug[b, kvh, 64:65, :], onesr[:], deps=[tor])
            mk_dma = []
            for j2 in range(2):
                tmk = k.do("v", "reduce_max", mkst[:, 8 + j2:9 + j2],
                           mkst[:, j2 * NTT:(j2 + 1) * NTT], axis=AX.X,
                           deps=mk_toks)
                tms = k.do("s", "activation", mkst[:, 10 + j2:11 + j2],
                           mkst[:, 8 + j2:9 + j2], AF.Sqrt, deps=[tmk])
                tng = k.do("v", "tensor_scalar_mul", mkst[:, 12 + j2:13 + j2],
                           mkst[:, 10 + j2:11 + j2], -1.0, deps=[tms])
                for r in range(2):
                    for rr in range(2):
                        mk_dma.append(k.dma(
                            "p", mk02[rr:rr + 1, 2 * j2 + r:2 * j2 + r + 1],
                            mkst[r:r + 1, 12 + j2:13 + j2], deps=[tng]))
            ph3end = k.tip()
            ph4 = []
            for j in range(NC):
                kvh = j // 2
                for tt in range(NTT):
                    i4 = j * NTT + tt
                    u = i4 % 2
                    war = ph4[-2] if i4 >= 2 else ph3end
                    tq = k.mm(ps[:, 2 + u, :], wqb[:, j * 128:(j + 1) * 128],
                              t1[:, tt * TT:(tt + 1) * TT], True, True,
                              deps=tph1 + war)
                    tsq = k.do("s", "activation", sqv[:, u, :], ps[:, 2 + u, :],
                               AF.Square, deps=war + [tq])
                    tn = k.mm(ps[0:2, 4 + u, :], hh2[:], sqv[:, u, :], True,
                              True, deps=war + [tsq])
                    tsr = k.do("s", "activation", nr[:, u, :], ps[0:2, 4 + u, :],
                               AF.Sqrt, deps=war + [tn])
                    tcr = k.do("v", "tensor_scalar_mul", crw[:, u, :],
                               nr[:, u, :], mk02[0:2, kvh:kvh + 1],
                               deps=war + [tsr] + mk_dma)
                    e1 = k.dma("p", qaug[b, 2 * j, 64:65, tt * TT:(tt + 1) * TT],
                               crw[0:1, u, :], deps=[tcr])
                    e2 = k.dma("p", qaug[b, 2 * j + 1, 64:65,
                                    tt * TT:(tt + 1) * TT],
                               crw[1:2, u, :], deps=[tcr])
                    t_o = rope_block(2 + u, u, tt, cd, sd, tq, war)
                    e3 = k.dma("p", qaug[b, 2 * j, 0:64, tt * TT:(tt + 1) * TT],
                               ro[0:64, u, :], deps=[t_o])
                    e4 = k.dma("p", qaug[b, 2 * j + 1, 0:64,
                                    tt * TT:(tt + 1) * TT],
                               ro[64:128, u, :], deps=[t_o])
                    ph4.append([tcr, t_o, e1, e2, e3, e4])
            bj = k.tip()
    return k.tip()


def emit_l0a_att(k, ap, deps):
    nc = k.nc
    qaug, kaug, vt0, o0 = ap["qaug"], ap["kaug"], ap["vt0"], ap["o0"]
    with ExitStack() as ctx:
        ka = ctx.enter_context(nc.sbuf_tensor("aa_ka", [65, T], BF16))
        va = ctx.enter_context(nc.sbuf_tensor("aa_va", [128, 16 * 65], BF16))
        qa = ctx.enter_context(nc.sbuf_tensor("aa_qa", [65, 2, TT], BF16))
        cmm = ctx.enter_context(nc.sbuf_tensor("aa_cm", [128, 4 * TT], BF16))
        ex = ctx.enter_context(nc.sbuf_tensor("aa_ex", [128, 2, TT], BF16))
        rec = ctx.enter_context(nc.sbuf_tensor("aa_rec", [1, TT], F32))
        osb = ctx.enter_context(nc.sbuf_tensor("aa_osb", [64, TT], F32))
        onb = ctx.enter_context(nc.sbuf_tensor("aa_onb", [64, TT], BF16))
        o1f = ctx.enter_context(nc.sbuf_tensor("aa_o1f", [1, 64], F32))
        ps = ctx.enter_context(nc.psum_tensor("aa_ps", [128, 8, TT], F32))
        tcm = k.dma("q", cmm[:], ap["c_cm"].rearrange("m p t -> p m t"),
                    deps=deps)
        tmv = k.do("v", "memset", va[:], 1.0, deps=deps)
        to1 = k.do("v", "memset", o1f[:], 1.0, deps=deps)
        pre = [tcm, tmv, to1]
        hb_join = list(deps) + pre
        for b in range(B):
            for kvh in range(4):
                tka = k.dma("q", ka[:], kaug[b, kvh, :, :], deps=hb_join)
                tva = [k.dma("q", va[:, kt * 65:kt * 65 + 64],
                             vt0[b, kvh, kt * 128:(kt + 1) * 128, :],
                             deps=hb_join) for kt in range(16)]
                qtj = hb_join + [tka, tva[-1]]
                for hh in range(4):
                    h = kvh * 4 + hh
                    for qt in range(NTT):
                        uq = qt % 2
                        tqa = k.dma("q", qa[:, uq, :],
                                    qaug[b, h, :, qt * TT:(qt + 1) * TT],
                                    deps=qtj)
                        nkt = 4 * qt + 4
                        evs = []
                        for kt in range(nkt):
                            u = kt % 2
                            deps_s = [tka, tqa]
                            if kt >= 2:
                                deps_s.append(evs[-2][0])
                            else:
                                deps_s += qtj
                            tsc = k.mm(ps[:, u, :],
                                       ka[:, kt * 128:(kt + 1) * 128],
                                       qa[:, uq, :], True, True, deps=deps_s)
                            dmask = kt - 4 * qt
                            if dmask >= 0:
                                tsc = k.do("v", "tensor_add", ps[:, u, :],
                                           ps[:, u, :],
                                           cmm[:, dmask * TT:(dmask + 1) * TT],
                                           deps=[tsc, tcm])
                            texdeps = [tsc]
                            if kt >= 2:
                                texdeps.append(evs[-2][1])
                            tex = k.do("s", "activation", ex[:, u, :],
                                       ps[:, u, :], AF.Exp, scale=0.125,
                                       deps=texdeps)
                            tav = k.mm(ps[0:65, 2, :],
                                       va[:, kt * 65:kt * 65 + 65],
                                       ex[:, u, :], start=(kt == 0),
                                       stop=(kt == nkt - 1),
                                       deps=[tex, tmv] + tva +
                                       (qtj if kt == 0 else []), mark=True)
                            evs.append((tex, tav))
                        trc = k.do("v", "reciprocal", rec[:], ps[64:65, 2, :],
                                   deps=[tav] + qtj)
                        tbc = k.mm(ps[0:64, 3, :], o1f[:], rec[:], True, True,
                                   deps=[trc, to1] + qtj)
                        tcp = k.do("v", "tensor_copy", osb[:], ps[0:64, 2, :],
                                   deps=[tav] + qtj)
                        tmu = k.do("v", "tensor_mul", onb[:], osb[:],
                                   ps[0:64, 3, :], deps=[tbc, tcp])
                        tst = k.dma("p", o0[b, h // 2,
                                         (h % 2) * 64:(h % 2) * 64 + 64,
                                         qt * TT:(qt + 1) * TT],
                                    onb[:], deps=[tmu])
                        qtj = [tmu, tst, trc, tcp]
                hb_join = k.tip()
    return k.tip()


def emit_matproj_add(k, ap, src_bf, w_ap, b_ap, add_src, dst, deps,
                     nin=NC, tag="mp"):
    nc = k.nc
    with ExitStack() as ctx:
        wsb = ctx.enter_context(nc.sbuf_tensor(tag + "_w", [128, nin * 1024], BF16))
        stg = ctx.enter_context(nc.sbuf_tensor(tag + "_st", [128, 2048], F32))
        bsb = ctx.enter_context(nc.sbuf_tensor(tag + "_b", [128, NC], F32))
        xsb = ctx.enter_context(nc.sbuf_tensor(tag + "_x", [128, 2, nin * TT], BF16))
        asb = ctx.enter_context(nc.sbuf_tensor(tag + "_a", [128, 2, NC * TT], F32))
        ysb = ctx.enter_context(nc.sbuf_tensor(tag + "_y", [128, 2, NC * TT], F32))
        ps = ctx.enter_context(nc.psum_tensor(tag + "_ps", [128, 8, TT], F32))
        tw = cast_weights(k, wsb[:], w_ap.rearrange("c p n -> p c n"),
                          deps, stage_f32=stg[:])
        if b_ap is not None:
            tb = k.dma("q", bsb[:], b_ap[:, :], deps=deps)
        else:
            tb = k.do("v", "memset", bsb[:], 0.0, deps=deps)
        joins = {}
        for i in range(B * NTT):
            b, tt = divmod(i, NTT)
            u = i % 2
            war = joins.get(i - 2, list(deps) + [tw, tb])
            t0 = tt * TT
            tx = k.dma("q", xsb[:, u, :].rearrange("p (c t) -> p c t", c=nin),
                       src_bf[b, :, :, t0:t0 + TT].rearrange("c p t -> p c t"),
                       deps=war)
            tad = k.dma("q", asb[:, u, :].rearrange("p (c t) -> p c t", c=NC),
                        _fmslice(add_src, b, t0), deps=war)
            bank_free = {}
            for j in range(NC):
                bank = 4 * u + j % 4
                wdeps = war + [tx, tw]
                if j % 4 in bank_free:
                    wdeps = wdeps + [bank_free[j % 4]]
                for c in range(nin):
                    tm = k.mm(ps[:, bank, :],
                              wsb[:, c * 1024 + j * 128:c * 1024 + j * 128 + 128],
                              xsb[:, u, c * TT:(c + 1) * TT],
                              start=(c == 0), stop=(c == nin - 1), deps=wdeps)
                ty = k.do("s", "activation", ysb[:, u, j * TT:(j + 1) * TT],
                          ps[:, bank, :], AF.Identity, bias=bsb[:, j:j + 1],
                          deps=war + [tm, tb])
                bank_free[j % 4] = ty
                tz = k.do("v", "tensor_add", ysb[:, u, j * TT:(j + 1) * TT],
                          ysb[:, u, j * TT:(j + 1) * TT],
                          asb[:, u, j * TT:(j + 1) * TT],
                          deps=war + [ty, tad])
                k.dma("p", dst[b, j, :, t0:t0 + TT], ysb[:, u, j * TT:(j + 1) * TT],
                      deps=[tz])
            joins[i] = k.tip()
    return k.tip()


def _shift_load(k, stage_u, src, b, t0, deps):
    toks = []
    toks.append(k.dma("q", stage_u[:, 0:4 * TT].rearrange("p (c t) -> p c t", c=4),
                      src[b, 0:4, :, t0:t0 + TT].rearrange("c p t -> p c t"),
                      deps=deps))
    if t0 > 0:
        toks.append(k.dma("q",
                          stage_u[:, 4 * TT:8 * TT].rearrange("p (c t) -> p c t", c=4),
                          src[b, 4:8, :, t0 - 1:t0 + TT - 1]
                          .rearrange("c p t -> p c t"), deps=deps))
    else:
        toks.append(k.dma("q",
                          stage_u[:, 4 * TT:8 * TT]
                          .rearrange("p (c t) -> p c t", c=4)[:, :, 1:TT],
                          src[b, 4:8, :, 0:TT - 1].rearrange("c p t -> p c t"),
                          deps=deps))
        for c2 in range(4):
            toks.append(k.do("v", "memset",
                             stage_u[:, (4 + c2) * TT:(4 + c2) * TT + 1], 0.0,
                             deps=deps))
    return toks


def emit_l0f_qkv(k, ap, l, deps):
    nc = k.nc
    src = ap["xfa"] if l == 0 else ap["xfb"]
    qf, kf, vt1 = ap["qf"], ap["kf"], ap["vt1"]
    with ExitStack() as ctx:
        wq = ctx.enter_context(nc.sbuf_tensor("fq_w", [128, NC * 1536], BF16))
        stg = ctx.enter_context(nc.sbuf_tensor("fq_st", [128, 2, NC * TT], F32))
        hb = ctx.enter_context(nc.sbuf_tensor("fq_hb", [128, NC * T], BF16))
        xqc = ctx.enter_context(nc.sbuf_tensor("fq_xqc", [128, T], F32))
        xqs = ctx.enter_context(nc.sbuf_tensor("fq_xqs", [128, T], F32))
        xkc = ctx.enter_context(nc.sbuf_tensor("fq_xkc", [128, T], F32))
        xks = ctx.enter_context(nc.sbuf_tensor("fq_xks", [128, T], F32))
        prt = ctx.enter_context(nc.sbuf_tensor("fq_prt", [128, 128], BF16))
        hh2 = ctx.enter_context(nc.sbuf_tensor("fq_hh2", [128, 2], BF16))
        qsel = ctx.enter_context(nc.sbuf_tensor("fq_qsel", [2, 128], F32))
        ksel = ctx.enter_context(nc.sbuf_tensor("fq_ksel", [2, 128], F32))
        sqv = ctx.enter_context(nc.sbuf_tensor("fq_sq", [128, 2, TT], BF16))
        nrw = ctx.enter_context(nc.sbuf_tensor("fq_nr", [2, 2, TT], F32))
        bcf = ctx.enter_context(nc.sbuf_tensor("fq_bc", [128, 2, TT], F32))
        qn = ctx.enter_context(nc.sbuf_tensor("fq_qn", [128, 2, TT], F32))
        qb = ctx.enter_context(nc.sbuf_tensor("fq_qb", [128, 2, TT], BF16))
        taf = ctx.enter_context(nc.sbuf_tensor("fq_ta", [128, 2, TT], F32))
        tbf = ctx.enter_context(nc.sbuf_tensor("fq_tb", [128, 2, TT], F32))
        ro = ctx.enter_context(nc.sbuf_tensor("fq_ro", [128, 2, TT], BF16))
        vb = ctx.enter_context(nc.sbuf_tensor("fq_vb", [128, 2, TT], BF16))
        ps = ctx.enter_context(nc.psum_tensor("fq_ps", [128, 8, TT], F32))
        tw = cast_weights(k, wq[:], ap["l0f_wqkv"][l].rearrange("c p n -> p c n"),
                          deps, stage_f32=stg[:, 0, :])
        tc1 = k.dma("q", xqc[:], ap["c_xqc"][:, :], deps=deps)
        tc2 = k.dma("q", xqs[:], ap["c_xqs"][:, :], deps=deps)
        tc3 = k.dma("q", xkc[:], ap["c_xkc"][:, :], deps=deps)
        tc4 = k.dma("q", xks[:], ap["c_xks"][:, :], deps=deps)
        tc5 = k.dma("q", prt[:], ap["c_prot"][:, :], deps=deps)
        th0 = k.do("v", "memset", hh2[:], 0.0, deps=deps)
        th1 = k.do("v", "memset", hh2[0:64, 0:1], 1.0, deps=[th0])
        th2 = k.do("v", "memset", hh2[64:128, 1:2], 1.0, deps=[th1])
        tq0 = k.do("v", "memset", qsel[:], 0.0, deps=deps)
        tk0 = k.do("v", "memset", ksel[:], 0.0, deps=deps)
        tq1 = k.dma("q", qsel[0:1, 0:64], ap["l0f_qs"][l:l + 1, :], deps=[tq0])
        tq2 = k.dma("q", qsel[1:2, 64:128], ap["l0f_qs"][l:l + 1, :], deps=[tq0])
        tk1 = k.dma("q", ksel[0:1, 0:64], ap["l0f_ks"][l:l + 1, :], deps=[tk0])
        tk2 = k.dma("q", ksel[1:2, 64:128], ap["l0f_ks"][l:l + 1, :], deps=[tk0])
        pre = [tw, tc1, tc2, tc3, tc4, tc5, th2, tq1, tq2, tk1, tk2]
        bj = list(deps) + pre
        bref = [0]

        def qk_head_block(j, tt, i, col0, sel, tbc, tbs, dst, war):
            b = bref[0]
            u = i % 2
            t0 = tt * TT
            tqm = None
            for c in range(NC):
                tqm = k.mm(ps[:, u, :],
                           wq[:, c * 1536 + col0 + j * 128:
                              c * 1536 + col0 + j * 128 + 128],
                           hb[:, c * T + t0:c * T + t0 + TT],
                           start=(c == 0), stop=(c == NC - 1),
                           deps=war + [tw])
            tsq = k.do("s", "activation", sqv[:, u, :], ps[:, u, :], AF.Square,
                       deps=war + [tqm])
            tn = k.mm(ps[0:2, 2 + u, :], hh2[:], sqv[:, u, :], True, True,
                      deps=war + [tsq, th2])
            tsr = k.do("s", "activation", nrw[:, u, :], ps[0:2, 2 + u, :],
                       AF.Sqrt, deps=war + [tn])
            tcl = k.do("v", "tensor_scalar_max", nrw[:, u, :], nrw[:, u, :],
                       1e-12, deps=war + [tsr])
            trc = k.do("v", "reciprocal", nrw[:, u, :], nrw[:, u, :],
                       deps=[tcl])
            tbm = k.mm(ps[:, 4 + u, :], sel, nrw[:, u, :], True, True,
                       deps=war + [trc])
            tbe = k.do("v", "tensor_copy", bcf[:, u, :], ps[:, 4 + u, :],
                       deps=war + [tbm])
            tqn = k.do("v", "tensor_mul", qn[:, u, :], ps[:, u, :],
                       bcf[:, u, :], deps=[tbe, tsq])
            tqb = k.do("s", "activation", qb[:, u, :], qn[:, u, :], AF.Copy,
                       deps=war + [tqn])
            tp = k.mm(ps[:, 6 + u, :], prt[:], qb[:, u, :], True, True,
                      deps=war + [tqb, tc5])
            t_a = k.do("v", "tensor_mul", taf[:, u, :], ps[:, 6 + u, :],
                       tbs[:, t0:t0 + TT], deps=war + [tp])
            t_b = k.do("v", "tensor_mul", tbf[:, u, :], qn[:, u, :],
                       tbc[:, t0:t0 + TT], deps=war + [t_a])
            t_o = k.do("v", "tensor_add", ro[:, u, :], taf[:, u, :],
                       tbf[:, u, :], deps=[t_b])
            td = k.dma("p", dst[b, j, :, t0:t0 + TT], ro[:, u, :], deps=[t_o])
            return [t_o, td, tqb, tqm]

        for b in range(B):
            bref[0] = b
            htoks = []
            for tt in range(NTT):
                u = tt % 2
                war = [htoks[-2][1]] if tt >= 2 else bj
                tl = _shift_load(k, stg[:, u, :], src, b, tt * TT, war)
                tcst = k.do("v", "tensor_copy",
                            hb[:, :].rearrange("p (c t) -> p c t", c=NC)
                            [:, :, tt * TT:(tt + 1) * TT],
                            stg[:, u, :].rearrange("p (c t) -> p c t", c=NC),
                            deps=war + tl)
                htoks.append((tl, tcst))
            hdone = [h[1] for h in htoks]
            vph = []
            for tt in range(NTT):
                for c4 in range(4):
                    iv = tt * 4 + c4
                    u = iv % 2
                    war = vph[-2] if iv >= 2 else hdone
                    tvm = None
                    for c in range(NC):
                        tvm = k.mm(ps[:, u, :],
                                   hb[:, c * T + tt * TT + c4 * 128:
                                      c * T + tt * TT + c4 * 128 + 128],
                                   wq[:, c * 1536 + 1024:c * 1536 + 1536],
                                   start=(c == 0), stop=(c == NC - 1),
                                   deps=hdone + list(war))
                    tvc = k.do("v", "tensor_copy", vb[:, u, :], ps[:, u, :],
                               deps=list(war) + [tvm])
                    tds = [k.dma("p",
                                 vt1[b, h, tt * TT + c4 * 128:
                                     tt * TT + c4 * 128 + 128, :],
                                 vb[:, u, h * 64:(h + 1) * 64], deps=[tvc])
                           for h in range(8)]
                    vph.append([tvc] + tds)
            vend = k.tip()
            kph = []
            for j in range(4):
                for tt in range(NTT):
                    ik = j * NTT + tt
                    war = kph[-2] if ik >= 2 else vend
                    kph.append(qk_head_block(j, tt, ik, 512, ksel[:], xkc, xks,
                                             kf, list(war)))
            kend = k.tip()
            qph = []
            for j in range(4):
                for tt in range(NTT):
                    iq = j * NTT + tt
                    war = qph[-2] if iq >= 2 else kend
                    qph.append(qk_head_block(j, tt, iq, 0, qsel[:], xqc, xqs,
                                             qf, list(war)))
            bj = k.tip()
    return k.tip()


def emit_l0f_att(k, ap, l, deps):
    nc = k.nc
    qf, kf, vt1, of = ap["qf"], ap["kf"], ap["vt1"], ap["of"]
    with ExitStack() as ctx:
        mtm = ctx.enter_context(nc.sbuf_tensor("fa_mtm", [128, 4 * 1024], BF16))
        mkm = ctx.enter_context(nc.sbuf_tensor("fa_mkm", [128, 8 * TT], BF16))
        idn = ctx.enter_context(nc.sbuf_tensor("fa_id", [128, 128], BF16))
        ka2 = ctx.enter_context(nc.sbuf_tensor("fa_k", [65, 2, 1024], BF16))
        va1 = ctx.enter_context(nc.sbuf_tensor("fa_v", [128, 2, 8 * 65], BF16))
        qa2 = ctx.enter_context(nc.sbuf_tensor("fa_qa", [65, 2, TT], BF16))
        junk = ctx.enter_context(nc.sbuf_tensor("fa_j", [128, 2, 1024], BF16))
        mxc = ctx.enter_context(nc.sbuf_tensor("fa_mx", [128, 2, 4], F32))
        mxb = ctx.enter_context(nc.sbuf_tensor("fa_mb", [128, 2, 4], BF16))
        mxr = ctx.enter_context(nc.sbuf_tensor("fa_mr", [4, 2, 128], BF16))
        ex = ctx.enter_context(nc.sbuf_tensor("fa_ex", [128, 2, 2, TT], BF16))
        rec = ctx.enter_context(nc.sbuf_tensor("fa_rec", [1, 2, TT], F32))
        osb = ctx.enter_context(nc.sbuf_tensor("fa_osb", [64, 2, TT], F32))
        onb = ctx.enter_context(nc.sbuf_tensor("fa_onb", [64, 2, TT], BF16))
        o1f = ctx.enter_context(nc.sbuf_tensor("fa_o1f", [1, 64], F32))
        psA = ctx.enter_context(nc.psum_tensor("fa_psA", [128, 2, 2, TT], F32))
        psO = ctx.enter_context(nc.psum_tensor("fa_psO", [128, 2, TT], F32))
        psB = ctx.enter_context(nc.psum_tensor("fa_psB", [128, TT], F32))
        psT = ctx.enter_context(nc.psum_tensor("fa_psT", [4, 128], BF16))
        tm1 = k.dma("q", mtm[:].rearrange("p (m t) -> p m t", m=4),
                    ap["c_mtm"].rearrange("m p t -> p m t"), deps=deps)
        tm2 = k.dma("q", mkm[:].rearrange("p (m t) -> p m t", m=8),
                    ap["c_mkm"].rearrange("m p t -> p m t"), deps=deps)
        tid = k.dma("q", idn[:], ap["c_id"][:, :], deps=deps)
        tv1 = k.do("v", "memset", va1[:], 1.0, deps=deps)
        tk1 = k.do("v", "memset", ka2[64:65, :, :], 1.0, deps=deps)
        to1 = k.do("v", "memset", o1f[:], 1.0, deps=deps)
        pre = [tm1, tm2, tid, tv1, tk1, to1]
        joins = {}
        prev_bc = None
        prev_tr = None
        idx = 0
        for b in range(B):
            for h in range(8):
                for w in range(4):
                    u = idx % 2
                    war = joins.get(idx - 2, list(deps) + pre)
                    jc, jp = h // 2, (h % 2) * 64
                    kt0 = 4 if w == 0 else 0
                    tql = k.dma("q", qa2[0:64, u, :],
                                qf[b, jc, jp:jp + 64, w * TT:(w + 1) * TT],
                                deps=war)
                    if w == 0:
                        tkl = k.dma("q", ka2[0:64, u, 512:1024],
                                    kf[b, jc, jp:jp + 64, 0:TT], deps=war)
                    else:
                        tkl = k.dma("q", ka2[0:64, u, :],
                                    kf[b, jc, jp:jp + 64,
                                       (w - 1) * TT:(w + 1) * TT], deps=war)
                    tvl = [k.dma("q", va1[:, u, kt * 65:kt * 65 + 64],
                                 vt1[b, h, (w - 1) * TT + kt * 128:
                                     (w - 1) * TT + kt * 128 + 128, :],
                                 deps=war) for kt in range(kt0, 8)]
                    p1 = []
                    for qc in range(4):
                        wq1 = [p1[-1][1]] if p1 else war
                        halves = range(1, 2) if w == 0 else range(2)
                        tms = None
                        for half in halves:
                            tms = k.mm(psA[:, u, half, :],
                                       qa2[0:64, u, qc * 128:(qc + 1) * 128],
                                       ka2[0:64, u, half * TT:(half + 1) * TT],
                                       True, True,
                                       deps=war + wq1 + [tql, tkl])
                        if w == 0:
                            tad = k.do("v", "tensor_add", psA[:, u, 1, :],
                                       psA[:, u, 1, :],
                                       mtm[:, qc * 1024 + 512:
                                           qc * 1024 + 1024],
                                       deps=war + [tms, tm1])
                            ttr = k.do("v", "reduce_max",
                                       mxc[:, u, qc:qc + 1],
                                       psA[:, u, 1, :], axis=AX.X,
                                       deps=[tad])
                        else:
                            tad = k.do("v", "tensor_add", psA[:, u, :, :],
                                       psA[:, u, :, :],
                                       mtm[:, qc * 1024:(qc + 1) * 1024],
                                       deps=war + [tms, tm1])
                            ttr = k.do("v", "reduce_max",
                                       mxc[:, u, qc:qc + 1],
                                       psA[:, u, :, :]
                                       .rearrange("p a b -> p (a b)"),
                                       axis=AX.X, deps=[tad])
                        p1.append((tms, ttr))
                    tng = k.do("v", "tensor_scalar_mul", mxb[:, u, :],
                               mxc[:, u, :], -1.0, deps=[p1[-1][1]])
                    ttp = k.do("t", "transpose", psT[:], mxb[:, u, :], idn[:],
                               deps=war + [tng, tid, prev_tr])
                    tmr = k.do("v", "tensor_copy", mxr[:, u, :], psT[:],
                               deps=war + [ttp, prev_tr])
                    prev_tr = tmr
                    tdm = k.dma("p", qa2[64:65, u, :], mxr[:, u, :],
                                deps=[tmr])
                    evs = []
                    tav = None
                    for kt in range(kt0, 8):
                        u2 = kt % 2
                        dsc = [tdm, tkl] + [p1[-1][1], p1[-2][1] if len(p1) > 1 else None]
                        if len(evs) >= 2:
                            dsc.append(evs[-2][0])
                        tsc = k.mm(psA[:, u, u2, :],
                                   ka2[:, u, kt * 128:(kt + 1) * 128],
                                   qa2[:, u, :], True, True, deps=dsc)
                        tmk = k.do("v", "tensor_add", psA[:, u, u2, :],
                                   psA[:, u, u2, :],
                                   mkm[:, kt * TT:(kt + 1) * TT],
                                   deps=[tsc, tm2])
                        dex = [tmk]
                        if len(evs) >= 2:
                            dex.append(evs[-2][1])
                        tex = k.do("s", "activation", ex[:, u, u2, :],
                                   psA[:, u, u2, :], AF.Exp, scale=8.0,
                                   deps=dex)
                        tav = k.mm(psO[0:65, u, :],
                                   va1[:, u, kt * 65:kt * 65 + 65],
                                   ex[:, u, u2, :], start=(kt == kt0),
                                   stop=(kt == 7),
                                   deps=[tex, tv1] + tvl +
                                   (war + [tmr] if kt == kt0 else []),
                                   mark=True)
                        evs.append((tex, tav))
                    trc = k.do("v", "reciprocal", rec[:, u, :],
                               psO[64:65, u, :], deps=war + [tav])
                    tbc = k.mm(psB[0:64, :], o1f[:], rec[:, u, :], True, True,
                               deps=[trc, to1, prev_bc])
                    tcp = k.do("v", "tensor_copy", osb[:, u, :],
                               psO[0:64, u, :], deps=war + [tav])
                    tmu = k.do("v", "tensor_mul", onb[:, u, :], osb[:, u, :],
                               psB[0:64, :], deps=[tbc, tcp])
                    prev_bc = tmu
                    tst = k.dma("p", of[b, jc, jp:jp + 64, w * TT:(w + 1) * TT],
                                onb[:, u, :], deps=[tmu])
                    joins[idx] = k.tip()
                    idx += 1
    return k.tip()


def emit_l0f_w1(k, ap, l, src_name, deps):
    nc = k.nc
    src, mgl = ap[src_name], ap["mgl"]
    with ExitStack() as ctx:
        w1 = ctx.enter_context(nc.sbuf_tensor("f1_w1", [128, NC * 2 * HIDP], BF16))
        b1 = ctx.enter_context(nc.sbuf_tensor("f1_b1", [128, 2 * NHC], F32))
        glb = ctx.enter_context(nc.sbuf_tensor("f1_glb", [2, NC * 128], F32))
        stg = ctx.enter_context(nc.sbuf_tensor("f1_st", [128, 2, NC * TT], F32))
        hbt = ctx.enter_context(nc.sbuf_tensor("f1_hb", [128, NC * TT], BF16))
        sq = ctx.enter_context(nc.sbuf_tensor("f1_sq", [128, NC * TT], BF16))
        ones = ctx.enter_context(nc.sbuf_tensor("f1_on", [128, 1], BF16))
        rows = ctx.enter_context(nc.sbuf_tensor("f1_rw", [1, 2, 6, TT], F32))
        rh = ctx.enter_context(nc.sbuf_tensor("f1_rh", [2, 2, 2, TT], F32))
        eps = ctx.enter_context(nc.sbuf_tensor("f1_eps", [1, 1], F32))
        hnf = ctx.enter_context(nc.sbuf_tensor("f1_hnf", [128, TT], F32))
        hnb = ctx.enter_context(nc.sbuf_tensor("f1_hnb", [128, 2, NC * TT], BF16))
        gl = ctx.enter_context(nc.sbuf_tensor("f1_gl", [128, 2, 2, TT], BF16))
        ab = ctx.enter_context(nc.sbuf_tensor("f1_ab", [128, 2, 2, TT], BF16))
        mo = ctx.enter_context(nc.sbuf_tensor("f1_mo", [128, 2, 2, TT], BF16))
        ps = ctx.enter_context(nc.psum_tensor("f1_ps", [128, 8, TT], F32))
        tw1 = cast_weights(k, w1[:], ap["l0f_w1"][l].rearrange("c p n -> p c n"),
                           deps, stage_f32=stg[:, 0, :])
        tb1 = k.dma("q", b1[:], ap["l0f_b1"][l], deps=deps)
        tg1 = k.dma("q", glb[0:1, :],
                    ap["l0f_lng"][l].rearrange("c p -> (c p)")[None, :],
                    deps=deps)
        tg2 = k.dma("q", glb[1:2, :],
                    ap["l0f_lnb"][l].rearrange("c p -> (c p)")[None, :],
                    deps=deps)
        ton = k.do("v", "memset", ones[:], 1.0, deps=deps)
        tep = k.do("v", "memset", eps[:], 1e-5, deps=deps)
        trh = k.do("v", "memset", rh[:], 0.0, deps=deps)
        tone = k.do("v", "memset", hnf[:1, :], 1.0, deps=deps)
        trh2a = k.dma("p", rh[1:2, 0, 1, :], hnf[0:1, 0:TT], deps=[trh, tone])
        trh2b = k.dma("p", rh[1:2, 1, 1, :], hnf[0:1, 0:TT], deps=[trh, tone])
        trh2 = k.do("v", "memset", hnf[:1, :], 0.0,
                    deps=[trh2a, trh2b])
        pre = [tw1, tb1, tg1, tg2, ton, tep, trh2a, trh2b, trh2]
        joins = {}
        for i in range(B * NTT):
            b, tt = divmod(i, NTT)
            u = i % 2
            war = joins.get(i - 2, list(deps) + pre)
            t0 = tt * TT
            tl = _shift_load(k, stg[:, u, :], src, b, t0, war)
            tcst = k.do("v", "tensor_copy", hbt[:], stg[:, u, :],
                        deps=war + tl)
            tsqr = k.do("s", "activation", sq[:], hbt[:], AF.Square,
                        deps=war + [tcst])
            for c in range(NC):
                ts0 = k.mm(ps[0:1, 4 * u, :], ones[:],
                           hbt[:, c * TT:(c + 1) * TT],
                           start=(c == 0), stop=(c == NC - 1),
                           deps=war + [tcst, ton])
            for c in range(NC):
                ts1 = k.mm(ps[0:1, 4 * u + 1, :], ones[:],
                           sq[:, c * TT:(c + 1) * TT],
                           start=(c == 0), stop=(c == NC - 1), deps=[tsqr])
            R = rows[:, u, :, :]
            tm = k.do("v", "tensor_scalar_mul", R[:, 0, :], ps[0:1, 4 * u, :],
                      1.0 / D, deps=war + [ts0])
            te = k.do("v", "tensor_scalar_mul", R[:, 1, :],
                      ps[0:1, 4 * u + 1, :], 1.0 / D, deps=[ts1])
            tmm = k.do("v", "tensor_mul", R[:, 2, :], R[:, 0, :], R[:, 0, :],
                       deps=[tm])
            tv = k.do("v", "tensor_sub", R[:, 3, :], R[:, 1, :], R[:, 2, :],
                      deps=[te, tmm])
            tsd = k.do("s", "activation", R[:, 4, :], R[:, 3, :], AF.Sqrt,
                       bias=eps[:], deps=war + [tv, tep])
            tr_ = k.do("v", "reciprocal", R[:, 5, :], R[:, 4, :], deps=[tsd])
            tra = k.do("v", "tensor_copy", rh[0:1, u, 0, :], R[:, 5, :],
                       deps=[tr_])
            tnm = k.do("v", "tensor_mul", R[:, 2, :], R[:, 0, :], R[:, 5, :],
                       deps=[tr_])
            tnm2 = k.do("v", "tensor_scalar_mul", rh[0:1, u, 1, :], R[:, 2, :],
                        -1.0, deps=[tnm])
            hn_toks = []
            for c in range(NC):
                ta = k.mm(ps[:, 4 * u + 2, :], glb[:, c * 128:(c + 1) * 128],
                          rh[:, u, 0, :], True, True,
                          deps=war + [tra, trh2, tg2] +
                          (hn_toks[-1:] if hn_toks else []))
                tb_ = k.mm(ps[:, 4 * u + 3, :], glb[:, c * 128:(c + 1) * 128],
                           rh[:, u, 1, :], True, True, deps=[tnm2])
                th1 = k.do("v", "tensor_mul", hnf[:],
                           stg[:, u, c * TT:(c + 1) * TT], ps[:, 4 * u + 2, :],
                           deps=war + [ta] + tl)
                th2_ = k.do("v", "tensor_add", hnf[:], hnf[:],
                            ps[:, 4 * u + 3, :], deps=[tb_, th1])
                th3 = k.do("v", "tensor_copy", hnb[:, u, c * TT:(c + 1) * TT],
                           hnf[:], deps=[th2_])
                hn_toks.append(th3)
            gm_prev = {}
            for j in range(NHC):
                jp = j % 2
                ba = 4 * u + jp
                bg = 4 * u + 2 + jp
                wdeps = war + [tw1, hn_toks[-1]]
                prev = gm_prev.get(jp)
                if prev is not None:
                    wdeps = wdeps + [prev[0]]
                tma = tmg = None
                for c in range(NC):
                    tma = k.mm(ps[:, ba, :],
                               w1[:, c * 2 * HIDP + j * 128:
                                  c * 2 * HIDP + j * 128 + 128],
                               hnb[:, u, c * TT:(c + 1) * TT],
                               start=(c == 0), stop=(c == NC - 1), deps=wdeps)
                for c in range(NC):
                    tmg = k.mm(ps[:, bg, :],
                               w1[:, c * 2 * HIDP + HIDP + j * 128:
                                  c * 2 * HIDP + HIDP + j * 128 + 128],
                               hnb[:, u, c * TT:(c + 1) * TT],
                               start=(c == 0), stop=(c == NC - 1), deps=wdeps)
                tgl = k.do("s", "activation", gl[:, u, jp, :], ps[:, bg, :],
                           AF.Gelu, bias=b1[:, NHC + j:NHC + j + 1],
                           deps=war + [tmg, tb1] +
                           ([prev[0]] if prev else []))
                tab = k.do("v", "tensor_scalar_add", ab[:, u, jp, :],
                           ps[:, ba, :], b1[:, j:j + 1],
                           deps=war + [tma, tb1] +
                           ([prev[0]] if prev else []))
                tm_ = k.do("v", "tensor_mul", mo[:, u, jp, :],
                           ab[:, u, jp, :], gl[:, u, jp, :],
                           deps=[tgl, tab] + ([prev[1]] if prev else []))
                td = k.dma("p", mgl[b, j, :, t0:t0 + TT], mo[:, u, jp, :],
                           deps=[tm_])
                gm_prev[jp] = (tm_, td)
            joins[i] = k.tip()
    return k.tip()


def emit_l1a_qkv(k, ap, deps):
    nc = k.nc
    xn2, q2aug, k2aug, vt2 = ap["xn2"], ap["q2aug"], ap["k2aug"], ap["vt2"]
    with ExitStack() as ctx:
        wqs = ctx.enter_context(nc.sbuf_tensor("lq_wq", [128, NC * 1024], BF16))
        wks = ctx.enter_context(nc.sbuf_tensor("lq_wk", [128, NC * 1024], BF16))
        wvs = ctx.enter_context(nc.sbuf_tensor("lq_wv", [128, NC * 1024], BF16))
        stg = ctx.enter_context(nc.sbuf_tensor("lq_st", [128, 2048], F32))
        bqs = ctx.enter_context(nc.sbuf_tensor("lq_bq", [128, NC], F32))
        bks = ctx.enter_context(nc.sbuf_tensor("lq_bk", [128, NC], F32))
        bvr = ctx.enter_context(nc.sbuf_tensor("lq_bv", [1, 1024], BF16))
        bvf = ctx.enter_context(nc.sbuf_tensor("lq_bvf", [1, 1024], F32))
        wqr = ctx.enter_context(nc.sbuf_tensor("lq_wqr", [128, 16], BF16))
        wkr = ctx.enter_context(nc.sbuf_tensor("lq_wkr", [128, 16], BF16))
        wrf = ctx.enter_context(nc.sbuf_tensor("lq_wrf", [128, 32], F32))
        onesc = ctx.enter_context(nc.sbuf_tensor("lq_on", [128, 1], BF16))
        onesr = ctx.enter_context(nc.sbuf_tensor("lq_or", [1, T], BF16))
        xnb = ctx.enter_context(nc.sbuf_tensor("lq_xnb", [128, 2, NC * TT], BF16))
        qk = ctx.enter_context(nc.sbuf_tensor("lq_qk", [128, 2, TT], BF16))
        sqr = ctx.enter_context(nc.sbuf_tensor("lq_sqr", [16, 2, TT], BF16))
        rr = ctx.enter_context(nc.sbuf_tensor("lq_rr", [16, 2, TT], BF16))
        nr = ctx.enter_context(nc.sbuf_tensor("lq_nr", [1, 2, TT], F32))
        crw = ctx.enter_context(nc.sbuf_tensor("lq_crw", [1, 2, TT], BF16))
        mks = ctx.enter_context(nc.sbuf_tensor("lq_mks", [1, 16, 6], F32))
        mkv = ctx.enter_context(nc.sbuf_tensor("lq_mkv", [1, 16], F32))
        vb = ctx.enter_context(nc.sbuf_tensor("lq_vb", [128, 2, TT], BF16))
        ps = ctx.enter_context(nc.psum_tensor("lq_ps", [128, 8, TT], F32))
        tw1 = cast_weights(k, wqs[:], ap["l1a_wq"].rearrange("c p n -> p c n"),
                           deps, stage_f32=stg[:])
        tw2 = cast_weights(k, wks[:], ap["l1a_wk"].rearrange("c p n -> p c n"),
                           [tw1], stage_f32=stg[:])
        tw3 = cast_weights(k, wvs[:], ap["l1a_wv"].rearrange("c p n -> p c n"),
                           [tw2], stage_f32=stg[:])
        tb1 = k.dma("q", bqs[:], ap["l1a_bq"][:, :], deps=deps)
        tb2 = k.dma("q", bks[:], ap["l1a_bk"][:, :], deps=deps)
        tb3 = k.dma("q", bvf[:], ap["l1a_bvrow"][:, :], deps=deps)
        tb4 = k.do("v", "tensor_copy", bvr[:], bvf[:], deps=[tb3])
        tw4a = k.dma("q", wrf[0:64, 0:16], ap["l1a_wqr"][:, :], deps=deps)
        tw4b = k.dma("q", wrf[64:128, 0:16], ap["l1a_wqr"][:, :], deps=deps)
        tw5a = k.dma("q", wrf[0:64, 16:32], ap["l1a_wkr"][:, :], deps=deps)
        tw5b = k.dma("q", wrf[64:128, 16:32], ap["l1a_wkr"][:, :], deps=deps)
        tw6 = k.do("v", "tensor_copy", wqr[:], wrf[:, 0:16],
                   deps=[tw4a, tw4b])
        tw7 = k.do("v", "tensor_copy", wkr[:], wrf[:, 16:32],
                   deps=[tw5a, tw5b])
        to1 = k.do("v", "memset", onesc[:], 1.0, deps=deps)
        to2 = k.do("v", "memset", onesr[:], 1.0, deps=deps)
        pre = [tw3, tb1, tb2, tb4, tw6, tw7, to1, to2]
        bj = list(deps) + pre

        def qk_phase(wsb, bsb, wr, dst, is_q, war0):
            ph = []
            rprev = {}
            nprev = {}
            shr = {}
            for tt in range(NTT):
                u = tt % 2
                war = ph[-2] if tt >= 2 else war0
                tl = k.dma("q", xnb[:, u, :].rearrange("p (c t) -> p c t", c=NC),
                           _fmslice(xn2, b, tt * TT), deps=war)
                toks = [tl]
                eprev = {}
                for j in range(NC):
                    bqk = (j % 2)
                    tpm = None
                    for c in range(NC):
                        tpm = k.mm(ps[:, bqk, :],
                                   wsb[:, c * 1024 + j * 128:
                                       c * 1024 + j * 128 + 128],
                                   xnb[:, u, c * TT:(c + 1) * TT],
                                   start=(c == 0), stop=(c == NC - 1),
                                   deps=list(war) + [tl] +
                                   ([eprev[bqk]] if bqk in eprev else []) +
                                   (rprev.get((bqk, 0), []) +
                                    rprev.get((bqk, 1), [])))
                    tev = k.do("s", "activation", qk[:, u, :], ps[:, bqk, :],
                               AF.Identity, bias=bsb[:, j:j + 1],
                               deps=list(war) + [tpm] +
                               ([eprev[bqk]] if bqk in eprev else []))
                    eprev[bqk] = tev
                    for hh in range(2):
                        if INNER14 < 1:
                            break
                        h = 2 * j + hh
                        rb = 2 + 2 * (j % 2) + hh
                        nb = 6 + hh
                        rpv = rprev.get((j % 2, hh), [])
                        npv = nprev.get(hh, [])
                        trm = k.mm(ps[0:16, rb, :],
                                   wr[hh * 64:hh * 64 + 64, :],
                                   qk[hh * 64:hh * 64 + 64, u, :], True, True,
                                   deps=[tev] + rpv)
                        if INNER14 >= 2:
                            tsq = k.do("s", "activation", sqr[:, u, :],
                                       ps[0:16, rb, :], AF.Square,
                                       deps=[trm, shr.get(("sq", u))] + npv)
                            tnm = tsq
                        if INNER14 >= 3:
                            tnm = k.mm(ps[0:1, nb, :], onesc[0:16, :],
                                       sqr[:, u, :], True, True,
                                       deps=[tsq, to1] + npv)
                        tcv = k.do("v", "tensor_copy", rr[:, u, :],
                                   ps[0:16, rb, :],
                                   deps=[trm, tsq if INNER14 >= 2 else None,
                                         shr.get(("rr", u))] + rpv)
                        td1 = k.dma("p", dst[b, h, 0:16, tt * TT:(tt + 1) * TT],
                                    rr[:, u, :], deps=[tcv])
                        shr[("rr", u)] = td1
                        if INNER14 >= 3:
                            shr[("sq", u)] = tnm
                        elif INNER14 >= 2:
                            shr[("sq", u)] = tsq
                            rprev[(j % 2, hh)] = [tsq, tcv, td1]
                        else:
                            rprev[(j % 2, hh)] = [tcv, td1]
                        if is_q and INNER14 >= 2:
                            tsr = k.do("s", "activation", nr[:, u, :],
                                       ps[0:1, nb, :], AF.Sqrt,
                                       deps=[tnm, shr.get(("nr", u))])
                            tcr = k.do("v", "tensor_scalar_mul", crw[:, u, :],
                                       nr[:, u, :], mkv[0:1, h:h + 1],
                                       deps=[tsr, mkr[0],
                                             shr.get(("cr", u))])
                            td2 = k.dma("p", dst[b, h, 16:17,
                                             tt * TT:(tt + 1) * TT],
                                        crw[:, u, :], deps=[tcr])
                            shr[("nr", u)] = tcr
                            shr[("cr", u)] = td2
                            nprev[hh] = [tsr]
                            toks += [tcr, td1, td2]
                        elif INNER14 >= 4:
                            tmx = k.do("v", "reduce_max",
                                       mks[0:1, h, tt:tt + 1],
                                       ps[0:1, nb, :], axis=AX.X,
                                       deps=[tnm])
                            if DBG14 and b == 0 and h == 0 and tt == 0:
                                tdg = k.do("v", "tensor_copy",
                                           rr[:, u, :], ps[0:16, rb, :],
                                           deps=[tnm, tmx, tcv])
                                k.dma("p", ap["dbg14"][1:2, 0:512],
                                      sqr[0:1, u, :], deps=[tnm, tsq])
                                tdh = k.do("v", "tensor_copy",
                                           nr[0:1, 0, :], ps[0:1, nb, :],
                                           deps=[tnm, tmx])
                                k.dma("p", ap["dbg14"][2:3, 0:512],
                                      nr[0:1, 0, :], deps=[tdh])
                                shr[("rr", u)] = tdg
                            nprev[hh] = [tmx]
                            toks += [tmx, td1]
                        elif INNER14 >= 2:
                            nprev[hh] = [tnm]
                            toks += [td1]
                        else:
                            toks += [td1]
                ph.append(toks)
            return [x for t_ in ph for x in t_]

        mkr = [None]
        for b in range(B):
            kj = qk_phase(wks, bks, wkr, k2aug, False, bj)
            if PH14 < 2:
                bj = k.tip()
                continue
            for h in range(16):
                k.dma("p", k2aug[b, h, 16:17, :], onesr[:], deps=[to2])
            mt = []
            for h in range(16):
                t1_ = k.do("v", "reduce_max", mks[0:1, h, 4:5],
                           mks[0:1, h, 0:NTT], axis=AX.X, deps=kj)
                t2_ = k.do("s", "activation", mks[0:1, h, 5:6],
                           mks[0:1, h, 4:5], AF.Sqrt, deps=[t1_])
                t3_ = k.do("v", "tensor_scalar_mul", mkv[0:1, h:h + 1],
                           mks[0:1, h, 5:6], -1.0, deps=[t2_])
                mt.append(t3_)
            mkr[0] = mt[-1]
            if b == 0:
                k.dma("p", ap["dbg14"][0:1, 0:96], mks[0:1, :, :], deps=mt)
                k.dma("p", ap["dbg14"][0:1, 96:112], mkv[:], deps=mt)
            kend = k.tip()
            if PH14 < 3:
                bj = k.tip()
                continue
            vph = []
            vprev = {}
            for tt in range(NTT):
                u = tt % 2
                war = vph[-2] if tt >= 2 else kend
                tl = k.dma("q", xnb[:, u, :].rearrange("p (c t) -> p c t", c=NC),
                           _fmslice(xn2, b, tt * TT), deps=war)
                toks = [tl]
                for tc4 in range(4):
                    for vc in range(2):
                        iv = tc4 * 2 + vc
                        uv = iv % 2
                        bv_ = 6 + uv
                        pv2 = vprev.get(uv, [])
                        tvm = None
                        for c in range(NC):
                            tvm = k.mm(ps[:, bv_, :],
                                       xnb[:, u, c * TT + tc4 * 128:
                                           c * TT + tc4 * 128 + 128],
                                       wvs[:, c * 1024 + vc * 512:
                                           c * 1024 + vc * 512 + 512],
                                       start=(c == 0), stop=False,
                                       deps=list(war) + [tl] + pv2[:1],
                                       mark=False)
                        tvm = k.mm(ps[:, bv_, :], onesr[0:1, 0:128],
                                   bvr[:, vc * 512:(vc + 1) * 512],
                                   start=False, stop=True,
                                   deps=[to2, tb4])
                        tvc = k.do("v", "tensor_copy", vb[:, uv, :],
                                   ps[:, bv_, :],
                                   deps=list(war) + [tvm] + pv2)
                        tds = [k.dma("p",
                                     vt2[b, vc * 8 + h2,
                                         tt * TT + tc4 * 128:
                                         tt * TT + tc4 * 128 + 128, :],
                                     vb[:, uv, h2 * 64:(h2 + 1) * 64],
                                     deps=[tvc]) for h2 in range(8)]
                        vprev[uv] = [tvc] + tds
                        toks += [tvc] + tds
                vph.append(toks)
            vend = k.tip()
            if PH14 >= 4:
                qj = qk_phase(wqs, bqs, wqr, q2aug, True, vend)
            bj = k.tip()
    return k.tip()


def emit_l1a_att(k, ap, deps):
    nc = k.nc
    q2aug, k2aug, vt2, o2 = ap["q2aug"], ap["k2aug"], ap["vt2"], ap["o2"]
    with ExitStack() as ctx:
        ka = ctx.enter_context(nc.sbuf_tensor("la_ka", [17, T], BF16))
        va = ctx.enter_context(nc.sbuf_tensor("la_va", [128, 16 * 65], BF16))
        qa = ctx.enter_context(nc.sbuf_tensor("la_qa", [17, 2, TT], BF16))
        ex = ctx.enter_context(nc.sbuf_tensor("la_ex", [128, 2, TT], BF16))
        rec = ctx.enter_context(nc.sbuf_tensor("la_rec", [1, TT], F32))
        osb = ctx.enter_context(nc.sbuf_tensor("la_osb", [64, TT], F32))
        onb = ctx.enter_context(nc.sbuf_tensor("la_onb", [64, TT], BF16))
        o1f = ctx.enter_context(nc.sbuf_tensor("la_o1f", [1, 64], F32))
        ps = ctx.enter_context(nc.psum_tensor("la_ps", [128, 8, TT], F32))
        tmv = k.do("v", "memset", va[:], 1.0, deps=deps)
        to1 = k.do("v", "memset", o1f[:], 1.0, deps=deps)
        pre = [tmv, to1]
        hb_join = list(deps) + pre
        for b in range(B):
            for h in range(16):
                tka = k.dma("q", ka[:], k2aug[b, h, :, :], deps=hb_join)
                tva = [k.dma("q", va[:, kt * 65:kt * 65 + 64],
                             vt2[b, h, kt * 128:(kt + 1) * 128, :],
                             deps=hb_join) for kt in range(16)]
                qtj = hb_join + [tka] + tva
                for qt in range(NTT):
                    uq = qt % 2
                    tqa = k.dma("q", qa[:, uq, :],
                                q2aug[b, h, :, qt * TT:(qt + 1) * TT],
                                deps=qtj)
                    evs = []
                    for kt in range(16):
                        u = kt % 2
                        deps_s = [tka, tqa]
                        if kt >= 2:
                            deps_s.append(evs[-2][0])
                        else:
                            deps_s += qtj
                        tsc = k.mm(ps[:, u, :], ka[:, kt * 128:(kt + 1) * 128],
                                   qa[:, uq, :], True, True, deps=deps_s)
                        texdeps = [tsc]
                        if kt >= 2:
                            texdeps.append(evs[-2][1])
                        tex = k.do("s", "activation", ex[:, u, :],
                                   ps[:, u, :], AF.Exp, scale=0.25,
                                   deps=texdeps)
                        tav = k.mm(ps[0:65, 2, :], va[:, kt * 65:kt * 65 + 65],
                                   ex[:, u, :], start=(kt == 0),
                                   stop=(kt == 15),
                                   deps=[tex, tmv] + tva +
                                   (qtj if kt == 0 else []), mark=True)
                        evs.append((tex, tav))
                    trc = k.do("v", "reciprocal", rec[:], ps[64:65, 2, :],
                               deps=[tav] + qtj)
                    tbc = k.mm(ps[0:64, 3, :], o1f[:], rec[:], True, True,
                               deps=[trc, to1] + qtj)
                    tcp = k.do("v", "tensor_copy", osb[:], ps[0:64, 2, :],
                               deps=[tav] + qtj)
                    tmu = k.do("v", "tensor_mul", onb[:], osb[:],
                               ps[0:64, 3, :], deps=[tbc, tcp])
                    tst = k.dma("p", o2[b, h // 2, (h % 2) * 64:(h % 2) * 64 + 64,
                                     qt * TT:(qt + 1) * TT],
                                onb[:], deps=[tmu])
                    qtj = [tmu, tst, trc, tcp]
                hb_join = k.tip()
    return k.tip()


def emit_gate(k, ap, deps):
    nc = k.nc
    xn2, m8, mbeta = ap["xn2"], ap["m8"], ap["mbeta"]
    with ExitStack() as ctx:
        gw = ctx.enter_context(nc.sbuf_tensor("gt_w", [128, NC * 8], BF16))
        stg = ctx.enter_context(nc.sbuf_tensor("gt_st", [128, NC * 8], F32))
        gb8 = ctx.enter_context(nc.sbuf_tensor("gt_b", [8, 1], F32))
        idn = ctx.enter_context(nc.sbuf_tensor("gt_id", [128, 128], BF16))
        xnb = ctx.enter_context(nc.sbuf_tensor("gt_x", [128, 2, NC * TT], BF16))
        lgb = ctx.enter_context(nc.sbuf_tensor("gt_lb", [8, 2, TT], BF16))
        lgt = ctx.enter_context(nc.sbuf_tensor("gt_lt", [128, 2, 8], F32))
        mxt = ctx.enter_context(nc.sbuf_tensor("gt_mx", [128, 2, 1], F32))
        mskb = ctx.enter_context(nc.sbuf_tensor("gt_mb", [128, 2, 8], BF16))
        mk8 = ctx.enter_context(nc.sbuf_tensor("gt_m8", [8, 2, 128], F32))
        mbt = ctx.enter_context(nc.sbuf_tensor("gt_bt", [8, 2, 128], F32))
        psg = ctx.enter_context(nc.psum_tensor("gt_ps", [128, 2, TT], F32))
        psa = ctx.enter_context(nc.psum_tensor("gt_pa", [128, 2, 8], BF16))
        psb = ctx.enter_context(nc.psum_tensor("gt_pb", [8, 2, 128], BF16))
        tw = cast_weights(k, gw[:], ap["l1f_gw"].rearrange("c p n -> p c n"),
                          deps, stage_f32=stg[:])
        tb = k.dma("q", gb8[:], ap["l1f_gbrow"][:, :], deps=deps)
        tid = k.dma("q", idn[:], ap["c_id"][:, :], deps=deps)
        pre = [tw, tb, tid]
        joins = {}
        tprev = {}
        for i in range(B * NTT):
            b, tt = divmod(i, NTT)
            u = i % 2
            war = joins.get(i - 2, list(deps) + pre)
            t0g = b * T + tt * TT
            tl = k.dma("q", xnb[:, u, :].rearrange("p (c t) -> p c t", c=NC),
                       _fmslice(xn2, b, tt * TT), deps=war)
            tg = None
            for c in range(NC):
                tg = k.mm(psg[0:8, u, :], gw[:, c * 8:(c + 1) * 8],
                          xnb[:, u, c * TT:(c + 1) * TT],
                          start=(c == 0), stop=(c == NC - 1),
                          deps=list(war) + [tl, tw])
            tev = k.do("s", "activation", lgb[:, u, :], psg[0:8, u, :],
                       AF.Identity, bias=gb8[:], deps=list(war) + [tg, tb])
            for tc4 in range(4):
                gc = i * 4 + tc4
                u4 = gc % 2
                pv = tprev.get(gc - 2, {})
                wv = list(war) if tc4 < 2 else []
                ttr = k.do("t", "transpose", psa[:, u4, :],
                           lgb[0:8, u, tc4 * 128:(tc4 + 1) * 128],
                           idn[0:8, 0:8],
                           deps=[tev, tid, pv.get("tcp")] + wv)
                tcp = k.do("v", "tensor_copy", lgt[:, u4, :], psa[:, u4, :],
                           deps=[ttr, pv.get("teq")] + wv)
                tmx = k.do("v", "reduce_max", mxt[:, u4, :], lgt[:, u4, :],
                           axis=AX.X, deps=[tcp, pv.get("teq")])
                teq = k.do("v", "tensor_scalar", mskb[:, u4, :],
                           lgt[:, u4, :], mxt[:, u4, :], None, ALU.is_equal,
                           deps=[tmx, pv.get("ttr2")])
                ttr2 = k.do("t", "transpose", psb[:, u4, :], mskb[:, u4, :],
                            idn[:], deps=[teq, pv.get("tm8")] + wv)
                tm8 = k.do("v", "tensor_copy", mk8[:, u4, :], psb[:, u4, :],
                           deps=[ttr2, pv.get("d1"), pv.get("tbt")])
                tbt = k.do("v", "tensor_scalar", mbt[:, u4, :], mk8[:, u4, :],
                           1.0, 1.0e9, ALU.subtract, ALU.mult,
                           deps=[tm8, pv.get("d2")])
                d1 = k.dma("p", m8[:, t0g + tc4 * 128:t0g + tc4 * 128 + 128],
                           mk8[:, u4, :], deps=[tm8])
                d2 = k.dma("p", mbeta[:, t0g + tc4 * 128:t0g + tc4 * 128 + 128],
                           mbt[:, u4, :], deps=[tbt])
                tprev[gc] = dict(ttr=ttr, tcp=tcp, tmx=tmx, teq=teq,
                                 ttr2=ttr2, tm8=tm8, tbt=tbt, d1=d1, d2=d2)
            joins[i] = k.tip()
    return k.tip()


def emit_moe(k, ap, deps):
    nc = k.nc
    xn2, m8, mbeta, out = ap["xn2"], ap["m8"], ap["mbeta"], ap["out"]
    acc_in = RD(ap, "out")
    NTU = B * NTT
    with ExitStack() as ctx:
        w1e = ctx.enter_context(nc.sbuf_tensor("mo_w1", [128, NC * 4096], BF16))
        w2e = ctx.enter_context(nc.sbuf_tensor("mo_w2", [128, 32 * 1024], BF16))
        stg = ctx.enter_context(nc.sbuf_tensor("mo_st", [128, 2048], F32))
        b1s = ctx.enter_context(nc.sbuf_tensor("mo_b1", [128, 32], F32))
        b2m = ctx.enter_context(nc.sbuf_tensor("mo_b2", [8, NC * 128], BF16))
        b2f = ctx.enter_context(nc.sbuf_tensor("mo_b2f", [8, NC * 128], F32))
        onesb = ctx.enter_context(nc.sbuf_tensor("mo_on", [1, 128], BF16))
        xg = ctx.enter_context(nc.sbuf_tensor("mo_x", [128, 2, NC * TT], BF16))
        m8f = ctx.enter_context(nc.sbuf_tensor("mo_m8f", [8, 2, TT], F32))
        m8s = ctx.enter_context(nc.sbuf_tensor("mo_m8s", [8, 2, TT], BF16))
        mbf = ctx.enter_context(nc.sbuf_tensor("mo_mbf", [1, 2, TT], F32))
        mbs = ctx.enter_context(nc.sbuf_tensor("mo_mbs", [1, 2, TT], BF16))
        hsb = ctx.enter_context(nc.sbuf_tensor("mo_h", [128, 32 * TT], BF16))
        acct = ctx.enter_context(nc.sbuf_tensor("mo_a", [128, 2, TT], F32))
        ps = ctx.enter_context(nc.psum_tensor("mo_ps", [128, 8, TT], F32))
        ton = k.do("v", "memset", onesb[:], 1.0, deps=deps)
        tb2l = k.dma("q", b2f[:].rearrange("p (c n) -> p c n", c=NC),
                     ap["l1f_b2m"].rearrange("c p n -> p c n"), deps=deps)
        tb2 = k.do("v", "tensor_copy", b2m[:], b2f[:], deps=[tb2l])
        pre = [ton, tb2]
        ej = list(deps) + pre
        region = {}
        for e in range(8):
            tw1 = cast_weights(k, w1e[:],
                               ap["l1f_w1t"][e].rearrange("c p n -> p c n"),
                               ej, stage_f32=stg[:])
            tw2 = cast_weights(k, w2e[:],
                               ap["l1f_w2t"][e].rearrange("c p n -> p c n"),
                               [tw1], stage_f32=stg[:])
            tb1l = k.dma("q", b1s[:], ap["l1f_b1"][e], deps=ej)
            epre = [tw2, tb1l]
            uprev = {}
            for ttu in range(NTU):
                b_, tt = divmod(ttu, NTT)
                t0 = tt * TT
                tg0 = b_ * T + t0
                ux = ttu % 2
                tlx = k.dma("q",
                            xg[:, ux, :].rearrange("p (c t) -> p c t", c=NC),
                            _fmslice(xn2, b_, t0),
                            deps=ej + epre + [uprev.get(("x", ux))])
                tmbf = k.dma("q", mbf[:, ux, :], mbeta[e:e + 1, tg0:tg0 + TT],
                             deps=ej + [uprev.get(("mbf", ux))])
                tmbs = k.do("v", "tensor_copy", mbs[:, ux, :], mbf[:, ux, :],
                            deps=[tmbf, uprev.get(("mbs", ux))])
                if e == 7:
                    tl8f = k.dma("q", m8f[:, ux, :], m8[:, tg0:tg0 + TT],
                                 deps=ej + [uprev.get(("m8f", ux))])
                    tl8 = k.do("v", "tensor_copy", m8s[:, ux, :],
                               m8f[:, ux, :],
                               deps=[tl8f, uprev.get(("m8s", ux))])
                else:
                    tl8 = None
                relu_toks = []
                for hc in range(32):
                    bh = 6 + (hc % 2)
                    wdeps = [tw1, tlx, tmbs, ton]
                    if hc >= 2:
                        wdeps.append(relu_toks[-2])
                    elif ("pb", 6 + hc) in uprev:
                        wdeps.append(uprev[("pb", 6 + hc)])
                    tph = None
                    for c in range(NC):
                        tph = k.mm(ps[:, bh, :],
                                   w1e[:, c * 4096 + hc * 128:
                                       c * 4096 + hc * 128 + 128],
                                   xg[:, ux, c * TT:(c + 1) * TT],
                                   start=(c == 0), stop=False, deps=wdeps,
                                   mark=False)
                    tph = k.mm(ps[:, bh, :], onesb[:],
                               mbs[:, ux, :], start=False, stop=True,
                               deps=wdeps + [tmbs])
                    trl = k.do("s", "activation", hsb[:, hc * TT:(hc + 1) * TT],
                               ps[:, bh, :], AF.Relu, bias=b1s[:, hc:hc + 1],
                               deps=[tph, tb1l] +
                               ([relu_toks[-2]] if hc >= 2 else
                                [uprev.get(("h", hc % 2))]))
                    relu_toks.append(trl)
                uprev[("x", ux)] = relu_toks[-1]
                uprev[("mbf", ux)] = tmbs
                uprev[("mbs", ux)] = relu_toks[-1]
                if e == 7:
                    uprev[("m8f", ux)] = tl8
                uprev[("h", 0)] = relu_toks[-2]
                uprev[("h", 1)] = relu_toks[-1]
                for dc in range(NC):
                    ua = dc % 2
                    src_acc = acc_in if e == 0 else out
                    tla = k.dma("q", acct[:, ua, :],
                                src_acc[b_, dc, :, t0:t0 + TT],
                                deps=[region.get((ttu, dc)),
                                      uprev.get(("a", ua))])
                    tpy = None
                    wdeps2 = [tw2, relu_toks[-1]]
                    if ("pb", dc) in uprev:
                        wdeps2.append(uprev[("pb", dc)])
                    for hc in range(32):
                        last = (hc == 31 and e != 7)
                        tpy = k.mm(ps[:, dc, :],
                                   w2e[:, hc * 1024 + dc * 128:
                                       hc * 1024 + dc * 128 + 128],
                                   hsb[:, hc * TT:(hc + 1) * TT],
                                   start=(hc == 0), stop=last,
                                   deps=wdeps2 + [relu_toks[hc]], mark=last)
                    if e == 7:
                        tpy = k.mm(ps[:, dc, :],
                                   b2m[:, dc * 128:(dc + 1) * 128],
                                   m8s[:, ux, :], start=False,
                                   stop=True, deps=[tb2, tl8])
                        uprev[("m8s", ux)] = tpy
                    tac = k.do("v", "tensor_add", acct[:, ua, :],
                               acct[:, ua, :], ps[:, dc, :],
                               deps=[tpy, tla])
                    tst = k.dma("p", out[b_, dc, :, t0:t0 + TT],
                                acct[:, ua, :], deps=[tac])
                    region[(ttu, dc)] = tst
                    uprev[("a", ua)] = tst
                    uprev[("pb", dc)] = tac
            ej = k.tip()
    return k.tip()



def _l0a_qkv_reads():
    return ["xn1", "l0a_wqa", "l0a_wqb", "l0a_wkva", "l0a_wkvbK", "l0a_wkvbV",
            "c_cd", "c_sd", "c_p128"]


def RD(ap, name):
    return ap.get(name + "__in", ap[name])


STAGES = {
    "s0_ln1": {
        "fn": lambda k, ap, d: emit_ln(k, ap, ap["xin"], ap["xn1"], ap["xfa"], d),
        "reads": ["xin"], "writes": ["xn1", "xfa"]},
    "s1_l0aqkv": {
        "fn": lambda k, ap, d: emit_l0a_qkv(k, ap, d),
        "reads": _l0a_qkv_reads(), "writes": ["qaug", "kaug", "vt0"]},
    "s2_l0aatt": {
        "fn": lambda k, ap, d: emit_l0a_att(k, ap, d),
        "reads": ["qaug", "kaug", "vt0", "c_cm"], "writes": ["o0"]},
    "s3_l0aout": {
        "fn": lambda k, ap, d: emit_matproj_add(
            k, ap, ap["o0"], ap["l0a_wout"], ap["l0a_bout"], ap["xin"],
            ap["x1"], d, tag="s3"),
        "reads": ["o0", "xin", "l0a_wout", "l0a_bout"], "writes": ["x1"]},
}

for _l in (0, 1):
    _src = "xfa" if _l == 0 else "xfb"
    _dst = "xfb" if _l == 0 else "xfa"
    STAGES[f"s4_l{_l}"] = {
        "fn": (lambda l: lambda k, ap, d: emit_l0f_qkv(k, ap, l, d))(_l),
        "reads": [_src, "l0f_wqkv", "l0f_qs", "l0f_ks", "c_xqc", "c_xqs",
                  "c_xkc", "c_xks", "c_prot"],
        "writes": ["qf", "kf", "vt1"]}
    STAGES[f"s5_l{_l}"] = {
        "fn": (lambda l: lambda k, ap, d: emit_l0f_att(k, ap, l, d))(_l),
        "reads": ["qf", "kf", "vt1", "c_mtm", "c_mkm", "c_id"],
        "writes": ["of"]}
    STAGES[f"s6_l{_l}"] = {
        "fn": (lambda l, s: lambda k, ap, d: emit_matproj_add(
            k, ap, ap["of"], ap["l0f_wo"][l], None, RD(ap, s), ap[s], d,
            nin=4, tag=f"s6_{l}"))(_l, _src),
        "reads": ["of", "l0f_wo", _src], "writes": [_src]}
    STAGES[f"s7a_l{_l}"] = {
        "fn": (lambda l, s: lambda k, ap, d: emit_l0f_w1(k, ap, l, s, d))(
            _l, _src),
        "reads": [_src, "l0f_w1", "l0f_b1", "l0f_lng", "l0f_lnb"],
        "writes": ["mgl"]}
    STAGES[f"s7b_l{_l}"] = {
        "fn": (lambda l, s, t: lambda k, ap, d: emit_matproj_add(
            k, ap, ap["mgl"], ap["l0f_w2"][l], ap["l0f_b2"][l], RD(ap, s),
            ap[t], d, nin=NHC, tag=f"s7b_{l}"))(_l, _src, _dst),
        "reads": ["mgl", "l0f_w2", "l0f_b2", _src], "writes": [_dst]}

STAGES["s12_add"] = {
    "fn": lambda k, ap, d: emit_add(k, ap, ap["x1"], ap["xfa"], ap["x2"], d),
    "reads": ["x1", "xfa"], "writes": ["x2"]}
STAGES["s13_ln2"] = {
    "fn": lambda k, ap, d: emit_ln(k, ap, ap["x2"], ap["xn2"], None, d),
    "reads": ["x2"], "writes": ["xn2"]}
STAGES["s14_l1aqkv"] = {
    "fn": lambda k, ap, d: emit_l1a_qkv(k, ap, d),
    "reads": ["xn2", "l1a_wq", "l1a_wk", "l1a_wv", "l1a_bq", "l1a_bk",
              "l1a_bvrow", "l1a_wqr", "l1a_wkr"],
    "writes": ["q2aug", "k2aug", "vt2", "dbg14"]}
STAGES["s15_l1aatt"] = {
    "fn": lambda k, ap, d: emit_l1a_att(k, ap, d),
    "reads": ["q2aug", "k2aug", "vt2"], "writes": ["o2"]}
STAGES["s16_l1aout"] = {
    "fn": lambda k, ap, d: emit_matproj_add(
        k, ap, ap["o2"], ap["l1a_wo"], ap["l1a_bo"], ap["x2"], ap["out"], d,
        tag="s16"),
    "reads": ["o2", "x2", "l1a_wo", "l1a_bo"], "writes": ["out"]}
STAGES["s17_gate"] = {
    "fn": lambda k, ap, d: emit_gate(k, ap, d),
    "reads": ["xn2", "l1f_gw", "l1f_gbrow", "c_id"],
    "writes": ["m8", "mbeta"]}
STAGES["s18_moe"] = {
    "fn": lambda k, ap, d: emit_moe(k, ap, d),
    "reads": ["xn2", "m8", "mbeta", "l1f_w1t", "l1f_b1", "l1f_w2t",
              "l1f_b2m", "out"],
    "writes": ["out"]}

STAGE_ORDER = ["s0_ln1", "s1_l0aqkv", "s2_l0aatt", "s3_l0aout",
               "s4_l0", "s5_l0", "s6_l0", "s7a_l0", "s7b_l0",
               "s4_l1", "s5_l1", "s6_l1", "s7a_l1", "s7b_l1",
               "s12_add", "s13_ln2", "s14_l1aqkv", "s15_l1aatt",
               "s16_l1aout", "s17_gate", "s18_moe"]

GROUPS = [[s] for s in STAGE_ORDER]


def _spec(name):
    if name in WEIGHTS:
        return WEIGHTS[name], F32
    if name in CONSTS:
        return CONSTS[name]
    if name in SCRATCH:
        return SCRATCH[name]
    if name in INPUTS:
        return INPUTS[name]
    if name in OUTPUTS:
        return OUTPUTS[name]
    raise KeyError(name)


def build_group(group, fused):
    nc = bass.Bass("TRN2", target_bir_lowering=False, debug=False)
    reads, writes = [], []
    for s in group:
        reads += [r for r in STAGES[s]["reads"] if r not in reads]
        writes += [w for w in STAGES[s]["writes"] if w not in writes]
    first = {}
    for s in group:
        for r in STAGES[s]["reads"]:
            first.setdefault(r, "r")
        for w in STAGES[s]["writes"]:
            first.setdefault(w, "w")
    ap = {}
    innames, outnames = [], []
    names = list(dict.fromkeys(reads + writes))
    for n in names:
        shape, dt = _spec(n)
        is_scratch = n in SCRATCH or n in OUTPUTS
        written = n in writes
        if not is_scratch or (fused and n not in OUTPUTS and n not in INPUTS):
            if not is_scratch:
                ap[n] = nc.dram_tensor(n, shape, dt, kind="ExternalInput").ap()
                innames.append(n)
            else:
                ap[n] = nc.dram_tensor(n, shape, dt, kind="Internal").ap()
        else:
            if written:
                if first.get(n) == "r" and not fused:
                    ap[n + "__in"] = nc.dram_tensor(
                        n + "__in", shape, dt, kind="ExternalInput").ap()
                    innames.append(n + "__in")
                ap[n] = nc.dram_tensor(n, shape, dt, kind="ExternalOutput").ap()
                outnames.append(n)
            else:
                ap[n] = nc.dram_tensor(n, shape, dt, kind="ExternalInput").ap()
                innames.append(n)
    if fused:
        pass
    k = K(nc)
    deps = []
    for s in group:
        deps = STAGES[s]["fn"](k, ap, list(deps))
    k.fence_all(k.tip())
    return nc, innames, outnames


_GROUP_CACHE = {}


def kernel(**inputs):
    w = host_weights(inputs)
    c = host_consts()
    xs = [host_core_inputs(inputs, i) for i in range(NCORE)]
    state = [dict() for _ in range(NCORE)]
    fused = len(GROUPS) == 1
    for gi, group in enumerate(GROUPS):
        key = (tuple(group), fused)
        if key not in _GROUP_CACHE:
            _GROUP_CACHE[key] = build_group(group, fused)
        nc, innames, outnames = _GROUP_CACHE[key]
        in_maps = []
        for core in range(NCORE):
            m = {}
            for n in innames:
                base = n[:-4] if n.endswith("__in") else n
                if base in w:
                    m[n] = w[base]
                elif base in c:
                    m[n] = c[base]
                elif base in INPUTS:
                    m[n] = xs[core][base]
                else:
                    arr = state[core].get(base)
                    if arr is None:
                        shape, dt = _spec(base)
                        arr = np.zeros(shape, mybir.dt.np(dt))
                    m[n] = arr
            in_maps.append(m)
        res = run_bass_kernel_spmd(nc, in_maps, core_ids=list(range(NCORE)))
        for core in range(NCORE):
            for n in outnames:
                state[core][n] = res.results[core][n]
    outs = []
    for core in range(NCORE):
        o = state[core].get("out")
        if o is None:
            o = np.zeros(OUTPUTS["out"][0], np.float32)
        outs.append(o.transpose(0, 3, 1, 2).reshape(B, T, D))
    return np.concatenate(outs, axis=0)


# version 03
# speedup vs baseline: 1.1845x; 1.1845x over previous
import numpy as np
import ml_dtypes
from contextlib import ExitStack
import concourse.bass as bass
import concourse.mybir as mybir
from concourse.bass_utils import run_bass_kernel_spmd

BF = ml_dtypes.bfloat16
F32 = mybir.dt.float32
BF16 = mybir.dt.bfloat16
AF = mybir.ActivationFunctionType
ALU = mybir.AluOpType
AX = mybir.AxisListType

NCORE = 8
B = 4
T = 2048
D = 1024
NC = 8
TT = 512
NTT = T // TT
NEG = -30000.0
BETA = -1.0e9
HIDP = 2816
NHC = 22
PH14 = 4
INNER14 = 4
DBG14 = False



class K:

    NPOOL = 8

    def __init__(self, nc):
        self.nc = nc
        self.eng = {"t": nc.tensor, "v": nc.vector, "s": nc.scalar,
                    "p": nc.gpsimd, "q": nc.sync}
        self.prog = {k: nc.alloc_semaphore("prog_" + k) for k in "tvs"}
        self.cnt = {k: 0 for k in "tvs"}
        self.dpool = {k: [nc.alloc_semaphore(f"dma_{k}{i}")
                          for i in range(self.NPOOL)] for k in "qsp"}
        self.dqi = {k: 0 for k in "qsp"}
        self.seen = {k: {} for k in "tvspq"}
        self.uid = ""

    def wait(self, ekey, deps):
        e = self.eng[ekey]
        seen = self.seen[ekey]
        for d in deps:
            if d is None:
                continue
            sem, val = d
            if val <= 0 or val <= seen.get(sem.num, 0):
                continue
            e.wait_ge(sem, val)
            seen[sem.num] = val

    def dma(self, qkey, out, in_, deps=(), **kw):
        slot = self.dqi[qkey] % self.NPOOL
        n_use = self.dqi[qkey] // self.NPOOL
        sem = self.dpool[qkey][slot]
        self.wait(qkey, list(deps) + [(sem, 16 * n_use)])
        i = self.eng[qkey].dma_start(out=out, in_=in_, **kw)
        i.then_inc(sem, 16)
        self.dqi[qkey] += 1
        return (sem, 16 * (n_use + 1))

    def do(self, ekey, method, *args, deps=(), **kw):
        self.wait(ekey, deps)
        i = getattr(self.eng[ekey], method)(*args, **kw)
        self.cnt[ekey] += 1
        i.then_inc(self.prog[ekey], 1)
        return (self.prog[ekey], self.cnt[ekey])

    def mm(self, out, lhsT, rhs, start, stop, deps=(), mark=None):
        if mark is None:
            mark = stop
        self.wait("t", deps)
        i = self.nc.tensor.matmul(out, lhsT, rhs, start=start, stop=stop)
        if mark:
            self.cnt["t"] += 1
            i.then_inc(self.prog["t"], 1)
            return (self.prog["t"], self.cnt["t"])
        return None

    def tip(self):
        toks = [(self.prog[k], self.cnt[k]) for k in "tvs"]
        for q in "qsp":
            n = self.dqi[q]
            for slot in range(self.NPOOL):
                uses = (n - slot + self.NPOOL - 1) // self.NPOOL
                if uses > 0:
                    toks.append((self.dpool[q][slot], 16 * uses))
        return toks

    def fence_all(self, deps):
        for ekey in "tvspq":
            self.wait(ekey, deps)


def cast_weights(k, dst_sb, src_ap, deps, qkey="q", stage_f32=None, tag=""):
    n = dst_sb.shape[1]
    ns = stage_f32.shape[1]
    half = ns // 2
    ld_prev = {}
    tok = None
    if len(src_ap.shape) == 3:
        C, m = src_ap.shape[1], src_ap.shape[2]
        assert C * m == n, (src_ap.shape, n)
        i = 0
        if m > half:
            for c0 in range(C):
                off = 0
                while off < m:
                    mm_ = min(half, m - off)
                    buf = i % 2
                    tl = k.dma(qkey, stage_f32[:, buf * half:buf * half + mm_],
                               src_ap[:, c0, off:off + mm_],
                               deps=list(deps) + [ld_prev.get(buf)])
                    tok = k.do("v", "tensor_copy",
                               dst_sb[:, c0 * m + off:c0 * m + off + mm_],
                               stage_f32[:, buf * half:buf * half + mm_],
                               deps=[tl])
                    ld_prev[buf] = tok
                    off += mm_
                    i += 1
            return tok
        per = max(1, half // m)
        c0 = 0
        while c0 < C:
            cc = min(per, C - c0)
            buf = i % 2
            tl = k.dma(qkey, stage_f32[:, buf * half:buf * half + cc * m]
                       .rearrange("p (c m) -> p c m", c=cc),
                       src_ap[:, c0:c0 + cc, :],
                       deps=list(deps) + [ld_prev.get(buf)])
            tok = k.do("v", "tensor_copy", dst_sb[:, c0 * m:(c0 + cc) * m],
                       stage_f32[:, buf * half:buf * half + cc * m],
                       deps=[tl])
            ld_prev[buf] = tok
            c0 += cc
            i += 1
        return tok
    off = 0
    i = 0
    while off < n:
        m = min(half, n - off)
        buf = i % 2
        tl = k.dma(qkey, stage_f32[:, buf * half:buf * half + m],
                   src_ap[:, off:off + m],
                   deps=list(deps) + [ld_prev.get(buf)])
        tok = k.do("v", "tensor_copy", dst_sb[:, off:off + m],
                   stage_f32[:, buf * half:buf * half + m],
                   deps=[tl])
        ld_prev[buf] = tok
        off += m
        i += 1
    return tok



WEIGHTS = {
    "l0a_wqa": [NC, 128, 64],
    "l0a_wqb": [64, 1024],
    "l0a_wkva": [NC, 128, 32],
    "l0a_wkvbK": [32, 256],
    "l0a_wkvbV": [32, 256],
    "l0a_wout": [NC, 128, 1024],
    "l0a_bout": [128, NC],
    "l0f_wqkv": [2, NC, 128, 1536],
    "l0f_qs": [2, 64],
    "l0f_ks": [2, 64],
    "l0f_wo": [2, 4, 128, 1024],
    "l0f_lng": [2, NC, 128],
    "l0f_lnb": [2, NC, 128],
    "l0f_w1": [2, NC, 128, 2 * HIDP],
    "l0f_b1": [2, 128, 2 * NHC],
    "l0f_w2": [2, NHC, 128, 1024],
    "l0f_b2": [2, 128, NC],
    "l1a_wq": [NC, 128, 1024],
    "l1a_wk": [NC, 128, 1024],
    "l1a_wv": [NC, 128, 1024],
    "l1a_wo": [NC, 128, 1024],
    "l1a_bq": [128, NC],
    "l1a_bk": [128, NC],
    "l1a_bo": [128, NC],
    "l1a_bvrow": [1, 1024],
    "l1a_wqr": [64, 16],
    "l1a_wkr": [64, 16],
    "l1f_gw": [NC, 128, 8],
    "l1f_gbrow": [1, 8],
    "l1f_w1t": [8, NC, 128, 4096],
    "l1f_b1": [8, 128, 32],
    "l1f_w2t": [8, 32, 128, 1024],
    "l1f_b2m": [NC, 8, 128],
}

CONSTS = {
    "c_id": ([128, 128], BF16),
    "c_p128": ([128, 128], BF16),
    "c_prot": ([128, 128], BF16),
    "c_cd": ([128, T], F32),
    "c_sd": ([128, T], F32),
    "c_xqc": ([128, T], F32),
    "c_xqs": ([128, T], F32),
    "c_xkc": ([128, T], F32),
    "c_xks": ([128, T], F32),
    "c_cm": ([4, 128, TT], BF16),
    "c_mtm": ([4, 128, 1024], BF16),
    "c_mkm": ([8, 128, TT], BF16),
}

SCRATCH = {
    "xn1": ([B, NC, 128, T], BF16),
    "xfa": ([B, NC, 128, T], F32),
    "xfb": ([B, NC, 128, T], F32),
    "qaug": ([B, 16, 65, T], BF16),
    "kaug": ([B, 4, 65, T], BF16),
    "vt0": ([B, 4, T, 64], BF16),
    "o0": ([B, NC, 128, T], BF16),
    "x1": ([B, NC, 128, T], F32),
    "qf": ([B, 4, 128, T], BF16),
    "kf": ([B, 4, 128, T], BF16),
    "vt1": ([B, 8, T, 64], BF16),
    "of": ([B, 4, 128, T], BF16),
    "mgl": ([B, NHC, 128, T], BF16),
    "x2": ([B, NC, 128, T], F32),
    "xn2": ([B, NC, 128, T], BF16),
    "q2aug": ([B, 16, 17, T], BF16),
    "k2aug": ([B, 16, 17, T], BF16),
    "vt2": ([B, 16, T, 64], BF16),
    "o2": ([B, NC, 128, T], BF16),
    "mbeta": ([8, B * T], F32),
    "dbg14": ([4, 1024], F32),
    "m8": ([8, B * T], F32),
}

INPUTS = {"xin": ([B, NC, 128, T], F32)}
OUTPUTS = {"out": ([B, NC, 128, T], F32)}



def _fm(x):
    b, t, d = x.shape
    return np.ascontiguousarray(
        x.reshape(b, t, d // 128, 128).transpose(0, 2, 3, 1))


def host_weights(inp):
    w = {}
    w["l0a_wqa"] = inp["L0_attn_Wqa"].reshape(NC, 128, 64)
    w["l0a_wqb"] = inp["L0_attn_Wqb"]
    w["l0a_wkva"] = inp["L0_attn_Wkva"].reshape(NC, 128, 32)
    kvb = np.asarray(inp["L0_attn_Wkvb"]).reshape(32, 4, 64, 2)
    w["l0a_wkvbK"] = kvb[..., 0].reshape(32, 256)
    w["l0a_wkvbV"] = kvb[..., 1].reshape(32, 256)
    w["l0a_wout"] = inp["L0_attn_Wout"].reshape(NC, 128, 1024)
    w["l0a_bout"] = np.ascontiguousarray(np.asarray(inp["L0_attn_bout"]).reshape(NC, 128).T)
    w["l0f_wqkv"] = inp["L0_ffn_wqkv"].reshape(2, NC, 128, 1536)
    w["l0f_qs"] = inp["L0_ffn_q_scale"]
    w["l0f_ks"] = inp["L0_ffn_k_scale"]
    w["l0f_wo"] = inp["L0_ffn_wo"].reshape(2, 4, 128, 1024)
    w["l0f_lng"] = inp["L0_ffn_ln_g"].reshape(2, NC, 128)
    w["l0f_lnb"] = inp["L0_ffn_ln_b"].reshape(2, NC, 128)
    w1 = np.asarray(inp["L0_ffn_w1"])
    w1p = np.zeros((2, 1024, 2 * HIDP), np.float32)
    w1p[:, :, :2730] = w1[:, :, :2730]
    w1p[:, :, HIDP:HIDP + 2730] = w1[:, :, 2730:]
    w["l0f_w1"] = w1p.reshape(2, NC, 128, 2 * HIDP)
    b1 = np.asarray(inp["L0_ffn_b1"])
    b1p = np.zeros((2, 2 * HIDP), np.float32)
    b1p[:, :2730] = b1[:, :2730]
    b1p[:, HIDP:HIDP + 2730] = b1[:, 2730:]
    w["l0f_b1"] = np.ascontiguousarray(b1p.reshape(2, 2 * NHC, 128).transpose(0, 2, 1))
    w2 = np.asarray(inp["L0_ffn_w2"])
    w2p = np.zeros((2, HIDP, 1024), np.float32)
    w2p[:, :2730] = w2
    w["l0f_w2"] = w2p.reshape(2, NHC, 128, 1024)
    w["l0f_b2"] = np.ascontiguousarray(np.asarray(inp["L0_ffn_b2"]).reshape(2, NC, 128).transpose(0, 2, 1))
    for nm, key in (("wq", "WQ"), ("wk", "WK"), ("wv", "WV"), ("wo", "WO")):
        w["l1a_" + nm] = np.asarray(inp["L1_attn_" + key]).reshape(NC, 128, 1024)
    for nm, key in (("bq", "bQ"), ("bk", "bK"), ("bo", "bO")):
        w["l1a_" + nm] = np.ascontiguousarray(
            np.asarray(inp["L1_attn_" + key]).reshape(NC, 128).T)
    w["l1a_bvrow"] = np.asarray(inp["L1_attn_bV"]).reshape(1, 1024)
    w["l1a_wqr"] = inp["L1_attn_WQr"]
    w["l1a_wkr"] = inp["L1_attn_WKr"]
    w["l1f_gw"] = np.ascontiguousarray(np.asarray(inp["L1_ffn_gate_w"]).T
                                       ).reshape(NC, 128, 8)
    w["l1f_gbrow"] = np.asarray(inp["L1_ffn_gate_b"]).reshape(1, 8)
    w["l1f_w1t"] = np.ascontiguousarray(
        np.asarray(inp["L1_ffn_w1"]).transpose(0, 2, 1)).reshape(8, NC, 128, 4096)
    w["l1f_b1"] = np.ascontiguousarray(np.asarray(inp["L1_ffn_b1"]).reshape(8, 32, 128).transpose(0, 2, 1))
    w["l1f_w2t"] = np.ascontiguousarray(
        np.asarray(inp["L1_ffn_w2"]).transpose(0, 2, 1)).reshape(8, 32, 128, 1024)
    w["l1f_b2m"] = np.ascontiguousarray(np.asarray(inp["L1_ffn_b2"]).reshape(8, 8, 128).transpose(1, 0, 2))
    return {k: np.ascontiguousarray(np.asarray(v), dtype=np.float32)
            for k, v in w.items()}


def host_consts():
    c = {}
    c["c_id"] = np.eye(128, dtype=np.float32).astype(BF)
    p = np.zeros((128, 128), np.float32)
    for i in range(64):
        p[2 * i, 2 * i + 1] = -1.0
        p[2 * i + 1, 2 * i] = 1.0
    c["c_p128"] = p.T.astype(BF)
    pr = np.zeros((128, 128), np.float32)
    for h in range(2):
        o = 64 * h
        for i in range(32):
            pr[o + i, o + 32 + i] = -1.0
            pr[o + 32 + i, o + i] = 1.0
    c["c_prot"] = pr.T.astype(BF)
    d = 64
    inv = 1.0 / 10000.0 ** (np.arange(0, d, 2, dtype=np.float64) / d)
    t = np.arange(T, dtype=np.float64)
    fr = inv[:, None] * t[None, :]
    cd = np.repeat(np.cos(fr), 2, axis=0)
    sd = np.repeat(np.sin(fr), 2, axis=0)
    c["c_cd"] = np.tile(cd, (2, 1)).astype(np.float32)
    c["c_sd"] = np.tile(sd, (2, 1)).astype(np.float32)
    half = np.arange(0, d, 2, dtype=np.float64)
    invf = 1.0 / 10000.0 ** (half / d)
    fr2 = invf[:, None] * t[None, :]
    fr2 = np.concatenate([fr2, fr2], axis=0)
    base = (half + 0.4 * d) / (1.4 * d)
    power = (t - T // 2) / 256.0
    scale = base[:, None] ** power[None, :]
    scale = np.concatenate([scale, scale], axis=0)
    cos, sin = np.cos(fr2), np.sin(fr2)
    c["c_xqc"] = np.tile(cos * scale, (2, 1)).astype(np.float32)
    c["c_xqs"] = np.tile(sin * scale, (2, 1)).astype(np.float32)
    c["c_xkc"] = np.tile(cos / scale, (2, 1)).astype(np.float32)
    c["c_xks"] = np.tile(sin / scale, (2, 1)).astype(np.float32)
    kk = np.arange(128)[:, None]
    qq = np.arange(TT)[None, :]
    cm = np.zeros((4, 128, TT), np.float32)
    for m in range(4):
        cm[m] = np.where(m * 128 + kk <= qq, 0.0, NEG)
    c["c_cm"] = cm.astype(BF)
    jj = np.arange(1024)[None, :]
    mtm = np.zeros((4, 128, 1024), np.float32)
    for qc in range(4):
        qrel = qc * 128 + np.arange(128)[:, None]
        mtm[qc] = np.where((jj >= qrel) & (jj <= qrel + 512), 0.0, NEG)
    c["c_mtm"] = mtm.astype(BF)
    mkm = np.zeros((8, 128, TT), np.float32)
    for kt in range(8):
        j = kt * 128 + np.arange(128)[:, None]
        mkm[kt] = np.where((j >= qq) & (j <= qq + 512), 0.0, NEG)
    c["c_mkm"] = mkm.astype(BF)
    return c


def host_core_inputs(inp, core):
    x = np.asarray(inp["L0_attn_x"], np.float32)[4 * core:4 * core + 4]
    return {"xin": _fm(x)}



def emit_ln(k, ap, src, dst_bf, dst_f32, deps):
    nc = k.nc
    with ExitStack() as ctx:
        xt = ctx.enter_context(nc.sbuf_tensor(k.uid + "ln_xt", [128, 2, NC * TT], F32))
        xb = ctx.enter_context(nc.sbuf_tensor(k.uid + "ln_xb", [128, 2, NC * TT], BF16))
        sq = ctx.enter_context(nc.sbuf_tensor(k.uid + "ln_sq", [128, 2, NC * TT], BF16))
        ones = ctx.enter_context(nc.sbuf_tensor(k.uid + "ln_o", [128, 1], BF16))
        o1 = ctx.enter_context(nc.sbuf_tensor(k.uid + "ln_o1", [1, 128], F32))
        rows = ctx.enter_context(nc.sbuf_tensor(k.uid + "ln_r", [1, 2, 6, TT], F32))
        xnf = ctx.enter_context(nc.sbuf_tensor(k.uid + "ln_xf", [128, 2, NC * TT], F32))
        xnb = ctx.enter_context(nc.sbuf_tensor(k.uid + "ln_xB", [128, 2, NC * TT], BF16))
        eps = ctx.enter_context(nc.sbuf_tensor(k.uid + "ln_eps", [1, 1], F32))
        ps = ctx.enter_context(nc.psum_tensor(k.uid + "ln_ps", [128, 8, TT], F32))
        t0tok = k.do("v", "memset", ones[:], 1.0, deps=deps)
        t1tok = k.do("v", "memset", o1[:], 1.0, deps=deps)
        tep = k.do("v", "memset", eps[:], 1e-5, deps=deps)
        joins = {}
        for i in range(B * NTT):
            b, tt = divmod(i, NTT)
            u = i % 2
            war = joins.get(i - 2, list(deps) + [t0tok, t1tok])
            t0 = tt * TT
            X = xt[:, u, :]
            XB = xb[:, u, :]
            SQ = sq[:, u, :]
            XF = xnf[:, u, :]
            XO = xnb[:, u, :]
            R = rows[:, u, :, :]
            s0, s1, pA, pB = 4 * u, 4 * u + 1, 4 * u + 2, 4 * u + 3
            tl = k.dma("q", X.rearrange("p (c t) -> p c t", c=NC),
                       src[b, :, :, t0:t0 + TT].rearrange("c p t -> p c t"),
                       deps=war)
            tc = k.do("v", "tensor_copy", XB, X, deps=war + [tl])
            tq = k.do("s", "activation", SQ, XB, AF.Square, deps=war + [tc])
            for c in range(NC):
                ts0 = k.mm(ps[0:1, s0, :], ones[:], XB[:, c * TT:(c + 1) * TT],
                           start=(c == 0), stop=(c == NC - 1), deps=war + [tc])
            for c in range(NC):
                ts1 = k.mm(ps[0:1, s1, :], ones[:], SQ[:, c * TT:(c + 1) * TT],
                           start=(c == 0), stop=(c == NC - 1), deps=[tq])
            tm = k.do("v", "tensor_scalar_mul", R[:, 0, :], ps[0:1, s0, :],
                      1.0 / D, deps=war + [ts0])
            te = k.do("v", "tensor_scalar_mul", R[:, 1, :], ps[0:1, s1, :],
                      1.0 / D, deps=[ts1])
            tmm = k.do("v", "tensor_mul", R[:, 2, :], R[:, 0, :], R[:, 0, :],
                       deps=[tm])
            tv = k.do("v", "tensor_sub", R[:, 3, :], R[:, 1, :], R[:, 2, :],
                      deps=[te, tmm])
            tsd = k.do("s", "activation", R[:, 4, :], R[:, 3, :], AF.Sqrt,
                       bias=eps[:], deps=[tv, tep])
            tr = k.do("v", "reciprocal", R[:, 5, :], R[:, 4, :], deps=[tsd])
            tnm = k.do("v", "tensor_mul", R[:, 2, :], R[:, 0, :], R[:, 5, :],
                       deps=[tr])
            tnm2 = k.do("v", "tensor_scalar_mul", R[:, 2, :], R[:, 2, :], -1.0,
                        deps=[tnm])
            ta = k.mm(ps[:, pA, :], o1[:], R[:, 5, :], True, True,
                      deps=war + [tr])
            tb = k.mm(ps[:, pB, :], o1[:], R[:, 2, :], True, True, deps=[tnm2])
            for c in range(NC):
                sl = slice(c * TT, (c + 1) * TT)
                tt1 = k.do("v", "tensor_mul", XF[:, sl], X[:, sl], ps[:, pA, :],
                           deps=[ta])
                tt2 = k.do("v", "tensor_add", XF[:, sl], XF[:, sl], ps[:, pB, :],
                           deps=[tb, tt1])
                tt3 = k.do("v", "tensor_copy", XO[:, sl], XF[:, sl], deps=[tt2])
                k.dma("p", dst_bf[b, c, :, t0:t0 + TT], XO[:, sl], deps=[tt3])
                if dst_f32 is not None:
                    k.dma("p", dst_f32[b, c, :, t0:t0 + TT], XF[:, sl],
                          deps=[tt3])
            joins[i] = k.tip()
    return k.tip()


def emit_add(k, ap, a, bsrc, dst, deps):
    nc = k.nc
    with ExitStack() as ctx:
        xa = ctx.enter_context(nc.sbuf_tensor(k.uid + "ad_a", [128, 2, NC * TT], F32))
        xb = ctx.enter_context(nc.sbuf_tensor(k.uid + "ad_b", [128, 2, NC * TT], F32))
        xo = ctx.enter_context(nc.sbuf_tensor(k.uid + "ad_o", [128, 2, NC * TT], F32))
        joins = {}
        for i in range(B * NTT):
            b, tt = divmod(i, NTT)
            u = i % 2
            war = joins.get(i - 2, list(deps))
            t0 = tt * TT
            ta = k.dma("q", xa[:, u, :].rearrange("p (c t) -> p c t", c=NC),
                       a[b, :, :, t0:t0 + TT].rearrange("c p t -> p c t"),
                       deps=war)
            tb = k.dma("q", xb[:, u, :].rearrange("p (c t) -> p c t", c=NC),
                       bsrc[b, :, :, t0:t0 + TT].rearrange("c p t -> p c t"),
                       deps=war)
            to = k.do("v", "tensor_add", xo[:, u, :], xa[:, u, :], xb[:, u, :],
                      deps=war + [ta, tb])
            k.dma("p", dst[b, :, :, t0:t0 + TT].rearrange("c p t -> p c t"),
                  xo[:, u, :].rearrange("p (c t) -> p c t", c=NC), deps=[to])
            joins[i] = k.tip()
    return k.tip()


def _fmslice(t4, b, t0, n=TT):
    return t4[b, :, :, t0:t0 + n].rearrange("c p t -> p c t")


def emit_l0a_qkv(k, ap, deps):
    nc = k.nc
    xn1, qaug, kaug, vt0 = ap["xn1"], ap["qaug"], ap["kaug"], ap["vt0"]
    with ExitStack() as ctx:
        wqa = ctx.enter_context(nc.sbuf_tensor(k.uid + "aq_wqa", [128, NC * 64], BF16))
        wqb = ctx.enter_context(nc.sbuf_tensor(k.uid + "aq_wqb", [64, 1024], BF16))
        wka = ctx.enter_context(nc.sbuf_tensor(k.uid + "aq_wka", [128, NC * 32], BF16))
        wkbK = ctx.enter_context(nc.sbuf_tensor(k.uid + "aq_wkbK", [32, 256], BF16))
        wkbV = ctx.enter_context(nc.sbuf_tensor(k.uid + "aq_wkbV", [32, 256], BF16))
        stg = ctx.enter_context(nc.sbuf_tensor(k.uid + "aq_stg", [128, 2048], F32))
        cd = ctx.enter_context(nc.sbuf_tensor(k.uid + "aq_cd", [128, T], F32))
        sd = ctx.enter_context(nc.sbuf_tensor(k.uid + "aq_sd", [128, T], F32))
        pP = ctx.enter_context(nc.sbuf_tensor(k.uid + "aq_p", [128, 128], BF16))
        hh2 = ctx.enter_context(nc.sbuf_tensor(k.uid + "aq_hh2", [128, 2], BF16))
        onesr = ctx.enter_context(nc.sbuf_tensor(k.uid + "aq_or", [1, T], BF16))
        mk02 = ctx.enter_context(nc.sbuf_tensor(k.uid + "aq_mk", [2, 4], F32))
        mkst = ctx.enter_context(nc.sbuf_tensor(k.uid + "aq_mks", [2, 14], F32))
        xnb = ctx.enter_context(nc.sbuf_tensor(k.uid + "aq_xnb", [128, 2, NC * TT], BF16))
        t1 = ctx.enter_context(nc.sbuf_tensor(k.uid + "aq_t1", [64, T], BF16))
        t2 = ctx.enter_context(nc.sbuf_tensor(k.uid + "aq_t2", [32, T], BF16))
        rb = ctx.enter_context(nc.sbuf_tensor(k.uid + "aq_rb", [128, 2, TT], BF16))
        sqv = ctx.enter_context(nc.sbuf_tensor(k.uid + "aq_sq", [128, 2, TT], BF16))
        ta_f = ctx.enter_context(nc.sbuf_tensor(k.uid + "aq_taf", [128, 2, TT], F32))
        tb_f = ctx.enter_context(nc.sbuf_tensor(k.uid + "aq_tbf", [128, 2, TT], F32))
        ro = ctx.enter_context(nc.sbuf_tensor(k.uid + "aq_ro", [128, 2, TT], BF16))
        nr = ctx.enter_context(nc.sbuf_tensor(k.uid + "aq_nr", [2, 2, TT], F32))
        crw = ctx.enter_context(nc.sbuf_tensor(k.uid + "aq_crw", [2, 2, TT], BF16))
        vb = ctx.enter_context(nc.sbuf_tensor(k.uid + "aq_vb", [128, 2, 256], BF16))
        ps = ctx.enter_context(nc.psum_tensor(k.uid + "aq_ps", [128, 8, TT], F32))

        tw = cast_weights(k, wqa[:], ap["l0a_wqa"].rearrange("c p n -> p c n"),
                          deps, stage_f32=stg[:])
        tw = cast_weights(k, wqb[:], ap["l0a_wqb"][:, :], [tw],
                          stage_f32=stg[0:64, :])
        tw = cast_weights(k, wka[:], ap["l0a_wkva"].rearrange("c p n -> p c n"),
                          [tw], stage_f32=stg[:])
        tw = cast_weights(k, wkbK[:], ap["l0a_wkvbK"][:, :], [tw],
                          stage_f32=stg[0:32, :])
        tw = cast_weights(k, wkbV[:], ap["l0a_wkvbV"][:, :], [tw],
                          stage_f32=stg[0:32, :])
        tcd = k.dma("q", cd[:], ap["c_cd"][:, :], deps=deps)
        tsd = k.dma("q", sd[:], ap["c_sd"][:, :], deps=deps)
        tpp = k.dma("q", pP[:], ap["c_p128"][:, :], deps=deps)
        th0 = k.do("v", "memset", hh2[:], 0.0, deps=deps)
        th1 = k.do("v", "memset", hh2[0:64, 0:1], 1.0, deps=[th0])
        th2 = k.do("v", "memset", hh2[64:128, 1:2], 1.0, deps=[th1])
        tor = k.do("v", "memset", onesr[:], 1.0, deps=deps)
        pre = [tw, tcd, tsd, tpp, th2, tor]

        def rope_block(bank, u, tt, tbl_c, tbl_s, dep_mm, war):
            tqb = k.do("s", "activation", rb[:, u, :], ps[:, bank, :],
                       AF.Copy, deps=war + [dep_mm])
            tp = k.mm(ps[:, 6 + u, :], pP[:], rb[:, u, :], True, True,
                      deps=war + [tqb, tpp])
            t_a = k.do("v", "tensor_mul", ta_f[:, u, :], ps[:, 6 + u, :],
                       tbl_s[:, tt * TT:(tt + 1) * TT],
                       deps=war + [tp])
            t_b = k.do("v", "tensor_mul", tb_f[:, u, :], ps[:, bank, :],
                       tbl_c[:, tt * TT:(tt + 1) * TT],
                       deps=war + [dep_mm, tqb])
            t_o = k.do("v", "tensor_add", ro[:, u, :], ta_f[:, u, :],
                       tb_f[:, u, :], deps=[t_a, t_b])
            return t_o

        bj = list(deps) + pre
        for b in range(B):
            ph1 = []
            for tt in range(NTT):
                u = tt % 2
                war = ph1[-2][2] if tt >= 2 else bj
                tl = k.dma("q", xnb[:, u, :].rearrange("p (c t) -> p c t", c=NC),
                           _fmslice(xn1, b, tt * TT), deps=war)
                for c in range(NC):
                    m1 = k.mm(ps[0:64, u, :], wqa[:, c * 64:(c + 1) * 64],
                              xnb[:, u, c * TT:(c + 1) * TT],
                              start=(c == 0), stop=(c == NC - 1),
                              deps=war + [tl, tw])
                for c in range(NC):
                    m2 = k.mm(ps[0:32, 2 + u, :], wka[:, c * 32:(c + 1) * 32],
                              xnb[:, u, c * TT:(c + 1) * TT],
                              start=(c == 0), stop=(c == NC - 1), deps=[tl])
                c1 = k.do("v", "tensor_copy", t1[:, tt * TT:(tt + 1) * TT],
                          ps[0:64, u, :], deps=war + [m1])
                c2 = k.do("v", "tensor_copy", t2[:, tt * TT:(tt + 1) * TT],
                          ps[0:32, 2 + u, :], deps=[m2])
                ph1.append((c1, c2, k.tip()))
            tph1 = [x for tpl in ph1 for x in tpl[:2]]
            ph2 = []
            for tc in range(T // 128):
                u = tc % 2
                war = ph2[-2] if tc >= 2 else []
                tv = k.mm(ps[:, u, 0:256], t2[:, tc * 128:(tc + 1) * 128],
                          wkbV[:], True, True, deps=tph1 + war)
                tvc = k.do("v", "tensor_copy", vb[:, u, :], ps[:, u, 0:256],
                           deps=war + [tv])
                dt = [k.dma("p", vt0[b, kvh, tc * 128:(tc + 1) * 128, :],
                            vb[:, u, kvh * 64:(kvh + 1) * 64], deps=[tvc])
                      for kvh in range(4)]
                ph2.append([tvc] + dt)
            ph2end = k.tip()
            mk_toks = []
            ph3 = []
            for j2 in range(2):
                for tt in range(NTT):
                    i3 = j2 * NTT + tt
                    u = i3 % 2
                    war = ph3[-2] if i3 >= 2 else ph2end
                    tk = k.mm(ps[:, 2 + u, :], wkbK[:, j2 * 128:(j2 + 1) * 128],
                              t2[:, tt * TT:(tt + 1) * TT], True, True,
                              deps=tph1 + war)
                    tsq = k.do("s", "activation", sqv[:, u, :], ps[:, 2 + u, :],
                               AF.Square, deps=war + [tk])
                    tn = k.mm(ps[0:2, 4 + u, :], hh2[:], sqv[:, u, :], True,
                              True, deps=war + [tsq, th2])
                    tmx = k.do("v", "reduce_max", mkst[:, i3:i3 + 1],
                               ps[0:2, 4 + u, :], axis=AX.X, deps=war + [tn])
                    mk_toks.append(tmx)
                    t_o = rope_block(2 + u, u, tt, cd, sd, tk, war)
                    d1 = k.dma("p", kaug[b, 2 * j2, 0:64, tt * TT:(tt + 1) * TT],
                               ro[0:64, u, :], deps=[t_o])
                    d2 = k.dma("p", kaug[b, 2 * j2 + 1, 0:64,
                                    tt * TT:(tt + 1) * TT],
                               ro[64:128, u, :], deps=[t_o])
                    ph3.append([tmx, t_o, d1, d2])
            for kvh in range(4):
                k.dma("p", kaug[b, kvh, 64:65, :], onesr[:], deps=[tor])
            mk_dma = []
            for j2 in range(2):
                tmk = k.do("v", "reduce_max", mkst[:, 8 + j2:9 + j2],
                           mkst[:, j2 * NTT:(j2 + 1) * NTT], axis=AX.X,
                           deps=mk_toks)
                tms = k.do("s", "activation", mkst[:, 10 + j2:11 + j2],
                           mkst[:, 8 + j2:9 + j2], AF.Sqrt, deps=[tmk])
                tng = k.do("v", "tensor_scalar_mul", mkst[:, 12 + j2:13 + j2],
                           mkst[:, 10 + j2:11 + j2], -1.0, deps=[tms])
                for r in range(2):
                    for rr in range(2):
                        mk_dma.append(k.dma(
                            "p", mk02[rr:rr + 1, 2 * j2 + r:2 * j2 + r + 1],
                            mkst[r:r + 1, 12 + j2:13 + j2], deps=[tng]))
            ph3end = k.tip()
            ph4 = []
            for j in range(NC):
                kvh = j // 2
                for tt in range(NTT):
                    i4 = j * NTT + tt
                    u = i4 % 2
                    war = ph4[-2] if i4 >= 2 else ph3end
                    tq = k.mm(ps[:, 2 + u, :], wqb[:, j * 128:(j + 1) * 128],
                              t1[:, tt * TT:(tt + 1) * TT], True, True,
                              deps=tph1 + war)
                    tsq = k.do("s", "activation", sqv[:, u, :], ps[:, 2 + u, :],
                               AF.Square, deps=war + [tq])
                    tn = k.mm(ps[0:2, 4 + u, :], hh2[:], sqv[:, u, :], True,
                              True, deps=war + [tsq])
                    tsr = k.do("s", "activation", nr[:, u, :], ps[0:2, 4 + u, :],
                               AF.Sqrt, deps=war + [tn])
                    tcr = k.do("v", "tensor_scalar_mul", crw[:, u, :],
                               nr[:, u, :], mk02[0:2, kvh:kvh + 1],
                               deps=war + [tsr] + mk_dma)
                    e1 = k.dma("p", qaug[b, 2 * j, 64:65, tt * TT:(tt + 1) * TT],
                               crw[0:1, u, :], deps=[tcr])
                    e2 = k.dma("p", qaug[b, 2 * j + 1, 64:65,
                                    tt * TT:(tt + 1) * TT],
                               crw[1:2, u, :], deps=[tcr])
                    t_o = rope_block(2 + u, u, tt, cd, sd, tq, war)
                    e3 = k.dma("p", qaug[b, 2 * j, 0:64, tt * TT:(tt + 1) * TT],
                               ro[0:64, u, :], deps=[t_o])
                    e4 = k.dma("p", qaug[b, 2 * j + 1, 0:64,
                                    tt * TT:(tt + 1) * TT],
                               ro[64:128, u, :], deps=[t_o])
                    ph4.append([tcr, t_o, e1, e2, e3, e4])
            bj = k.tip()
    return k.tip()


def emit_l0a_att(k, ap, deps):
    nc = k.nc
    qaug, kaug, vt0, o0 = ap["qaug"], ap["kaug"], ap["vt0"], ap["o0"]
    with ExitStack() as ctx:
        ka = ctx.enter_context(nc.sbuf_tensor(k.uid + "aa_ka", [65, T], BF16))
        va = ctx.enter_context(nc.sbuf_tensor(k.uid + "aa_va", [128, 16 * 65], BF16))
        qa = ctx.enter_context(nc.sbuf_tensor(k.uid + "aa_qa", [65, 2, TT], BF16))
        cmm = ctx.enter_context(nc.sbuf_tensor(k.uid + "aa_cm", [128, 4 * TT], BF16))
        ex = ctx.enter_context(nc.sbuf_tensor(k.uid + "aa_ex", [128, 2, TT], BF16))
        rec = ctx.enter_context(nc.sbuf_tensor(k.uid + "aa_rec", [1, TT], F32))
        osb = ctx.enter_context(nc.sbuf_tensor(k.uid + "aa_osb", [64, TT], F32))
        onb = ctx.enter_context(nc.sbuf_tensor(k.uid + "aa_onb", [64, TT], BF16))
        o1f = ctx.enter_context(nc.sbuf_tensor(k.uid + "aa_o1f", [1, 64], F32))
        ps = ctx.enter_context(nc.psum_tensor(k.uid + "aa_ps", [128, 8, TT], F32))
        tcm = k.dma("q", cmm[:], ap["c_cm"].rearrange("m p t -> p m t"),
                    deps=deps)
        tmv = k.do("v", "memset", va[:], 1.0, deps=deps)
        to1 = k.do("v", "memset", o1f[:], 1.0, deps=deps)
        pre = [tcm, tmv, to1]
        hb_join = list(deps) + pre
        for b in range(B):
            for kvh in range(4):
                tka = k.dma("q", ka[:], kaug[b, kvh, :, :], deps=hb_join)
                tva = [k.dma("q", va[:, kt * 65:kt * 65 + 64],
                             vt0[b, kvh, kt * 128:(kt + 1) * 128, :],
                             deps=hb_join) for kt in range(16)]
                qtj = hb_join + [tka, tva[-1]]
                for hh in range(4):
                    h = kvh * 4 + hh
                    for qt in range(NTT):
                        uq = qt % 2
                        tqa = k.dma("q", qa[:, uq, :],
                                    qaug[b, h, :, qt * TT:(qt + 1) * TT],
                                    deps=qtj)
                        nkt = 4 * qt + 4
                        evs = []
                        for kt in range(nkt):
                            u = kt % 2
                            deps_s = [tka, tqa]
                            if kt >= 2:
                                deps_s.append(evs[-2][0])
                            else:
                                deps_s += qtj
                            tsc = k.mm(ps[:, u, :],
                                       ka[:, kt * 128:(kt + 1) * 128],
                                       qa[:, uq, :], True, True, deps=deps_s)
                            dmask = kt - 4 * qt
                            if dmask >= 0:
                                tsc = k.do("v", "tensor_add", ps[:, u, :],
                                           ps[:, u, :],
                                           cmm[:, dmask * TT:(dmask + 1) * TT],
                                           deps=[tsc, tcm])
                            texdeps = [tsc]
                            if kt >= 2:
                                texdeps.append(evs[-2][1])
                            tex = k.do("s", "activation", ex[:, u, :],
                                       ps[:, u, :], AF.Exp, scale=0.125,
                                       deps=texdeps)
                            tav = k.mm(ps[0:65, 2, :],
                                       va[:, kt * 65:kt * 65 + 65],
                                       ex[:, u, :], start=(kt == 0),
                                       stop=(kt == nkt - 1),
                                       deps=[tex, tmv] + tva +
                                       (qtj if kt == 0 else []), mark=True)
                            evs.append((tex, tav))
                        trc = k.do("v", "reciprocal", rec[:], ps[64:65, 2, :],
                                   deps=[tav] + qtj)
                        tbc = k.mm(ps[0:64, 3, :], o1f[:], rec[:], True, True,
                                   deps=[trc, to1] + qtj)
                        tcp = k.do("v", "tensor_copy", osb[:], ps[0:64, 2, :],
                                   deps=[tav] + qtj)
                        tmu = k.do("v", "tensor_mul", onb[:], osb[:],
                                   ps[0:64, 3, :], deps=[tbc, tcp])
                        tst = k.dma("p", o0[b, h // 2,
                                         (h % 2) * 64:(h % 2) * 64 + 64,
                                         qt * TT:(qt + 1) * TT],
                                    onb[:], deps=[tmu])
                        qtj = [tmu, tst, trc, tcp]
                hb_join = k.tip()
    return k.tip()


def emit_matproj_add(k, ap, src_bf, w_ap, b_ap, add_src, dst, deps,
                     nin=NC, tag="mp"):
    nc = k.nc
    with ExitStack() as ctx:
        wsb = ctx.enter_context(nc.sbuf_tensor(tag + "_w", [128, nin * 1024], BF16))
        stg = ctx.enter_context(nc.sbuf_tensor(tag + "_st", [128, 2048], F32))
        bsb = ctx.enter_context(nc.sbuf_tensor(tag + "_b", [128, NC], F32))
        xsb = ctx.enter_context(nc.sbuf_tensor(tag + "_x", [128, 2, nin * TT], BF16))
        asb = ctx.enter_context(nc.sbuf_tensor(tag + "_a", [128, 2, NC * TT], F32))
        ysb = ctx.enter_context(nc.sbuf_tensor(tag + "_y", [128, 2, NC * TT], F32))
        ps = ctx.enter_context(nc.psum_tensor(tag + "_ps", [128, 8, TT], F32))
        tw = cast_weights(k, wsb[:], w_ap.rearrange("c p n -> p c n"),
                          deps, stage_f32=stg[:])
        if b_ap is not None:
            tb = k.dma("q", bsb[:], b_ap[:, :], deps=deps)
        else:
            tb = k.do("v", "memset", bsb[:], 0.0, deps=deps)
        joins = {}
        for i in range(B * NTT):
            b, tt = divmod(i, NTT)
            u = i % 2
            war = joins.get(i - 2, list(deps) + [tw, tb])
            t0 = tt * TT
            tx = k.dma("q", xsb[:, u, :].rearrange("p (c t) -> p c t", c=nin),
                       src_bf[b, :, :, t0:t0 + TT].rearrange("c p t -> p c t"),
                       deps=war)
            tad = k.dma("q", asb[:, u, :].rearrange("p (c t) -> p c t", c=NC),
                        _fmslice(add_src, b, t0), deps=war)
            bank_free = {}
            for j in range(NC):
                bank = 4 * u + j % 4
                wdeps = war + [tx, tw]
                if j % 4 in bank_free:
                    wdeps = wdeps + [bank_free[j % 4]]
                for c in range(nin):
                    tm = k.mm(ps[:, bank, :],
                              wsb[:, c * 1024 + j * 128:c * 1024 + j * 128 + 128],
                              xsb[:, u, c * TT:(c + 1) * TT],
                              start=(c == 0), stop=(c == nin - 1), deps=wdeps)
                ty = k.do("s", "activation", ysb[:, u, j * TT:(j + 1) * TT],
                          ps[:, bank, :], AF.Identity, bias=bsb[:, j:j + 1],
                          deps=war + [tm, tb])
                bank_free[j % 4] = ty
                tz = k.do("v", "tensor_add", ysb[:, u, j * TT:(j + 1) * TT],
                          ysb[:, u, j * TT:(j + 1) * TT],
                          asb[:, u, j * TT:(j + 1) * TT],
                          deps=war + [ty, tad])
                k.dma("p", dst[b, j, :, t0:t0 + TT], ysb[:, u, j * TT:(j + 1) * TT],
                      deps=[tz])
            joins[i] = k.tip()
    return k.tip()


def _shift_load(k, stage_u, src, b, t0, deps):
    toks = []
    toks.append(k.dma("q", stage_u[:, 0:4 * TT].rearrange("p (c t) -> p c t", c=4),
                      src[b, 0:4, :, t0:t0 + TT].rearrange("c p t -> p c t"),
                      deps=deps))
    if t0 > 0:
        toks.append(k.dma("q",
                          stage_u[:, 4 * TT:8 * TT].rearrange("p (c t) -> p c t", c=4),
                          src[b, 4:8, :, t0 - 1:t0 + TT - 1]
                          .rearrange("c p t -> p c t"), deps=deps))
    else:
        toks.append(k.dma("q",
                          stage_u[:, 4 * TT:8 * TT]
                          .rearrange("p (c t) -> p c t", c=4)[:, :, 1:TT],
                          src[b, 4:8, :, 0:TT - 1].rearrange("c p t -> p c t"),
                          deps=deps))
        for c2 in range(4):
            toks.append(k.do("v", "memset",
                             stage_u[:, (4 + c2) * TT:(4 + c2) * TT + 1], 0.0,
                             deps=deps))
    return toks


def emit_l0f_qkv(k, ap, l, deps):
    nc = k.nc
    src = ap["xfa"] if l == 0 else ap["xfb"]
    qf, kf, vt1 = ap["qf"], ap["kf"], ap["vt1"]
    with ExitStack() as ctx:
        wq = ctx.enter_context(nc.sbuf_tensor(k.uid + "fq_w", [128, NC * 1536], BF16))
        stg = ctx.enter_context(nc.sbuf_tensor(k.uid + "fq_st", [128, 2, NC * TT], F32))
        hb = ctx.enter_context(nc.sbuf_tensor(k.uid + "fq_hb", [128, NC * T], BF16))
        xqc = ctx.enter_context(nc.sbuf_tensor(k.uid + "fq_xqc", [128, T], F32))
        xqs = ctx.enter_context(nc.sbuf_tensor(k.uid + "fq_xqs", [128, T], F32))
        xkc = ctx.enter_context(nc.sbuf_tensor(k.uid + "fq_xkc", [128, T], F32))
        xks = ctx.enter_context(nc.sbuf_tensor(k.uid + "fq_xks", [128, T], F32))
        prt = ctx.enter_context(nc.sbuf_tensor(k.uid + "fq_prt", [128, 128], BF16))
        hh2 = ctx.enter_context(nc.sbuf_tensor(k.uid + "fq_hh2", [128, 2], BF16))
        qsel = ctx.enter_context(nc.sbuf_tensor(k.uid + "fq_qsel", [2, 128], F32))
        ksel = ctx.enter_context(nc.sbuf_tensor(k.uid + "fq_ksel", [2, 128], F32))
        sqv = ctx.enter_context(nc.sbuf_tensor(k.uid + "fq_sq", [128, 2, TT], BF16))
        nrw = ctx.enter_context(nc.sbuf_tensor(k.uid + "fq_nr", [2, 2, TT], F32))
        bcf = ctx.enter_context(nc.sbuf_tensor(k.uid + "fq_bc", [128, 2, TT], F32))
        qn = ctx.enter_context(nc.sbuf_tensor(k.uid + "fq_qn", [128, 2, TT], F32))
        qb = ctx.enter_context(nc.sbuf_tensor(k.uid + "fq_qb", [128, 2, TT], BF16))
        taf = ctx.enter_context(nc.sbuf_tensor(k.uid + "fq_ta", [128, 2, TT], F32))
        tbf = ctx.enter_context(nc.sbuf_tensor(k.uid + "fq_tb", [128, 2, TT], F32))
        ro = ctx.enter_context(nc.sbuf_tensor(k.uid + "fq_ro", [128, 2, TT], BF16))
        vb = ctx.enter_context(nc.sbuf_tensor(k.uid + "fq_vb", [128, 2, TT], BF16))
        ps = ctx.enter_context(nc.psum_tensor(k.uid + "fq_ps", [128, 8, TT], F32))
        tw = cast_weights(k, wq[:], ap["l0f_wqkv"][l].rearrange("c p n -> p c n"),
                          deps, stage_f32=stg[:, 0, :])
        tc1 = k.dma("q", xqc[:], ap["c_xqc"][:, :], deps=deps)
        tc2 = k.dma("q", xqs[:], ap["c_xqs"][:, :], deps=deps)
        tc3 = k.dma("q", xkc[:], ap["c_xkc"][:, :], deps=deps)
        tc4 = k.dma("q", xks[:], ap["c_xks"][:, :], deps=deps)
        tc5 = k.dma("q", prt[:], ap["c_prot"][:, :], deps=deps)
        th0 = k.do("v", "memset", hh2[:], 0.0, deps=deps)
        th1 = k.do("v", "memset", hh2[0:64, 0:1], 1.0, deps=[th0])
        th2 = k.do("v", "memset", hh2[64:128, 1:2], 1.0, deps=[th1])
        tq0 = k.do("v", "memset", qsel[:], 0.0, deps=deps)
        tk0 = k.do("v", "memset", ksel[:], 0.0, deps=deps)
        tq1 = k.dma("q", qsel[0:1, 0:64], ap["l0f_qs"][l:l + 1, :], deps=[tq0])
        tq2 = k.dma("q", qsel[1:2, 64:128], ap["l0f_qs"][l:l + 1, :], deps=[tq0])
        tk1 = k.dma("q", ksel[0:1, 0:64], ap["l0f_ks"][l:l + 1, :], deps=[tk0])
        tk2 = k.dma("q", ksel[1:2, 64:128], ap["l0f_ks"][l:l + 1, :], deps=[tk0])
        pre = [tw, tc1, tc2, tc3, tc4, tc5, th2, tq1, tq2, tk1, tk2]
        bj = list(deps) + pre
        bref = [0]

        def qk_head_block(j, tt, i, col0, sel, tbc, tbs, dst, war):
            b = bref[0]
            u = i % 2
            t0 = tt * TT
            tqm = None
            for c in range(NC):
                tqm = k.mm(ps[:, u, :],
                           wq[:, c * 1536 + col0 + j * 128:
                              c * 1536 + col0 + j * 128 + 128],
                           hb[:, c * T + t0:c * T + t0 + TT],
                           start=(c == 0), stop=(c == NC - 1),
                           deps=war + [tw])
            tsq = k.do("s", "activation", sqv[:, u, :], ps[:, u, :], AF.Square,
                       deps=war + [tqm])
            tn = k.mm(ps[0:2, 2 + u, :], hh2[:], sqv[:, u, :], True, True,
                      deps=war + [tsq, th2])
            tsr = k.do("s", "activation", nrw[:, u, :], ps[0:2, 2 + u, :],
                       AF.Sqrt, deps=war + [tn])
            tcl = k.do("v", "tensor_scalar_max", nrw[:, u, :], nrw[:, u, :],
                       1e-12, deps=war + [tsr])
            trc = k.do("v", "reciprocal", nrw[:, u, :], nrw[:, u, :],
                       deps=[tcl])
            tbm = k.mm(ps[:, 4 + u, :], sel, nrw[:, u, :], True, True,
                       deps=war + [trc])
            tbe = k.do("v", "tensor_copy", bcf[:, u, :], ps[:, 4 + u, :],
                       deps=war + [tbm])
            tqn = k.do("v", "tensor_mul", qn[:, u, :], ps[:, u, :],
                       bcf[:, u, :], deps=[tbe, tsq])
            tqb = k.do("s", "activation", qb[:, u, :], qn[:, u, :], AF.Copy,
                       deps=war + [tqn])
            tp = k.mm(ps[:, 6 + u, :], prt[:], qb[:, u, :], True, True,
                      deps=war + [tqb, tc5])
            t_a = k.do("v", "tensor_mul", taf[:, u, :], ps[:, 6 + u, :],
                       tbs[:, t0:t0 + TT], deps=war + [tp])
            t_b = k.do("v", "tensor_mul", tbf[:, u, :], qn[:, u, :],
                       tbc[:, t0:t0 + TT], deps=war + [t_a])
            t_o = k.do("v", "tensor_add", ro[:, u, :], taf[:, u, :],
                       tbf[:, u, :], deps=[t_b])
            td = k.dma("p", dst[b, j, :, t0:t0 + TT], ro[:, u, :], deps=[t_o])
            return [t_o, td, tqb, tqm]

        for b in range(B):
            bref[0] = b
            htoks = []
            for tt in range(NTT):
                u = tt % 2
                war = [htoks[-2][1]] if tt >= 2 else bj
                tl = _shift_load(k, stg[:, u, :], src, b, tt * TT, war)
                tcst = k.do("v", "tensor_copy",
                            hb[:, :].rearrange("p (c t) -> p c t", c=NC)
                            [:, :, tt * TT:(tt + 1) * TT],
                            stg[:, u, :].rearrange("p (c t) -> p c t", c=NC),
                            deps=war + tl)
                htoks.append((tl, tcst))
            hdone = [h[1] for h in htoks]
            vph = []
            for tt in range(NTT):
                for c4 in range(4):
                    iv = tt * 4 + c4
                    u = iv % 2
                    war = vph[-2] if iv >= 2 else hdone
                    tvm = None
                    for c in range(NC):
                        tvm = k.mm(ps[:, u, :],
                                   hb[:, c * T + tt * TT + c4 * 128:
                                      c * T + tt * TT + c4 * 128 + 128],
                                   wq[:, c * 1536 + 1024:c * 1536 + 1536],
                                   start=(c == 0), stop=(c == NC - 1),
                                   deps=hdone + list(war))
                    tvc = k.do("v", "tensor_copy", vb[:, u, :], ps[:, u, :],
                               deps=list(war) + [tvm])
                    tds = [k.dma("p",
                                 vt1[b, h, tt * TT + c4 * 128:
                                     tt * TT + c4 * 128 + 128, :],
                                 vb[:, u, h * 64:(h + 1) * 64], deps=[tvc])
                           for h in range(8)]
                    vph.append([tvc] + tds)
            vend = k.tip()
            kph = []
            for j in range(4):
                for tt in range(NTT):
                    ik = j * NTT + tt
                    war = kph[-2] if ik >= 2 else vend
                    kph.append(qk_head_block(j, tt, ik, 512, ksel[:], xkc, xks,
                                             kf, list(war)))
            kend = k.tip()
            qph = []
            for j in range(4):
                for tt in range(NTT):
                    iq = j * NTT + tt
                    war = qph[-2] if iq >= 2 else kend
                    qph.append(qk_head_block(j, tt, iq, 0, qsel[:], xqc, xqs,
                                             qf, list(war)))
            bj = k.tip()
    return k.tip()


def emit_l0f_att(k, ap, l, deps):
    nc = k.nc
    qf, kf, vt1, of = ap["qf"], ap["kf"], ap["vt1"], ap["of"]
    with ExitStack() as ctx:
        mtm = ctx.enter_context(nc.sbuf_tensor(k.uid + "fa_mtm", [128, 4 * 1024], BF16))
        mkm = ctx.enter_context(nc.sbuf_tensor(k.uid + "fa_mkm", [128, 8 * TT], BF16))
        idn = ctx.enter_context(nc.sbuf_tensor(k.uid + "fa_id", [128, 128], BF16))
        ka2 = ctx.enter_context(nc.sbuf_tensor(k.uid + "fa_k", [65, 2, 1024], BF16))
        va1 = ctx.enter_context(nc.sbuf_tensor(k.uid + "fa_v", [128, 2, 8 * 65], BF16))
        qa2 = ctx.enter_context(nc.sbuf_tensor(k.uid + "fa_qa", [65, 2, TT], BF16))
        junk = ctx.enter_context(nc.sbuf_tensor(k.uid + "fa_j", [128, 2, 1024], BF16))
        mxc = ctx.enter_context(nc.sbuf_tensor(k.uid + "fa_mx", [128, 2, 4], F32))
        mxb = ctx.enter_context(nc.sbuf_tensor(k.uid + "fa_mb", [128, 2, 4], BF16))
        mxr = ctx.enter_context(nc.sbuf_tensor(k.uid + "fa_mr", [4, 2, 128], BF16))
        ex = ctx.enter_context(nc.sbuf_tensor(k.uid + "fa_ex", [128, 2, 2, TT], BF16))
        rec = ctx.enter_context(nc.sbuf_tensor(k.uid + "fa_rec", [1, 2, TT], F32))
        osb = ctx.enter_context(nc.sbuf_tensor(k.uid + "fa_osb", [64, 2, TT], F32))
        onb = ctx.enter_context(nc.sbuf_tensor(k.uid + "fa_onb", [64, 2, TT], BF16))
        o1f = ctx.enter_context(nc.sbuf_tensor(k.uid + "fa_o1f", [1, 64], F32))
        psA = ctx.enter_context(nc.psum_tensor(k.uid + "fa_psA", [128, 2, 2, TT], F32))
        psO = ctx.enter_context(nc.psum_tensor(k.uid + "fa_psO", [128, 2, TT], F32))
        psB = ctx.enter_context(nc.psum_tensor(k.uid + "fa_psB", [128, TT], F32))
        psT = ctx.enter_context(nc.psum_tensor(k.uid + "fa_psT", [4, 128], BF16))
        tm1 = k.dma("q", mtm[:].rearrange("p (m t) -> p m t", m=4),
                    ap["c_mtm"].rearrange("m p t -> p m t"), deps=deps)
        tm2 = k.dma("q", mkm[:].rearrange("p (m t) -> p m t", m=8),
                    ap["c_mkm"].rearrange("m p t -> p m t"), deps=deps)
        tid = k.dma("q", idn[:], ap["c_id"][:, :], deps=deps)
        tv1 = k.do("v", "memset", va1[:], 1.0, deps=deps)
        tk1 = k.do("v", "memset", ka2[64:65, :, :], 1.0, deps=deps)
        to1 = k.do("v", "memset", o1f[:], 1.0, deps=deps)
        pre = [tm1, tm2, tid, tv1, tk1, to1]
        joins = {}
        prev_bc = None
        prev_tr = None
        idx = 0
        for b in range(B):
            for h in range(8):
                for w in range(4):
                    u = idx % 2
                    war = joins.get(idx - 2, list(deps) + pre)
                    jc, jp = h // 2, (h % 2) * 64
                    kt0 = 4 if w == 0 else 0
                    tql = k.dma("q", qa2[0:64, u, :],
                                qf[b, jc, jp:jp + 64, w * TT:(w + 1) * TT],
                                deps=war)
                    if w == 0:
                        tkl = k.dma("q", ka2[0:64, u, 512:1024],
                                    kf[b, jc, jp:jp + 64, 0:TT], deps=war)
                    else:
                        tkl = k.dma("q", ka2[0:64, u, :],
                                    kf[b, jc, jp:jp + 64,
                                       (w - 1) * TT:(w + 1) * TT], deps=war)
                    tvl = [k.dma("q", va1[:, u, kt * 65:kt * 65 + 64],
                                 vt1[b, h, (w - 1) * TT + kt * 128:
                                     (w - 1) * TT + kt * 128 + 128, :],
                                 deps=war) for kt in range(kt0, 8)]
                    p1 = []
                    for qc in range(4):
                        wq1 = [p1[-1][1]] if p1 else war
                        halves = range(1, 2) if w == 0 else range(2)
                        tms = None
                        for half in halves:
                            tms = k.mm(psA[:, u, half, :],
                                       qa2[0:64, u, qc * 128:(qc + 1) * 128],
                                       ka2[0:64, u, half * TT:(half + 1) * TT],
                                       True, True,
                                       deps=war + wq1 + [tql, tkl])
                        if w == 0:
                            tad = k.do("v", "tensor_add", psA[:, u, 1, :],
                                       psA[:, u, 1, :],
                                       mtm[:, qc * 1024 + 512:
                                           qc * 1024 + 1024],
                                       deps=war + [tms, tm1])
                            ttr = k.do("v", "reduce_max",
                                       mxc[:, u, qc:qc + 1],
                                       psA[:, u, 1, :], axis=AX.X,
                                       deps=[tad])
                        else:
                            tad = k.do("v", "tensor_add", psA[:, u, :, :],
                                       psA[:, u, :, :],
                                       mtm[:, qc * 1024:(qc + 1) * 1024],
                                       deps=war + [tms, tm1])
                            ttr = k.do("v", "reduce_max",
                                       mxc[:, u, qc:qc + 1],
                                       psA[:, u, :, :]
                                       .rearrange("p a b -> p (a b)"),
                                       axis=AX.X, deps=[tad])
                        p1.append((tms, ttr))
                    tng = k.do("v", "tensor_scalar_mul", mxb[:, u, :],
                               mxc[:, u, :], -1.0, deps=[p1[-1][1]])
                    ttp = k.do("t", "transpose", psT[:], mxb[:, u, :], idn[:],
                               deps=war + [tng, tid, prev_tr])
                    tmr = k.do("v", "tensor_copy", mxr[:, u, :], psT[:],
                               deps=war + [ttp, prev_tr])
                    prev_tr = tmr
                    tdm = k.dma("p", qa2[64:65, u, :], mxr[:, u, :],
                                deps=[tmr])
                    evs = []
                    tav = None
                    for kt in range(kt0, 8):
                        u2 = kt % 2
                        dsc = [tdm, tkl] + [p1[-1][1], p1[-2][1] if len(p1) > 1 else None]
                        if len(evs) >= 2:
                            dsc.append(evs[-2][0])
                        tsc = k.mm(psA[:, u, u2, :],
                                   ka2[:, u, kt * 128:(kt + 1) * 128],
                                   qa2[:, u, :], True, True, deps=dsc)
                        tmk = k.do("v", "tensor_add", psA[:, u, u2, :],
                                   psA[:, u, u2, :],
                                   mkm[:, kt * TT:(kt + 1) * TT],
                                   deps=[tsc, tm2])
                        dex = [tmk]
                        if len(evs) >= 2:
                            dex.append(evs[-2][1])
                        tex = k.do("s", "activation", ex[:, u, u2, :],
                                   psA[:, u, u2, :], AF.Exp, scale=8.0,
                                   deps=dex)
                        tav = k.mm(psO[0:65, u, :],
                                   va1[:, u, kt * 65:kt * 65 + 65],
                                   ex[:, u, u2, :], start=(kt == kt0),
                                   stop=(kt == 7),
                                   deps=[tex, tv1] + tvl +
                                   (war + [tmr] if kt == kt0 else []),
                                   mark=True)
                        evs.append((tex, tav))
                    trc = k.do("v", "reciprocal", rec[:, u, :],
                               psO[64:65, u, :], deps=war + [tav])
                    tbc = k.mm(psB[0:64, :], o1f[:], rec[:, u, :], True, True,
                               deps=[trc, to1, prev_bc])
                    tcp = k.do("v", "tensor_copy", osb[:, u, :],
                               psO[0:64, u, :], deps=war + [tav])
                    tmu = k.do("v", "tensor_mul", onb[:, u, :], osb[:, u, :],
                               psB[0:64, :], deps=[tbc, tcp])
                    prev_bc = tmu
                    tst = k.dma("p", of[b, jc, jp:jp + 64, w * TT:(w + 1) * TT],
                                onb[:, u, :], deps=[tmu])
                    joins[idx] = k.tip()
                    idx += 1
    return k.tip()


def emit_l0f_w1(k, ap, l, src_name, deps):
    nc = k.nc
    src, mgl = ap[src_name], ap["mgl"]
    with ExitStack() as ctx:
        w1 = ctx.enter_context(nc.sbuf_tensor(k.uid + "f1_w1", [128, NC * 2 * HIDP], BF16))
        b1 = ctx.enter_context(nc.sbuf_tensor(k.uid + "f1_b1", [128, 2 * NHC], F32))
        glb = ctx.enter_context(nc.sbuf_tensor(k.uid + "f1_glb", [2, NC * 128], F32))
        stg = ctx.enter_context(nc.sbuf_tensor(k.uid + "f1_st", [128, 2, NC * TT], F32))
        hbt = ctx.enter_context(nc.sbuf_tensor(k.uid + "f1_hb", [128, NC * TT], BF16))
        sq = ctx.enter_context(nc.sbuf_tensor(k.uid + "f1_sq", [128, NC * TT], BF16))
        ones = ctx.enter_context(nc.sbuf_tensor(k.uid + "f1_on", [128, 1], BF16))
        rows = ctx.enter_context(nc.sbuf_tensor(k.uid + "f1_rw", [1, 2, 6, TT], F32))
        rh = ctx.enter_context(nc.sbuf_tensor(k.uid + "f1_rh", [2, 2, 2, TT], F32))
        eps = ctx.enter_context(nc.sbuf_tensor(k.uid + "f1_eps", [1, 1], F32))
        hnf = ctx.enter_context(nc.sbuf_tensor(k.uid + "f1_hnf", [128, TT], F32))
        hnb = ctx.enter_context(nc.sbuf_tensor(k.uid + "f1_hnb", [128, 2, NC * TT], BF16))
        gl = ctx.enter_context(nc.sbuf_tensor(k.uid + "f1_gl", [128, 2, 2, TT], BF16))
        ab = ctx.enter_context(nc.sbuf_tensor(k.uid + "f1_ab", [128, 2, 2, TT], BF16))
        mo = ctx.enter_context(nc.sbuf_tensor(k.uid + "f1_mo", [128, 2, 2, TT], BF16))
        ps = ctx.enter_context(nc.psum_tensor(k.uid + "f1_ps", [128, 8, TT], F32))
        tw1 = cast_weights(k, w1[:], ap["l0f_w1"][l].rearrange("c p n -> p c n"),
                           deps, stage_f32=stg[:, 0, :])
        tb1 = k.dma("q", b1[:], ap["l0f_b1"][l], deps=deps)
        tg1 = k.dma("q", glb[0:1, :],
                    ap["l0f_lng"][l].rearrange("c p -> (c p)")[None, :],
                    deps=deps)
        tg2 = k.dma("q", glb[1:2, :],
                    ap["l0f_lnb"][l].rearrange("c p -> (c p)")[None, :],
                    deps=deps)
        ton = k.do("v", "memset", ones[:], 1.0, deps=deps)
        tep = k.do("v", "memset", eps[:], 1e-5, deps=deps)
        trh = k.do("v", "memset", rh[:], 0.0, deps=deps)
        tone = k.do("v", "memset", hnf[:1, :], 1.0, deps=deps)
        trh2a = k.dma("p", rh[1:2, 0, 1, :], hnf[0:1, 0:TT], deps=[trh, tone])
        trh2b = k.dma("p", rh[1:2, 1, 1, :], hnf[0:1, 0:TT], deps=[trh, tone])
        trh2 = k.do("v", "memset", hnf[:1, :], 0.0,
                    deps=[trh2a, trh2b])
        pre = [tw1, tb1, tg1, tg2, ton, tep, trh2a, trh2b, trh2]
        joins = {}
        for i in range(B * NTT):
            b, tt = divmod(i, NTT)
            u = i % 2
            war = joins.get(i - 2, list(deps) + pre)
            t0 = tt * TT
            tl = _shift_load(k, stg[:, u, :], src, b, t0, war)
            tcst = k.do("v", "tensor_copy", hbt[:], stg[:, u, :],
                        deps=war + tl)
            tsqr = k.do("s", "activation", sq[:], hbt[:], AF.Square,
                        deps=war + [tcst])
            for c in range(NC):
                ts0 = k.mm(ps[0:1, 4 * u, :], ones[:],
                           hbt[:, c * TT:(c + 1) * TT],
                           start=(c == 0), stop=(c == NC - 1),
                           deps=war + [tcst, ton])
            for c in range(NC):
                ts1 = k.mm(ps[0:1, 4 * u + 1, :], ones[:],
                           sq[:, c * TT:(c + 1) * TT],
                           start=(c == 0), stop=(c == NC - 1), deps=[tsqr])
            R = rows[:, u, :, :]
            tm = k.do("v", "tensor_scalar_mul", R[:, 0, :], ps[0:1, 4 * u, :],
                      1.0 / D, deps=war + [ts0])
            te = k.do("v", "tensor_scalar_mul", R[:, 1, :],
                      ps[0:1, 4 * u + 1, :], 1.0 / D, deps=[ts1])
            tmm = k.do("v", "tensor_mul", R[:, 2, :], R[:, 0, :], R[:, 0, :],
                       deps=[tm])
            tv = k.do("v", "tensor_sub", R[:, 3, :], R[:, 1, :], R[:, 2, :],
                      deps=[te, tmm])
            tsd = k.do("s", "activation", R[:, 4, :], R[:, 3, :], AF.Sqrt,
                       bias=eps[:], deps=war + [tv, tep])
            tr_ = k.do("v", "reciprocal", R[:, 5, :], R[:, 4, :], deps=[tsd])
            tra = k.do("v", "tensor_copy", rh[0:1, u, 0, :], R[:, 5, :],
                       deps=[tr_])
            tnm = k.do("v", "tensor_mul", R[:, 2, :], R[:, 0, :], R[:, 5, :],
                       deps=[tr_])
            tnm2 = k.do("v", "tensor_scalar_mul", rh[0:1, u, 1, :], R[:, 2, :],
                        -1.0, deps=[tnm])
            hn_toks = []
            for c in range(NC):
                ta = k.mm(ps[:, 4 * u + 2, :], glb[:, c * 128:(c + 1) * 128],
                          rh[:, u, 0, :], True, True,
                          deps=war + [tra, trh2, tg2] +
                          (hn_toks[-1:] if hn_toks else []))
                tb_ = k.mm(ps[:, 4 * u + 3, :], glb[:, c * 128:(c + 1) * 128],
                           rh[:, u, 1, :], True, True, deps=[tnm2])
                th1 = k.do("v", "tensor_mul", hnf[:],
                           stg[:, u, c * TT:(c + 1) * TT], ps[:, 4 * u + 2, :],
                           deps=war + [ta] + tl)
                th2_ = k.do("v", "tensor_add", hnf[:], hnf[:],
                            ps[:, 4 * u + 3, :], deps=[tb_, th1])
                th3 = k.do("v", "tensor_copy", hnb[:, u, c * TT:(c + 1) * TT],
                           hnf[:], deps=[th2_])
                hn_toks.append(th3)
            gm_prev = {}
            for j in range(NHC):
                jp = j % 2
                ba = 4 * u + jp
                bg = 4 * u + 2 + jp
                wdeps = war + [tw1, hn_toks[-1]]
                prev = gm_prev.get(jp)
                if prev is not None:
                    wdeps = wdeps + [prev[0]]
                tma = tmg = None
                for c in range(NC):
                    tma = k.mm(ps[:, ba, :],
                               w1[:, c * 2 * HIDP + j * 128:
                                  c * 2 * HIDP + j * 128 + 128],
                               hnb[:, u, c * TT:(c + 1) * TT],
                               start=(c == 0), stop=(c == NC - 1), deps=wdeps)
                for c in range(NC):
                    tmg = k.mm(ps[:, bg, :],
                               w1[:, c * 2 * HIDP + HIDP + j * 128:
                                  c * 2 * HIDP + HIDP + j * 128 + 128],
                               hnb[:, u, c * TT:(c + 1) * TT],
                               start=(c == 0), stop=(c == NC - 1), deps=wdeps)
                tgl = k.do("s", "activation", gl[:, u, jp, :], ps[:, bg, :],
                           AF.Gelu, bias=b1[:, NHC + j:NHC + j + 1],
                           deps=war + [tmg, tb1] +
                           ([prev[0]] if prev else []))
                tab = k.do("v", "tensor_scalar_add", ab[:, u, jp, :],
                           ps[:, ba, :], b1[:, j:j + 1],
                           deps=war + [tma, tb1] +
                           ([prev[0]] if prev else []))
                tm_ = k.do("v", "tensor_mul", mo[:, u, jp, :],
                           ab[:, u, jp, :], gl[:, u, jp, :],
                           deps=[tgl, tab] + ([prev[1]] if prev else []))
                td = k.dma("p", mgl[b, j, :, t0:t0 + TT], mo[:, u, jp, :],
                           deps=[tm_])
                gm_prev[jp] = (tm_, td)
            joins[i] = k.tip()
    return k.tip()


def emit_l1a_qkv(k, ap, deps):
    nc = k.nc
    xn2, q2aug, k2aug, vt2 = ap["xn2"], ap["q2aug"], ap["k2aug"], ap["vt2"]
    with ExitStack() as ctx:
        wqs = ctx.enter_context(nc.sbuf_tensor(k.uid + "lq_wq", [128, NC * 1024], BF16))
        wks = ctx.enter_context(nc.sbuf_tensor(k.uid + "lq_wk", [128, NC * 1024], BF16))
        wvs = ctx.enter_context(nc.sbuf_tensor(k.uid + "lq_wv", [128, NC * 1024], BF16))
        stg = ctx.enter_context(nc.sbuf_tensor(k.uid + "lq_st", [128, 2048], F32))
        bqs = ctx.enter_context(nc.sbuf_tensor(k.uid + "lq_bq", [128, NC], F32))
        bks = ctx.enter_context(nc.sbuf_tensor(k.uid + "lq_bk", [128, NC], F32))
        bvr = ctx.enter_context(nc.sbuf_tensor(k.uid + "lq_bv", [1, 1024], BF16))
        bvf = ctx.enter_context(nc.sbuf_tensor(k.uid + "lq_bvf", [1, 1024], F32))
        wqr = ctx.enter_context(nc.sbuf_tensor(k.uid + "lq_wqr", [128, 16], BF16))
        wkr = ctx.enter_context(nc.sbuf_tensor(k.uid + "lq_wkr", [128, 16], BF16))
        wrf = ctx.enter_context(nc.sbuf_tensor(k.uid + "lq_wrf", [128, 32], F32))
        onesc = ctx.enter_context(nc.sbuf_tensor(k.uid + "lq_on", [128, 1], BF16))
        onesr = ctx.enter_context(nc.sbuf_tensor(k.uid + "lq_or", [1, T], BF16))
        xnb = ctx.enter_context(nc.sbuf_tensor(k.uid + "lq_xnb", [128, 2, NC * TT], BF16))
        qk = ctx.enter_context(nc.sbuf_tensor(k.uid + "lq_qk", [128, 2, TT], BF16))
        sqr = ctx.enter_context(nc.sbuf_tensor(k.uid + "lq_sqr", [16, 2, TT], BF16))
        rr = ctx.enter_context(nc.sbuf_tensor(k.uid + "lq_rr", [16, 2, TT], BF16))
        nr = ctx.enter_context(nc.sbuf_tensor(k.uid + "lq_nr", [1, 2, TT], F32))
        crw = ctx.enter_context(nc.sbuf_tensor(k.uid + "lq_crw", [1, 2, TT], BF16))
        mks = ctx.enter_context(nc.sbuf_tensor(k.uid + "lq_mks", [1, 16, 6], F32))
        mkv = ctx.enter_context(nc.sbuf_tensor(k.uid + "lq_mkv", [1, 16], F32))
        vb = ctx.enter_context(nc.sbuf_tensor(k.uid + "lq_vb", [128, 2, TT], BF16))
        ps = ctx.enter_context(nc.psum_tensor(k.uid + "lq_ps", [128, 8, TT], F32))
        tw1 = cast_weights(k, wqs[:], ap["l1a_wq"].rearrange("c p n -> p c n"),
                           deps, stage_f32=stg[:])
        tw2 = cast_weights(k, wks[:], ap["l1a_wk"].rearrange("c p n -> p c n"),
                           [tw1], stage_f32=stg[:])
        tw3 = cast_weights(k, wvs[:], ap["l1a_wv"].rearrange("c p n -> p c n"),
                           [tw2], stage_f32=stg[:])
        tb1 = k.dma("q", bqs[:], ap["l1a_bq"][:, :], deps=deps)
        tb2 = k.dma("q", bks[:], ap["l1a_bk"][:, :], deps=deps)
        tb3 = k.dma("q", bvf[:], ap["l1a_bvrow"][:, :], deps=deps)
        tb4 = k.do("v", "tensor_copy", bvr[:], bvf[:], deps=[tb3])
        tw4a = k.dma("q", wrf[0:64, 0:16], ap["l1a_wqr"][:, :], deps=deps)
        tw4b = k.dma("q", wrf[64:128, 0:16], ap["l1a_wqr"][:, :], deps=deps)
        tw5a = k.dma("q", wrf[0:64, 16:32], ap["l1a_wkr"][:, :], deps=deps)
        tw5b = k.dma("q", wrf[64:128, 16:32], ap["l1a_wkr"][:, :], deps=deps)
        tw6 = k.do("v", "tensor_copy", wqr[:], wrf[:, 0:16],
                   deps=[tw4a, tw4b])
        tw7 = k.do("v", "tensor_copy", wkr[:], wrf[:, 16:32],
                   deps=[tw5a, tw5b])
        to1 = k.do("v", "memset", onesc[:], 1.0, deps=deps)
        to2 = k.do("v", "memset", onesr[:], 1.0, deps=deps)
        pre = [tw3, tb1, tb2, tb4, tw6, tw7, to1, to2]
        bj = list(deps) + pre

        def qk_phase(wsb, bsb, wr, dst, is_q, war0):
            ph = []
            rprev = {}
            nprev = {}
            shr = {}
            for tt in range(NTT):
                u = tt % 2
                war = ph[-2] if tt >= 2 else war0
                tl = k.dma("q", xnb[:, u, :].rearrange("p (c t) -> p c t", c=NC),
                           _fmslice(xn2, b, tt * TT), deps=war)
                toks = [tl]
                eprev = {}
                for j in range(NC):
                    bqk = (j % 2)
                    tpm = None
                    for c in range(NC):
                        tpm = k.mm(ps[:, bqk, :],
                                   wsb[:, c * 1024 + j * 128:
                                       c * 1024 + j * 128 + 128],
                                   xnb[:, u, c * TT:(c + 1) * TT],
                                   start=(c == 0), stop=(c == NC - 1),
                                   deps=list(war) + [tl] +
                                   ([eprev[bqk]] if bqk in eprev else []) +
                                   (rprev.get((bqk, 0), []) +
                                    rprev.get((bqk, 1), [])))
                    tev = k.do("s", "activation", qk[:, u, :], ps[:, bqk, :],
                               AF.Identity, bias=bsb[:, j:j + 1],
                               deps=list(war) + [tpm] +
                               ([eprev[bqk]] if bqk in eprev else []))
                    eprev[bqk] = tev
                    for hh in range(2):
                        if INNER14 < 1:
                            break
                        h = 2 * j + hh
                        rb = 2 + 2 * (j % 2) + hh
                        nb = 6 + hh
                        rpv = rprev.get((j % 2, hh), [])
                        npv = nprev.get(hh, [])
                        trm = k.mm(ps[0:16, rb, :],
                                   wr[hh * 64:hh * 64 + 64, :],
                                   qk[hh * 64:hh * 64 + 64, u, :], True, True,
                                   deps=[tev] + rpv)
                        if INNER14 >= 2:
                            tsq = k.do("s", "activation", sqr[:, u, :],
                                       ps[0:16, rb, :], AF.Square,
                                       deps=[trm, shr.get(("sq", u))] + npv)
                            tnm = tsq
                        if INNER14 >= 3:
                            tnm = k.mm(ps[0:1, nb, :], onesc[0:16, :],
                                       sqr[:, u, :], True, True,
                                       deps=[tsq, to1] + npv)
                        tcv = k.do("v", "tensor_copy", rr[:, u, :],
                                   ps[0:16, rb, :],
                                   deps=[trm, tsq if INNER14 >= 2 else None,
                                         shr.get(("rr", u))] + rpv)
                        td1 = k.dma("p", dst[b, h, 0:16, tt * TT:(tt + 1) * TT],
                                    rr[:, u, :], deps=[tcv])
                        shr[("rr", u)] = td1
                        if INNER14 >= 3:
                            shr[("sq", u)] = tnm
                        elif INNER14 >= 2:
                            shr[("sq", u)] = tsq
                            rprev[(j % 2, hh)] = [tsq, tcv, td1]
                        else:
                            rprev[(j % 2, hh)] = [tcv, td1]
                        if is_q and INNER14 >= 2:
                            tsr = k.do("s", "activation", nr[:, u, :],
                                       ps[0:1, nb, :], AF.Sqrt,
                                       deps=[tnm, shr.get(("nr", u))])
                            tcr = k.do("v", "tensor_scalar_mul", crw[:, u, :],
                                       nr[:, u, :], mkv[0:1, h:h + 1],
                                       deps=[tsr, mkr[0],
                                             shr.get(("cr", u))])
                            td2 = k.dma("p", dst[b, h, 16:17,
                                             tt * TT:(tt + 1) * TT],
                                        crw[:, u, :], deps=[tcr])
                            shr[("nr", u)] = tcr
                            shr[("cr", u)] = td2
                            nprev[hh] = [tsr]
                            toks += [tcr, td1, td2]
                        elif INNER14 >= 4:
                            tmx = k.do("v", "reduce_max",
                                       mks[0:1, h, tt:tt + 1],
                                       ps[0:1, nb, :], axis=AX.X,
                                       deps=[tnm])
                            if DBG14 and b == 0 and h == 0 and tt == 0:
                                tdg = k.do("v", "tensor_copy",
                                           rr[:, u, :], ps[0:16, rb, :],
                                           deps=[tnm, tmx, tcv])
                                k.dma("p", ap["dbg14"][1:2, 0:512],
                                      sqr[0:1, u, :], deps=[tnm, tsq])
                                tdh = k.do("v", "tensor_copy",
                                           nr[0:1, 0, :], ps[0:1, nb, :],
                                           deps=[tnm, tmx])
                                k.dma("p", ap["dbg14"][2:3, 0:512],
                                      nr[0:1, 0, :], deps=[tdh])
                                shr[("rr", u)] = tdg
                            nprev[hh] = [tmx]
                            toks += [tmx, td1]
                        elif INNER14 >= 2:
                            nprev[hh] = [tnm]
                            toks += [td1]
                        else:
                            toks += [td1]
                ph.append(toks)
            return [x for t_ in ph for x in t_]

        mkr = [None]
        for b in range(B):
            kj = qk_phase(wks, bks, wkr, k2aug, False, bj)
            if PH14 < 2:
                bj = k.tip()
                continue
            for h in range(16):
                k.dma("p", k2aug[b, h, 16:17, :], onesr[:], deps=[to2])
            mt = []
            for h in range(16):
                t1_ = k.do("v", "reduce_max", mks[0:1, h, 4:5],
                           mks[0:1, h, 0:NTT], axis=AX.X, deps=kj)
                t2_ = k.do("s", "activation", mks[0:1, h, 5:6],
                           mks[0:1, h, 4:5], AF.Sqrt, deps=[t1_])
                t3_ = k.do("v", "tensor_scalar_mul", mkv[0:1, h:h + 1],
                           mks[0:1, h, 5:6], -1.0, deps=[t2_])
                mt.append(t3_)
            mkr[0] = mt[-1]
            if b == 0:
                k.dma("p", ap["dbg14"][0:1, 0:96], mks[0:1, :, :], deps=mt)
                k.dma("p", ap["dbg14"][0:1, 96:112], mkv[:], deps=mt)
            kend = k.tip()
            if PH14 < 3:
                bj = k.tip()
                continue
            vph = []
            vprev = {}
            for tt in range(NTT):
                u = tt % 2
                war = vph[-2] if tt >= 2 else kend
                tl = k.dma("q", xnb[:, u, :].rearrange("p (c t) -> p c t", c=NC),
                           _fmslice(xn2, b, tt * TT), deps=war)
                toks = [tl]
                for tc4 in range(4):
                    for vc in range(2):
                        iv = tc4 * 2 + vc
                        uv = iv % 2
                        bv_ = 6 + uv
                        pv2 = vprev.get(uv, [])
                        tvm = None
                        for c in range(NC):
                            tvm = k.mm(ps[:, bv_, :],
                                       xnb[:, u, c * TT + tc4 * 128:
                                           c * TT + tc4 * 128 + 128],
                                       wvs[:, c * 1024 + vc * 512:
                                           c * 1024 + vc * 512 + 512],
                                       start=(c == 0), stop=False,
                                       deps=list(war) + [tl] + pv2[:1],
                                       mark=False)
                        tvm = k.mm(ps[:, bv_, :], onesr[0:1, 0:128],
                                   bvr[:, vc * 512:(vc + 1) * 512],
                                   start=False, stop=True,
                                   deps=[to2, tb4])
                        tvc = k.do("v", "tensor_copy", vb[:, uv, :],
                                   ps[:, bv_, :],
                                   deps=list(war) + [tvm] + pv2)
                        tds = [k.dma("p",
                                     vt2[b, vc * 8 + h2,
                                         tt * TT + tc4 * 128:
                                         tt * TT + tc4 * 128 + 128, :],
                                     vb[:, uv, h2 * 64:(h2 + 1) * 64],
                                     deps=[tvc]) for h2 in range(8)]
                        vprev[uv] = [tvc] + tds
                        toks += [tvc] + tds
                vph.append(toks)
            vend = k.tip()
            if PH14 >= 4:
                qj = qk_phase(wqs, bqs, wqr, q2aug, True, vend)
            bj = k.tip()
    return k.tip()


def emit_l1a_att(k, ap, deps):
    nc = k.nc
    q2aug, k2aug, vt2, o2 = ap["q2aug"], ap["k2aug"], ap["vt2"], ap["o2"]
    with ExitStack() as ctx:
        ka = ctx.enter_context(nc.sbuf_tensor(k.uid + "la_ka", [17, T], BF16))
        va = ctx.enter_context(nc.sbuf_tensor(k.uid + "la_va", [128, 16 * 65], BF16))
        qa = ctx.enter_context(nc.sbuf_tensor(k.uid + "la_qa", [17, 2, TT], BF16))
        ex = ctx.enter_context(nc.sbuf_tensor(k.uid + "la_ex", [128, 2, TT], BF16))
        rec = ctx.enter_context(nc.sbuf_tensor(k.uid + "la_rec", [1, TT], F32))
        osb = ctx.enter_context(nc.sbuf_tensor(k.uid + "la_osb", [64, TT], F32))
        onb = ctx.enter_context(nc.sbuf_tensor(k.uid + "la_onb", [64, TT], BF16))
        o1f = ctx.enter_context(nc.sbuf_tensor(k.uid + "la_o1f", [1, 64], F32))
        ps = ctx.enter_context(nc.psum_tensor(k.uid + "la_ps", [128, 8, TT], F32))
        tmv = k.do("v", "memset", va[:], 1.0, deps=deps)
        to1 = k.do("v", "memset", o1f[:], 1.0, deps=deps)
        pre = [tmv, to1]
        hb_join = list(deps) + pre
        for b in range(B):
            for h in range(16):
                tka = k.dma("q", ka[:], k2aug[b, h, :, :], deps=hb_join)
                tva = [k.dma("q", va[:, kt * 65:kt * 65 + 64],
                             vt2[b, h, kt * 128:(kt + 1) * 128, :],
                             deps=hb_join) for kt in range(16)]
                qtj = hb_join + [tka] + tva
                for qt in range(NTT):
                    uq = qt % 2
                    tqa = k.dma("q", qa[:, uq, :],
                                q2aug[b, h, :, qt * TT:(qt + 1) * TT],
                                deps=qtj)
                    evs = []
                    for kt in range(16):
                        u = kt % 2
                        deps_s = [tka, tqa]
                        if kt >= 2:
                            deps_s.append(evs[-2][0])
                        else:
                            deps_s += qtj
                        tsc = k.mm(ps[:, u, :], ka[:, kt * 128:(kt + 1) * 128],
                                   qa[:, uq, :], True, True, deps=deps_s)
                        texdeps = [tsc]
                        if kt >= 2:
                            texdeps.append(evs[-2][1])
                        tex = k.do("s", "activation", ex[:, u, :],
                                   ps[:, u, :], AF.Exp, scale=0.25,
                                   deps=texdeps)
                        tav = k.mm(ps[0:65, 2, :], va[:, kt * 65:kt * 65 + 65],
                                   ex[:, u, :], start=(kt == 0),
                                   stop=(kt == 15),
                                   deps=[tex, tmv] + tva +
                                   (qtj if kt == 0 else []), mark=True)
                        evs.append((tex, tav))
                    trc = k.do("v", "reciprocal", rec[:], ps[64:65, 2, :],
                               deps=[tav] + qtj)
                    tbc = k.mm(ps[0:64, 3, :], o1f[:], rec[:], True, True,
                               deps=[trc, to1] + qtj)
                    tcp = k.do("v", "tensor_copy", osb[:], ps[0:64, 2, :],
                               deps=[tav] + qtj)
                    tmu = k.do("v", "tensor_mul", onb[:], osb[:],
                               ps[0:64, 3, :], deps=[tbc, tcp])
                    tst = k.dma("p", o2[b, h // 2, (h % 2) * 64:(h % 2) * 64 + 64,
                                     qt * TT:(qt + 1) * TT],
                                onb[:], deps=[tmu])
                    qtj = [tmu, tst, trc, tcp]
                hb_join = k.tip()
    return k.tip()


def emit_gate(k, ap, deps):
    nc = k.nc
    xn2, m8, mbeta = ap["xn2"], ap["m8"], ap["mbeta"]
    with ExitStack() as ctx:
        gw = ctx.enter_context(nc.sbuf_tensor(k.uid + "gt_w", [128, NC * 8], BF16))
        stg = ctx.enter_context(nc.sbuf_tensor(k.uid + "gt_st", [128, NC * 8], F32))
        gb8 = ctx.enter_context(nc.sbuf_tensor(k.uid + "gt_b", [8, 1], F32))
        idn = ctx.enter_context(nc.sbuf_tensor(k.uid + "gt_id", [128, 128], BF16))
        xnb = ctx.enter_context(nc.sbuf_tensor(k.uid + "gt_x", [128, 2, NC * TT], BF16))
        lgb = ctx.enter_context(nc.sbuf_tensor(k.uid + "gt_lb", [8, 2, TT], BF16))
        lgt = ctx.enter_context(nc.sbuf_tensor(k.uid + "gt_lt", [128, 2, 8], F32))
        mxt = ctx.enter_context(nc.sbuf_tensor(k.uid + "gt_mx", [128, 2, 1], F32))
        mskb = ctx.enter_context(nc.sbuf_tensor(k.uid + "gt_mb", [128, 2, 8], BF16))
        mk8 = ctx.enter_context(nc.sbuf_tensor(k.uid + "gt_m8", [8, 2, 128], F32))
        mbt = ctx.enter_context(nc.sbuf_tensor(k.uid + "gt_bt", [8, 2, 128], F32))
        psg = ctx.enter_context(nc.psum_tensor(k.uid + "gt_ps", [128, 2, TT], F32))
        psa = ctx.enter_context(nc.psum_tensor(k.uid + "gt_pa", [128, 2, 8], BF16))
        psb = ctx.enter_context(nc.psum_tensor(k.uid + "gt_pb", [8, 2, 128], BF16))
        tw = cast_weights(k, gw[:], ap["l1f_gw"].rearrange("c p n -> p c n"),
                          deps, stage_f32=stg[:])
        tb = k.dma("q", gb8[:], ap["l1f_gbrow"][:, :], deps=deps)
        tid = k.dma("q", idn[:], ap["c_id"][:, :], deps=deps)
        pre = [tw, tb, tid]
        joins = {}
        tprev = {}
        for i in range(B * NTT):
            b, tt = divmod(i, NTT)
            u = i % 2
            war = joins.get(i - 2, list(deps) + pre)
            t0g = b * T + tt * TT
            tl = k.dma("q", xnb[:, u, :].rearrange("p (c t) -> p c t", c=NC),
                       _fmslice(xn2, b, tt * TT), deps=war)
            tg = None
            for c in range(NC):
                tg = k.mm(psg[0:8, u, :], gw[:, c * 8:(c + 1) * 8],
                          xnb[:, u, c * TT:(c + 1) * TT],
                          start=(c == 0), stop=(c == NC - 1),
                          deps=list(war) + [tl, tw])
            tev = k.do("s", "activation", lgb[:, u, :], psg[0:8, u, :],
                       AF.Identity, bias=gb8[:], deps=list(war) + [tg, tb])
            for tc4 in range(4):
                gc = i * 4 + tc4
                u4 = gc % 2
                pv = tprev.get(gc - 2, {})
                wv = list(war) if tc4 < 2 else []
                ttr = k.do("t", "transpose", psa[:, u4, :],
                           lgb[0:8, u, tc4 * 128:(tc4 + 1) * 128],
                           idn[0:8, 0:8],
                           deps=[tev, tid, pv.get("tcp")] + wv)
                tcp = k.do("v", "tensor_copy", lgt[:, u4, :], psa[:, u4, :],
                           deps=[ttr, pv.get("teq")] + wv)
                tmx = k.do("v", "reduce_max", mxt[:, u4, :], lgt[:, u4, :],
                           axis=AX.X, deps=[tcp, pv.get("teq")])
                teq = k.do("v", "tensor_scalar", mskb[:, u4, :],
                           lgt[:, u4, :], mxt[:, u4, :], None, ALU.is_equal,
                           deps=[tmx, pv.get("ttr2")])
                ttr2 = k.do("t", "transpose", psb[:, u4, :], mskb[:, u4, :],
                            idn[:], deps=[teq, pv.get("tm8")] + wv)
                tm8 = k.do("v", "tensor_copy", mk8[:, u4, :], psb[:, u4, :],
                           deps=[ttr2, pv.get("d1"), pv.get("tbt")])
                tbt = k.do("v", "tensor_scalar", mbt[:, u4, :], mk8[:, u4, :],
                           1.0, 1.0e9, ALU.subtract, ALU.mult,
                           deps=[tm8, pv.get("d2")])
                d1 = k.dma("p", m8[:, t0g + tc4 * 128:t0g + tc4 * 128 + 128],
                           mk8[:, u4, :], deps=[tm8])
                d2 = k.dma("p", mbeta[:, t0g + tc4 * 128:t0g + tc4 * 128 + 128],
                           mbt[:, u4, :], deps=[tbt])
                tprev[gc] = dict(ttr=ttr, tcp=tcp, tmx=tmx, teq=teq,
                                 ttr2=ttr2, tm8=tm8, tbt=tbt, d1=d1, d2=d2)
            joins[i] = k.tip()
    return k.tip()


def emit_moe(k, ap, deps):
    nc = k.nc
    xn2, m8, mbeta, out = ap["xn2"], ap["m8"], ap["mbeta"], ap["out"]
    acc_in = RD(ap, "out")
    NTU = B * NTT
    with ExitStack() as ctx:
        w1e = ctx.enter_context(nc.sbuf_tensor(k.uid + "mo_w1", [128, NC * 4096], BF16))
        w2e = ctx.enter_context(nc.sbuf_tensor(k.uid + "mo_w2", [128, 32 * 1024], BF16))
        stg = ctx.enter_context(nc.sbuf_tensor(k.uid + "mo_st", [128, 2048], F32))
        b1s = ctx.enter_context(nc.sbuf_tensor(k.uid + "mo_b1", [128, 32], F32))
        b2m = ctx.enter_context(nc.sbuf_tensor(k.uid + "mo_b2", [8, NC * 128], BF16))
        b2f = ctx.enter_context(nc.sbuf_tensor(k.uid + "mo_b2f", [8, NC * 128], F32))
        onesb = ctx.enter_context(nc.sbuf_tensor(k.uid + "mo_on", [1, 128], BF16))
        xg = ctx.enter_context(nc.sbuf_tensor(k.uid + "mo_x", [128, 2, NC * TT], BF16))
        m8f = ctx.enter_context(nc.sbuf_tensor(k.uid + "mo_m8f", [8, 2, TT], F32))
        m8s = ctx.enter_context(nc.sbuf_tensor(k.uid + "mo_m8s", [8, 2, TT], BF16))
        mbf = ctx.enter_context(nc.sbuf_tensor(k.uid + "mo_mbf", [1, 2, TT], F32))
        mbs = ctx.enter_context(nc.sbuf_tensor(k.uid + "mo_mbs", [1, 2, TT], BF16))
        hsb = ctx.enter_context(nc.sbuf_tensor(k.uid + "mo_h", [128, 32 * TT], BF16))
        acct = ctx.enter_context(nc.sbuf_tensor(k.uid + "mo_a", [128, 2, TT], F32))
        ps = ctx.enter_context(nc.psum_tensor(k.uid + "mo_ps", [128, 8, TT], F32))
        ton = k.do("v", "memset", onesb[:], 1.0, deps=deps)
        tb2l = k.dma("q", b2f[:].rearrange("p (c n) -> p c n", c=NC),
                     ap["l1f_b2m"].rearrange("c p n -> p c n"), deps=deps)
        tb2 = k.do("v", "tensor_copy", b2m[:], b2f[:], deps=[tb2l])
        pre = [ton, tb2]
        ej = list(deps) + pre
        region = {}
        for e in range(8):
            tw1 = cast_weights(k, w1e[:],
                               ap["l1f_w1t"][e].rearrange("c p n -> p c n"),
                               ej, stage_f32=stg[:])
            tw2 = cast_weights(k, w2e[:],
                               ap["l1f_w2t"][e].rearrange("c p n -> p c n"),
                               [tw1], stage_f32=stg[:])
            tb1l = k.dma("q", b1s[:], ap["l1f_b1"][e], deps=ej)
            epre = [tw2, tb1l]
            uprev = {}
            for ttu in range(NTU):
                b_, tt = divmod(ttu, NTT)
                t0 = tt * TT
                tg0 = b_ * T + t0
                ux = ttu % 2
                tlx = k.dma("q",
                            xg[:, ux, :].rearrange("p (c t) -> p c t", c=NC),
                            _fmslice(xn2, b_, t0),
                            deps=ej + epre + [uprev.get(("x", ux))])
                tmbf = k.dma("q", mbf[:, ux, :], mbeta[e:e + 1, tg0:tg0 + TT],
                             deps=ej + [uprev.get(("mbf", ux))])
                tmbs = k.do("v", "tensor_copy", mbs[:, ux, :], mbf[:, ux, :],
                            deps=[tmbf, uprev.get(("mbs", ux))])
                if e == 7:
                    tl8f = k.dma("q", m8f[:, ux, :], m8[:, tg0:tg0 + TT],
                                 deps=ej + [uprev.get(("m8f", ux))])
                    tl8 = k.do("v", "tensor_copy", m8s[:, ux, :],
                               m8f[:, ux, :],
                               deps=[tl8f, uprev.get(("m8s", ux))])
                else:
                    tl8 = None
                relu_toks = []
                for hc in range(32):
                    bh = 6 + (hc % 2)
                    wdeps = [tw1, tlx, tmbs, ton]
                    if hc >= 2:
                        wdeps.append(relu_toks[-2])
                    elif ("pb", 6 + hc) in uprev:
                        wdeps.append(uprev[("pb", 6 + hc)])
                    tph = None
                    for c in range(NC):
                        tph = k.mm(ps[:, bh, :],
                                   w1e[:, c * 4096 + hc * 128:
                                       c * 4096 + hc * 128 + 128],
                                   xg[:, ux, c * TT:(c + 1) * TT],
                                   start=(c == 0), stop=False, deps=wdeps,
                                   mark=False)
                    tph = k.mm(ps[:, bh, :], onesb[:],
                               mbs[:, ux, :], start=False, stop=True,
                               deps=wdeps + [tmbs])
                    trl = k.do("s", "activation", hsb[:, hc * TT:(hc + 1) * TT],
                               ps[:, bh, :], AF.Relu, bias=b1s[:, hc:hc + 1],
                               deps=[tph, tb1l] +
                               ([relu_toks[-2]] if hc >= 2 else
                                [uprev.get(("h", hc % 2))]))
                    relu_toks.append(trl)
                uprev[("x", ux)] = relu_toks[-1]
                uprev[("mbf", ux)] = tmbs
                uprev[("mbs", ux)] = relu_toks[-1]
                if e == 7:
                    uprev[("m8f", ux)] = tl8
                uprev[("h", 0)] = relu_toks[-2]
                uprev[("h", 1)] = relu_toks[-1]
                for dc in range(NC):
                    ua = dc % 2
                    src_acc = acc_in if e == 0 else out
                    tla = k.dma("q", acct[:, ua, :],
                                src_acc[b_, dc, :, t0:t0 + TT],
                                deps=[region.get((ttu, dc)),
                                      uprev.get(("a", ua))])
                    tpy = None
                    wdeps2 = [tw2, relu_toks[-1]]
                    if ("pb", dc) in uprev:
                        wdeps2.append(uprev[("pb", dc)])
                    for hc in range(32):
                        last = (hc == 31 and e != 7)
                        tpy = k.mm(ps[:, dc, :],
                                   w2e[:, hc * 1024 + dc * 128:
                                       hc * 1024 + dc * 128 + 128],
                                   hsb[:, hc * TT:(hc + 1) * TT],
                                   start=(hc == 0), stop=last,
                                   deps=wdeps2 + [relu_toks[hc]], mark=last)
                    if e == 7:
                        tpy = k.mm(ps[:, dc, :],
                                   b2m[:, dc * 128:(dc + 1) * 128],
                                   m8s[:, ux, :], start=False,
                                   stop=True, deps=[tb2, tl8])
                        uprev[("m8s", ux)] = tpy
                    tac = k.do("v", "tensor_add", acct[:, ua, :],
                               acct[:, ua, :], ps[:, dc, :],
                               deps=[tpy, tla])
                    tst = k.dma("p", out[b_, dc, :, t0:t0 + TT],
                                acct[:, ua, :], deps=[tac])
                    region[(ttu, dc)] = tst
                    uprev[("a", ua)] = tst
                    uprev[("pb", dc)] = tac
            ej = k.tip()
    return k.tip()



def _l0a_qkv_reads():
    return ["xn1", "l0a_wqa", "l0a_wqb", "l0a_wkva", "l0a_wkvbK", "l0a_wkvbV",
            "c_cd", "c_sd", "c_p128"]


def RD(ap, name):
    return ap.get(name + "__in", ap[name])


STAGES = {
    "s0_ln1": {
        "fn": lambda k, ap, d: emit_ln(k, ap, ap["xin"], ap["xn1"], ap["xfa"], d),
        "reads": ["xin"], "writes": ["xn1", "xfa"]},
    "s1_l0aqkv": {
        "fn": lambda k, ap, d: emit_l0a_qkv(k, ap, d),
        "reads": _l0a_qkv_reads(), "writes": ["qaug", "kaug", "vt0"]},
    "s2_l0aatt": {
        "fn": lambda k, ap, d: emit_l0a_att(k, ap, d),
        "reads": ["qaug", "kaug", "vt0", "c_cm"], "writes": ["o0"]},
    "s3_l0aout": {
        "fn": lambda k, ap, d: emit_matproj_add(
            k, ap, ap["o0"], ap["l0a_wout"], ap["l0a_bout"], ap["xin"],
            ap["x1"], d, tag="s3"),
        "reads": ["o0", "xin", "l0a_wout", "l0a_bout"], "writes": ["x1"]},
}

for _l in (0, 1):
    _src = "xfa" if _l == 0 else "xfb"
    _dst = "xfb" if _l == 0 else "xfa"
    STAGES[f"s4_l{_l}"] = {
        "fn": (lambda l: lambda k, ap, d: emit_l0f_qkv(k, ap, l, d))(_l),
        "reads": [_src, "l0f_wqkv", "l0f_qs", "l0f_ks", "c_xqc", "c_xqs",
                  "c_xkc", "c_xks", "c_prot"],
        "writes": ["qf", "kf", "vt1"]}
    STAGES[f"s5_l{_l}"] = {
        "fn": (lambda l: lambda k, ap, d: emit_l0f_att(k, ap, l, d))(_l),
        "reads": ["qf", "kf", "vt1", "c_mtm", "c_mkm", "c_id"],
        "writes": ["of"]}
    STAGES[f"s6_l{_l}"] = {
        "fn": (lambda l, s: lambda k, ap, d: emit_matproj_add(
            k, ap, ap["of"], ap["l0f_wo"][l], None, RD(ap, s), ap[s], d,
            nin=4, tag=f"s6_{l}"))(_l, _src),
        "reads": ["of", "l0f_wo", _src], "writes": [_src]}
    STAGES[f"s7a_l{_l}"] = {
        "fn": (lambda l, s: lambda k, ap, d: emit_l0f_w1(k, ap, l, s, d))(
            _l, _src),
        "reads": [_src, "l0f_w1", "l0f_b1", "l0f_lng", "l0f_lnb"],
        "writes": ["mgl"]}
    STAGES[f"s7b_l{_l}"] = {
        "fn": (lambda l, s, t: lambda k, ap, d: emit_matproj_add(
            k, ap, ap["mgl"], ap["l0f_w2"][l], ap["l0f_b2"][l], RD(ap, s),
            ap[t], d, nin=NHC, tag=f"s7b_{l}"))(_l, _src, _dst),
        "reads": ["mgl", "l0f_w2", "l0f_b2", _src], "writes": [_dst]}

STAGES["s12_add"] = {
    "fn": lambda k, ap, d: emit_add(k, ap, ap["x1"], ap["xfa"], ap["x2"], d),
    "reads": ["x1", "xfa"], "writes": ["x2"]}
STAGES["s13_ln2"] = {
    "fn": lambda k, ap, d: emit_ln(k, ap, ap["x2"], ap["xn2"], None, d),
    "reads": ["x2"], "writes": ["xn2"]}
STAGES["s14_l1aqkv"] = {
    "fn": lambda k, ap, d: emit_l1a_qkv(k, ap, d),
    "reads": ["xn2", "l1a_wq", "l1a_wk", "l1a_wv", "l1a_bq", "l1a_bk",
              "l1a_bvrow", "l1a_wqr", "l1a_wkr"],
    "writes": ["q2aug", "k2aug", "vt2", "dbg14"]}
STAGES["s15_l1aatt"] = {
    "fn": lambda k, ap, d: emit_l1a_att(k, ap, d),
    "reads": ["q2aug", "k2aug", "vt2"], "writes": ["o2"]}
STAGES["s16_l1aout"] = {
    "fn": lambda k, ap, d: emit_matproj_add(
        k, ap, ap["o2"], ap["l1a_wo"], ap["l1a_bo"], ap["x2"], ap["out"], d,
        tag="s16"),
    "reads": ["o2", "x2", "l1a_wo", "l1a_bo"], "writes": ["out"]}
STAGES["s17_gate"] = {
    "fn": lambda k, ap, d: emit_gate(k, ap, d),
    "reads": ["xn2", "l1f_gw", "l1f_gbrow", "c_id"],
    "writes": ["m8", "mbeta"]}
STAGES["s18_moe"] = {
    "fn": lambda k, ap, d: emit_moe(k, ap, d),
    "reads": ["xn2", "m8", "mbeta", "l1f_w1t", "l1f_b1", "l1f_w2t",
              "l1f_b2m", "out"],
    "writes": ["out"]}

STAGE_ORDER = ["s0_ln1", "s1_l0aqkv", "s2_l0aatt", "s3_l0aout",
               "s4_l0", "s5_l0", "s6_l0", "s7a_l0", "s7b_l0",
               "s4_l1", "s5_l1", "s6_l1", "s7a_l1", "s7b_l1",
               "s12_add", "s13_ln2", "s14_l1aqkv", "s15_l1aatt",
               "s16_l1aout", "s17_gate", "s18_moe"]

GROUPS = [list(STAGE_ORDER)]


def _spec(name):
    if name in WEIGHTS:
        return WEIGHTS[name], F32
    if name in CONSTS:
        return CONSTS[name]
    if name in SCRATCH:
        return SCRATCH[name]
    if name in INPUTS:
        return INPUTS[name]
    if name in OUTPUTS:
        return OUTPUTS[name]
    raise KeyError(name)


def build_group(group, fused):
    nc = bass.Bass("TRN2", target_bir_lowering=False, debug=False)
    reads, writes = [], []
    for s in group:
        reads += [r for r in STAGES[s]["reads"] if r not in reads]
        writes += [w for w in STAGES[s]["writes"] if w not in writes]
    first = {}
    for s in group:
        for r in STAGES[s]["reads"]:
            first.setdefault(r, "r")
        for w in STAGES[s]["writes"]:
            first.setdefault(w, "w")
    ap = {}
    innames, outnames = [], []
    names = list(dict.fromkeys(reads + writes))
    for n in names:
        shape, dt = _spec(n)
        is_scratch = n in SCRATCH or n in OUTPUTS
        written = n in writes
        if not is_scratch or (fused and n not in OUTPUTS and n not in INPUTS):
            if not is_scratch:
                ap[n] = nc.dram_tensor(n, shape, dt, kind="ExternalInput").ap()
                innames.append(n)
            else:
                ap[n] = nc.dram_tensor(n, shape, dt, kind="Internal").ap()
        else:
            if written:
                if first.get(n) == "r" and not fused:
                    ap[n + "__in"] = nc.dram_tensor(
                        n + "__in", shape, dt, kind="ExternalInput").ap()
                    innames.append(n + "__in")
                ap[n] = nc.dram_tensor(n, shape, dt, kind="ExternalOutput").ap()
                outnames.append(n)
            else:
                ap[n] = nc.dram_tensor(n, shape, dt, kind="ExternalInput").ap()
                innames.append(n)
    if fused:
        pass
    k = K(nc)
    deps = []
    for si, s in enumerate(group):
        k.uid = f"g{si}_"
        deps = STAGES[s]["fn"](k, ap, list(deps))
    k.fence_all(k.tip())
    return nc, innames, outnames


_GROUP_CACHE = {}


def kernel(**inputs):
    w = host_weights(inputs)
    c = host_consts()
    xs = [host_core_inputs(inputs, i) for i in range(NCORE)]
    state = [dict() for _ in range(NCORE)]
    fused = len(GROUPS) == 1
    for gi, group in enumerate(GROUPS):
        key = (tuple(group), fused)
        if key not in _GROUP_CACHE:
            _GROUP_CACHE[key] = build_group(group, fused)
        nc, innames, outnames = _GROUP_CACHE[key]
        in_maps = []
        for core in range(NCORE):
            m = {}
            for n in innames:
                base = n[:-4] if n.endswith("__in") else n
                if base in w:
                    m[n] = w[base]
                elif base in c:
                    m[n] = c[base]
                elif base in INPUTS:
                    m[n] = xs[core][base]
                else:
                    arr = state[core].get(base)
                    if arr is None:
                        shape, dt = _spec(base)
                        arr = np.zeros(shape, mybir.dt.np(dt))
                    m[n] = arr
            in_maps.append(m)
        res = run_bass_kernel_spmd(nc, in_maps, core_ids=list(range(NCORE)))
        for core in range(NCORE):
            for n in outnames:
                state[core][n] = res.results[core][n]
    outs = []
    for core in range(NCORE):
        o = state[core].get("out")
        if o is None:
            o = np.zeros(OUTPUTS["out"][0], np.float32)
        outs.append(o.transpose(0, 3, 1, 2).reshape(B, T, D))
    return np.concatenate(outs, axis=0)


# version 04
# speedup vs baseline: 1.4341x; 1.2107x over previous
import numpy as np
import ml_dtypes
from contextlib import ExitStack
import concourse.bass as bass
import concourse.mybir as mybir
from concourse.bass_utils import run_bass_kernel_spmd

BF = ml_dtypes.bfloat16
F32 = mybir.dt.float32
BF16 = mybir.dt.bfloat16
AF = mybir.ActivationFunctionType
ALU = mybir.AluOpType
AX = mybir.AxisListType

NCORE = 8
B = 4
T = 2048
D = 1024
NC = 8
TT = 512
NTT = T // TT
NEG = -30000.0
BETA = -1.0e9
HIDP = 2816
NHC = 22
PH14 = 4
INNER14 = 4
DBG14 = False



class K:

    NPOOL = 8

    def __init__(self, nc):
        self.nc = nc
        self.eng = {"t": nc.tensor, "v": nc.vector, "s": nc.scalar,
                    "p": nc.gpsimd, "q": nc.sync}
        self.prog = {k: nc.alloc_semaphore("prog_" + k) for k in "tvs"}
        self.cnt = {k: 0 for k in "tvs"}
        self.dpool = {k: [nc.alloc_semaphore(f"dma_{k}{i}")
                          for i in range(self.NPOOL)] for k in "qsp"}
        self.dqi = {k: 0 for k in "qsp"}
        self.seen = {k: {} for k in "tvspq"}
        self.uid = ""

    def wait(self, ekey, deps):
        e = self.eng[ekey]
        seen = self.seen[ekey]
        for d in deps:
            if d is None:
                continue
            sem, val = d
            if val <= 0 or val <= seen.get(sem.num, 0):
                continue
            e.wait_ge(sem, val)
            seen[sem.num] = val

    def dma(self, qkey, out, in_, deps=(), **kw):
        slot = self.dqi[qkey] % self.NPOOL
        n_use = self.dqi[qkey] // self.NPOOL
        sem = self.dpool[qkey][slot]
        self.wait(qkey, list(deps) + [(sem, 16 * n_use)])
        i = self.eng[qkey].dma_start(out=out, in_=in_, **kw)
        i.then_inc(sem, 16)
        self.dqi[qkey] += 1
        return (sem, 16 * (n_use + 1))

    def do(self, ekey, method, *args, deps=(), **kw):
        self.wait(ekey, deps)
        i = getattr(self.eng[ekey], method)(*args, **kw)
        self.cnt[ekey] += 1
        i.then_inc(self.prog[ekey], 1)
        return (self.prog[ekey], self.cnt[ekey])

    def mm(self, out, lhsT, rhs, start, stop, deps=(), mark=None):
        if mark is None:
            mark = stop
        self.wait("t", deps)
        i = self.nc.tensor.matmul(out, lhsT, rhs, start=start, stop=stop)
        if mark:
            self.cnt["t"] += 1
            i.then_inc(self.prog["t"], 1)
            return (self.prog["t"], self.cnt["t"])
        return None

    def tip(self):
        toks = [(self.prog[k], self.cnt[k]) for k in "tvs"]
        for q in "qsp":
            n = self.dqi[q]
            for slot in range(self.NPOOL):
                uses = (n - slot + self.NPOOL - 1) // self.NPOOL
                if uses > 0:
                    toks.append((self.dpool[q][slot], 16 * uses))
        return toks

    def fence_all(self, deps):
        for ekey in "tvspq":
            self.wait(ekey, deps)


def cast_weights(k, dst_sb, src_ap, deps, qkey="q", stage_f32=None, tag=""):
    n = dst_sb.shape[1]
    ns = stage_f32.shape[1]
    half = ns // 2
    ld_prev = {}
    tok = None
    if len(src_ap.shape) == 3:
        C, m = src_ap.shape[1], src_ap.shape[2]
        assert C * m == n, (src_ap.shape, n)
        i = 0
        if m > half:
            for c0 in range(C):
                off = 0
                while off < m:
                    mm_ = min(half, m - off)
                    buf = i % 2
                    tl = k.dma(qkey, stage_f32[:, buf * half:buf * half + mm_],
                               src_ap[:, c0, off:off + mm_],
                               deps=list(deps) + [ld_prev.get(buf)])
                    tok = k.do("v", "tensor_copy",
                               dst_sb[:, c0 * m + off:c0 * m + off + mm_],
                               stage_f32[:, buf * half:buf * half + mm_],
                               deps=[tl])
                    ld_prev[buf] = tok
                    off += mm_
                    i += 1
            return tok
        per = max(1, half // m)
        c0 = 0
        while c0 < C:
            cc = min(per, C - c0)
            buf = i % 2
            tl = k.dma(qkey, stage_f32[:, buf * half:buf * half + cc * m]
                       .rearrange("p (c m) -> p c m", c=cc),
                       src_ap[:, c0:c0 + cc, :],
                       deps=list(deps) + [ld_prev.get(buf)])
            tok = k.do("v", "tensor_copy", dst_sb[:, c0 * m:(c0 + cc) * m],
                       stage_f32[:, buf * half:buf * half + cc * m],
                       deps=[tl])
            ld_prev[buf] = tok
            c0 += cc
            i += 1
        return tok
    off = 0
    i = 0
    while off < n:
        m = min(half, n - off)
        buf = i % 2
        tl = k.dma(qkey, stage_f32[:, buf * half:buf * half + m],
                   src_ap[:, off:off + m],
                   deps=list(deps) + [ld_prev.get(buf)])
        tok = k.do("v", "tensor_copy", dst_sb[:, off:off + m],
                   stage_f32[:, buf * half:buf * half + m],
                   deps=[tl])
        ld_prev[buf] = tok
        off += m
        i += 1
    return tok



WEIGHTS = {
    "l0a_wqa": [NC, 128, 64],
    "l0a_wqb": [64, 1024],
    "l0a_wkva": [NC, 128, 32],
    "l0a_wkvbK": [32, 256],
    "l0a_wkvbV": [32, 256],
    "l0a_wout": [NC, 128, 1024],
    "l0a_bout": [128, NC],
    "l0f_wqkv": [2, NC, 128, 1536],
    "l0f_qs": [2, 64],
    "l0f_ks": [2, 64],
    "l0f_wo": [2, 4, 128, 1024],
    "l0f_lng": [2, NC, 128],
    "l0f_lnb": [2, NC, 128],
    "l0f_w1": [2, NC, 128, 2 * HIDP],
    "l0f_b1": [2, 128, 2 * NHC],
    "l0f_w2": [2, NHC, 128, 1024],
    "l0f_b2": [2, 128, NC],
    "l1a_wq": [NC, 128, 1024],
    "l1a_wk": [NC, 128, 1024],
    "l1a_wv": [NC, 128, 1024],
    "l1a_wo": [NC, 128, 1024],
    "l1a_bq": [128, NC],
    "l1a_bk": [128, NC],
    "l1a_bo": [128, NC],
    "l1a_bvrow": [1, 1024],
    "l1a_wqr": [64, 16],
    "l1a_wkr": [64, 16],
    "l1f_gw": [NC, 128, 8],
    "l1f_gbrow": [1, 8],
    "l1f_w1t": [8, NC, 128, 4096],
    "l1f_b1": [8, 128, 32],
    "l1f_w2t": [8, 32, 128, 1024],
    "l1f_b2m": [NC, 8, 128],
}

CONSTS = {
    "c_id": ([128, 128], BF16),
    "c_p128": ([128, 128], BF16),
    "c_prot": ([128, 128], BF16),
    "c_cd": ([128, T], F32),
    "c_sd": ([128, T], F32),
    "c_xqc": ([128, T], F32),
    "c_xqs": ([128, T], F32),
    "c_xkc": ([128, T], F32),
    "c_xks": ([128, T], F32),
    "c_cm": ([4, 128, TT], BF16),
    "c_mtm": ([4, 128, 1024], BF16),
    "c_mkm": ([8, 128, TT], BF16),
}

SCRATCH = {
    "xn1": ([B, NC, 128, T], BF16),
    "xfa": ([B, NC, 128, T], F32),
    "xfb": ([B, NC, 128, T], F32),
    "qaug": ([B, 16, 65, T], BF16),
    "kaug": ([B, 4, 65, T], BF16),
    "vt0": ([B, 4, T, 64], BF16),
    "o0": ([B, NC, 128, T], BF16),
    "x1": ([B, NC, 128, T], F32),
    "qf": ([B, 4, 128, T], BF16),
    "kf": ([B, 4, 128, T], BF16),
    "vt1": ([B, 8, T, 64], BF16),
    "of": ([B, 4, 128, T], BF16),
    "mgl": ([B, NHC, 128, T], BF16),
    "x2": ([B, NC, 128, T], F32),
    "xn2": ([B, NC, 128, T], BF16),
    "q2aug": ([B, 16, 17, T], BF16),
    "k2aug": ([B, 16, 17, T], BF16),
    "vt2": ([B, 16, T, 64], BF16),
    "o2": ([B, NC, 128, T], BF16),
    "mbeta": ([8, B * T], F32),
    "dbg14": ([4, 1024], F32),
    "m8": ([8, B * T], F32),
}

INPUTS = {"xin": ([B, NC, 128, T], F32)}
OUTPUTS = {"out": ([B, NC, 128, T], F32)}



def _fm(x):
    b, t, d = x.shape
    return np.ascontiguousarray(
        x.reshape(b, t, d // 128, 128).transpose(0, 2, 3, 1))


def host_weights(inp):
    w = {}
    w["l0a_wqa"] = inp["L0_attn_Wqa"].reshape(NC, 128, 64)
    w["l0a_wqb"] = inp["L0_attn_Wqb"]
    w["l0a_wkva"] = inp["L0_attn_Wkva"].reshape(NC, 128, 32)
    kvb = np.asarray(inp["L0_attn_Wkvb"]).reshape(32, 4, 64, 2)
    w["l0a_wkvbK"] = kvb[..., 0].reshape(32, 256)
    w["l0a_wkvbV"] = kvb[..., 1].reshape(32, 256)
    w["l0a_wout"] = inp["L0_attn_Wout"].reshape(NC, 128, 1024)
    w["l0a_bout"] = np.ascontiguousarray(np.asarray(inp["L0_attn_bout"]).reshape(NC, 128).T)
    w["l0f_wqkv"] = inp["L0_ffn_wqkv"].reshape(2, NC, 128, 1536)
    w["l0f_qs"] = inp["L0_ffn_q_scale"]
    w["l0f_ks"] = inp["L0_ffn_k_scale"]
    w["l0f_wo"] = inp["L0_ffn_wo"].reshape(2, 4, 128, 1024)
    w["l0f_lng"] = inp["L0_ffn_ln_g"].reshape(2, NC, 128)
    w["l0f_lnb"] = inp["L0_ffn_ln_b"].reshape(2, NC, 128)
    w1 = np.asarray(inp["L0_ffn_w1"])
    w1p = np.zeros((2, 1024, 2 * HIDP), np.float32)
    w1p[:, :, :2730] = w1[:, :, :2730]
    w1p[:, :, HIDP:HIDP + 2730] = w1[:, :, 2730:]
    w["l0f_w1"] = w1p.reshape(2, NC, 128, 2 * HIDP)
    b1 = np.asarray(inp["L0_ffn_b1"])
    b1p = np.zeros((2, 2 * HIDP), np.float32)
    b1p[:, :2730] = b1[:, :2730]
    b1p[:, HIDP:HIDP + 2730] = b1[:, 2730:]
    w["l0f_b1"] = np.ascontiguousarray(b1p.reshape(2, 2 * NHC, 128).transpose(0, 2, 1))
    w2 = np.asarray(inp["L0_ffn_w2"])
    w2p = np.zeros((2, HIDP, 1024), np.float32)
    w2p[:, :2730] = w2
    w["l0f_w2"] = w2p.reshape(2, NHC, 128, 1024)
    w["l0f_b2"] = np.ascontiguousarray(np.asarray(inp["L0_ffn_b2"]).reshape(2, NC, 128).transpose(0, 2, 1))
    for nm, key in (("wq", "WQ"), ("wk", "WK"), ("wv", "WV"), ("wo", "WO")):
        w["l1a_" + nm] = np.asarray(inp["L1_attn_" + key]).reshape(NC, 128, 1024)
    for nm, key in (("bq", "bQ"), ("bk", "bK"), ("bo", "bO")):
        w["l1a_" + nm] = np.ascontiguousarray(
            np.asarray(inp["L1_attn_" + key]).reshape(NC, 128).T)
    w["l1a_bvrow"] = np.asarray(inp["L1_attn_bV"]).reshape(1, 1024)
    w["l1a_wqr"] = inp["L1_attn_WQr"]
    w["l1a_wkr"] = inp["L1_attn_WKr"]
    w["l1f_gw"] = np.ascontiguousarray(np.asarray(inp["L1_ffn_gate_w"]).T
                                       ).reshape(NC, 128, 8)
    w["l1f_gbrow"] = np.asarray(inp["L1_ffn_gate_b"]).reshape(1, 8)
    w["l1f_w1t"] = np.ascontiguousarray(
        np.asarray(inp["L1_ffn_w1"]).transpose(0, 2, 1)).reshape(8, NC, 128, 4096)
    w["l1f_b1"] = np.ascontiguousarray(np.asarray(inp["L1_ffn_b1"]).reshape(8, 32, 128).transpose(0, 2, 1))
    w["l1f_w2t"] = np.ascontiguousarray(
        np.asarray(inp["L1_ffn_w2"]).transpose(0, 2, 1)).reshape(8, 32, 128, 1024)
    w["l1f_b2m"] = np.ascontiguousarray(np.asarray(inp["L1_ffn_b2"]).reshape(8, 8, 128).transpose(1, 0, 2))
    return {k: np.ascontiguousarray(np.asarray(v), dtype=np.float32)
            for k, v in w.items()}


def host_consts():
    c = {}
    c["c_id"] = np.eye(128, dtype=np.float32).astype(BF)
    p = np.zeros((128, 128), np.float32)
    for i in range(64):
        p[2 * i, 2 * i + 1] = -1.0
        p[2 * i + 1, 2 * i] = 1.0
    c["c_p128"] = p.T.astype(BF)
    pr = np.zeros((128, 128), np.float32)
    for h in range(2):
        o = 64 * h
        for i in range(32):
            pr[o + i, o + 32 + i] = -1.0
            pr[o + 32 + i, o + i] = 1.0
    c["c_prot"] = pr.T.astype(BF)
    d = 64
    inv = 1.0 / 10000.0 ** (np.arange(0, d, 2, dtype=np.float64) / d)
    t = np.arange(T, dtype=np.float64)
    fr = inv[:, None] * t[None, :]
    cd = np.repeat(np.cos(fr), 2, axis=0)
    sd = np.repeat(np.sin(fr), 2, axis=0)
    c["c_cd"] = np.tile(cd, (2, 1)).astype(np.float32)
    c["c_sd"] = np.tile(sd, (2, 1)).astype(np.float32)
    half = np.arange(0, d, 2, dtype=np.float64)
    invf = 1.0 / 10000.0 ** (half / d)
    fr2 = invf[:, None] * t[None, :]
    fr2 = np.concatenate([fr2, fr2], axis=0)
    base = (half + 0.4 * d) / (1.4 * d)
    power = (t - T // 2) / 256.0
    scale = base[:, None] ** power[None, :]
    scale = np.concatenate([scale, scale], axis=0)
    cos, sin = np.cos(fr2), np.sin(fr2)
    c["c_xqc"] = np.tile(cos * scale, (2, 1)).astype(np.float32)
    c["c_xqs"] = np.tile(sin * scale, (2, 1)).astype(np.float32)
    c["c_xkc"] = np.tile(cos / scale, (2, 1)).astype(np.float32)
    c["c_xks"] = np.tile(sin / scale, (2, 1)).astype(np.float32)
    kk = np.arange(128)[:, None]
    qq = np.arange(TT)[None, :]
    cm = np.zeros((4, 128, TT), np.float32)
    for m in range(4):
        cm[m] = np.where(m * 128 + kk <= qq, 0.0, NEG)
    c["c_cm"] = cm.astype(BF)
    jj = np.arange(1024)[None, :]
    mtm = np.zeros((4, 128, 1024), np.float32)
    for qc in range(4):
        qrel = qc * 128 + np.arange(128)[:, None]
        mtm[qc] = np.where((jj >= qrel) & (jj <= qrel + 512), 0.0, NEG)
    c["c_mtm"] = mtm.astype(BF)
    mkm = np.zeros((8, 128, TT), np.float32)
    for kt in range(8):
        j = kt * 128 + np.arange(128)[:, None]
        mkm[kt] = np.where((j >= qq) & (j <= qq + 512), 0.0, NEG)
    c["c_mkm"] = mkm.astype(BF)
    return c


def host_core_inputs(inp, core):
    x = np.asarray(inp["L0_attn_x"], np.float32)[4 * core:4 * core + 4]
    return {"xin": _fm(x)}



def emit_ln(k, ap, src, dst_bf, dst_f32, deps):
    nc = k.nc
    with ExitStack() as ctx:
        xt = ctx.enter_context(nc.sbuf_tensor(k.uid + "ln_xt", [128, 2, NC * TT], F32))
        xb = ctx.enter_context(nc.sbuf_tensor(k.uid + "ln_xb", [128, 2, NC * TT], BF16))
        sq = ctx.enter_context(nc.sbuf_tensor(k.uid + "ln_sq", [128, 2, NC * TT], BF16))
        ones = ctx.enter_context(nc.sbuf_tensor(k.uid + "ln_o", [128, 1], BF16))
        o1 = ctx.enter_context(nc.sbuf_tensor(k.uid + "ln_o1", [1, 128], F32))
        rows = ctx.enter_context(nc.sbuf_tensor(k.uid + "ln_r", [1, 2, 6, TT], F32))
        xnf = ctx.enter_context(nc.sbuf_tensor(k.uid + "ln_xf", [128, 2, NC * TT], F32))
        xnb = ctx.enter_context(nc.sbuf_tensor(k.uid + "ln_xB", [128, 2, NC * TT], BF16))
        eps = ctx.enter_context(nc.sbuf_tensor(k.uid + "ln_eps", [1, 1], F32))
        ps = ctx.enter_context(nc.psum_tensor(k.uid + "ln_ps", [128, 8, TT], F32))
        t0tok = k.do("v", "memset", ones[:], 1.0, deps=deps)
        t1tok = k.do("v", "memset", o1[:], 1.0, deps=deps)
        tep = k.do("v", "memset", eps[:], 1e-5, deps=deps)
        joins = {}
        for i in range(B * NTT):
            b, tt = divmod(i, NTT)
            u = i % 2
            war = joins.get(i - 2, list(deps) + [t0tok, t1tok])
            t0 = tt * TT
            X = xt[:, u, :]
            XB = xb[:, u, :]
            SQ = sq[:, u, :]
            XF = xnf[:, u, :]
            XO = xnb[:, u, :]
            R = rows[:, u, :, :]
            s0, s1, pA, pB = 4 * u, 4 * u + 1, 4 * u + 2, 4 * u + 3
            tl = k.dma("q", X.rearrange("p (c t) -> p c t", c=NC),
                       src[b, :, :, t0:t0 + TT].rearrange("c p t -> p c t"),
                       deps=war)
            tc = k.do("v", "tensor_copy", XB, X, deps=war + [tl])
            tq = k.do("s", "activation", SQ, XB, AF.Square, deps=war + [tc])
            for c in range(NC):
                ts0 = k.mm(ps[0:1, s0, :], ones[:], XB[:, c * TT:(c + 1) * TT],
                           start=(c == 0), stop=(c == NC - 1), deps=war + [tc])
            for c in range(NC):
                ts1 = k.mm(ps[0:1, s1, :], ones[:], SQ[:, c * TT:(c + 1) * TT],
                           start=(c == 0), stop=(c == NC - 1), deps=[tq])
            tm = k.do("v", "tensor_scalar_mul", R[:, 0, :], ps[0:1, s0, :],
                      1.0 / D, deps=war + [ts0])
            te = k.do("v", "tensor_scalar_mul", R[:, 1, :], ps[0:1, s1, :],
                      1.0 / D, deps=[ts1])
            tmm = k.do("v", "tensor_mul", R[:, 2, :], R[:, 0, :], R[:, 0, :],
                       deps=[tm])
            tv = k.do("v", "tensor_sub", R[:, 3, :], R[:, 1, :], R[:, 2, :],
                      deps=[te, tmm])
            tsd = k.do("s", "activation", R[:, 4, :], R[:, 3, :], AF.Sqrt,
                       bias=eps[:], deps=[tv, tep])
            tr = k.do("v", "reciprocal", R[:, 5, :], R[:, 4, :], deps=[tsd])
            tnm = k.do("v", "tensor_mul", R[:, 2, :], R[:, 0, :], R[:, 5, :],
                       deps=[tr])
            tnm2 = k.do("v", "tensor_scalar_mul", R[:, 2, :], R[:, 2, :], -1.0,
                        deps=[tnm])
            ta = k.mm(ps[:, pA, :], o1[:], R[:, 5, :], True, True,
                      deps=war + [tr])
            tb = k.mm(ps[:, pB, :], o1[:], R[:, 2, :], True, True, deps=[tnm2])
            for c in range(NC):
                sl = slice(c * TT, (c + 1) * TT)
                tt1 = k.do("v", "tensor_mul", XF[:, sl], X[:, sl], ps[:, pA, :],
                           deps=[ta])
                tt2 = k.do("v", "tensor_add", XF[:, sl], XF[:, sl], ps[:, pB, :],
                           deps=[tb, tt1])
                tt3 = k.do("v", "tensor_copy", XO[:, sl], XF[:, sl], deps=[tt2])
                k.dma("p", dst_bf[b, c, :, t0:t0 + TT], XO[:, sl], deps=[tt3])
                if dst_f32 is not None:
                    k.dma("p", dst_f32[b, c, :, t0:t0 + TT], XF[:, sl],
                          deps=[tt3])
            joins[i] = k.tip()
    return k.tip()


def emit_add(k, ap, a, bsrc, dst, deps):
    nc = k.nc
    with ExitStack() as ctx:
        xa = ctx.enter_context(nc.sbuf_tensor(k.uid + "ad_a", [128, 2, NC * TT], F32))
        xb = ctx.enter_context(nc.sbuf_tensor(k.uid + "ad_b", [128, 2, NC * TT], F32))
        xo = ctx.enter_context(nc.sbuf_tensor(k.uid + "ad_o", [128, 2, NC * TT], F32))
        joins = {}
        for i in range(B * NTT):
            b, tt = divmod(i, NTT)
            u = i % 2
            war = joins.get(i - 2, list(deps))
            t0 = tt * TT
            ta = k.dma("q", xa[:, u, :].rearrange("p (c t) -> p c t", c=NC),
                       a[b, :, :, t0:t0 + TT].rearrange("c p t -> p c t"),
                       deps=war)
            tb = k.dma("q", xb[:, u, :].rearrange("p (c t) -> p c t", c=NC),
                       bsrc[b, :, :, t0:t0 + TT].rearrange("c p t -> p c t"),
                       deps=war)
            to = k.do("v", "tensor_add", xo[:, u, :], xa[:, u, :], xb[:, u, :],
                      deps=war + [ta, tb])
            k.dma("p", dst[b, :, :, t0:t0 + TT].rearrange("c p t -> p c t"),
                  xo[:, u, :].rearrange("p (c t) -> p c t", c=NC), deps=[to])
            joins[i] = k.tip()
    return k.tip()


def _fmslice(t4, b, t0, n=TT):
    return t4[b, :, :, t0:t0 + n].rearrange("c p t -> p c t")


def emit_l0a_qkv(k, ap, deps):
    nc = k.nc
    xn1, qaug, kaug, vt0 = ap["xn1"], ap["qaug"], ap["kaug"], ap["vt0"]
    with ExitStack() as ctx:
        wqa = ctx.enter_context(nc.sbuf_tensor(k.uid + "aq_wqa", [128, NC * 64], BF16))
        wqb = ctx.enter_context(nc.sbuf_tensor(k.uid + "aq_wqb", [64, 1024], BF16))
        wka = ctx.enter_context(nc.sbuf_tensor(k.uid + "aq_wka", [128, NC * 32], BF16))
        wkbK = ctx.enter_context(nc.sbuf_tensor(k.uid + "aq_wkbK", [32, 256], BF16))
        wkbV = ctx.enter_context(nc.sbuf_tensor(k.uid + "aq_wkbV", [32, 256], BF16))
        stg = ctx.enter_context(nc.sbuf_tensor(k.uid + "aq_stg", [128, 2048], F32))
        cd = ctx.enter_context(nc.sbuf_tensor(k.uid + "aq_cd", [128, T], F32))
        sd = ctx.enter_context(nc.sbuf_tensor(k.uid + "aq_sd", [128, T], F32))
        pP = ctx.enter_context(nc.sbuf_tensor(k.uid + "aq_p", [128, 128], BF16))
        hh2 = ctx.enter_context(nc.sbuf_tensor(k.uid + "aq_hh2", [128, 2], BF16))
        onesr = ctx.enter_context(nc.sbuf_tensor(k.uid + "aq_or", [1, T], BF16))
        mk02 = ctx.enter_context(nc.sbuf_tensor(k.uid + "aq_mk", [2, 4], F32))
        mkst = ctx.enter_context(nc.sbuf_tensor(k.uid + "aq_mks", [2, 14], F32))
        xnb = ctx.enter_context(nc.sbuf_tensor(k.uid + "aq_xnb", [128, 2, NC * TT], BF16))
        t1 = ctx.enter_context(nc.sbuf_tensor(k.uid + "aq_t1", [64, T], BF16))
        t2 = ctx.enter_context(nc.sbuf_tensor(k.uid + "aq_t2", [32, T], BF16))
        rb = ctx.enter_context(nc.sbuf_tensor(k.uid + "aq_rb", [128, 2, TT], BF16))
        sqv = ctx.enter_context(nc.sbuf_tensor(k.uid + "aq_sq", [128, 2, TT], BF16))
        ta_f = ctx.enter_context(nc.sbuf_tensor(k.uid + "aq_taf", [128, 2, TT], F32))
        tb_f = ctx.enter_context(nc.sbuf_tensor(k.uid + "aq_tbf", [128, 2, TT], F32))
        ro = ctx.enter_context(nc.sbuf_tensor(k.uid + "aq_ro", [128, 2, TT], BF16))
        nr = ctx.enter_context(nc.sbuf_tensor(k.uid + "aq_nr", [2, 2, TT], F32))
        crw = ctx.enter_context(nc.sbuf_tensor(k.uid + "aq_crw", [2, 2, TT], BF16))
        vb = ctx.enter_context(nc.sbuf_tensor(k.uid + "aq_vb", [128, 2, 256], BF16))
        ps = ctx.enter_context(nc.psum_tensor(k.uid + "aq_ps", [128, 8, TT], F32))

        tw = cast_weights(k, wqa[:], ap["l0a_wqa"].rearrange("c p n -> p c n"),
                          deps, stage_f32=stg[:])
        tw = cast_weights(k, wqb[:], ap["l0a_wqb"][:, :], [tw],
                          stage_f32=stg[0:64, :])
        tw = cast_weights(k, wka[:], ap["l0a_wkva"].rearrange("c p n -> p c n"),
                          [tw], stage_f32=stg[:])
        tw = cast_weights(k, wkbK[:], ap["l0a_wkvbK"][:, :], [tw],
                          stage_f32=stg[0:32, :])
        tw = cast_weights(k, wkbV[:], ap["l0a_wkvbV"][:, :], [tw],
                          stage_f32=stg[0:32, :])
        tcd = k.dma("q", cd[:], ap["c_cd"][:, :], deps=deps)
        tsd = k.dma("q", sd[:], ap["c_sd"][:, :], deps=deps)
        tpp = k.dma("q", pP[:], ap["c_p128"][:, :], deps=deps)
        th0 = k.do("v", "memset", hh2[:], 0.0, deps=deps)
        th1 = k.do("v", "memset", hh2[0:64, 0:1], 1.0, deps=[th0])
        th2 = k.do("v", "memset", hh2[64:128, 1:2], 1.0, deps=[th1])
        tor = k.do("v", "memset", onesr[:], 1.0, deps=deps)
        pre = [tw, tcd, tsd, tpp, th2, tor]

        def rope_block(bank, u, tt, tbl_c, tbl_s, dep_mm, war):
            tqb = k.do("s", "activation", rb[:, u, :], ps[:, bank, :],
                       AF.Copy, deps=war + [dep_mm])
            tp = k.mm(ps[:, 6 + u, :], pP[:], rb[:, u, :], True, True,
                      deps=war + [tqb, tpp])
            t_a = k.do("v", "tensor_mul", ta_f[:, u, :], ps[:, 6 + u, :],
                       tbl_s[:, tt * TT:(tt + 1) * TT],
                       deps=war + [tp])
            t_b = k.do("v", "tensor_mul", tb_f[:, u, :], ps[:, bank, :],
                       tbl_c[:, tt * TT:(tt + 1) * TT],
                       deps=war + [dep_mm, tqb])
            t_o = k.do("v", "tensor_add", ro[:, u, :], ta_f[:, u, :],
                       tb_f[:, u, :], deps=[t_a, t_b])
            return t_o

        bj = list(deps) + pre
        for b in range(B):
            ph1 = []
            for tt in range(NTT):
                u = tt % 2
                war = ph1[-2][2] if tt >= 2 else bj
                tl = k.dma("q", xnb[:, u, :].rearrange("p (c t) -> p c t", c=NC),
                           _fmslice(xn1, b, tt * TT), deps=war)
                for c in range(NC):
                    m1 = k.mm(ps[0:64, u, :], wqa[:, c * 64:(c + 1) * 64],
                              xnb[:, u, c * TT:(c + 1) * TT],
                              start=(c == 0), stop=(c == NC - 1),
                              deps=war + [tl, tw])
                for c in range(NC):
                    m2 = k.mm(ps[0:32, 2 + u, :], wka[:, c * 32:(c + 1) * 32],
                              xnb[:, u, c * TT:(c + 1) * TT],
                              start=(c == 0), stop=(c == NC - 1), deps=[tl])
                c1 = k.do("v", "tensor_copy", t1[:, tt * TT:(tt + 1) * TT],
                          ps[0:64, u, :], deps=war + [m1])
                c2 = k.do("v", "tensor_copy", t2[:, tt * TT:(tt + 1) * TT],
                          ps[0:32, 2 + u, :], deps=[m2])
                ph1.append((c1, c2, k.tip()))
            tph1 = [x for tpl in ph1 for x in tpl[:2]]
            ph2 = []
            for tc in range(T // 128):
                u = tc % 2
                war = ph2[-2] if tc >= 2 else []
                tv = k.mm(ps[:, u, 0:256], t2[:, tc * 128:(tc + 1) * 128],
                          wkbV[:], True, True, deps=tph1 + war)
                tvc = k.do("v", "tensor_copy", vb[:, u, :], ps[:, u, 0:256],
                           deps=war + [tv])
                dt = [k.dma("p", vt0[b, kvh, tc * 128:(tc + 1) * 128, :],
                            vb[:, u, kvh * 64:(kvh + 1) * 64], deps=[tvc])
                      for kvh in range(4)]
                ph2.append([tvc] + dt)
            ph2end = k.tip()
            mk_toks = []
            ph3 = []
            for j2 in range(2):
                for tt in range(NTT):
                    i3 = j2 * NTT + tt
                    u = i3 % 2
                    war = ph3[-2] if i3 >= 2 else ph2end
                    tk = k.mm(ps[:, 2 + u, :], wkbK[:, j2 * 128:(j2 + 1) * 128],
                              t2[:, tt * TT:(tt + 1) * TT], True, True,
                              deps=tph1 + war)
                    tsq = k.do("s", "activation", sqv[:, u, :], ps[:, 2 + u, :],
                               AF.Square, deps=war + [tk])
                    tn = k.mm(ps[0:2, 4 + u, :], hh2[:], sqv[:, u, :], True,
                              True, deps=war + [tsq, th2])
                    tmx = k.do("v", "reduce_max", mkst[:, i3:i3 + 1],
                               ps[0:2, 4 + u, :], axis=AX.X, deps=war + [tn])
                    mk_toks.append(tmx)
                    t_o = rope_block(2 + u, u, tt, cd, sd, tk, war)
                    d1 = k.dma("p", kaug[b, 2 * j2, 0:64, tt * TT:(tt + 1) * TT],
                               ro[0:64, u, :], deps=[t_o])
                    d2 = k.dma("p", kaug[b, 2 * j2 + 1, 0:64,
                                    tt * TT:(tt + 1) * TT],
                               ro[64:128, u, :], deps=[t_o])
                    ph3.append([tmx, t_o, d1, d2])
            for kvh in range(4):
                k.dma("p", kaug[b, kvh, 64:65, :], onesr[:], deps=[tor])
            mk_dma = []
            for j2 in range(2):
                tmk = k.do("v", "reduce_max", mkst[:, 8 + j2:9 + j2],
                           mkst[:, j2 * NTT:(j2 + 1) * NTT], axis=AX.X,
                           deps=mk_toks)
                tms = k.do("s", "activation", mkst[:, 10 + j2:11 + j2],
                           mkst[:, 8 + j2:9 + j2], AF.Sqrt, deps=[tmk])
                tng = k.do("v", "tensor_scalar_mul", mkst[:, 12 + j2:13 + j2],
                           mkst[:, 10 + j2:11 + j2], -1.0, deps=[tms])
                for r in range(2):
                    for rr in range(2):
                        mk_dma.append(k.dma(
                            "p", mk02[rr:rr + 1, 2 * j2 + r:2 * j2 + r + 1],
                            mkst[r:r + 1, 12 + j2:13 + j2], deps=[tng]))
            ph3end = k.tip()
            ph4 = []
            for j in range(NC):
                kvh = j // 2
                for tt in range(NTT):
                    i4 = j * NTT + tt
                    u = i4 % 2
                    war = ph4[-2] if i4 >= 2 else ph3end
                    tq = k.mm(ps[:, 2 + u, :], wqb[:, j * 128:(j + 1) * 128],
                              t1[:, tt * TT:(tt + 1) * TT], True, True,
                              deps=tph1 + war)
                    tsq = k.do("s", "activation", sqv[:, u, :], ps[:, 2 + u, :],
                               AF.Square, deps=war + [tq])
                    tn = k.mm(ps[0:2, 4 + u, :], hh2[:], sqv[:, u, :], True,
                              True, deps=war + [tsq])
                    tsr = k.do("s", "activation", nr[:, u, :], ps[0:2, 4 + u, :],
                               AF.Sqrt, deps=war + [tn])
                    tcr = k.do("v", "tensor_scalar_mul", crw[:, u, :],
                               nr[:, u, :], mk02[0:2, kvh:kvh + 1],
                               deps=war + [tsr] + mk_dma)
                    e1 = k.dma("p", qaug[b, 2 * j, 64:65, tt * TT:(tt + 1) * TT],
                               crw[0:1, u, :], deps=[tcr])
                    e2 = k.dma("p", qaug[b, 2 * j + 1, 64:65,
                                    tt * TT:(tt + 1) * TT],
                               crw[1:2, u, :], deps=[tcr])
                    t_o = rope_block(2 + u, u, tt, cd, sd, tq, war)
                    e3 = k.dma("p", qaug[b, 2 * j, 0:64, tt * TT:(tt + 1) * TT],
                               ro[0:64, u, :], deps=[t_o])
                    e4 = k.dma("p", qaug[b, 2 * j + 1, 0:64,
                                    tt * TT:(tt + 1) * TT],
                               ro[64:128, u, :], deps=[t_o])
                    ph4.append([tcr, t_o, e1, e2, e3, e4])
            bj = k.tip()
    return k.tip()


def emit_l0a_att(k, ap, deps):
    nc = k.nc
    qaug, kaug, vt0, o0 = ap["qaug"], ap["kaug"], ap["vt0"], ap["o0"]
    with ExitStack() as ctx:
        ka = ctx.enter_context(nc.sbuf_tensor(k.uid + "aa_ka", [65, 2, T], BF16))
        va = ctx.enter_context(nc.sbuf_tensor(k.uid + "aa_va", [128, 2, 16 * 65], BF16))
        qa = ctx.enter_context(nc.sbuf_tensor(k.uid + "aa_qa", [65, 2, TT], BF16))
        cmm = ctx.enter_context(nc.sbuf_tensor(k.uid + "aa_cm", [128, 4 * TT], BF16))
        ex = ctx.enter_context(nc.sbuf_tensor(k.uid + "aa_ex", [128, 4, TT], BF16))
        rec = ctx.enter_context(nc.sbuf_tensor(k.uid + "aa_rec", [1, 2, TT], F32))
        osb = ctx.enter_context(nc.sbuf_tensor(k.uid + "aa_osb", [64, 2, TT], F32))
        onb = ctx.enter_context(nc.sbuf_tensor(k.uid + "aa_onb", [64, 2, TT], BF16))
        o1f = ctx.enter_context(nc.sbuf_tensor(k.uid + "aa_o1f", [1, 64], F32))
        ps = ctx.enter_context(nc.psum_tensor(k.uid + "aa_ps", [128, 8, TT], F32))
        tcm = k.dma("q", cmm[:].rearrange("p (m t) -> p m t", m=4),
                    ap["c_cm"].rearrange("m p t -> p m t"), deps=deps)
        tmv = k.do("v", "memset", va[:], 1.0, deps=deps)
        to1 = k.do("v", "memset", o1f[:], 1.0, deps=deps)
        pre = [tcm, tmv, to1]
        ldj = {}
        epi = {}
        exd = {}
        hb_join = list(deps) + pre
        ih = 0
        for b in range(B):
            for kvh in range(4):
                uh = ih % 2
                war_h = ldj.get(ih - 2, hb_join)
                tka = k.dma("q", ka[:, uh, :], kaug[b, kvh, :, :], deps=war_h)
                tva = [k.dma("q", va[:, uh, kt * 65:kt * 65 + 64],
                             vt0[b, kvh, kt * 128:(kt + 1) * 128, :],
                             deps=war_h) for kt in range(16)]
                consumed = []
                qaload = {}

                def loadq(gq):
                    hh_, qt_ = divmod(gq, NTT)
                    if hh_ >= 4:
                        return
                    pv_ = epi.get(ih * 16 + gq - 2, [])
                    qaload[gq] = k.dma(
                        "q", qa[:, gq % 2, :],
                        qaug[b, kvh * 4 + hh_, :, qt_ * TT:(qt_ + 1) * TT],
                        deps=list(pv_) + (hb_join if ih == 0 and gq < 2
                                          else []))
                loadq(0)
                loadq(1)
                for hh in range(4):
                    h = kvh * 4 + hh
                    for qt in range(NTT):
                        uq = qt % 2
                        gq = hh * 4 + qt
                        pv = epi.get((ih * 16 + gq) - 2, [])
                        tqa = qaload[gq]
                        nkt = 4 * qt + 4
                        tav = None
                        SK = 2
                        texs = {}
                        for step in range(nkt + SK):
                            if step < nkt:
                                kt = step
                                u4 = kt % 4
                                pe = exd.get(u4)
                                tsc = k.mm(ps[:, u4, :],
                                           ka[:, uh, kt * 128:(kt + 1) * 128],
                                           qa[:, uq, :], True, True,
                                           deps=[tka, tqa] +
                                           ([pe[0]] if pe else list(pv)))
                                dmask = kt - 4 * qt
                                if dmask >= 0:
                                    tsc = k.do("v", "tensor_add",
                                               ps[:, u4, :], ps[:, u4, :],
                                               cmm[:, dmask * TT:
                                                   (dmask + 1) * TT],
                                               deps=[tsc, tcm])
                                tex = k.do("s", "activation", ex[:, u4, :],
                                           ps[:, u4, :], AF.Exp, scale=0.125,
                                           deps=[tsc] +
                                           ([pe[1]] if pe else []))
                                texs[kt] = tex
                                exd[u4] = (tex, None)
                            if step >= SK:
                                kt = step - SK
                                u4 = kt % 4
                                tav = k.mm(ps[0:65, 4 + uq, :],
                                           va[:, uh, kt * 65:kt * 65 + 65],
                                           ex[:, u4, :], start=(kt == 0),
                                           stop=(kt == nkt - 1),
                                           deps=[texs[kt], tmv, tva[kt]] +
                                           (list(pv) if kt == 0 else []),
                                           mark=True)
                                exd[u4] = (texs[kt], tav)
                        trc = k.do("v", "reciprocal", rec[:, uq, :],
                                   ps[64:65, 4 + uq, :], deps=[tav] + pv)
                        tbc = k.mm(ps[0:64, 6 + uq, :], o1f[:], rec[:, uq, :],
                                   True, True, deps=[trc, to1] + pv)
                        tcp = k.do("v", "tensor_copy", osb[:, uq, :],
                                   ps[0:64, 4 + uq, :], deps=[tav] + pv)
                        tmu = k.do("v", "tensor_mul", onb[:, uq, :],
                                   osb[:, uq, :], ps[0:64, 6 + uq, :],
                                   deps=[tbc, tcp])
                        tst = k.dma("p", o0[b, h // 2,
                                         (h % 2) * 64:(h % 2) * 64 + 64,
                                         qt * TT:(qt + 1) * TT],
                                    onb[:, uq, :], deps=[tmu])
                        epi[ih * 16 + hh * 4 + qt] = [tav, tmu, tst, trc,
                                                      tcp]
                        consumed += [tav]
                        loadq(gq + 2)
                ldj[ih] = [tav, k.tip()[0]]
                ih += 1
    return k.tip()


def emit_l1a_att(k, ap, deps):
    nc = k.nc
    q2aug, k2aug, vt2, o2 = ap["q2aug"], ap["k2aug"], ap["vt2"], ap["o2"]
    with ExitStack() as ctx:
        ka = ctx.enter_context(nc.sbuf_tensor(k.uid + "la_ka", [17, 2, T], BF16))
        va = ctx.enter_context(nc.sbuf_tensor(k.uid + "la_va", [128, 2, 16 * 65], BF16))
        qa = ctx.enter_context(nc.sbuf_tensor(k.uid + "la_qa", [17, 2, TT], BF16))
        ex = ctx.enter_context(nc.sbuf_tensor(k.uid + "la_ex", [128, 4, TT], BF16))
        rec = ctx.enter_context(nc.sbuf_tensor(k.uid + "la_rec", [1, 2, TT], F32))
        osb = ctx.enter_context(nc.sbuf_tensor(k.uid + "la_osb", [64, 2, TT], F32))
        onb = ctx.enter_context(nc.sbuf_tensor(k.uid + "la_onb", [64, 2, TT], BF16))
        o1f = ctx.enter_context(nc.sbuf_tensor(k.uid + "la_o1f", [1, 64], F32))
        ps = ctx.enter_context(nc.psum_tensor(k.uid + "la_ps", [128, 8, TT], F32))
        tmv = k.do("v", "memset", va[:], 1.0, deps=deps)
        to1 = k.do("v", "memset", o1f[:], 1.0, deps=deps)
        pre = [tmv, to1]
        ldj = {}
        epi = {}
        exd = {}
        hb_join = list(deps) + pre
        ih = 0
        for b in range(B):
            for h in range(16):
                uh = ih % 2
                war_h = ldj.get(ih - 2, hb_join)
                tka = k.dma("q", ka[:, uh, :], k2aug[b, h, :, :], deps=war_h)
                tva = [k.dma("q", va[:, uh, kt * 65:kt * 65 + 64],
                             vt2[b, h, kt * 128:(kt + 1) * 128, :],
                             deps=war_h) for kt in range(16)]
                tav = None
                qaload = {}

                def loadq(qt_):
                    if qt_ >= NTT:
                        return
                    pv_ = epi.get((ih * 4 + qt_) - 2, [])
                    qaload[qt_] = k.dma(
                        "q", qa[:, qt_ % 2, :],
                        q2aug[b, h, :, qt_ * TT:(qt_ + 1) * TT],
                        deps=list(pv_) + (hb_join if ih == 0 and qt_ < 2
                                          else []))
                loadq(0)
                loadq(1)
                for qt in range(NTT):
                    uq = qt % 2
                    pv = epi.get((ih * 4 + qt) - 2, [])
                    tqa = qaload[qt]
                    SK = 2
                    texs = {}
                    for step in range(16 + SK):
                        if step < 16:
                            kt = step
                            u4 = kt % 4
                            pe = exd.get(u4)
                            tsc = k.mm(ps[:, u4, :],
                                       ka[:, uh, kt * 128:(kt + 1) * 128],
                                       qa[:, uq, :], True, True,
                                       deps=[tka, tqa] +
                                       ([pe[0]] if pe else list(pv)))
                            tex = k.do("s", "activation", ex[:, u4, :],
                                       ps[:, u4, :], AF.Exp, scale=0.25,
                                       deps=[tsc] + ([pe[1]] if pe else []))
                            texs[kt] = tex
                            exd[u4] = (tex, None)
                        if step >= SK:
                            kt = step - SK
                            u4 = kt % 4
                            tav = k.mm(ps[0:65, 4 + uq, :],
                                       va[:, uh, kt * 65:kt * 65 + 65],
                                       ex[:, u4, :], start=(kt == 0),
                                       stop=(kt == 15),
                                       deps=[texs[kt], tmv, tva[kt]] +
                                       (list(pv) if kt == 0 else []),
                                       mark=True)
                            exd[u4] = (texs[kt], tav)
                    trc = k.do("v", "reciprocal", rec[:, uq, :],
                               ps[64:65, 4 + uq, :], deps=[tav] + pv)
                    tbc = k.mm(ps[0:64, 6 + uq, :], o1f[:], rec[:, uq, :],
                               True, True, deps=[trc, to1] + pv)
                    tcp = k.do("v", "tensor_copy", osb[:, uq, :],
                               ps[0:64, 4 + uq, :], deps=[tav] + pv)
                    tmu = k.do("v", "tensor_mul", onb[:, uq, :],
                               osb[:, uq, :], ps[0:64, 6 + uq, :],
                               deps=[tbc, tcp])
                    tst = k.dma("p", o2[b, h // 2,
                                     (h % 2) * 64:(h % 2) * 64 + 64,
                                     qt * TT:(qt + 1) * TT],
                                onb[:, uq, :], deps=[tmu])
                    epi[ih * 4 + qt] = [tav, tmu, tst, trc, tcp]
                    loadq(qt + 2)
                ldj[ih] = [tav, k.tip()[0]]
                ih += 1
    return k.tip()


def emit_matproj_add(k, ap, src_bf, w_ap, b_ap, add_src, dst, deps,
                     nin=NC, tag="mp"):
    nc = k.nc
    with ExitStack() as ctx:
        wsb = ctx.enter_context(nc.sbuf_tensor(tag + "_w", [128, nin * 1024], BF16))
        stg = ctx.enter_context(nc.sbuf_tensor(tag + "_st", [128, 2048], F32))
        bsb = ctx.enter_context(nc.sbuf_tensor(tag + "_b", [128, NC], F32))
        xsb = ctx.enter_context(nc.sbuf_tensor(tag + "_x", [128, 2, nin * TT], BF16))
        asb = ctx.enter_context(nc.sbuf_tensor(tag + "_a", [128, 2, NC * TT], F32))
        ysb = ctx.enter_context(nc.sbuf_tensor(tag + "_y", [128, 2, NC * TT], F32))
        ps = ctx.enter_context(nc.psum_tensor(tag + "_ps", [128, 8, TT], F32))
        tw = cast_weights(k, wsb[:], w_ap.rearrange("c p n -> p c n"),
                          deps, stage_f32=stg[:])
        if b_ap is not None:
            tb = k.dma("q", bsb[:], b_ap[:, :], deps=deps)
        else:
            tb = k.do("v", "memset", bsb[:], 0.0, deps=deps)
        joins = {}
        for i in range(B * NTT):
            b, tt = divmod(i, NTT)
            u = i % 2
            war = joins.get(i - 2, list(deps) + [tw, tb])
            t0 = tt * TT
            tx = k.dma("q", xsb[:, u, :].rearrange("p (c t) -> p c t", c=nin),
                       src_bf[b, :, :, t0:t0 + TT].rearrange("c p t -> p c t"),
                       deps=war)
            tad = k.dma("q", asb[:, u, :].rearrange("p (c t) -> p c t", c=NC),
                        _fmslice(add_src, b, t0), deps=war)
            bank_free = {}
            for j in range(NC):
                bank = 4 * u + j % 4
                wdeps = war + [tx, tw]
                if j % 4 in bank_free:
                    wdeps = wdeps + [bank_free[j % 4]]
                for c in range(nin):
                    tm = k.mm(ps[:, bank, :],
                              wsb[:, c * 1024 + j * 128:c * 1024 + j * 128 + 128],
                              xsb[:, u, c * TT:(c + 1) * TT],
                              start=(c == 0), stop=(c == nin - 1), deps=wdeps)
                ty = k.do("s", "activation", ysb[:, u, j * TT:(j + 1) * TT],
                          ps[:, bank, :], AF.Identity, bias=bsb[:, j:j + 1],
                          deps=war + [tm, tb])
                bank_free[j % 4] = ty
                tz = k.do("v", "tensor_add", ysb[:, u, j * TT:(j + 1) * TT],
                          ysb[:, u, j * TT:(j + 1) * TT],
                          asb[:, u, j * TT:(j + 1) * TT],
                          deps=war + [ty, tad])
                k.dma("p", dst[b, j, :, t0:t0 + TT], ysb[:, u, j * TT:(j + 1) * TT],
                      deps=[tz])
            joins[i] = k.tip()
    return k.tip()


def _shift_load(k, stage_u, src, b, t0, deps):
    toks = []
    toks.append(k.dma("q", stage_u[:, 0:4 * TT].rearrange("p (c t) -> p c t", c=4),
                      src[b, 0:4, :, t0:t0 + TT].rearrange("c p t -> p c t"),
                      deps=deps))
    if t0 > 0:
        toks.append(k.dma("q",
                          stage_u[:, 4 * TT:8 * TT].rearrange("p (c t) -> p c t", c=4),
                          src[b, 4:8, :, t0 - 1:t0 + TT - 1]
                          .rearrange("c p t -> p c t"), deps=deps))
    else:
        toks.append(k.dma("q",
                          stage_u[:, 4 * TT:8 * TT]
                          .rearrange("p (c t) -> p c t", c=4)[:, :, 1:TT],
                          src[b, 4:8, :, 0:TT - 1].rearrange("c p t -> p c t"),
                          deps=deps))
        for c2 in range(4):
            toks.append(k.do("v", "memset",
                             stage_u[:, (4 + c2) * TT:(4 + c2) * TT + 1], 0.0,
                             deps=deps))
    return toks


def emit_l0f_qkv(k, ap, l, deps):
    nc = k.nc
    src = ap["xfa"] if l == 0 else ap["xfb"]
    qf, kf, vt1 = ap["qf"], ap["kf"], ap["vt1"]
    with ExitStack() as ctx:
        wq = ctx.enter_context(nc.sbuf_tensor(k.uid + "fq_w", [128, NC * 1536], BF16))
        stg = ctx.enter_context(nc.sbuf_tensor(k.uid + "fq_st", [128, 2, NC * TT], F32))
        hb = ctx.enter_context(nc.sbuf_tensor(k.uid + "fq_hb", [128, NC * T], BF16))
        xqc = ctx.enter_context(nc.sbuf_tensor(k.uid + "fq_xqc", [128, T], F32))
        xqs = ctx.enter_context(nc.sbuf_tensor(k.uid + "fq_xqs", [128, T], F32))
        xkc = ctx.enter_context(nc.sbuf_tensor(k.uid + "fq_xkc", [128, T], F32))
        xks = ctx.enter_context(nc.sbuf_tensor(k.uid + "fq_xks", [128, T], F32))
        prt = ctx.enter_context(nc.sbuf_tensor(k.uid + "fq_prt", [128, 128], BF16))
        hh2 = ctx.enter_context(nc.sbuf_tensor(k.uid + "fq_hh2", [128, 2], BF16))
        qsel = ctx.enter_context(nc.sbuf_tensor(k.uid + "fq_qsel", [2, 128], F32))
        ksel = ctx.enter_context(nc.sbuf_tensor(k.uid + "fq_ksel", [2, 128], F32))
        sqv = ctx.enter_context(nc.sbuf_tensor(k.uid + "fq_sq", [128, 2, TT], BF16))
        nrw = ctx.enter_context(nc.sbuf_tensor(k.uid + "fq_nr", [2, 2, TT], F32))
        bcf = ctx.enter_context(nc.sbuf_tensor(k.uid + "fq_bc", [128, 2, TT], F32))
        qn = ctx.enter_context(nc.sbuf_tensor(k.uid + "fq_qn", [128, 2, TT], F32))
        qb = ctx.enter_context(nc.sbuf_tensor(k.uid + "fq_qb", [128, 2, TT], BF16))
        taf = ctx.enter_context(nc.sbuf_tensor(k.uid + "fq_ta", [128, 2, TT], F32))
        tbf = ctx.enter_context(nc.sbuf_tensor(k.uid + "fq_tb", [128, 2, TT], F32))
        ro = ctx.enter_context(nc.sbuf_tensor(k.uid + "fq_ro", [128, 2, TT], BF16))
        vb = ctx.enter_context(nc.sbuf_tensor(k.uid + "fq_vb", [128, 2, TT], BF16))
        ps = ctx.enter_context(nc.psum_tensor(k.uid + "fq_ps", [128, 8, TT], F32))
        tw = cast_weights(k, wq[:], ap["l0f_wqkv"][l].rearrange("c p n -> p c n"),
                          deps, stage_f32=stg[:, 0, :])
        tc1 = k.dma("q", xqc[:], ap["c_xqc"][:, :], deps=deps)
        tc2 = k.dma("q", xqs[:], ap["c_xqs"][:, :], deps=deps)
        tc3 = k.dma("q", xkc[:], ap["c_xkc"][:, :], deps=deps)
        tc4 = k.dma("q", xks[:], ap["c_xks"][:, :], deps=deps)
        tc5 = k.dma("q", prt[:], ap["c_prot"][:, :], deps=deps)
        th0 = k.do("v", "memset", hh2[:], 0.0, deps=deps)
        th1 = k.do("v", "memset", hh2[0:64, 0:1], 1.0, deps=[th0])
        th2 = k.do("v", "memset", hh2[64:128, 1:2], 1.0, deps=[th1])
        tq0 = k.do("v", "memset", qsel[:], 0.0, deps=deps)
        tk0 = k.do("v", "memset", ksel[:], 0.0, deps=deps)
        tq1 = k.dma("q", qsel[0:1, 0:64], ap["l0f_qs"][l:l + 1, :], deps=[tq0])
        tq2 = k.dma("q", qsel[1:2, 64:128], ap["l0f_qs"][l:l + 1, :], deps=[tq0])
        tk1 = k.dma("q", ksel[0:1, 0:64], ap["l0f_ks"][l:l + 1, :], deps=[tk0])
        tk2 = k.dma("q", ksel[1:2, 64:128], ap["l0f_ks"][l:l + 1, :], deps=[tk0])
        pre = [tw, tc1, tc2, tc3, tc4, tc5, th2, tq1, tq2, tk1, tk2]
        bj = list(deps) + pre
        bref = [0]

        def qk_head_block(j, tt, i, col0, sel, tbc, tbs, dst, war):
            b = bref[0]
            u = i % 2
            t0 = tt * TT
            tqm = None
            for c in range(NC):
                tqm = k.mm(ps[:, u, :],
                           wq[:, c * 1536 + col0 + j * 128:
                              c * 1536 + col0 + j * 128 + 128],
                           hb[:, c * T + t0:c * T + t0 + TT],
                           start=(c == 0), stop=(c == NC - 1),
                           deps=war + [tw])
            tsq = k.do("s", "activation", sqv[:, u, :], ps[:, u, :], AF.Square,
                       deps=war + [tqm])
            tn = k.mm(ps[0:2, 2 + u, :], hh2[:], sqv[:, u, :], True, True,
                      deps=war + [tsq, th2])
            tsr = k.do("s", "activation", nrw[:, u, :], ps[0:2, 2 + u, :],
                       AF.Sqrt, deps=war + [tn])
            tcl = k.do("v", "tensor_scalar_max", nrw[:, u, :], nrw[:, u, :],
                       1e-12, deps=war + [tsr])
            trc = k.do("v", "reciprocal", nrw[:, u, :], nrw[:, u, :],
                       deps=[tcl])
            tbm = k.mm(ps[:, 4 + u, :], sel, nrw[:, u, :], True, True,
                       deps=war + [trc])
            tbe = k.do("v", "tensor_copy", bcf[:, u, :], ps[:, 4 + u, :],
                       deps=war + [tbm])
            tqn = k.do("v", "tensor_mul", qn[:, u, :], ps[:, u, :],
                       bcf[:, u, :], deps=[tbe, tsq])
            tqb = k.do("s", "activation", qb[:, u, :], qn[:, u, :], AF.Copy,
                       deps=war + [tqn])
            tp = k.mm(ps[:, 6 + u, :], prt[:], qb[:, u, :], True, True,
                      deps=war + [tqb, tc5])
            t_a = k.do("v", "tensor_mul", taf[:, u, :], ps[:, 6 + u, :],
                       tbs[:, t0:t0 + TT], deps=war + [tp])
            t_b = k.do("v", "tensor_mul", tbf[:, u, :], qn[:, u, :],
                       tbc[:, t0:t0 + TT], deps=war + [t_a])
            t_o = k.do("v", "tensor_add", ro[:, u, :], taf[:, u, :],
                       tbf[:, u, :], deps=[t_b])
            td = k.dma("p", dst[b, j, :, t0:t0 + TT], ro[:, u, :], deps=[t_o])
            return [t_o, td, tqb, tqm]

        for b in range(B):
            bref[0] = b
            htoks = []
            for tt in range(NTT):
                u = tt % 2
                war = [htoks[-2][1]] if tt >= 2 else bj
                tl = _shift_load(k, stg[:, u, :], src, b, tt * TT, war)
                tcst = k.do("v", "tensor_copy",
                            hb[:, :].rearrange("p (c t) -> p c t", c=NC)
                            [:, :, tt * TT:(tt + 1) * TT],
                            stg[:, u, :].rearrange("p (c t) -> p c t", c=NC),
                            deps=war + tl)
                htoks.append((tl, tcst))
            hdone = [h[1] for h in htoks]
            vph = []
            for tt in range(NTT):
                for c4 in range(4):
                    iv = tt * 4 + c4
                    u = iv % 2
                    war = vph[-2] if iv >= 2 else hdone
                    tvm = None
                    for c in range(NC):
                        tvm = k.mm(ps[:, u, :],
                                   hb[:, c * T + tt * TT + c4 * 128:
                                      c * T + tt * TT + c4 * 128 + 128],
                                   wq[:, c * 1536 + 1024:c * 1536 + 1536],
                                   start=(c == 0), stop=(c == NC - 1),
                                   deps=hdone + list(war))
                    tvc = k.do("v", "tensor_copy", vb[:, u, :], ps[:, u, :],
                               deps=list(war) + [tvm])
                    tds = [k.dma("p",
                                 vt1[b, h, tt * TT + c4 * 128:
                                     tt * TT + c4 * 128 + 128, :],
                                 vb[:, u, h * 64:(h + 1) * 64], deps=[tvc])
                           for h in range(8)]
                    vph.append([tvc] + tds)
            vend = k.tip()
            kph = []
            for j in range(4):
                for tt in range(NTT):
                    ik = j * NTT + tt
                    war = kph[-2] if ik >= 2 else vend
                    kph.append(qk_head_block(j, tt, ik, 512, ksel[:], xkc, xks,
                                             kf, list(war)))
            kend = k.tip()
            qph = []
            for j in range(4):
                for tt in range(NTT):
                    iq = j * NTT + tt
                    war = qph[-2] if iq >= 2 else kend
                    qph.append(qk_head_block(j, tt, iq, 0, qsel[:], xqc, xqs,
                                             qf, list(war)))
            bj = k.tip()
    return k.tip()


def emit_l0f_att(k, ap, l, deps):
    nc = k.nc
    qf, kf, vt1, of = ap["qf"], ap["kf"], ap["vt1"], ap["of"]
    with ExitStack() as ctx:
        mtm = ctx.enter_context(nc.sbuf_tensor(k.uid + "fa_mtm", [128, 4 * 1024], BF16))
        mkm = ctx.enter_context(nc.sbuf_tensor(k.uid + "fa_mkm", [128, 8 * TT], BF16))
        idn = ctx.enter_context(nc.sbuf_tensor(k.uid + "fa_id", [128, 128], BF16))
        ka2 = ctx.enter_context(nc.sbuf_tensor(k.uid + "fa_k", [65, 2, 1024], BF16))
        va1 = ctx.enter_context(nc.sbuf_tensor(k.uid + "fa_v", [128, 2, 8 * 65], BF16))
        qa2 = ctx.enter_context(nc.sbuf_tensor(k.uid + "fa_qa", [65, 2, TT], BF16))
        junk = ctx.enter_context(nc.sbuf_tensor(k.uid + "fa_j", [128, 2, 1024], BF16))
        mxc = ctx.enter_context(nc.sbuf_tensor(k.uid + "fa_mx", [128, 2, 4], F32))
        mxb = ctx.enter_context(nc.sbuf_tensor(k.uid + "fa_mb", [128, 2, 4], BF16))
        mxr = ctx.enter_context(nc.sbuf_tensor(k.uid + "fa_mr", [4, 2, 128], BF16))
        ex = ctx.enter_context(nc.sbuf_tensor(k.uid + "fa_ex", [128, 2, 2, TT], BF16))
        rec = ctx.enter_context(nc.sbuf_tensor(k.uid + "fa_rec", [1, 2, TT], F32))
        osb = ctx.enter_context(nc.sbuf_tensor(k.uid + "fa_osb", [64, 2, TT], F32))
        onb = ctx.enter_context(nc.sbuf_tensor(k.uid + "fa_onb", [64, 2, TT], BF16))
        o1f = ctx.enter_context(nc.sbuf_tensor(k.uid + "fa_o1f", [1, 64], F32))
        psA = ctx.enter_context(nc.psum_tensor(k.uid + "fa_psA", [128, 2, 2, TT], F32))
        psO = ctx.enter_context(nc.psum_tensor(k.uid + "fa_psO", [128, 2, TT], F32))
        psB = ctx.enter_context(nc.psum_tensor(k.uid + "fa_psB", [128, TT], F32))
        psT = ctx.enter_context(nc.psum_tensor(k.uid + "fa_psT", [4, 128], BF16))
        tm1 = k.dma("q", mtm[:].rearrange("p (m t) -> p m t", m=4),
                    ap["c_mtm"].rearrange("m p t -> p m t"), deps=deps)
        tm2 = k.dma("q", mkm[:].rearrange("p (m t) -> p m t", m=8),
                    ap["c_mkm"].rearrange("m p t -> p m t"), deps=deps)
        tid = k.dma("q", idn[:], ap["c_id"][:, :], deps=deps)
        tv1 = k.do("v", "memset", va1[:], 1.0, deps=deps)
        tk1 = k.do("v", "memset", ka2[64:65, :, :], 1.0, deps=deps)
        to1 = k.do("v", "memset", o1f[:], 1.0, deps=deps)
        pre = [tm1, tm2, tid, tv1, tk1, to1]
        joins = {}
        prev_bc = None
        prev_tr = None
        idx = 0
        for b in range(B):
            for h in range(8):
                for w in range(4):
                    u = idx % 2
                    war = joins.get(idx - 2, list(deps) + pre)
                    jc, jp = h // 2, (h % 2) * 64
                    kt0 = 4 if w == 0 else 0
                    tql = k.dma("q", qa2[0:64, u, :],
                                qf[b, jc, jp:jp + 64, w * TT:(w + 1) * TT],
                                deps=war)
                    if w == 0:
                        tkl = k.dma("q", ka2[0:64, u, 512:1024],
                                    kf[b, jc, jp:jp + 64, 0:TT], deps=war)
                    else:
                        tkl = k.dma("q", ka2[0:64, u, :],
                                    kf[b, jc, jp:jp + 64,
                                       (w - 1) * TT:(w + 1) * TT], deps=war)
                    tvl = [k.dma("q", va1[:, u, kt * 65:kt * 65 + 64],
                                 vt1[b, h, (w - 1) * TT + kt * 128:
                                     (w - 1) * TT + kt * 128 + 128, :],
                                 deps=war) for kt in range(kt0, 8)]
                    p1 = []
                    for qc in range(4):
                        wq1 = [p1[-1][1]] if p1 else war
                        halves = range(1, 2) if w == 0 else range(2)
                        tms = None
                        for half in halves:
                            tms = k.mm(psA[:, u, half, :],
                                       qa2[0:64, u, qc * 128:(qc + 1) * 128],
                                       ka2[0:64, u, half * TT:(half + 1) * TT],
                                       True, True,
                                       deps=war + wq1 + [tql, tkl])
                        if w == 0:
                            tad = k.do("v", "tensor_add", psA[:, u, 1, :],
                                       psA[:, u, 1, :],
                                       mtm[:, qc * 1024 + 512:
                                           qc * 1024 + 1024],
                                       deps=war + [tms, tm1])
                            ttr = k.do("v", "reduce_max",
                                       mxc[:, u, qc:qc + 1],
                                       psA[:, u, 1, :], axis=AX.X,
                                       deps=[tad])
                        else:
                            tad = k.do("v", "tensor_add", psA[:, u, :, :],
                                       psA[:, u, :, :],
                                       mtm[:, qc * 1024:(qc + 1) * 1024],
                                       deps=war + [tms, tm1])
                            ttr = k.do("v", "reduce_max",
                                       mxc[:, u, qc:qc + 1],
                                       psA[:, u, :, :]
                                       .rearrange("p a b -> p (a b)"),
                                       axis=AX.X, deps=[tad])
                        p1.append((tms, ttr))
                    tng = k.do("v", "tensor_scalar_mul", mxb[:, u, :],
                               mxc[:, u, :], -1.0, deps=[p1[-1][1]])
                    ttp = k.do("t", "transpose", psT[:], mxb[:, u, :], idn[:],
                               deps=war + [tng, tid, prev_tr])
                    tmr = k.do("v", "tensor_copy", mxr[:, u, :], psT[:],
                               deps=war + [ttp, prev_tr])
                    prev_tr = tmr
                    tdm = k.dma("p", qa2[64:65, u, :], mxr[:, u, :],
                                deps=[tmr])
                    evs = []
                    tav = None
                    for kt in range(kt0, 8):
                        u2 = kt % 2
                        dsc = [tdm, tkl] + [p1[-1][1], p1[-2][1] if len(p1) > 1 else None]
                        if len(evs) >= 2:
                            dsc.append(evs[-2][0])
                        tsc = k.mm(psA[:, u, u2, :],
                                   ka2[:, u, kt * 128:(kt + 1) * 128],
                                   qa2[:, u, :], True, True, deps=dsc)
                        tmk = k.do("v", "tensor_add", psA[:, u, u2, :],
                                   psA[:, u, u2, :],
                                   mkm[:, kt * TT:(kt + 1) * TT],
                                   deps=[tsc, tm2])
                        dex = [tmk]
                        if len(evs) >= 2:
                            dex.append(evs[-2][1])
                        tex = k.do("s", "activation", ex[:, u, u2, :],
                                   psA[:, u, u2, :], AF.Exp, scale=8.0,
                                   deps=dex)
                        tav = k.mm(psO[0:65, u, :],
                                   va1[:, u, kt * 65:kt * 65 + 65],
                                   ex[:, u, u2, :], start=(kt == kt0),
                                   stop=(kt == 7),
                                   deps=[tex, tv1] + tvl +
                                   (war + [tmr] if kt == kt0 else []),
                                   mark=True)
                        evs.append((tex, tav))
                    tln = k.do("s", "activation", rec[:, u, :],
                               psO[64:65, u, :], AF.Ln, deps=war + [tav])
                    trc = k.do("s", "activation", rec[:, u, :],
                               rec[:, u, :], AF.Exp, scale=-1.0, deps=[tln])
                    tbc = k.mm(psB[0:64, :], o1f[:], rec[:, u, :], True, True,
                               deps=[trc, to1, prev_bc])
                    tcp = k.do("v", "tensor_copy", osb[:, u, :],
                               psO[0:64, u, :], deps=war + [tav])
                    tmu = k.do("v", "tensor_mul", onb[:, u, :], osb[:, u, :],
                               psB[0:64, :], deps=[tbc, tcp])
                    prev_bc = tmu
                    tst = k.dma("p", of[b, jc, jp:jp + 64, w * TT:(w + 1) * TT],
                                onb[:, u, :], deps=[tmu])
                    joins[idx] = k.tip()
                    idx += 1
    return k.tip()


def emit_l0f_w1(k, ap, l, src_name, deps):
    nc = k.nc
    src, mgl = ap[src_name], ap["mgl"]
    with ExitStack() as ctx:
        w1 = ctx.enter_context(nc.sbuf_tensor(k.uid + "f1_w1", [128, NC * 2 * HIDP], BF16))
        b1 = ctx.enter_context(nc.sbuf_tensor(k.uid + "f1_b1", [128, 2 * NHC], F32))
        glb = ctx.enter_context(nc.sbuf_tensor(k.uid + "f1_glb", [2, NC * 128], F32))
        stg = ctx.enter_context(nc.sbuf_tensor(k.uid + "f1_st", [128, 2, NC * TT], F32))
        hbt = ctx.enter_context(nc.sbuf_tensor(k.uid + "f1_hb", [128, NC * TT], BF16))
        sq = ctx.enter_context(nc.sbuf_tensor(k.uid + "f1_sq", [128, NC * TT], BF16))
        ones = ctx.enter_context(nc.sbuf_tensor(k.uid + "f1_on", [128, 1], BF16))
        rows = ctx.enter_context(nc.sbuf_tensor(k.uid + "f1_rw", [1, 2, 6, TT], F32))
        rh = ctx.enter_context(nc.sbuf_tensor(k.uid + "f1_rh", [2, 2, 2, TT], F32))
        eps = ctx.enter_context(nc.sbuf_tensor(k.uid + "f1_eps", [1, 1], F32))
        hnf = ctx.enter_context(nc.sbuf_tensor(k.uid + "f1_hnf", [128, TT], F32))
        hnb = ctx.enter_context(nc.sbuf_tensor(k.uid + "f1_hnb", [128, 2, NC * TT], BF16))
        gl = ctx.enter_context(nc.sbuf_tensor(k.uid + "f1_gl", [128, 2, 2, TT], BF16))
        ab = ctx.enter_context(nc.sbuf_tensor(k.uid + "f1_ab", [128, 2, 2, TT], BF16))
        mo = ctx.enter_context(nc.sbuf_tensor(k.uid + "f1_mo", [128, 2, 2, TT], BF16))
        ps = ctx.enter_context(nc.psum_tensor(k.uid + "f1_ps", [128, 8, TT], F32))
        tw1 = cast_weights(k, w1[:], ap["l0f_w1"][l].rearrange("c p n -> p c n"),
                           deps, stage_f32=stg[:, 0, :])
        tb1 = k.dma("q", b1[:], ap["l0f_b1"][l], deps=deps)
        tg1 = k.dma("q", glb[0:1, :],
                    ap["l0f_lng"][l].rearrange("c p -> (c p)")[None, :],
                    deps=deps)
        tg2 = k.dma("q", glb[1:2, :],
                    ap["l0f_lnb"][l].rearrange("c p -> (c p)")[None, :],
                    deps=deps)
        ton = k.do("v", "memset", ones[:], 1.0, deps=deps)
        tep = k.do("v", "memset", eps[:], 1e-5, deps=deps)
        trh = k.do("v", "memset", rh[:], 0.0, deps=deps)
        tone = k.do("v", "memset", hnf[:1, :], 1.0, deps=deps)
        trh2a = k.dma("p", rh[1:2, 0, 1, :], hnf[0:1, 0:TT], deps=[trh, tone])
        trh2b = k.dma("p", rh[1:2, 1, 1, :], hnf[0:1, 0:TT], deps=[trh, tone])
        trh2 = k.do("v", "memset", hnf[:1, :], 0.0,
                    deps=[trh2a, trh2b])
        pre = [tw1, tb1, tg1, tg2, ton, tep, trh2a, trh2b, trh2]
        joins = {}
        for i in range(B * NTT):
            b, tt = divmod(i, NTT)
            u = i % 2
            war = joins.get(i - 2, list(deps) + pre)
            t0 = tt * TT
            tl = _shift_load(k, stg[:, u, :], src, b, t0, war)
            tcst = k.do("v", "tensor_copy", hbt[:], stg[:, u, :],
                        deps=war + tl)
            tsqr = k.do("s", "activation", sq[:], hbt[:], AF.Square,
                        deps=war + [tcst])
            for c in range(NC):
                ts0 = k.mm(ps[0:1, 4 * u, :], ones[:],
                           hbt[:, c * TT:(c + 1) * TT],
                           start=(c == 0), stop=(c == NC - 1),
                           deps=war + [tcst, ton])
            for c in range(NC):
                ts1 = k.mm(ps[0:1, 4 * u + 1, :], ones[:],
                           sq[:, c * TT:(c + 1) * TT],
                           start=(c == 0), stop=(c == NC - 1), deps=[tsqr])
            R = rows[:, u, :, :]
            tm = k.do("v", "tensor_scalar_mul", R[:, 0, :], ps[0:1, 4 * u, :],
                      1.0 / D, deps=war + [ts0])
            te = k.do("v", "tensor_scalar_mul", R[:, 1, :],
                      ps[0:1, 4 * u + 1, :], 1.0 / D, deps=[ts1])
            tmm = k.do("v", "tensor_mul", R[:, 2, :], R[:, 0, :], R[:, 0, :],
                       deps=[tm])
            tv = k.do("v", "tensor_sub", R[:, 3, :], R[:, 1, :], R[:, 2, :],
                      deps=[te, tmm])
            tsd = k.do("s", "activation", R[:, 4, :], R[:, 3, :], AF.Sqrt,
                       bias=eps[:], deps=war + [tv, tep])
            tr_ = k.do("v", "reciprocal", R[:, 5, :], R[:, 4, :], deps=[tsd])
            tra = k.do("v", "tensor_copy", rh[0:1, u, 0, :], R[:, 5, :],
                       deps=[tr_])
            tnm = k.do("v", "tensor_mul", R[:, 2, :], R[:, 0, :], R[:, 5, :],
                       deps=[tr_])
            tnm2 = k.do("v", "tensor_scalar_mul", rh[0:1, u, 1, :], R[:, 2, :],
                        -1.0, deps=[tnm])
            hn_toks = []
            for c in range(NC):
                ta = k.mm(ps[:, 4 * u + 2, :], glb[:, c * 128:(c + 1) * 128],
                          rh[:, u, 0, :], True, True,
                          deps=war + [tra, trh2, tg2] +
                          (hn_toks[-1:] if hn_toks else []))
                tb_ = k.mm(ps[:, 4 * u + 3, :], glb[:, c * 128:(c + 1) * 128],
                           rh[:, u, 1, :], True, True, deps=[tnm2])
                th1 = k.do("v", "tensor_mul", hnf[:],
                           stg[:, u, c * TT:(c + 1) * TT], ps[:, 4 * u + 2, :],
                           deps=war + [ta] + tl)
                th2_ = k.do("v", "tensor_add", hnf[:], hnf[:],
                            ps[:, 4 * u + 3, :], deps=[tb_, th1])
                th3 = k.do("v", "tensor_copy", hnb[:, u, c * TT:(c + 1) * TT],
                           hnf[:], deps=[th2_])
                hn_toks.append(th3)
            gm_prev = {}
            for j in range(NHC):
                jp = j % 2
                ba = 4 * u + jp
                bg = 4 * u + 2 + jp
                wdeps = war + [tw1, hn_toks[-1]]
                prev = gm_prev.get(jp)
                if prev is not None:
                    wdeps = wdeps + [prev[0]]
                tma = tmg = None
                for c in range(NC):
                    tma = k.mm(ps[:, ba, :],
                               w1[:, c * 2 * HIDP + j * 128:
                                  c * 2 * HIDP + j * 128 + 128],
                               hnb[:, u, c * TT:(c + 1) * TT],
                               start=(c == 0), stop=(c == NC - 1), deps=wdeps)
                for c in range(NC):
                    tmg = k.mm(ps[:, bg, :],
                               w1[:, c * 2 * HIDP + HIDP + j * 128:
                                  c * 2 * HIDP + HIDP + j * 128 + 128],
                               hnb[:, u, c * TT:(c + 1) * TT],
                               start=(c == 0), stop=(c == NC - 1), deps=wdeps)
                tgl = k.do("s", "activation", gl[:, u, jp, :], ps[:, bg, :],
                           AF.Gelu, bias=b1[:, NHC + j:NHC + j + 1],
                           deps=war + [tmg, tb1] +
                           ([prev[0]] if prev else []))
                tab = k.do("v", "tensor_scalar_add", ab[:, u, jp, :],
                           ps[:, ba, :], b1[:, j:j + 1],
                           deps=war + [tma, tb1] +
                           ([prev[0]] if prev else []))
                tm_ = k.do("v", "tensor_mul", mo[:, u, jp, :],
                           ab[:, u, jp, :], gl[:, u, jp, :],
                           deps=[tgl, tab] + ([prev[1]] if prev else []))
                td = k.dma("p", mgl[b, j, :, t0:t0 + TT], mo[:, u, jp, :],
                           deps=[tm_])
                gm_prev[jp] = (tm_, td)
            joins[i] = k.tip()
    return k.tip()


def emit_l1a_qkv(k, ap, deps):
    nc = k.nc
    xn2, q2aug, k2aug, vt2 = ap["xn2"], ap["q2aug"], ap["k2aug"], ap["vt2"]
    with ExitStack() as ctx:
        wqs = ctx.enter_context(nc.sbuf_tensor(k.uid + "lq_wq", [128, NC * 1024], BF16))
        wks = ctx.enter_context(nc.sbuf_tensor(k.uid + "lq_wk", [128, NC * 1024], BF16))
        wvs = ctx.enter_context(nc.sbuf_tensor(k.uid + "lq_wv", [128, NC * 1024], BF16))
        stg = ctx.enter_context(nc.sbuf_tensor(k.uid + "lq_st", [128, 2048], F32))
        bqs = ctx.enter_context(nc.sbuf_tensor(k.uid + "lq_bq", [128, NC], F32))
        bks = ctx.enter_context(nc.sbuf_tensor(k.uid + "lq_bk", [128, NC], F32))
        bvr = ctx.enter_context(nc.sbuf_tensor(k.uid + "lq_bv", [1, 1024], BF16))
        bvf = ctx.enter_context(nc.sbuf_tensor(k.uid + "lq_bvf", [1, 1024], F32))
        wqr = ctx.enter_context(nc.sbuf_tensor(k.uid + "lq_wqr", [128, 16], BF16))
        wkr = ctx.enter_context(nc.sbuf_tensor(k.uid + "lq_wkr", [128, 16], BF16))
        wrf = ctx.enter_context(nc.sbuf_tensor(k.uid + "lq_wrf", [128, 32], F32))
        onesc = ctx.enter_context(nc.sbuf_tensor(k.uid + "lq_on", [128, 1], BF16))
        onesr = ctx.enter_context(nc.sbuf_tensor(k.uid + "lq_or", [1, T], BF16))
        xnb = ctx.enter_context(nc.sbuf_tensor(k.uid + "lq_xnb", [128, 2, NC * TT], BF16))
        qk = ctx.enter_context(nc.sbuf_tensor(k.uid + "lq_qk", [128, 2, TT], BF16))
        sqr = ctx.enter_context(nc.sbuf_tensor(k.uid + "lq_sqr", [16, 2, TT], BF16))
        rr = ctx.enter_context(nc.sbuf_tensor(k.uid + "lq_rr", [16, 2, TT], BF16))
        nr = ctx.enter_context(nc.sbuf_tensor(k.uid + "lq_nr", [1, 2, TT], F32))
        crw = ctx.enter_context(nc.sbuf_tensor(k.uid + "lq_crw", [1, 2, TT], BF16))
        mks = ctx.enter_context(nc.sbuf_tensor(k.uid + "lq_mks", [1, 16, 6], F32))
        mkv = ctx.enter_context(nc.sbuf_tensor(k.uid + "lq_mkv", [1, 16], F32))
        vb = ctx.enter_context(nc.sbuf_tensor(k.uid + "lq_vb", [128, 2, TT], BF16))
        ps = ctx.enter_context(nc.psum_tensor(k.uid + "lq_ps", [128, 8, TT], F32))
        tw1 = cast_weights(k, wqs[:], ap["l1a_wq"].rearrange("c p n -> p c n"),
                           deps, stage_f32=stg[:])
        tw2 = cast_weights(k, wks[:], ap["l1a_wk"].rearrange("c p n -> p c n"),
                           [tw1], stage_f32=stg[:])
        tw3 = cast_weights(k, wvs[:], ap["l1a_wv"].rearrange("c p n -> p c n"),
                           [tw2], stage_f32=stg[:])
        tb1 = k.dma("q", bqs[:], ap["l1a_bq"][:, :], deps=deps)
        tb2 = k.dma("q", bks[:], ap["l1a_bk"][:, :], deps=deps)
        tb3 = k.dma("q", bvf[:], ap["l1a_bvrow"][:, :], deps=deps)
        tb4 = k.do("v", "tensor_copy", bvr[:], bvf[:], deps=[tb3])
        tw4a = k.dma("q", wrf[0:64, 0:16], ap["l1a_wqr"][:, :], deps=deps)
        tw4b = k.dma("q", wrf[64:128, 0:16], ap["l1a_wqr"][:, :], deps=deps)
        tw5a = k.dma("q", wrf[0:64, 16:32], ap["l1a_wkr"][:, :], deps=deps)
        tw5b = k.dma("q", wrf[64:128, 16:32], ap["l1a_wkr"][:, :], deps=deps)
        tw6 = k.do("v", "tensor_copy", wqr[:], wrf[:, 0:16],
                   deps=[tw4a, tw4b])
        tw7 = k.do("v", "tensor_copy", wkr[:], wrf[:, 16:32],
                   deps=[tw5a, tw5b])
        to1 = k.do("v", "memset", onesc[:], 1.0, deps=deps)
        to2 = k.do("v", "memset", onesr[:], 1.0, deps=deps)
        pre = [tw3, tb1, tb2, tb4, tw6, tw7, to1, to2]
        bj = list(deps) + pre

        def qk_phase(wsb, bsb, wr, dst, is_q, war0):
            ph = []
            rprev = {}
            nprev = {}
            shr = {}
            for tt in range(NTT):
                u = tt % 2
                war = ph[-2] if tt >= 2 else war0
                tl = k.dma("q", xnb[:, u, :].rearrange("p (c t) -> p c t", c=NC),
                           _fmslice(xn2, b, tt * TT), deps=war)
                toks = [tl]
                eprev = {}
                for j in range(NC):
                    bqk = (j % 2)
                    tpm = None
                    for c in range(NC):
                        tpm = k.mm(ps[:, bqk, :],
                                   wsb[:, c * 1024 + j * 128:
                                       c * 1024 + j * 128 + 128],
                                   xnb[:, u, c * TT:(c + 1) * TT],
                                   start=(c == 0), stop=(c == NC - 1),
                                   deps=list(war) + [tl] +
                                   ([eprev[bqk]] if bqk in eprev else []) +
                                   (rprev.get((bqk, 0), []) +
                                    rprev.get((bqk, 1), [])))
                    tev = k.do("s", "activation", qk[:, u, :], ps[:, bqk, :],
                               AF.Identity, bias=bsb[:, j:j + 1],
                               deps=list(war) + [tpm] +
                               ([eprev[bqk]] if bqk in eprev else []))
                    eprev[bqk] = tev
                    for hh in range(2):
                        if INNER14 < 1:
                            break
                        h = 2 * j + hh
                        rb = 2 + 2 * (j % 2) + hh
                        nb = 6 + hh
                        rpv = rprev.get((j % 2, hh), [])
                        npv = nprev.get(hh, [])
                        trm = k.mm(ps[0:16, rb, :],
                                   wr[hh * 64:hh * 64 + 64, :],
                                   qk[hh * 64:hh * 64 + 64, u, :], True, True,
                                   deps=[tev] + rpv)
                        if INNER14 >= 2:
                            tsq = k.do("s", "activation", sqr[:, u, :],
                                       ps[0:16, rb, :], AF.Square,
                                       deps=[trm, shr.get(("sq", u))] + npv)
                            tnm = tsq
                        if INNER14 >= 3:
                            tnm = k.mm(ps[0:1, nb, :], onesc[0:16, :],
                                       sqr[:, u, :], True, True,
                                       deps=[tsq, to1] + npv)
                        tcv = k.do("v", "tensor_copy", rr[:, u, :],
                                   ps[0:16, rb, :],
                                   deps=[trm, tsq if INNER14 >= 2 else None,
                                         shr.get(("rr", u))] + rpv)
                        td1 = k.dma("p", dst[b, h, 0:16, tt * TT:(tt + 1) * TT],
                                    rr[:, u, :], deps=[tcv])
                        shr[("rr", u)] = td1
                        if INNER14 >= 3:
                            shr[("sq", u)] = tnm
                        elif INNER14 >= 2:
                            shr[("sq", u)] = tsq
                            rprev[(j % 2, hh)] = [tsq, tcv, td1]
                        else:
                            rprev[(j % 2, hh)] = [tcv, td1]
                        if is_q and INNER14 >= 2:
                            tsr = k.do("s", "activation", nr[:, u, :],
                                       ps[0:1, nb, :], AF.Sqrt,
                                       deps=[tnm, shr.get(("nr", u))])
                            tcr = k.do("v", "tensor_scalar_mul", crw[:, u, :],
                                       nr[:, u, :], mkv[0:1, h:h + 1],
                                       deps=[tsr, mkr[0],
                                             shr.get(("cr", u))])
                            td2 = k.dma("p", dst[b, h, 16:17,
                                             tt * TT:(tt + 1) * TT],
                                        crw[:, u, :], deps=[tcr])
                            shr[("nr", u)] = tcr
                            shr[("cr", u)] = td2
                            nprev[hh] = [tsr]
                            toks += [tcr, td1, td2]
                        elif INNER14 >= 4:
                            tmx = k.do("v", "reduce_max",
                                       mks[0:1, h, tt:tt + 1],
                                       ps[0:1, nb, :], axis=AX.X,
                                       deps=[tnm])
                            if DBG14 and b == 0 and h == 0 and tt == 0:
                                tdg = k.do("v", "tensor_copy",
                                           rr[:, u, :], ps[0:16, rb, :],
                                           deps=[tnm, tmx, tcv])
                                k.dma("p", ap["dbg14"][1:2, 0:512],
                                      sqr[0:1, u, :], deps=[tnm, tsq])
                                tdh = k.do("v", "tensor_copy",
                                           nr[0:1, 0, :], ps[0:1, nb, :],
                                           deps=[tnm, tmx])
                                k.dma("p", ap["dbg14"][2:3, 0:512],
                                      nr[0:1, 0, :], deps=[tdh])
                                shr[("rr", u)] = tdg
                            nprev[hh] = [tmx]
                            toks += [tmx, td1]
                        elif INNER14 >= 2:
                            nprev[hh] = [tnm]
                            toks += [td1]
                        else:
                            toks += [td1]
                ph.append(toks)
            return [x for t_ in ph for x in t_]

        mkr = [None]
        for b in range(B):
            kj = qk_phase(wks, bks, wkr, k2aug, False, bj)
            if PH14 < 2:
                bj = k.tip()
                continue
            for h in range(16):
                k.dma("p", k2aug[b, h, 16:17, :], onesr[:], deps=[to2])
            mt = []
            for h in range(16):
                t1_ = k.do("v", "reduce_max", mks[0:1, h, 4:5],
                           mks[0:1, h, 0:NTT], axis=AX.X, deps=kj)
                t2_ = k.do("s", "activation", mks[0:1, h, 5:6],
                           mks[0:1, h, 4:5], AF.Sqrt, deps=[t1_])
                t3_ = k.do("v", "tensor_scalar_mul", mkv[0:1, h:h + 1],
                           mks[0:1, h, 5:6], -1.0, deps=[t2_])
                mt.append(t3_)
            mkr[0] = mt[-1]
            if b == 0:
                k.dma("p", ap["dbg14"][0:1, 0:96], mks[0:1, :, :], deps=mt)
                k.dma("p", ap["dbg14"][0:1, 96:112], mkv[:], deps=mt)
            kend = k.tip()
            if PH14 < 3:
                bj = k.tip()
                continue
            vph = []
            vprev = {}
            for tt in range(NTT):
                u = tt % 2
                war = vph[-2] if tt >= 2 else kend
                tl = k.dma("q", xnb[:, u, :].rearrange("p (c t) -> p c t", c=NC),
                           _fmslice(xn2, b, tt * TT), deps=war)
                toks = [tl]
                for tc4 in range(4):
                    for vc in range(2):
                        iv = tc4 * 2 + vc
                        uv = iv % 2
                        bv_ = 6 + uv
                        pv2 = vprev.get(uv, [])
                        tvm = None
                        for c in range(NC):
                            tvm = k.mm(ps[:, bv_, :],
                                       xnb[:, u, c * TT + tc4 * 128:
                                           c * TT + tc4 * 128 + 128],
                                       wvs[:, c * 1024 + vc * 512:
                                           c * 1024 + vc * 512 + 512],
                                       start=(c == 0), stop=False,
                                       deps=list(war) + [tl] + pv2[:1],
                                       mark=False)
                        tvm = k.mm(ps[:, bv_, :], onesr[0:1, 0:128],
                                   bvr[:, vc * 512:(vc + 1) * 512],
                                   start=False, stop=True,
                                   deps=[to2, tb4])
                        tvc = k.do("v", "tensor_copy", vb[:, uv, :],
                                   ps[:, bv_, :],
                                   deps=list(war) + [tvm] + pv2)
                        tds = [k.dma("p",
                                     vt2[b, vc * 8 + h2,
                                         tt * TT + tc4 * 128:
                                         tt * TT + tc4 * 128 + 128, :],
                                     vb[:, uv, h2 * 64:(h2 + 1) * 64],
                                     deps=[tvc]) for h2 in range(8)]
                        vprev[uv] = [tvc] + tds
                        toks += [tvc] + tds
                vph.append(toks)
            vend = k.tip()
            if PH14 >= 4:
                qj = qk_phase(wqs, bqs, wqr, q2aug, True, vend)
            bj = k.tip()
    return k.tip()


def emit_gate(k, ap, deps):
    nc = k.nc
    xn2, m8, mbeta = ap["xn2"], ap["m8"], ap["mbeta"]
    with ExitStack() as ctx:
        gw = ctx.enter_context(nc.sbuf_tensor(k.uid + "gt_w", [128, NC * 8], BF16))
        stg = ctx.enter_context(nc.sbuf_tensor(k.uid + "gt_st", [128, NC * 8], F32))
        gb8 = ctx.enter_context(nc.sbuf_tensor(k.uid + "gt_b", [8, 1], F32))
        idn = ctx.enter_context(nc.sbuf_tensor(k.uid + "gt_id", [128, 128], BF16))
        xnb = ctx.enter_context(nc.sbuf_tensor(k.uid + "gt_x", [128, 2, NC * TT], BF16))
        lgb = ctx.enter_context(nc.sbuf_tensor(k.uid + "gt_lb", [8, 2, TT], BF16))
        lgt = ctx.enter_context(nc.sbuf_tensor(k.uid + "gt_lt", [128, 2, 8], F32))
        mxt = ctx.enter_context(nc.sbuf_tensor(k.uid + "gt_mx", [128, 2, 1], F32))
        mskb = ctx.enter_context(nc.sbuf_tensor(k.uid + "gt_mb", [128, 2, 8], BF16))
        mk8 = ctx.enter_context(nc.sbuf_tensor(k.uid + "gt_m8", [8, 2, 128], F32))
        mbt = ctx.enter_context(nc.sbuf_tensor(k.uid + "gt_bt", [8, 2, 128], F32))
        psg = ctx.enter_context(nc.psum_tensor(k.uid + "gt_ps", [128, 2, TT], F32))
        psa = ctx.enter_context(nc.psum_tensor(k.uid + "gt_pa", [128, 2, 8], BF16))
        psb = ctx.enter_context(nc.psum_tensor(k.uid + "gt_pb", [8, 2, 128], BF16))
        tw = cast_weights(k, gw[:], ap["l1f_gw"].rearrange("c p n -> p c n"),
                          deps, stage_f32=stg[:])
        tb = k.dma("q", gb8[:], ap["l1f_gbrow"][:, :], deps=deps)
        tid = k.dma("q", idn[:], ap["c_id"][:, :], deps=deps)
        pre = [tw, tb, tid]
        joins = {}
        tprev = {}
        for i in range(B * NTT):
            b, tt = divmod(i, NTT)
            u = i % 2
            war = joins.get(i - 2, list(deps) + pre)
            t0g = b * T + tt * TT
            tl = k.dma("q", xnb[:, u, :].rearrange("p (c t) -> p c t", c=NC),
                       _fmslice(xn2, b, tt * TT), deps=war)
            tg = None
            for c in range(NC):
                tg = k.mm(psg[0:8, u, :], gw[:, c * 8:(c + 1) * 8],
                          xnb[:, u, c * TT:(c + 1) * TT],
                          start=(c == 0), stop=(c == NC - 1),
                          deps=list(war) + [tl, tw])
            tev = k.do("s", "activation", lgb[:, u, :], psg[0:8, u, :],
                       AF.Identity, bias=gb8[:], deps=list(war) + [tg, tb])
            for tc4 in range(4):
                gc = i * 4 + tc4
                u4 = gc % 2
                pv = tprev.get(gc - 2, {})
                wv = list(war) if tc4 < 2 else []
                ttr = k.do("t", "transpose", psa[:, u4, :],
                           lgb[0:8, u, tc4 * 128:(tc4 + 1) * 128],
                           idn[0:8, 0:8],
                           deps=[tev, tid, pv.get("tcp")] + wv)
                tcp = k.do("v", "tensor_copy", lgt[:, u4, :], psa[:, u4, :],
                           deps=[ttr, pv.get("teq")] + wv)
                tmx = k.do("v", "reduce_max", mxt[:, u4, :], lgt[:, u4, :],
                           axis=AX.X, deps=[tcp, pv.get("teq")])
                teq = k.do("v", "tensor_scalar", mskb[:, u4, :],
                           lgt[:, u4, :], mxt[:, u4, :], None, ALU.is_equal,
                           deps=[tmx, pv.get("ttr2")])
                ttr2 = k.do("t", "transpose", psb[:, u4, :], mskb[:, u4, :],
                            idn[:], deps=[teq, pv.get("tm8")] + wv)
                tm8 = k.do("v", "tensor_copy", mk8[:, u4, :], psb[:, u4, :],
                           deps=[ttr2, pv.get("d1"), pv.get("tbt")])
                tbt = k.do("v", "tensor_scalar", mbt[:, u4, :], mk8[:, u4, :],
                           1.0, 1.0e9, ALU.subtract, ALU.mult,
                           deps=[tm8, pv.get("d2")])
                d1 = k.dma("p", m8[:, t0g + tc4 * 128:t0g + tc4 * 128 + 128],
                           mk8[:, u4, :], deps=[tm8])
                d2 = k.dma("p", mbeta[:, t0g + tc4 * 128:t0g + tc4 * 128 + 128],
                           mbt[:, u4, :], deps=[tbt])
                tprev[gc] = dict(ttr=ttr, tcp=tcp, tmx=tmx, teq=teq,
                                 ttr2=ttr2, tm8=tm8, tbt=tbt, d1=d1, d2=d2)
            joins[i] = k.tip()
    return k.tip()


def emit_moe(k, ap, deps):
    nc = k.nc
    xn2, m8, mbeta, out = ap["xn2"], ap["m8"], ap["mbeta"], ap["out"]
    acc_in = RD(ap, "out")
    NTU = B * NTT
    with ExitStack() as ctx:
        w1e = ctx.enter_context(nc.sbuf_tensor(k.uid + "mo_w1", [128, NC * 4096], BF16))
        w2e = ctx.enter_context(nc.sbuf_tensor(k.uid + "mo_w2", [128, 32 * 1024], BF16))
        stg = ctx.enter_context(nc.sbuf_tensor(k.uid + "mo_st", [128, 2048], F32))
        b1s = ctx.enter_context(nc.sbuf_tensor(k.uid + "mo_b1", [128, 32], F32))
        b2m = ctx.enter_context(nc.sbuf_tensor(k.uid + "mo_b2", [8, NC * 128], BF16))
        b2f = ctx.enter_context(nc.sbuf_tensor(k.uid + "mo_b2f", [8, NC * 128], F32))
        onesb = ctx.enter_context(nc.sbuf_tensor(k.uid + "mo_on", [1, 128], BF16))
        xg = ctx.enter_context(nc.sbuf_tensor(k.uid + "mo_x", [128, 2, NC * TT], BF16))
        m8f = ctx.enter_context(nc.sbuf_tensor(k.uid + "mo_m8f", [8, 2, TT], F32))
        m8s = ctx.enter_context(nc.sbuf_tensor(k.uid + "mo_m8s", [8, 2, TT], BF16))
        mbf = ctx.enter_context(nc.sbuf_tensor(k.uid + "mo_mbf", [1, 2, TT], F32))
        mbs = ctx.enter_context(nc.sbuf_tensor(k.uid + "mo_mbs", [1, 2, TT], BF16))
        hsb = ctx.enter_context(nc.sbuf_tensor(k.uid + "mo_h", [128, 32 * TT], BF16))
        acct = ctx.enter_context(nc.sbuf_tensor(k.uid + "mo_a", [128, 2, TT], F32))
        ps = ctx.enter_context(nc.psum_tensor(k.uid + "mo_ps", [128, 8, TT], F32))
        ton = k.do("v", "memset", onesb[:], 1.0, deps=deps)
        tb2l = k.dma("q", b2f[:].rearrange("p (c n) -> p c n", c=NC),
                     ap["l1f_b2m"].rearrange("c p n -> p c n"), deps=deps)
        tb2 = k.do("v", "tensor_copy", b2m[:], b2f[:], deps=[tb2l])
        pre = [ton, tb2]
        ej = list(deps) + pre
        region = {}
        for e in range(8):
            tw1 = cast_weights(k, w1e[:],
                               ap["l1f_w1t"][e].rearrange("c p n -> p c n"),
                               ej, stage_f32=stg[:])
            tw2 = cast_weights(k, w2e[:],
                               ap["l1f_w2t"][e].rearrange("c p n -> p c n"),
                               [tw1], stage_f32=stg[:])
            tb1l = k.dma("q", b1s[:], ap["l1f_b1"][e], deps=ej)
            epre = [tw2, tb1l]
            uprev = {}
            for ttu in range(NTU):
                b_, tt = divmod(ttu, NTT)
                t0 = tt * TT
                tg0 = b_ * T + t0
                ux = ttu % 2
                tlx = k.dma("q",
                            xg[:, ux, :].rearrange("p (c t) -> p c t", c=NC),
                            _fmslice(xn2, b_, t0),
                            deps=ej + epre + [uprev.get(("x", ux))])
                tmbf = k.dma("q", mbf[:, ux, :], mbeta[e:e + 1, tg0:tg0 + TT],
                             deps=ej + [uprev.get(("mbf", ux))])
                tmbs = k.do("v", "tensor_copy", mbs[:, ux, :], mbf[:, ux, :],
                            deps=[tmbf, uprev.get(("mbs", ux))])
                if e == 7:
                    tl8f = k.dma("q", m8f[:, ux, :], m8[:, tg0:tg0 + TT],
                                 deps=ej + [uprev.get(("m8f", ux))])
                    tl8 = k.do("v", "tensor_copy", m8s[:, ux, :],
                               m8f[:, ux, :],
                               deps=[tl8f, uprev.get(("m8s", ux))])
                else:
                    tl8 = None
                relu_toks = []
                for hc in range(32):
                    bh = 6 + (hc % 2)
                    wdeps = [tw1, tlx, tmbs, ton]
                    if hc >= 2:
                        wdeps.append(relu_toks[-2])
                    elif ("pb", 6 + hc) in uprev:
                        wdeps.append(uprev[("pb", 6 + hc)])
                    tph = None
                    for c in range(NC):
                        tph = k.mm(ps[:, bh, :],
                                   w1e[:, c * 4096 + hc * 128:
                                       c * 4096 + hc * 128 + 128],
                                   xg[:, ux, c * TT:(c + 1) * TT],
                                   start=(c == 0), stop=False, deps=wdeps,
                                   mark=False)
                    tph = k.mm(ps[:, bh, :], onesb[:],
                               mbs[:, ux, :], start=False, stop=True,
                               deps=wdeps + [tmbs])
                    trl = k.do("s", "activation", hsb[:, hc * TT:(hc + 1) * TT],
                               ps[:, bh, :], AF.Relu, bias=b1s[:, hc:hc + 1],
                               deps=[tph, tb1l] +
                               ([relu_toks[-2]] if hc >= 2 else
                                [uprev.get(("h", hc % 2))]))
                    relu_toks.append(trl)
                uprev[("x", ux)] = relu_toks[-1]
                uprev[("mbf", ux)] = tmbs
                uprev[("mbs", ux)] = relu_toks[-1]
                if e == 7:
                    uprev[("m8f", ux)] = tl8
                uprev[("h", 0)] = relu_toks[-2]
                uprev[("h", 1)] = relu_toks[-1]
                for dc in range(NC):
                    ua = dc % 2
                    src_acc = acc_in if e == 0 else out
                    tla = k.dma("q", acct[:, ua, :],
                                src_acc[b_, dc, :, t0:t0 + TT],
                                deps=[region.get((ttu, dc)),
                                      uprev.get(("a", ua))])
                    tpy = None
                    wdeps2 = [tw2, relu_toks[-1]]
                    if ("pb", dc) in uprev:
                        wdeps2.append(uprev[("pb", dc)])
                    for hc in range(32):
                        last = (hc == 31 and e != 7)
                        tpy = k.mm(ps[:, dc, :],
                                   w2e[:, hc * 1024 + dc * 128:
                                       hc * 1024 + dc * 128 + 128],
                                   hsb[:, hc * TT:(hc + 1) * TT],
                                   start=(hc == 0), stop=last,
                                   deps=wdeps2 + [relu_toks[hc]], mark=last)
                    if e == 7:
                        tpy = k.mm(ps[:, dc, :],
                                   b2m[:, dc * 128:(dc + 1) * 128],
                                   m8s[:, ux, :], start=False,
                                   stop=True, deps=[tb2, tl8])
                        uprev[("m8s", ux)] = tpy
                    tac = k.do("v", "tensor_add", acct[:, ua, :],
                               acct[:, ua, :], ps[:, dc, :],
                               deps=[tpy, tla])
                    tst = k.dma("p", out[b_, dc, :, t0:t0 + TT],
                                acct[:, ua, :], deps=[tac])
                    region[(ttu, dc)] = tst
                    uprev[("a", ua)] = tst
                    uprev[("pb", dc)] = tac
            ej = k.tip()
    return k.tip()



def _l0a_qkv_reads():
    return ["xn1", "l0a_wqa", "l0a_wqb", "l0a_wkva", "l0a_wkvbK", "l0a_wkvbV",
            "c_cd", "c_sd", "c_p128"]


def RD(ap, name):
    return ap.get(name + "__in", ap[name])


STAGES = {
    "s0_ln1": {
        "fn": lambda k, ap, d: emit_ln(k, ap, ap["xin"], ap["xn1"], ap["xfa"], d),
        "reads": ["xin"], "writes": ["xn1", "xfa"]},
    "s1_l0aqkv": {
        "fn": lambda k, ap, d: emit_l0a_qkv(k, ap, d),
        "reads": _l0a_qkv_reads(), "writes": ["qaug", "kaug", "vt0"]},
    "s2_l0aatt": {
        "fn": lambda k, ap, d: emit_l0a_att(k, ap, d),
        "reads": ["qaug", "kaug", "vt0", "c_cm"], "writes": ["o0"]},
    "s3_l0aout": {
        "fn": lambda k, ap, d: emit_matproj_add(
            k, ap, ap["o0"], ap["l0a_wout"], ap["l0a_bout"], ap["xin"],
            ap["x1"], d, tag="s3"),
        "reads": ["o0", "xin", "l0a_wout", "l0a_bout"], "writes": ["x1"]},
}

for _l in (0, 1):
    _src = "xfa" if _l == 0 else "xfb"
    _dst = "xfb" if _l == 0 else "xfa"
    STAGES[f"s4_l{_l}"] = {
        "fn": (lambda l: lambda k, ap, d: emit_l0f_qkv(k, ap, l, d))(_l),
        "reads": [_src, "l0f_wqkv", "l0f_qs", "l0f_ks", "c_xqc", "c_xqs",
                  "c_xkc", "c_xks", "c_prot"],
        "writes": ["qf", "kf", "vt1"]}
    STAGES[f"s5_l{_l}"] = {
        "fn": (lambda l: lambda k, ap, d: emit_l0f_att(k, ap, l, d))(_l),
        "reads": ["qf", "kf", "vt1", "c_mtm", "c_mkm", "c_id"],
        "writes": ["of"]}
    STAGES[f"s6_l{_l}"] = {
        "fn": (lambda l, s: lambda k, ap, d: emit_matproj_add(
            k, ap, ap["of"], ap["l0f_wo"][l], None, RD(ap, s), ap[s], d,
            nin=4, tag=f"s6_{l}"))(_l, _src),
        "reads": ["of", "l0f_wo", _src], "writes": [_src]}
    STAGES[f"s7a_l{_l}"] = {
        "fn": (lambda l, s: lambda k, ap, d: emit_l0f_w1(k, ap, l, s, d))(
            _l, _src),
        "reads": [_src, "l0f_w1", "l0f_b1", "l0f_lng", "l0f_lnb"],
        "writes": ["mgl"]}
    STAGES[f"s7b_l{_l}"] = {
        "fn": (lambda l, s, t: lambda k, ap, d: emit_matproj_add(
            k, ap, ap["mgl"], ap["l0f_w2"][l], ap["l0f_b2"][l], RD(ap, s),
            ap[t], d, nin=NHC, tag=f"s7b_{l}"))(_l, _src, _dst),
        "reads": ["mgl", "l0f_w2", "l0f_b2", _src], "writes": [_dst]}

STAGES["s12_add"] = {
    "fn": lambda k, ap, d: emit_add(k, ap, ap["x1"], ap["xfa"], ap["x2"], d),
    "reads": ["x1", "xfa"], "writes": ["x2"]}
STAGES["s13_ln2"] = {
    "fn": lambda k, ap, d: emit_ln(k, ap, ap["x2"], ap["xn2"], None, d),
    "reads": ["x2"], "writes": ["xn2"]}
STAGES["s14_l1aqkv"] = {
    "fn": lambda k, ap, d: emit_l1a_qkv(k, ap, d),
    "reads": ["xn2", "l1a_wq", "l1a_wk", "l1a_wv", "l1a_bq", "l1a_bk",
              "l1a_bvrow", "l1a_wqr", "l1a_wkr"],
    "writes": ["q2aug", "k2aug", "vt2", "dbg14"]}
STAGES["s15_l1aatt"] = {
    "fn": lambda k, ap, d: emit_l1a_att(k, ap, d),
    "reads": ["q2aug", "k2aug", "vt2"], "writes": ["o2"]}
STAGES["s16_l1aout"] = {
    "fn": lambda k, ap, d: emit_matproj_add(
        k, ap, ap["o2"], ap["l1a_wo"], ap["l1a_bo"], ap["x2"], ap["out"], d,
        tag="s16"),
    "reads": ["o2", "x2", "l1a_wo", "l1a_bo"], "writes": ["out"]}
STAGES["s17_gate"] = {
    "fn": lambda k, ap, d: emit_gate(k, ap, d),
    "reads": ["xn2", "l1f_gw", "l1f_gbrow", "c_id"],
    "writes": ["m8", "mbeta"]}
STAGES["s18_moe"] = {
    "fn": lambda k, ap, d: emit_moe(k, ap, d),
    "reads": ["xn2", "m8", "mbeta", "l1f_w1t", "l1f_b1", "l1f_w2t",
              "l1f_b2m", "out"],
    "writes": ["out"]}

STAGE_ORDER = ["s0_ln1", "s1_l0aqkv", "s2_l0aatt", "s3_l0aout",
               "s4_l0", "s5_l0", "s6_l0", "s7a_l0", "s7b_l0",
               "s4_l1", "s5_l1", "s6_l1", "s7a_l1", "s7b_l1",
               "s12_add", "s13_ln2", "s14_l1aqkv", "s15_l1aatt",
               "s16_l1aout", "s17_gate", "s18_moe"]

GROUPS = [list(STAGE_ORDER)]


def _spec(name):
    if name in WEIGHTS:
        return WEIGHTS[name], F32
    if name in CONSTS:
        return CONSTS[name]
    if name in SCRATCH:
        return SCRATCH[name]
    if name in INPUTS:
        return INPUTS[name]
    if name in OUTPUTS:
        return OUTPUTS[name]
    raise KeyError(name)


def build_group(group, fused):
    nc = bass.Bass("TRN2", target_bir_lowering=False, debug=False)
    reads, writes = [], []
    for s in group:
        reads += [r for r in STAGES[s]["reads"] if r not in reads]
        writes += [w for w in STAGES[s]["writes"] if w not in writes]
    first = {}
    for s in group:
        for r in STAGES[s]["reads"]:
            first.setdefault(r, "r")
        for w in STAGES[s]["writes"]:
            first.setdefault(w, "w")
    ap = {}
    innames, outnames = [], []
    names = list(dict.fromkeys(reads + writes))
    for n in names:
        shape, dt = _spec(n)
        is_scratch = n in SCRATCH or n in OUTPUTS
        written = n in writes
        if not is_scratch or (fused and n not in OUTPUTS and n not in INPUTS):
            if not is_scratch:
                ap[n] = nc.dram_tensor(n, shape, dt, kind="ExternalInput").ap()
                innames.append(n)
            else:
                ap[n] = nc.dram_tensor(n, shape, dt, kind="Internal").ap()
        else:
            if written:
                if first.get(n) == "r" and not fused:
                    ap[n + "__in"] = nc.dram_tensor(
                        n + "__in", shape, dt, kind="ExternalInput").ap()
                    innames.append(n + "__in")
                ap[n] = nc.dram_tensor(n, shape, dt, kind="ExternalOutput").ap()
                outnames.append(n)
            else:
                ap[n] = nc.dram_tensor(n, shape, dt, kind="ExternalInput").ap()
                innames.append(n)
    if fused:
        pass
    k = K(nc)
    deps = []
    for si, s in enumerate(group):
        k.uid = f"g{si}_"
        deps = STAGES[s]["fn"](k, ap, list(deps))
    k.fence_all(k.tip())
    return nc, innames, outnames


_GROUP_CACHE = {}


def kernel(**inputs):
    w = host_weights(inputs)
    c = host_consts()
    xs = [host_core_inputs(inputs, i) for i in range(NCORE)]
    state = [dict() for _ in range(NCORE)]
    fused = len(GROUPS) == 1
    for gi, group in enumerate(GROUPS):
        key = (tuple(group), fused)
        if key not in _GROUP_CACHE:
            _GROUP_CACHE[key] = build_group(group, fused)
        nc, innames, outnames = _GROUP_CACHE[key]
        in_maps = []
        for core in range(NCORE):
            m = {}
            for n in innames:
                base = n[:-4] if n.endswith("__in") else n
                if base in w:
                    m[n] = w[base]
                elif base in c:
                    m[n] = c[base]
                elif base in INPUTS:
                    m[n] = xs[core][base]
                else:
                    arr = state[core].get(base)
                    if arr is None:
                        shape, dt = _spec(base)
                        arr = np.zeros(shape, mybir.dt.np(dt))
                    m[n] = arr
            in_maps.append(m)
        res = run_bass_kernel_spmd(nc, in_maps, core_ids=list(range(NCORE)))
        for core in range(NCORE):
            for n in outnames:
                state[core][n] = res.results[core][n]
    outs = []
    for core in range(NCORE):
        o = state[core].get("out")
        if o is None:
            o = np.zeros(OUTPUTS["out"][0], np.float32)
        outs.append(o.transpose(0, 3, 1, 2).reshape(B, T, D))
    return np.concatenate(outs, axis=0)


# version 05
# speedup vs baseline: 1.4673x; 1.0231x over previous
import numpy as np
import ml_dtypes
from contextlib import ExitStack
import concourse.bass as bass
import concourse.mybir as mybir
from concourse.bass_utils import run_bass_kernel_spmd

BF = ml_dtypes.bfloat16
F32 = mybir.dt.float32
BF16 = mybir.dt.bfloat16
AF = mybir.ActivationFunctionType
ALU = mybir.AluOpType
AX = mybir.AxisListType

NCORE = 8
B = 4
T = 2048
D = 1024
NC = 8
TT = 512
NTT = T // TT
NEG = -30000.0
BETA = -1.0e9
HIDP = 2816
NHC = 22
PH14 = 4
INNER14 = 4
DBG14 = False



class K:

    NPOOL = 8

    def __init__(self, nc):
        self.nc = nc
        self.eng = {"t": nc.tensor, "v": nc.vector, "s": nc.scalar,
                    "p": nc.gpsimd, "q": nc.sync}
        self.prog = {k: nc.alloc_semaphore("prog_" + k) for k in "tvs"}
        self.cnt = {k: 0 for k in "tvs"}
        self.dpool = {k: [nc.alloc_semaphore(f"dma_{k}{i}")
                          for i in range(self.NPOOL)] for k in "qsp"}
        self.dqi = {k: 0 for k in "qsp"}
        self.seen = {k: {} for k in "tvspq"}
        self.uid = ""

    def wait(self, ekey, deps):
        e = self.eng[ekey]
        seen = self.seen[ekey]
        for d in deps:
            if d is None:
                continue
            sem, val = d
            if val <= 0 or val <= seen.get(sem.num, 0):
                continue
            e.wait_ge(sem, val)
            seen[sem.num] = val

    def dma(self, qkey, out, in_, deps=(), **kw):
        slot = self.dqi[qkey] % self.NPOOL
        n_use = self.dqi[qkey] // self.NPOOL
        sem = self.dpool[qkey][slot]
        self.wait(qkey, list(deps) + [(sem, 16 * n_use)])
        i = self.eng[qkey].dma_start(out=out, in_=in_, **kw)
        i.then_inc(sem, 16)
        self.dqi[qkey] += 1
        return (sem, 16 * (n_use + 1))

    def do(self, ekey, method, *args, deps=(), **kw):
        self.wait(ekey, deps)
        i = getattr(self.eng[ekey], method)(*args, **kw)
        self.cnt[ekey] += 1
        i.then_inc(self.prog[ekey], 1)
        return (self.prog[ekey], self.cnt[ekey])

    def mm(self, out, lhsT, rhs, start, stop, deps=(), mark=None):
        if mark is None:
            mark = stop
        self.wait("t", deps)
        i = self.nc.tensor.matmul(out, lhsT, rhs, start=start, stop=stop)
        if mark:
            self.cnt["t"] += 1
            i.then_inc(self.prog["t"], 1)
            return (self.prog["t"], self.cnt["t"])
        return None

    def tip(self):
        toks = [(self.prog[k], self.cnt[k]) for k in "tvs"]
        for q in "qsp":
            n = self.dqi[q]
            for slot in range(self.NPOOL):
                uses = (n - slot + self.NPOOL - 1) // self.NPOOL
                if uses > 0:
                    toks.append((self.dpool[q][slot], 16 * uses))
        return toks

    def fence_all(self, deps):
        for ekey in "tvspq":
            self.wait(ekey, deps)


def cast_weights(k, dst_sb, src_ap, deps, qkey="q", stage_f32=None, tag=""):
    n = dst_sb.shape[1]
    ns = stage_f32.shape[1]
    half = ns // 2
    ld_prev = {}
    tok = None
    if len(src_ap.shape) == 3:
        C, m = src_ap.shape[1], src_ap.shape[2]
        assert C * m == n, (src_ap.shape, n)
        i = 0
        if m > half:
            for c0 in range(C):
                off = 0
                while off < m:
                    mm_ = min(half, m - off)
                    buf = i % 2
                    tl = k.dma(qkey, stage_f32[:, buf * half:buf * half + mm_],
                               src_ap[:, c0, off:off + mm_],
                               deps=list(deps) + [ld_prev.get(buf)])
                    tok = k.do("v", "tensor_copy",
                               dst_sb[:, c0 * m + off:c0 * m + off + mm_],
                               stage_f32[:, buf * half:buf * half + mm_],
                               deps=[tl])
                    ld_prev[buf] = tok
                    off += mm_
                    i += 1
            return tok
        per = max(1, half // m)
        c0 = 0
        while c0 < C:
            cc = min(per, C - c0)
            buf = i % 2
            tl = k.dma(qkey, stage_f32[:, buf * half:buf * half + cc * m]
                       .rearrange("p (c m) -> p c m", c=cc),
                       src_ap[:, c0:c0 + cc, :],
                       deps=list(deps) + [ld_prev.get(buf)])
            tok = k.do("v", "tensor_copy", dst_sb[:, c0 * m:(c0 + cc) * m],
                       stage_f32[:, buf * half:buf * half + cc * m],
                       deps=[tl])
            ld_prev[buf] = tok
            c0 += cc
            i += 1
        return tok
    off = 0
    i = 0
    while off < n:
        m = min(half, n - off)
        buf = i % 2
        tl = k.dma(qkey, stage_f32[:, buf * half:buf * half + m],
                   src_ap[:, off:off + m],
                   deps=list(deps) + [ld_prev.get(buf)])
        tok = k.do("v", "tensor_copy", dst_sb[:, off:off + m],
                   stage_f32[:, buf * half:buf * half + m],
                   deps=[tl])
        ld_prev[buf] = tok
        off += m
        i += 1
    return tok



WEIGHTS = {
    "l0a_wqa": [NC, 128, 64],
    "l0a_wqb": [64, 1024],
    "l0a_wkva": [NC, 128, 32],
    "l0a_wkvbK": [32, 256],
    "l0a_wkvbV": [32, 256],
    "l0a_wout": [NC, 128, 1024],
    "l0a_bout": [128, NC],
    "l0f_wqkv": [2, NC, 128, 1536],
    "l0f_qs": [2, 64],
    "l0f_ks": [2, 64],
    "l0f_wo": [2, 4, 128, 1024],
    "l0f_lng": [2, NC, 128],
    "l0f_lnb": [2, NC, 128],
    "l0f_w1": [2, NC, 128, 2 * HIDP],
    "l0f_b1": [2, 128, 2 * NHC],
    "l0f_w2": [2, NHC, 128, 1024],
    "l0f_b2": [2, 128, NC],
    "l1a_wq": [NC, 128, 1024],
    "l1a_wk": [NC, 128, 1024],
    "l1a_wv": [NC, 128, 1024],
    "l1a_wo": [NC, 128, 1024],
    "l1a_bq": [128, NC],
    "l1a_bk": [128, NC],
    "l1a_bo": [128, NC],
    "l1a_bvrow": [1, 1024],
    "l1a_wqr": [64, 16],
    "l1a_wkr": [64, 16],
    "l1f_gw": [NC, 128, 8],
    "l1f_gbrow": [1, 8],
    "l1f_w1t": [8, NC, 128, 4096],
    "l1f_b1": [8, 128, 32],
    "l1f_w2t": [8, 32, 128, 1024],
    "l1f_b2m": [NC, 8, 128],
}

CONSTS = {
    "c_id": ([128, 128], BF16),
    "c_p128": ([128, 128], BF16),
    "c_prot": ([128, 128], BF16),
    "c_cd": ([128, T], F32),
    "c_sd": ([128, T], F32),
    "c_xqc": ([128, T], F32),
    "c_xqs": ([128, T], F32),
    "c_xkc": ([128, T], F32),
    "c_xks": ([128, T], F32),
    "c_cm": ([4, 128, TT], BF16),
    "c_mtm": ([4, 128, 1024], BF16),
    "c_mkm": ([8, 128, TT], BF16),
}

SCRATCH = {
    "xn1": ([B, NC, 128, T], BF16),
    "xfa": ([B, NC, 128, T], F32),
    "xfb": ([B, NC, 128, T], F32),
    "qaug": ([B, 16, 65, T], BF16),
    "kaug": ([B, 4, 65, T], BF16),
    "vt0": ([B, 4, T, 64], BF16),
    "o0": ([B, NC, 128, T], BF16),
    "x1": ([B, NC, 128, T], F32),
    "qf": ([B, 4, 128, T], BF16),
    "kf": ([B, 4, 128, T], BF16),
    "vt1": ([B, 8, T, 64], BF16),
    "of": ([B, 4, 128, T], BF16),
    "mgl": ([B, NHC, 128, T], BF16),
    "x2": ([B, NC, 128, T], F32),
    "xn2": ([B, NC, 128, T], BF16),
    "q2aug": ([B, 16, 17, T], BF16),
    "k2aug": ([B, 16, 17, T], BF16),
    "vt2": ([B, 16, T, 64], BF16),
    "o2": ([B, NC, 128, T], BF16),
    "mbeta": ([8, B * T], F32),
    "dbg14": ([4, 1024], F32),
    "m8": ([8, B * T], F32),
}

INPUTS = {"xin": ([B, NC, 128, T], F32)}
OUTPUTS = {"out": ([B, NC, 128, T], F32)}



def _fm(x):
    b, t, d = x.shape
    return np.ascontiguousarray(
        x.reshape(b, t, d // 128, 128).transpose(0, 2, 3, 1))


def host_weights(inp):
    w = {}
    w["l0a_wqa"] = inp["L0_attn_Wqa"].reshape(NC, 128, 64)
    w["l0a_wqb"] = inp["L0_attn_Wqb"]
    w["l0a_wkva"] = inp["L0_attn_Wkva"].reshape(NC, 128, 32)
    kvb = np.asarray(inp["L0_attn_Wkvb"]).reshape(32, 4, 64, 2)
    w["l0a_wkvbK"] = kvb[..., 0].reshape(32, 256)
    w["l0a_wkvbV"] = kvb[..., 1].reshape(32, 256)
    w["l0a_wout"] = inp["L0_attn_Wout"].reshape(NC, 128, 1024)
    w["l0a_bout"] = np.ascontiguousarray(np.asarray(inp["L0_attn_bout"]).reshape(NC, 128).T)
    w["l0f_wqkv"] = inp["L0_ffn_wqkv"].reshape(2, NC, 128, 1536)
    w["l0f_qs"] = inp["L0_ffn_q_scale"]
    w["l0f_ks"] = inp["L0_ffn_k_scale"]
    w["l0f_wo"] = inp["L0_ffn_wo"].reshape(2, 4, 128, 1024)
    w["l0f_lng"] = inp["L0_ffn_ln_g"].reshape(2, NC, 128)
    w["l0f_lnb"] = inp["L0_ffn_ln_b"].reshape(2, NC, 128)
    w1 = np.asarray(inp["L0_ffn_w1"])
    w1p = np.zeros((2, 1024, 2 * HIDP), np.float32)
    w1p[:, :, :2730] = w1[:, :, :2730]
    w1p[:, :, HIDP:HIDP + 2730] = w1[:, :, 2730:]
    w["l0f_w1"] = w1p.reshape(2, NC, 128, 2 * HIDP)
    b1 = np.asarray(inp["L0_ffn_b1"])
    b1p = np.zeros((2, 2 * HIDP), np.float32)
    b1p[:, :2730] = b1[:, :2730]
    b1p[:, HIDP:HIDP + 2730] = b1[:, 2730:]
    w["l0f_b1"] = np.ascontiguousarray(b1p.reshape(2, 2 * NHC, 128).transpose(0, 2, 1))
    w2 = np.asarray(inp["L0_ffn_w2"])
    w2p = np.zeros((2, HIDP, 1024), np.float32)
    w2p[:, :2730] = w2
    w["l0f_w2"] = w2p.reshape(2, NHC, 128, 1024)
    w["l0f_b2"] = np.ascontiguousarray(np.asarray(inp["L0_ffn_b2"]).reshape(2, NC, 128).transpose(0, 2, 1))
    for nm, key in (("wq", "WQ"), ("wk", "WK"), ("wv", "WV"), ("wo", "WO")):
        w["l1a_" + nm] = np.asarray(inp["L1_attn_" + key]).reshape(NC, 128, 1024)
    for nm, key in (("bq", "bQ"), ("bk", "bK"), ("bo", "bO")):
        w["l1a_" + nm] = np.ascontiguousarray(
            np.asarray(inp["L1_attn_" + key]).reshape(NC, 128).T)
    w["l1a_bvrow"] = np.asarray(inp["L1_attn_bV"]).reshape(1, 1024)
    w["l1a_wqr"] = inp["L1_attn_WQr"]
    w["l1a_wkr"] = inp["L1_attn_WKr"]
    w["l1f_gw"] = np.ascontiguousarray(np.asarray(inp["L1_ffn_gate_w"]).T
                                       ).reshape(NC, 128, 8)
    w["l1f_gbrow"] = np.asarray(inp["L1_ffn_gate_b"]).reshape(1, 8)
    w["l1f_w1t"] = np.ascontiguousarray(
        np.asarray(inp["L1_ffn_w1"]).transpose(0, 2, 1)).reshape(8, NC, 128, 4096)
    w["l1f_b1"] = np.ascontiguousarray(np.asarray(inp["L1_ffn_b1"]).reshape(8, 32, 128).transpose(0, 2, 1))
    w["l1f_w2t"] = np.ascontiguousarray(
        np.asarray(inp["L1_ffn_w2"]).transpose(0, 2, 1)).reshape(8, 32, 128, 1024)
    w["l1f_b2m"] = np.ascontiguousarray(np.asarray(inp["L1_ffn_b2"]).reshape(8, 8, 128).transpose(1, 0, 2))
    return {k: np.ascontiguousarray(np.asarray(v), dtype=np.float32)
            for k, v in w.items()}


def host_consts():
    c = {}
    c["c_id"] = np.eye(128, dtype=np.float32).astype(BF)
    p = np.zeros((128, 128), np.float32)
    for i in range(64):
        p[2 * i, 2 * i + 1] = -1.0
        p[2 * i + 1, 2 * i] = 1.0
    c["c_p128"] = p.T.astype(BF)
    pr = np.zeros((128, 128), np.float32)
    for h in range(2):
        o = 64 * h
        for i in range(32):
            pr[o + i, o + 32 + i] = -1.0
            pr[o + 32 + i, o + i] = 1.0
    c["c_prot"] = pr.T.astype(BF)
    d = 64
    inv = 1.0 / 10000.0 ** (np.arange(0, d, 2, dtype=np.float64) / d)
    t = np.arange(T, dtype=np.float64)
    fr = inv[:, None] * t[None, :]
    cd = np.repeat(np.cos(fr), 2, axis=0)
    sd = np.repeat(np.sin(fr), 2, axis=0)
    c["c_cd"] = np.tile(cd, (2, 1)).astype(np.float32)
    c["c_sd"] = np.tile(sd, (2, 1)).astype(np.float32)
    half = np.arange(0, d, 2, dtype=np.float64)
    invf = 1.0 / 10000.0 ** (half / d)
    fr2 = invf[:, None] * t[None, :]
    fr2 = np.concatenate([fr2, fr2], axis=0)
    base = (half + 0.4 * d) / (1.4 * d)
    power = (t - T // 2) / 256.0
    scale = base[:, None] ** power[None, :]
    scale = np.concatenate([scale, scale], axis=0)
    cos, sin = np.cos(fr2), np.sin(fr2)
    c["c_xqc"] = np.tile(cos * scale, (2, 1)).astype(np.float32)
    c["c_xqs"] = np.tile(sin * scale, (2, 1)).astype(np.float32)
    c["c_xkc"] = np.tile(cos / scale, (2, 1)).astype(np.float32)
    c["c_xks"] = np.tile(sin / scale, (2, 1)).astype(np.float32)
    kk = np.arange(128)[:, None]
    qq = np.arange(TT)[None, :]
    cm = np.zeros((4, 128, TT), np.float32)
    for m in range(4):
        cm[m] = np.where(m * 128 + kk <= qq, 0.0, NEG)
    c["c_cm"] = cm.astype(BF)
    jj = np.arange(1024)[None, :]
    mtm = np.zeros((4, 128, 1024), np.float32)
    for qc in range(4):
        qrel = qc * 128 + np.arange(128)[:, None]
        mtm[qc] = np.where((jj >= qrel) & (jj <= qrel + 512), 0.0, NEG)
    c["c_mtm"] = mtm.astype(BF)
    mkm = np.zeros((8, 128, TT), np.float32)
    for kt in range(8):
        j = kt * 128 + np.arange(128)[:, None]
        mkm[kt] = np.where((j >= qq) & (j <= qq + 512), 0.0, NEG)
    c["c_mkm"] = mkm.astype(BF)
    return c


def host_core_inputs(inp, core):
    x = np.asarray(inp["L0_attn_x"], np.float32)[4 * core:4 * core + 4]
    return {"xin": _fm(x)}



def emit_ln(k, ap, src, dst_bf, dst_f32, deps):
    nc = k.nc
    with ExitStack() as ctx:
        xt = ctx.enter_context(nc.sbuf_tensor(k.uid + "ln_xt", [128, 2, NC * TT], F32))
        xb = ctx.enter_context(nc.sbuf_tensor(k.uid + "ln_xb", [128, 2, NC * TT], BF16))
        sq = ctx.enter_context(nc.sbuf_tensor(k.uid + "ln_sq", [128, 2, NC * TT], BF16))
        ones = ctx.enter_context(nc.sbuf_tensor(k.uid + "ln_o", [128, 1], BF16))
        o1 = ctx.enter_context(nc.sbuf_tensor(k.uid + "ln_o1", [1, 128], F32))
        rows = ctx.enter_context(nc.sbuf_tensor(k.uid + "ln_r", [1, 2, 6, TT], F32))
        xnf = ctx.enter_context(nc.sbuf_tensor(k.uid + "ln_xf", [128, 2, NC * TT], F32))
        xnb = ctx.enter_context(nc.sbuf_tensor(k.uid + "ln_xB", [128, 2, NC * TT], BF16))
        eps = ctx.enter_context(nc.sbuf_tensor(k.uid + "ln_eps", [1, 1], F32))
        ps = ctx.enter_context(nc.psum_tensor(k.uid + "ln_ps", [128, 8, TT], F32))
        t0tok = k.do("v", "memset", ones[:], 1.0, deps=deps)
        t1tok = k.do("v", "memset", o1[:], 1.0, deps=deps)
        tep = k.do("v", "memset", eps[:], 1e-5, deps=deps)
        joins = {}
        for i in range(B * NTT):
            b, tt = divmod(i, NTT)
            u = i % 2
            war = joins.get(i - 2, list(deps) + [t0tok, t1tok])
            t0 = tt * TT
            X = xt[:, u, :]
            XB = xb[:, u, :]
            SQ = sq[:, u, :]
            XF = xnf[:, u, :]
            XO = xnb[:, u, :]
            R = rows[:, u, :, :]
            s0, s1, pA, pB = 4 * u, 4 * u + 1, 4 * u + 2, 4 * u + 3
            tl = k.dma("q", X.rearrange("p (c t) -> p c t", c=NC),
                       src[b, :, :, t0:t0 + TT].rearrange("c p t -> p c t"),
                       deps=war)
            tc = k.do("v", "tensor_copy", XB, X, deps=war + [tl])
            tq = k.do("s", "activation", SQ, XB, AF.Square, deps=war + [tc])
            for c in range(NC):
                ts0 = k.mm(ps[0:1, s0, :], ones[:], XB[:, c * TT:(c + 1) * TT],
                           start=(c == 0), stop=(c == NC - 1), deps=war + [tc])
            for c in range(NC):
                ts1 = k.mm(ps[0:1, s1, :], ones[:], SQ[:, c * TT:(c + 1) * TT],
                           start=(c == 0), stop=(c == NC - 1), deps=[tq])
            tm = k.do("v", "tensor_scalar_mul", R[:, 0, :], ps[0:1, s0, :],
                      1.0 / D, deps=war + [ts0])
            te = k.do("v", "tensor_scalar_mul", R[:, 1, :], ps[0:1, s1, :],
                      1.0 / D, deps=[ts1])
            tmm = k.do("v", "tensor_mul", R[:, 2, :], R[:, 0, :], R[:, 0, :],
                       deps=[tm])
            tv = k.do("v", "tensor_sub", R[:, 3, :], R[:, 1, :], R[:, 2, :],
                      deps=[te, tmm])
            tsd = k.do("s", "activation", R[:, 4, :], R[:, 3, :], AF.Sqrt,
                       bias=eps[:], deps=[tv, tep])
            tr = k.do("v", "reciprocal", R[:, 5, :], R[:, 4, :], deps=[tsd])
            tnm = k.do("v", "tensor_mul", R[:, 2, :], R[:, 0, :], R[:, 5, :],
                       deps=[tr])
            tnm2 = k.do("v", "tensor_scalar_mul", R[:, 2, :], R[:, 2, :], -1.0,
                        deps=[tnm])
            ta = k.mm(ps[:, pA, :], o1[:], R[:, 5, :], True, True,
                      deps=war + [tr])
            tb = k.mm(ps[:, pB, :], o1[:], R[:, 2, :], True, True, deps=[tnm2])
            for c in range(NC):
                sl = slice(c * TT, (c + 1) * TT)
                tt1 = k.do("v", "tensor_mul", XF[:, sl], X[:, sl], ps[:, pA, :],
                           deps=[ta])
                tt2 = k.do("v", "tensor_add", XF[:, sl], XF[:, sl], ps[:, pB, :],
                           deps=[tb, tt1])
                tt3 = k.do("v", "tensor_copy", XO[:, sl], XF[:, sl], deps=[tt2])
                k.dma("p", dst_bf[b, c, :, t0:t0 + TT], XO[:, sl], deps=[tt3])
                if dst_f32 is not None:
                    k.dma("p", dst_f32[b, c, :, t0:t0 + TT], XF[:, sl],
                          deps=[tt3])
            joins[i] = k.tip()
    return k.tip()


def emit_add(k, ap, a, bsrc, dst, deps):
    nc = k.nc
    with ExitStack() as ctx:
        xa = ctx.enter_context(nc.sbuf_tensor(k.uid + "ad_a", [128, 2, NC * TT], F32))
        xb = ctx.enter_context(nc.sbuf_tensor(k.uid + "ad_b", [128, 2, NC * TT], F32))
        xo = ctx.enter_context(nc.sbuf_tensor(k.uid + "ad_o", [128, 2, NC * TT], F32))
        joins = {}
        for i in range(B * NTT):
            b, tt = divmod(i, NTT)
            u = i % 2
            war = joins.get(i - 2, list(deps))
            t0 = tt * TT
            ta = k.dma("q", xa[:, u, :].rearrange("p (c t) -> p c t", c=NC),
                       a[b, :, :, t0:t0 + TT].rearrange("c p t -> p c t"),
                       deps=war)
            tb = k.dma("q", xb[:, u, :].rearrange("p (c t) -> p c t", c=NC),
                       bsrc[b, :, :, t0:t0 + TT].rearrange("c p t -> p c t"),
                       deps=war)
            to = k.do("v", "tensor_add", xo[:, u, :], xa[:, u, :], xb[:, u, :],
                      deps=war + [ta, tb])
            k.dma("p", dst[b, :, :, t0:t0 + TT].rearrange("c p t -> p c t"),
                  xo[:, u, :].rearrange("p (c t) -> p c t", c=NC), deps=[to])
            joins[i] = k.tip()
    return k.tip()


def _fmslice(t4, b, t0, n=TT):
    return t4[b, :, :, t0:t0 + n].rearrange("c p t -> p c t")


def emit_l0a_qkv(k, ap, deps):
    nc = k.nc
    xn1, qaug, kaug, vt0 = ap["xn1"], ap["qaug"], ap["kaug"], ap["vt0"]
    with ExitStack() as ctx:
        wqa = ctx.enter_context(nc.sbuf_tensor(k.uid + "aq_wqa", [128, NC * 64], BF16))
        wqb = ctx.enter_context(nc.sbuf_tensor(k.uid + "aq_wqb", [64, 1024], BF16))
        wka = ctx.enter_context(nc.sbuf_tensor(k.uid + "aq_wka", [128, NC * 32], BF16))
        wkbK = ctx.enter_context(nc.sbuf_tensor(k.uid + "aq_wkbK", [32, 256], BF16))
        wkbV = ctx.enter_context(nc.sbuf_tensor(k.uid + "aq_wkbV", [32, 256], BF16))
        stg = ctx.enter_context(nc.sbuf_tensor(k.uid + "aq_stg", [128, 2048], F32))
        cd = ctx.enter_context(nc.sbuf_tensor(k.uid + "aq_cd", [128, T], F32))
        sd = ctx.enter_context(nc.sbuf_tensor(k.uid + "aq_sd", [128, T], F32))
        pP = ctx.enter_context(nc.sbuf_tensor(k.uid + "aq_p", [128, 128], BF16))
        hh2 = ctx.enter_context(nc.sbuf_tensor(k.uid + "aq_hh2", [128, 2], BF16))
        onesr = ctx.enter_context(nc.sbuf_tensor(k.uid + "aq_or", [1, T], BF16))
        mk02 = ctx.enter_context(nc.sbuf_tensor(k.uid + "aq_mk", [2, 4], F32))
        mkst = ctx.enter_context(nc.sbuf_tensor(k.uid + "aq_mks", [2, 14], F32))
        xnb = ctx.enter_context(nc.sbuf_tensor(k.uid + "aq_xnb", [128, 2, NC * TT], BF16))
        t1 = ctx.enter_context(nc.sbuf_tensor(k.uid + "aq_t1", [64, T], BF16))
        t2 = ctx.enter_context(nc.sbuf_tensor(k.uid + "aq_t2", [32, T], BF16))
        rb = ctx.enter_context(nc.sbuf_tensor(k.uid + "aq_rb", [128, 2, TT], BF16))
        sqv = ctx.enter_context(nc.sbuf_tensor(k.uid + "aq_sq", [128, 2, TT], BF16))
        ta_f = ctx.enter_context(nc.sbuf_tensor(k.uid + "aq_taf", [128, 2, TT], F32))
        tb_f = ctx.enter_context(nc.sbuf_tensor(k.uid + "aq_tbf", [128, 2, TT], F32))
        ro = ctx.enter_context(nc.sbuf_tensor(k.uid + "aq_ro", [128, 2, TT], BF16))
        nr = ctx.enter_context(nc.sbuf_tensor(k.uid + "aq_nr", [2, 2, TT], F32))
        crw = ctx.enter_context(nc.sbuf_tensor(k.uid + "aq_crw", [2, 2, TT], BF16))
        vb = ctx.enter_context(nc.sbuf_tensor(k.uid + "aq_vb", [128, 2, 256], BF16))
        ps = ctx.enter_context(nc.psum_tensor(k.uid + "aq_ps", [128, 8, TT], F32))

        tw = cast_weights(k, wqa[:], ap["l0a_wqa"].rearrange("c p n -> p c n"),
                          deps, stage_f32=stg[:])
        tw = cast_weights(k, wqb[:], ap["l0a_wqb"][:, :], [tw],
                          stage_f32=stg[0:64, :])
        tw = cast_weights(k, wka[:], ap["l0a_wkva"].rearrange("c p n -> p c n"),
                          [tw], stage_f32=stg[:])
        tw = cast_weights(k, wkbK[:], ap["l0a_wkvbK"][:, :], [tw],
                          stage_f32=stg[0:32, :])
        tw = cast_weights(k, wkbV[:], ap["l0a_wkvbV"][:, :], [tw],
                          stage_f32=stg[0:32, :])
        tcd = k.dma("q", cd[:], ap["c_cd"][:, :], deps=deps)
        tsd = k.dma("q", sd[:], ap["c_sd"][:, :], deps=deps)
        tpp = k.dma("q", pP[:], ap["c_p128"][:, :], deps=deps)
        th0 = k.do("v", "memset", hh2[:], 0.0, deps=deps)
        th1 = k.do("v", "memset", hh2[0:64, 0:1], 1.0, deps=[th0])
        th2 = k.do("v", "memset", hh2[64:128, 1:2], 1.0, deps=[th1])
        tor = k.do("v", "memset", onesr[:], 1.0, deps=deps)
        pre = [tw, tcd, tsd, tpp, th2, tor]

        def rope_block(bank, u, tt, tbl_c, tbl_s, dep_mm, war):
            tqb = k.do("s", "activation", rb[:, u, :], ps[:, bank, :],
                       AF.Copy, deps=war + [dep_mm])
            tp = k.mm(ps[:, 6 + u, :], pP[:], rb[:, u, :], True, True,
                      deps=war + [tqb, tpp])
            t_a = k.do("v", "tensor_mul", ta_f[:, u, :], ps[:, 6 + u, :],
                       tbl_s[:, tt * TT:(tt + 1) * TT],
                       deps=war + [tp])
            t_b = k.do("v", "tensor_mul", tb_f[:, u, :], ps[:, bank, :],
                       tbl_c[:, tt * TT:(tt + 1) * TT],
                       deps=war + [dep_mm, tqb])
            t_o = k.do("v", "tensor_add", ro[:, u, :], ta_f[:, u, :],
                       tb_f[:, u, :], deps=[t_a, t_b])
            return t_o

        bj = list(deps) + pre
        for b in range(B):
            ph1 = []
            for tt in range(NTT):
                u = tt % 2
                war = ph1[-2][2] if tt >= 2 else bj
                tl = k.dma("q", xnb[:, u, :].rearrange("p (c t) -> p c t", c=NC),
                           _fmslice(xn1, b, tt * TT), deps=war)
                for c in range(NC):
                    m1 = k.mm(ps[0:64, u, :], wqa[:, c * 64:(c + 1) * 64],
                              xnb[:, u, c * TT:(c + 1) * TT],
                              start=(c == 0), stop=(c == NC - 1),
                              deps=war + [tl, tw])
                for c in range(NC):
                    m2 = k.mm(ps[0:32, 2 + u, :], wka[:, c * 32:(c + 1) * 32],
                              xnb[:, u, c * TT:(c + 1) * TT],
                              start=(c == 0), stop=(c == NC - 1), deps=[tl])
                c1 = k.do("v", "tensor_copy", t1[:, tt * TT:(tt + 1) * TT],
                          ps[0:64, u, :], deps=war + [m1])
                c2 = k.do("v", "tensor_copy", t2[:, tt * TT:(tt + 1) * TT],
                          ps[0:32, 2 + u, :], deps=[m2])
                ph1.append((c1, c2, k.tip()))
            tph1 = [x for tpl in ph1 for x in tpl[:2]]
            ph2 = []
            for tc in range(T // 128):
                u = tc % 2
                war = ph2[-2] if tc >= 2 else []
                tv = k.mm(ps[:, u, 0:256], t2[:, tc * 128:(tc + 1) * 128],
                          wkbV[:], True, True, deps=tph1 + war)
                tvc = k.do("v", "tensor_copy", vb[:, u, :], ps[:, u, 0:256],
                           deps=war + [tv])
                dt = [k.dma("p", vt0[b, kvh, tc * 128:(tc + 1) * 128, :],
                            vb[:, u, kvh * 64:(kvh + 1) * 64], deps=[tvc])
                      for kvh in range(4)]
                ph2.append([tvc] + dt)
            ph2end = k.tip()
            mk_toks = []
            ph3 = []
            for j2 in range(2):
                for tt in range(NTT):
                    i3 = j2 * NTT + tt
                    u = i3 % 2
                    war = ph3[-2] if i3 >= 2 else ph2end
                    tk = k.mm(ps[:, 2 + u, :], wkbK[:, j2 * 128:(j2 + 1) * 128],
                              t2[:, tt * TT:(tt + 1) * TT], True, True,
                              deps=tph1 + war)
                    tsq = k.do("s", "activation", sqv[:, u, :], ps[:, 2 + u, :],
                               AF.Square, deps=war + [tk])
                    tn = k.mm(ps[0:2, 4 + u, :], hh2[:], sqv[:, u, :], True,
                              True, deps=war + [tsq, th2])
                    tmx = k.do("v", "reduce_max", mkst[:, i3:i3 + 1],
                               ps[0:2, 4 + u, :], axis=AX.X, deps=war + [tn])
                    mk_toks.append(tmx)
                    t_o = rope_block(2 + u, u, tt, cd, sd, tk, war)
                    d1 = k.dma("p", kaug[b, 2 * j2, 0:64, tt * TT:(tt + 1) * TT],
                               ro[0:64, u, :], deps=[t_o])
                    d2 = k.dma("p", kaug[b, 2 * j2 + 1, 0:64,
                                    tt * TT:(tt + 1) * TT],
                               ro[64:128, u, :], deps=[t_o])
                    ph3.append([tmx, t_o, d1, d2])
            for kvh in range(4):
                k.dma("p", kaug[b, kvh, 64:65, :], onesr[:], deps=[tor])
            mk_dma = []
            for j2 in range(2):
                tmk = k.do("v", "reduce_max", mkst[:, 8 + j2:9 + j2],
                           mkst[:, j2 * NTT:(j2 + 1) * NTT], axis=AX.X,
                           deps=mk_toks)
                tms = k.do("s", "activation", mkst[:, 10 + j2:11 + j2],
                           mkst[:, 8 + j2:9 + j2], AF.Sqrt, deps=[tmk])
                tng = k.do("v", "tensor_scalar_mul", mkst[:, 12 + j2:13 + j2],
                           mkst[:, 10 + j2:11 + j2], -1.0, deps=[tms])
                for r in range(2):
                    for rr in range(2):
                        mk_dma.append(k.dma(
                            "p", mk02[rr:rr + 1, 2 * j2 + r:2 * j2 + r + 1],
                            mkst[r:r + 1, 12 + j2:13 + j2], deps=[tng]))
            ph3end = k.tip()
            ph4 = []
            for j in range(NC):
                kvh = j // 2
                for tt in range(NTT):
                    i4 = j * NTT + tt
                    u = i4 % 2
                    war = ph4[-2] if i4 >= 2 else ph3end
                    tq = k.mm(ps[:, 2 + u, :], wqb[:, j * 128:(j + 1) * 128],
                              t1[:, tt * TT:(tt + 1) * TT], True, True,
                              deps=tph1 + war)
                    tsq = k.do("s", "activation", sqv[:, u, :], ps[:, 2 + u, :],
                               AF.Square, deps=war + [tq])
                    tn = k.mm(ps[0:2, 4 + u, :], hh2[:], sqv[:, u, :], True,
                              True, deps=war + [tsq])
                    tsr = k.do("s", "activation", nr[:, u, :], ps[0:2, 4 + u, :],
                               AF.Sqrt, deps=war + [tn])
                    tcr = k.do("v", "tensor_scalar_mul", crw[:, u, :],
                               nr[:, u, :], mk02[0:2, kvh:kvh + 1],
                               deps=war + [tsr] + mk_dma)
                    e1 = k.dma("p", qaug[b, 2 * j, 64:65, tt * TT:(tt + 1) * TT],
                               crw[0:1, u, :], deps=[tcr])
                    e2 = k.dma("p", qaug[b, 2 * j + 1, 64:65,
                                    tt * TT:(tt + 1) * TT],
                               crw[1:2, u, :], deps=[tcr])
                    t_o = rope_block(2 + u, u, tt, cd, sd, tq, war)
                    e3 = k.dma("p", qaug[b, 2 * j, 0:64, tt * TT:(tt + 1) * TT],
                               ro[0:64, u, :], deps=[t_o])
                    e4 = k.dma("p", qaug[b, 2 * j + 1, 0:64,
                                    tt * TT:(tt + 1) * TT],
                               ro[64:128, u, :], deps=[t_o])
                    ph4.append([tcr, t_o, e1, e2, e3, e4])
            bj = k.tip()
    return k.tip()


def emit_l0a_att(k, ap, deps):
    nc = k.nc
    qaug, kaug, vt0, o0 = ap["qaug"], ap["kaug"], ap["vt0"], ap["o0"]
    with ExitStack() as ctx:
        ka = ctx.enter_context(nc.sbuf_tensor(k.uid + "aa_ka", [65, 2, T], BF16))
        va = ctx.enter_context(nc.sbuf_tensor(k.uid + "aa_va", [128, 2, 16 * 65], BF16))
        qa = ctx.enter_context(nc.sbuf_tensor(k.uid + "aa_qa", [65, 2, TT], BF16))
        cmm = ctx.enter_context(nc.sbuf_tensor(k.uid + "aa_cm", [128, 4 * TT], BF16))
        ex = ctx.enter_context(nc.sbuf_tensor(k.uid + "aa_ex", [128, 4, TT], BF16))
        exf = ctx.enter_context(nc.sbuf_tensor(k.uid + "aa_exf", [128, 4, TT], F32))
        rec = ctx.enter_context(nc.sbuf_tensor(k.uid + "aa_rec", [1, 2, TT], F32))
        osb = ctx.enter_context(nc.sbuf_tensor(k.uid + "aa_osb", [64, 2, TT], F32))
        onb = ctx.enter_context(nc.sbuf_tensor(k.uid + "aa_onb", [64, 2, TT], BF16))
        o1f = ctx.enter_context(nc.sbuf_tensor(k.uid + "aa_o1f", [1, 64], F32))
        ps = ctx.enter_context(nc.psum_tensor(k.uid + "aa_ps", [128, 8, TT], F32))
        tcm = k.dma("q", cmm[:].rearrange("p (m t) -> p m t", m=4),
                    ap["c_cm"].rearrange("m p t -> p m t"), deps=deps)
        tmv = k.do("v", "memset", va[:], 1.0, deps=deps)
        to1 = k.do("v", "memset", o1f[:], 1.0, deps=deps)
        pre = [tcm, tmv, to1]
        ldj = {}
        epi = {}
        exd = {}
        hb_join = list(deps) + pre
        ih = 0
        for b in range(B):
            for kvh in range(4):
                uh = ih % 2
                war_h = ldj.get(ih - 2, hb_join)
                tka = k.dma("q", ka[:, uh, :], kaug[b, kvh, :, :], deps=war_h)
                tva = [k.dma("q", va[:, uh, kt * 65:kt * 65 + 64],
                             vt0[b, kvh, kt * 128:(kt + 1) * 128, :],
                             deps=war_h) for kt in range(16)]
                consumed = []
                qaload = {}

                def loadq(gq):
                    hh_, qt_ = divmod(gq, NTT)
                    if hh_ >= 4:
                        return
                    pv_ = epi.get(ih * 16 + gq - 2, [])
                    qaload[gq] = k.dma(
                        "q", qa[:, gq % 2, :],
                        qaug[b, kvh * 4 + hh_, :, qt_ * TT:(qt_ + 1) * TT],
                        deps=list(pv_) + (hb_join if ih == 0 and gq < 2
                                          else []))
                loadq(0)
                loadq(1)
                pend = None

                def flush_epi():
                    nonlocal pend
                    if pend is None:
                        return
                    (gq_, h_, qt_, uq_, tav_, pv_) = pend
                    pend = None
                    tln = k.do("s", "activation", rec[:, uq_, :],
                               ps[64:65, 4 + uq_, :], AF.Ln,
                               deps=[tav_] + pv_)
                    trc = k.do("s", "activation", rec[:, uq_, :],
                               rec[:, uq_, :], AF.Exp, scale=-1.0,
                               deps=[tln])
                    tbc = k.mm(ps[0:64, 6 + uq_, :], o1f[:], rec[:, uq_, :],
                               True, True, deps=[trc, to1] + pv_)
                    tcp = k.do("v", "tensor_copy", osb[:, uq_, :],
                               ps[0:64, 4 + uq_, :], deps=[tav_, trc] + pv_)
                    tmu = k.do("v", "tensor_mul", onb[:, uq_, :],
                               osb[:, uq_, :], ps[0:64, 6 + uq_, :],
                               deps=[tbc, tcp])
                    tst = k.dma("p", o0[b, h_ // 2,
                                     (h_ % 2) * 64:(h_ % 2) * 64 + 64,
                                     qt_ * TT:(qt_ + 1) * TT],
                                onb[:, uq_, :], deps=[tmu])
                    epi[ih * 16 + gq_] = [tav_, tmu, tst, trc, tcp]
                    loadq(gq_ + 2)

                for hh in range(4):
                    h = kvh * 4 + hh
                    for qt in range(NTT):
                        uq = qt % 2
                        gq = hh * 4 + qt
                        pv = epi.get((ih * 16 + gq) - 2, [])
                        tqa = qaload[gq]
                        nkt = 4 * qt + 4
                        emitted_first = [False]
                        tav = None
                        SK = 2
                        texs = {}
                        for step in range(nkt + SK):
                            if step < nkt:
                                kt = step
                                u4 = kt % 4
                                pe = exd.get(u4)
                                tsc = k.mm(ps[:, u4, :],
                                           ka[:, uh, kt * 128:(kt + 1) * 128],
                                           qa[:, uq, :], True, True,
                                           deps=[tka, tqa] +
                                           ([pe[0]] if pe else list(pv)))
                                dmask = kt - 4 * qt
                                if dmask >= 0:
                                    tsc = k.do("v", "tensor_add",
                                               ps[:, u4, :], ps[:, u4, :],
                                               cmm[:, dmask * TT:
                                                   (dmask + 1) * TT],
                                               deps=[tsc, tcm])
                                tex = k.do("s", "activation", ex[:, u4, :],
                                           ps[:, u4, :], AF.Exp,
                                           scale=0.125,
                                           deps=[tsc] +
                                           ([pe[1]] if pe else []))
                                texs[kt] = tex
                                exd[u4] = (tex, None, None)
                                if kt == 1:
                                    flush_epi()
                            if step >= SK:
                                kt = step - SK
                                u4 = kt % 4
                                tav = k.mm(ps[0:65, 4 + uq, :],
                                           va[:, uh, kt * 65:kt * 65 + 65],
                                           ex[:, u4, :], start=(kt == 0),
                                           stop=(kt == nkt - 1),
                                           deps=[texs[kt], tmv, tva[kt]] +
                                           (list(pv) if kt == 0 else []),
                                           mark=True)
                                exd[u4] = (exd[u4][0], tav, texs[kt])
                        pend = (gq, h, qt, uq, tav, pv)
                        consumed += [tav]
                        if gq == 15:
                            flush_epi()
                ldj[ih] = [tav, k.tip()[0]]
                ih += 1
    return k.tip()


def emit_l1a_att(k, ap, deps):
    nc = k.nc
    q2aug, k2aug, vt2, o2 = ap["q2aug"], ap["k2aug"], ap["vt2"], ap["o2"]
    with ExitStack() as ctx:
        ka = ctx.enter_context(nc.sbuf_tensor(k.uid + "la_ka", [17, 2, T], BF16))
        va = ctx.enter_context(nc.sbuf_tensor(k.uid + "la_va", [128, 2, 16 * 65], BF16))
        qa = ctx.enter_context(nc.sbuf_tensor(k.uid + "la_qa", [17, 2, TT], BF16))
        ex = ctx.enter_context(nc.sbuf_tensor(k.uid + "la_ex", [128, 4, TT], BF16))
        rec = ctx.enter_context(nc.sbuf_tensor(k.uid + "la_rec", [1, 2, TT], F32))
        osb = ctx.enter_context(nc.sbuf_tensor(k.uid + "la_osb", [64, 2, TT], F32))
        onb = ctx.enter_context(nc.sbuf_tensor(k.uid + "la_onb", [64, 2, TT], BF16))
        o1f = ctx.enter_context(nc.sbuf_tensor(k.uid + "la_o1f", [1, 64], F32))
        ps = ctx.enter_context(nc.psum_tensor(k.uid + "la_ps", [128, 8, TT], F32))
        tmv = k.do("v", "memset", va[:], 1.0, deps=deps)
        to1 = k.do("v", "memset", o1f[:], 1.0, deps=deps)
        pre = [tmv, to1]
        ldj = {}
        epi = {}
        exd = {}
        hb_join = list(deps) + pre
        ih = 0
        for b in range(B):
            for h in range(16):
                uh = ih % 2
                war_h = ldj.get(ih - 2, hb_join)
                tka = k.dma("q", ka[:, uh, :], k2aug[b, h, :, :], deps=war_h)
                tva = [k.dma("q", va[:, uh, kt * 65:kt * 65 + 64],
                             vt2[b, h, kt * 128:(kt + 1) * 128, :],
                             deps=war_h) for kt in range(16)]
                qaload = {}

                def loadq(qt_):
                    if qt_ >= NTT:
                        return
                    pv_ = epi.get((ih * 4 + qt_) - 2, [])
                    qaload[qt_] = k.dma(
                        "q", qa[:, qt_ % 2, :],
                        q2aug[b, h, :, qt_ * TT:(qt_ + 1) * TT],
                        deps=list(pv_) + (hb_join if ih == 0 and qt_ < 2
                                          else []))
                loadq(0)
                loadq(1)
                pend = []

                def flush_epi():
                    nonlocal pend
                    for (qt_, uq_, tav_, pv_) in pend:
                        tln = k.do("s", "activation", rec[:, uq_, :],
                                   ps[64:65, 4 + uq_, :], AF.Ln,
                                   deps=[tav_] + pv_)
                        trc = k.do("s", "activation", rec[:, uq_, :],
                                   rec[:, uq_, :], AF.Exp, scale=-1.0,
                                   deps=[tln])
                        tbc = k.mm(ps[0:64, 6 + uq_, :], o1f[:],
                                   rec[:, uq_, :], True, True,
                                   deps=[trc, to1] + pv_)
                        tcp = k.do("v", "tensor_copy", osb[:, uq_, :],
                                   ps[0:64, 4 + uq_, :],
                                   deps=[tav_, trc] + pv_)
                        tmu = k.do("v", "tensor_mul", onb[:, uq_, :],
                                   osb[:, uq_, :], ps[0:64, 6 + uq_, :],
                                   deps=[tbc, tcp])
                        tst = k.dma("p", o2[b, h // 2,
                                         (h % 2) * 64:(h % 2) * 64 + 64,
                                         qt_ * TT:(qt_ + 1) * TT],
                                    onb[:, uq_, :], deps=[tmu])
                        epi[ih * 4 + qt_] = [tav_, tmu, tst, trc, tcp]
                    pend = []

                tav = None
                for qtp in (0, 2):
                    SK = 2
                    texs = {}
                    tavs = {0: None, 1: None}
                    def pv_of(qt_):
                        return epi.get((ih * 4 + qt_) - 2, [])
                    for step in range(16 + SK):
                        for m in (0, 1):
                            if step >= SK:
                                kt = step - SK
                                qt = qtp + m
                                uq = qt % 2
                                u4 = 2 * uq + kt % 2
                                tav = k.mm(ps[0:65, 4 + uq, :],
                                           va[:, uh, kt * 65:kt * 65 + 65],
                                           ex[:, u4, :], start=(kt == 0),
                                           stop=(kt == 15),
                                           deps=[texs[(m, kt)], tmv,
                                                 tva[kt]] +
                                           (pv_of(qtp + m) if kt == 0
                                            else []), mark=True)
                                exd[u4] = (texs[(m, kt)], tav)
                                tavs[m] = tav
                        for m in (0, 1):
                            if step < 16:
                                kt = step
                                qt = qtp + m
                                uq = qt % 2
                                u4 = 2 * uq + kt % 2
                                pe = exd.get(u4)
                                tsc = k.mm(ps[:, u4, :],
                                           ka[:, uh, kt * 128:(kt + 1) * 128],
                                           qa[:, uq, :], True, True,
                                           deps=[tka, qaload[qt]] +
                                           ([pe[0]] if pe else pv_of(qt)))
                                tex = k.do("s", "activation", ex[:, u4, :],
                                           ps[:, u4, :], AF.Exp, scale=0.25,
                                           deps=[tsc] +
                                           ([pe[1] if pe[1] else pe[0]]
                                            if pe else []))
                                texs[(m, kt)] = tex
                                exd[u4] = (tex, exd.get(u4, (None, None))[1])
                                if kt == 1 and m == 1:
                                    flush_epi()
                    for m in (0, 1):
                        qt_n = qtp + 2 + m
                        if qt_n < NTT:
                            qaload[qt_n] = k.dma(
                                "q", qa[:, qt_n % 2, :],
                                q2aug[b, h, :, qt_n * TT:(qt_n + 1) * TT],
                                deps=[tavs[m]])
                        pend.append((qtp + m, (qtp + m) % 2, tavs[m],
                                     pv_of(qtp + m)))
                flush_epi()
                ldj[ih] = [tav, k.tip()[0]]
                ih += 1
    return k.tip()


def emit_matproj_add(k, ap, src_bf, w_ap, b_ap, add_src, dst, deps,
                     nin=NC, tag="mp"):
    nc = k.nc
    with ExitStack() as ctx:
        wsb = ctx.enter_context(nc.sbuf_tensor(tag + "_w", [128, nin * 1024], BF16))
        stg = ctx.enter_context(nc.sbuf_tensor(tag + "_st", [128, 2048], F32))
        bsb = ctx.enter_context(nc.sbuf_tensor(tag + "_b", [128, NC], F32))
        xsb = ctx.enter_context(nc.sbuf_tensor(tag + "_x", [128, 2, nin * TT], BF16))
        asb = ctx.enter_context(nc.sbuf_tensor(tag + "_a", [128, 2, NC * TT], F32))
        ysb = ctx.enter_context(nc.sbuf_tensor(tag + "_y", [128, 2, NC * TT], F32))
        ps = ctx.enter_context(nc.psum_tensor(tag + "_ps", [128, 8, TT], F32))
        tw = cast_weights(k, wsb[:], w_ap.rearrange("c p n -> p c n"),
                          deps, stage_f32=stg[:])
        if b_ap is not None:
            tb = k.dma("q", bsb[:], b_ap[:, :], deps=deps)
        else:
            tb = k.do("v", "memset", bsb[:], 0.0, deps=deps)
        joins = {}
        for i in range(B * NTT):
            b, tt = divmod(i, NTT)
            u = i % 2
            war = joins.get(i - 2, list(deps) + [tw, tb])
            t0 = tt * TT
            tx = k.dma("q", xsb[:, u, :].rearrange("p (c t) -> p c t", c=nin),
                       src_bf[b, :, :, t0:t0 + TT].rearrange("c p t -> p c t"),
                       deps=war)
            tad = k.dma("q", asb[:, u, :].rearrange("p (c t) -> p c t", c=NC),
                        _fmslice(add_src, b, t0), deps=war)
            bank_free = {}
            for j in range(NC):
                bank = 4 * u + j % 4
                wdeps = war + [tx, tw]
                if j % 4 in bank_free:
                    wdeps = wdeps + [bank_free[j % 4]]
                for c in range(nin):
                    tm = k.mm(ps[:, bank, :],
                              wsb[:, c * 1024 + j * 128:c * 1024 + j * 128 + 128],
                              xsb[:, u, c * TT:(c + 1) * TT],
                              start=(c == 0), stop=(c == nin - 1), deps=wdeps)
                ty = k.do("s", "activation", ysb[:, u, j * TT:(j + 1) * TT],
                          ps[:, bank, :], AF.Identity, bias=bsb[:, j:j + 1],
                          deps=war + [tm, tb])
                bank_free[j % 4] = ty
                tz = k.do("v", "tensor_add", ysb[:, u, j * TT:(j + 1) * TT],
                          ysb[:, u, j * TT:(j + 1) * TT],
                          asb[:, u, j * TT:(j + 1) * TT],
                          deps=war + [ty, tad])
                k.dma("p", dst[b, j, :, t0:t0 + TT], ysb[:, u, j * TT:(j + 1) * TT],
                      deps=[tz])
            joins[i] = k.tip()
    return k.tip()


def _shift_load(k, stage_u, src, b, t0, deps):
    toks = []
    toks.append(k.dma("q", stage_u[:, 0:4 * TT].rearrange("p (c t) -> p c t", c=4),
                      src[b, 0:4, :, t0:t0 + TT].rearrange("c p t -> p c t"),
                      deps=deps))
    if t0 > 0:
        toks.append(k.dma("q",
                          stage_u[:, 4 * TT:8 * TT].rearrange("p (c t) -> p c t", c=4),
                          src[b, 4:8, :, t0 - 1:t0 + TT - 1]
                          .rearrange("c p t -> p c t"), deps=deps))
    else:
        toks.append(k.dma("q",
                          stage_u[:, 4 * TT:8 * TT]
                          .rearrange("p (c t) -> p c t", c=4)[:, :, 1:TT],
                          src[b, 4:8, :, 0:TT - 1].rearrange("c p t -> p c t"),
                          deps=deps))
        for c2 in range(4):
            toks.append(k.do("v", "memset",
                             stage_u[:, (4 + c2) * TT:(4 + c2) * TT + 1], 0.0,
                             deps=deps))
    return toks


def emit_l0f_qkv(k, ap, l, deps):
    nc = k.nc
    src = ap["xfa"] if l == 0 else ap["xfb"]
    qf, kf, vt1 = ap["qf"], ap["kf"], ap["vt1"]
    with ExitStack() as ctx:
        wq = ctx.enter_context(nc.sbuf_tensor(k.uid + "fq_w", [128, NC * 1536], BF16))
        stg = ctx.enter_context(nc.sbuf_tensor(k.uid + "fq_st", [128, 2, NC * TT], F32))
        hb = ctx.enter_context(nc.sbuf_tensor(k.uid + "fq_hb", [128, NC * T], BF16))
        xqc = ctx.enter_context(nc.sbuf_tensor(k.uid + "fq_xqc", [128, T], F32))
        xqs = ctx.enter_context(nc.sbuf_tensor(k.uid + "fq_xqs", [128, T], F32))
        xkc = ctx.enter_context(nc.sbuf_tensor(k.uid + "fq_xkc", [128, T], F32))
        xks = ctx.enter_context(nc.sbuf_tensor(k.uid + "fq_xks", [128, T], F32))
        prt = ctx.enter_context(nc.sbuf_tensor(k.uid + "fq_prt", [128, 128], BF16))
        hh2 = ctx.enter_context(nc.sbuf_tensor(k.uid + "fq_hh2", [128, 2], BF16))
        qsel = ctx.enter_context(nc.sbuf_tensor(k.uid + "fq_qsel", [2, 128], F32))
        ksel = ctx.enter_context(nc.sbuf_tensor(k.uid + "fq_ksel", [2, 128], F32))
        sqv = ctx.enter_context(nc.sbuf_tensor(k.uid + "fq_sq", [128, 2, TT], BF16))
        nrw = ctx.enter_context(nc.sbuf_tensor(k.uid + "fq_nr", [2, 2, TT], F32))
        bcf = ctx.enter_context(nc.sbuf_tensor(k.uid + "fq_bc", [128, 2, TT], F32))
        qn = ctx.enter_context(nc.sbuf_tensor(k.uid + "fq_qn", [128, 2, TT], F32))
        qb = ctx.enter_context(nc.sbuf_tensor(k.uid + "fq_qb", [128, 2, TT], BF16))
        taf = ctx.enter_context(nc.sbuf_tensor(k.uid + "fq_ta", [128, 2, TT], F32))
        tbf = ctx.enter_context(nc.sbuf_tensor(k.uid + "fq_tb", [128, 2, TT], F32))
        ro = ctx.enter_context(nc.sbuf_tensor(k.uid + "fq_ro", [128, 2, TT], BF16))
        vb = ctx.enter_context(nc.sbuf_tensor(k.uid + "fq_vb", [128, 2, TT], BF16))
        ps = ctx.enter_context(nc.psum_tensor(k.uid + "fq_ps", [128, 8, TT], F32))
        tw = cast_weights(k, wq[:], ap["l0f_wqkv"][l].rearrange("c p n -> p c n"),
                          deps, stage_f32=stg[:, 0, :])
        tc1 = k.dma("q", xqc[:], ap["c_xqc"][:, :], deps=deps)
        tc2 = k.dma("q", xqs[:], ap["c_xqs"][:, :], deps=deps)
        tc3 = k.dma("q", xkc[:], ap["c_xkc"][:, :], deps=deps)
        tc4 = k.dma("q", xks[:], ap["c_xks"][:, :], deps=deps)
        tc5 = k.dma("q", prt[:], ap["c_prot"][:, :], deps=deps)
        th0 = k.do("v", "memset", hh2[:], 0.0, deps=deps)
        th1 = k.do("v", "memset", hh2[0:64, 0:1], 1.0, deps=[th0])
        th2 = k.do("v", "memset", hh2[64:128, 1:2], 1.0, deps=[th1])
        tq0 = k.do("v", "memset", qsel[:], 0.0, deps=deps)
        tk0 = k.do("v", "memset", ksel[:], 0.0, deps=deps)
        tq1 = k.dma("q", qsel[0:1, 0:64], ap["l0f_qs"][l:l + 1, :], deps=[tq0])
        tq2 = k.dma("q", qsel[1:2, 64:128], ap["l0f_qs"][l:l + 1, :], deps=[tq0])
        tk1 = k.dma("q", ksel[0:1, 0:64], ap["l0f_ks"][l:l + 1, :], deps=[tk0])
        tk2 = k.dma("q", ksel[1:2, 64:128], ap["l0f_ks"][l:l + 1, :], deps=[tk0])
        pre = [tw, tc1, tc2, tc3, tc4, tc5, th2, tq1, tq2, tk1, tk2]
        bj = list(deps) + pre
        bref = [0]

        def qk_head_block(j, tt, i, col0, sel, tbc, tbs, dst, war):
            b = bref[0]
            u = i % 2
            t0 = tt * TT
            tqm = None
            for c in range(NC):
                tqm = k.mm(ps[:, u, :],
                           wq[:, c * 1536 + col0 + j * 128:
                              c * 1536 + col0 + j * 128 + 128],
                           hb[:, c * T + t0:c * T + t0 + TT],
                           start=(c == 0), stop=(c == NC - 1),
                           deps=war + [tw])
            tsq = k.do("s", "activation", sqv[:, u, :], ps[:, u, :], AF.Square,
                       deps=war + [tqm])
            tn = k.mm(ps[0:2, 2 + u, :], hh2[:], sqv[:, u, :], True, True,
                      deps=war + [tsq, th2])
            tsr = k.do("s", "activation", nrw[:, u, :], ps[0:2, 2 + u, :],
                       AF.Sqrt, deps=war + [tn])
            tcl = k.do("v", "tensor_scalar_max", nrw[:, u, :], nrw[:, u, :],
                       1e-12, deps=war + [tsr])
            trc = k.do("v", "reciprocal", nrw[:, u, :], nrw[:, u, :],
                       deps=[tcl])
            tbm = k.mm(ps[:, 4 + u, :], sel, nrw[:, u, :], True, True,
                       deps=war + [trc])
            tbe = k.do("v", "tensor_copy", bcf[:, u, :], ps[:, 4 + u, :],
                       deps=war + [tbm])
            tqn = k.do("v", "tensor_mul", qn[:, u, :], ps[:, u, :],
                       bcf[:, u, :], deps=[tbe, tsq])
            tqb = k.do("s", "activation", qb[:, u, :], qn[:, u, :], AF.Copy,
                       deps=war + [tqn])
            tp = k.mm(ps[:, 6 + u, :], prt[:], qb[:, u, :], True, True,
                      deps=war + [tqb, tc5])
            t_a = k.do("v", "tensor_mul", taf[:, u, :], ps[:, 6 + u, :],
                       tbs[:, t0:t0 + TT], deps=war + [tp])
            t_b = k.do("v", "tensor_mul", tbf[:, u, :], qn[:, u, :],
                       tbc[:, t0:t0 + TT], deps=war + [t_a])
            t_o = k.do("v", "tensor_add", ro[:, u, :], taf[:, u, :],
                       tbf[:, u, :], deps=[t_b])
            td = k.dma("p", dst[b, j, :, t0:t0 + TT], ro[:, u, :], deps=[t_o])
            return [t_o, td, tqb, tqm]

        for b in range(B):
            bref[0] = b
            htoks = []
            for tt in range(NTT):
                u = tt % 2
                war = [htoks[-2][1]] if tt >= 2 else bj
                tl = _shift_load(k, stg[:, u, :], src, b, tt * TT, war)
                tcst = k.do("v", "tensor_copy",
                            hb[:, :].rearrange("p (c t) -> p c t", c=NC)
                            [:, :, tt * TT:(tt + 1) * TT],
                            stg[:, u, :].rearrange("p (c t) -> p c t", c=NC),
                            deps=war + tl)
                htoks.append((tl, tcst))
            hdone = [h[1] for h in htoks]
            vph = []
            for tt in range(NTT):
                for c4 in range(4):
                    iv = tt * 4 + c4
                    u = iv % 2
                    war = vph[-2] if iv >= 2 else hdone
                    tvm = None
                    for c in range(NC):
                        tvm = k.mm(ps[:, u, :],
                                   hb[:, c * T + tt * TT + c4 * 128:
                                      c * T + tt * TT + c4 * 128 + 128],
                                   wq[:, c * 1536 + 1024:c * 1536 + 1536],
                                   start=(c == 0), stop=(c == NC - 1),
                                   deps=hdone + list(war))
                    tvc = k.do("v", "tensor_copy", vb[:, u, :], ps[:, u, :],
                               deps=list(war) + [tvm])
                    tds = [k.dma("p",
                                 vt1[b, h, tt * TT + c4 * 128:
                                     tt * TT + c4 * 128 + 128, :],
                                 vb[:, u, h * 64:(h + 1) * 64], deps=[tvc])
                           for h in range(8)]
                    vph.append([tvc] + tds)
            vend = k.tip()
            kph = []
            for j in range(4):
                for tt in range(NTT):
                    ik = j * NTT + tt
                    war = kph[-2] if ik >= 2 else vend
                    kph.append(qk_head_block(j, tt, ik, 512, ksel[:], xkc, xks,
                                             kf, list(war)))
            kend = k.tip()
            qph = []
            for j in range(4):
                for tt in range(NTT):
                    iq = j * NTT + tt
                    war = qph[-2] if iq >= 2 else kend
                    qph.append(qk_head_block(j, tt, iq, 0, qsel[:], xqc, xqs,
                                             qf, list(war)))
            bj = k.tip()
    return k.tip()


def emit_l0f_att(k, ap, l, deps):
    nc = k.nc
    qf, kf, vt1, of = ap["qf"], ap["kf"], ap["vt1"], ap["of"]
    with ExitStack() as ctx:
        mtm = ctx.enter_context(nc.sbuf_tensor(k.uid + "fa_mtm", [128, 4 * 1024], BF16))
        mkm = ctx.enter_context(nc.sbuf_tensor(k.uid + "fa_mkm", [128, 8 * TT], BF16))
        idn = ctx.enter_context(nc.sbuf_tensor(k.uid + "fa_id", [128, 128], BF16))
        ka2 = ctx.enter_context(nc.sbuf_tensor(k.uid + "fa_k", [65, 2, 1024], BF16))
        va1 = ctx.enter_context(nc.sbuf_tensor(k.uid + "fa_v", [128, 2, 8 * 65], BF16))
        qa2 = ctx.enter_context(nc.sbuf_tensor(k.uid + "fa_qa", [65, 2, TT], BF16))
        junk = ctx.enter_context(nc.sbuf_tensor(k.uid + "fa_j", [128, 2, 1024], BF16))
        mxc = ctx.enter_context(nc.sbuf_tensor(k.uid + "fa_mx", [128, 2, 4], F32))
        mxb = ctx.enter_context(nc.sbuf_tensor(k.uid + "fa_mb", [128, 2, 4], BF16))
        mxr = ctx.enter_context(nc.sbuf_tensor(k.uid + "fa_mr", [4, 2, 128], BF16))
        ex = ctx.enter_context(nc.sbuf_tensor(k.uid + "fa_ex", [128, 2, 2, TT], BF16))
        rec = ctx.enter_context(nc.sbuf_tensor(k.uid + "fa_rec", [1, 2, TT], F32))
        osb = ctx.enter_context(nc.sbuf_tensor(k.uid + "fa_osb", [64, 2, TT], F32))
        onb = ctx.enter_context(nc.sbuf_tensor(k.uid + "fa_onb", [64, 2, TT], BF16))
        o1f = ctx.enter_context(nc.sbuf_tensor(k.uid + "fa_o1f", [1, 64], F32))
        psA = ctx.enter_context(nc.psum_tensor(k.uid + "fa_psA", [128, 2, 2, TT], F32))
        psO = ctx.enter_context(nc.psum_tensor(k.uid + "fa_psO", [128, 2, TT], F32))
        psB = ctx.enter_context(nc.psum_tensor(k.uid + "fa_psB", [128, TT], F32))
        psT = ctx.enter_context(nc.psum_tensor(k.uid + "fa_psT", [4, 128], BF16))
        tm1 = k.dma("q", mtm[:].rearrange("p (m t) -> p m t", m=4),
                    ap["c_mtm"].rearrange("m p t -> p m t"), deps=deps)
        tm2 = k.dma("q", mkm[:].rearrange("p (m t) -> p m t", m=8),
                    ap["c_mkm"].rearrange("m p t -> p m t"), deps=deps)
        tid = k.dma("q", idn[:], ap["c_id"][:, :], deps=deps)
        tv1 = k.do("v", "memset", va1[:], 1.0, deps=deps)
        tk1 = k.do("v", "memset", ka2[64:65, :, :], 1.0, deps=deps)
        to1 = k.do("v", "memset", o1f[:], 1.0, deps=deps)
        pre = [tm1, tm2, tid, tv1, tk1, to1]
        joins = {}
        prev_bc = None
        prev_tr = None
        idx = 0
        for b in range(B):
            for h in range(8):
                for w in range(4):
                    u = idx % 2
                    war = joins.get(idx - 2, list(deps) + pre)
                    jc, jp = h // 2, (h % 2) * 64
                    kt0 = 4 if w == 0 else 0
                    tql = k.dma("q", qa2[0:64, u, :],
                                qf[b, jc, jp:jp + 64, w * TT:(w + 1) * TT],
                                deps=war)
                    if w == 0:
                        tkl = k.dma("q", ka2[0:64, u, 512:1024],
                                    kf[b, jc, jp:jp + 64, 0:TT], deps=war)
                    else:
                        tkl = k.dma("q", ka2[0:64, u, :],
                                    kf[b, jc, jp:jp + 64,
                                       (w - 1) * TT:(w + 1) * TT], deps=war)
                    tvl = [k.dma("q", va1[:, u, kt * 65:kt * 65 + 64],
                                 vt1[b, h, (w - 1) * TT + kt * 128:
                                     (w - 1) * TT + kt * 128 + 128, :],
                                 deps=war) for kt in range(kt0, 8)]
                    p1 = []
                    for qc in range(4):
                        wq1 = [p1[-1][1]] if p1 else war
                        halves = range(1, 2) if w == 0 else range(2)
                        tms = None
                        for half in halves:
                            tms = k.mm(psA[:, u, half, :],
                                       qa2[0:64, u, qc * 128:(qc + 1) * 128],
                                       ka2[0:64, u, half * TT:(half + 1) * TT],
                                       True, True,
                                       deps=war + wq1 + [tql, tkl])
                        if w == 0:
                            tad = k.do("v", "tensor_add", psA[:, u, 1, :],
                                       psA[:, u, 1, :],
                                       mtm[:, qc * 1024 + 512:
                                           qc * 1024 + 1024],
                                       deps=war + [tms, tm1])
                            ttr = k.do("v", "reduce_max",
                                       mxc[:, u, qc:qc + 1],
                                       psA[:, u, 1, :], axis=AX.X,
                                       deps=[tad])
                        else:
                            tad = k.do("v", "tensor_add", psA[:, u, :, :],
                                       psA[:, u, :, :],
                                       mtm[:, qc * 1024:(qc + 1) * 1024],
                                       deps=war + [tms, tm1])
                            ttr = k.do("v", "reduce_max",
                                       mxc[:, u, qc:qc + 1],
                                       psA[:, u, :, :]
                                       .rearrange("p a b -> p (a b)"),
                                       axis=AX.X, deps=[tad])
                        p1.append((tms, ttr))
                    tng = k.do("v", "tensor_scalar_mul", mxb[:, u, :],
                               mxc[:, u, :], -1.0, deps=[p1[-1][1]])
                    ttp = k.do("t", "transpose", psT[:], mxb[:, u, :], idn[:],
                               deps=war + [tng, tid, prev_tr])
                    tmr = k.do("v", "tensor_copy", mxr[:, u, :], psT[:],
                               deps=war + [ttp, prev_tr])
                    prev_tr = tmr
                    tdm = k.dma("p", qa2[64:65, u, :], mxr[:, u, :],
                                deps=[tmr])
                    evs = []
                    tav = None
                    for kt in range(kt0, 8):
                        u2 = kt % 2
                        dsc = [tdm, tkl] + [p1[-1][1], p1[-2][1] if len(p1) > 1 else None]
                        if len(evs) >= 2:
                            dsc.append(evs[-2][0])
                        tsc = k.mm(psA[:, u, u2, :],
                                   ka2[:, u, kt * 128:(kt + 1) * 128],
                                   qa2[:, u, :], True, True, deps=dsc)
                        tmk = k.do("v", "tensor_add", psA[:, u, u2, :],
                                   psA[:, u, u2, :],
                                   mkm[:, kt * TT:(kt + 1) * TT],
                                   deps=[tsc, tm2])
                        dex = [tmk]
                        if len(evs) >= 2:
                            dex.append(evs[-2][1])
                        tex = k.do("s", "activation", ex[:, u, u2, :],
                                   psA[:, u, u2, :], AF.Exp, scale=8.0,
                                   deps=dex)
                        tav = k.mm(psO[0:65, u, :],
                                   va1[:, u, kt * 65:kt * 65 + 65],
                                   ex[:, u, u2, :], start=(kt == kt0),
                                   stop=(kt == 7),
                                   deps=[tex, tv1] + tvl +
                                   (war + [tmr] if kt == kt0 else []),
                                   mark=True)
                        evs.append((tex, tav))
                    tln = k.do("s", "activation", rec[:, u, :],
                               psO[64:65, u, :], AF.Ln, deps=war + [tav])
                    trc = k.do("s", "activation", rec[:, u, :],
                               rec[:, u, :], AF.Exp, scale=-1.0, deps=[tln])
                    tbc = k.mm(psB[0:64, :], o1f[:], rec[:, u, :], True, True,
                               deps=[trc, to1, prev_bc])
                    tcp = k.do("v", "tensor_copy", osb[:, u, :],
                               psO[0:64, u, :], deps=war + [tav, trc])
                    tmu = k.do("v", "tensor_mul", onb[:, u, :], osb[:, u, :],
                               psB[0:64, :], deps=[tbc, tcp])
                    prev_bc = tmu
                    tst = k.dma("p", of[b, jc, jp:jp + 64, w * TT:(w + 1) * TT],
                                onb[:, u, :], deps=[tmu])
                    joins[idx] = k.tip()
                    idx += 1
    return k.tip()


def emit_l0f_w1(k, ap, l, src_name, deps):
    nc = k.nc
    src, mgl = ap[src_name], ap["mgl"]
    with ExitStack() as ctx:
        w1 = ctx.enter_context(nc.sbuf_tensor(k.uid + "f1_w1", [128, NC * 2 * HIDP], BF16))
        b1 = ctx.enter_context(nc.sbuf_tensor(k.uid + "f1_b1", [128, 2 * NHC], F32))
        glb = ctx.enter_context(nc.sbuf_tensor(k.uid + "f1_glb", [2, NC * 128], F32))
        stg = ctx.enter_context(nc.sbuf_tensor(k.uid + "f1_st", [128, 2, NC * TT], F32))
        hbt = ctx.enter_context(nc.sbuf_tensor(k.uid + "f1_hb", [128, NC * TT], BF16))
        sq = ctx.enter_context(nc.sbuf_tensor(k.uid + "f1_sq", [128, NC * TT], BF16))
        ones = ctx.enter_context(nc.sbuf_tensor(k.uid + "f1_on", [128, 1], BF16))
        rows = ctx.enter_context(nc.sbuf_tensor(k.uid + "f1_rw", [1, 2, 6, TT], F32))
        rh = ctx.enter_context(nc.sbuf_tensor(k.uid + "f1_rh", [2, 2, 2, TT], F32))
        eps = ctx.enter_context(nc.sbuf_tensor(k.uid + "f1_eps", [1, 1], F32))
        hnf = ctx.enter_context(nc.sbuf_tensor(k.uid + "f1_hnf", [128, TT], F32))
        hnb = ctx.enter_context(nc.sbuf_tensor(k.uid + "f1_hnb", [128, 2, NC * TT], BF16))
        gl = ctx.enter_context(nc.sbuf_tensor(k.uid + "f1_gl", [128, 2, 2, TT], BF16))
        ab = ctx.enter_context(nc.sbuf_tensor(k.uid + "f1_ab", [128, 2, 2, TT], BF16))
        mo = ctx.enter_context(nc.sbuf_tensor(k.uid + "f1_mo", [128, 2, 2, TT], BF16))
        ps = ctx.enter_context(nc.psum_tensor(k.uid + "f1_ps", [128, 8, TT], F32))
        tw1 = cast_weights(k, w1[:], ap["l0f_w1"][l].rearrange("c p n -> p c n"),
                           deps, stage_f32=stg[:, 0, :])
        tb1 = k.dma("q", b1[:], ap["l0f_b1"][l], deps=deps)
        tg1 = k.dma("q", glb[0:1, :],
                    ap["l0f_lng"][l].rearrange("c p -> (c p)")[None, :],
                    deps=deps)
        tg2 = k.dma("q", glb[1:2, :],
                    ap["l0f_lnb"][l].rearrange("c p -> (c p)")[None, :],
                    deps=deps)
        ton = k.do("v", "memset", ones[:], 1.0, deps=deps)
        tep = k.do("v", "memset", eps[:], 1e-5, deps=deps)
        trh = k.do("v", "memset", rh[:], 0.0, deps=deps)
        tone = k.do("v", "memset", hnf[:1, :], 1.0, deps=deps)
        trh2a = k.dma("p", rh[1:2, 0, 1, :], hnf[0:1, 0:TT], deps=[trh, tone])
        trh2b = k.dma("p", rh[1:2, 1, 1, :], hnf[0:1, 0:TT], deps=[trh, tone])
        trh2 = k.do("v", "memset", hnf[:1, :], 0.0,
                    deps=[trh2a, trh2b])
        pre = [tw1, tb1, tg1, tg2, ton, tep, trh2a, trh2b, trh2]
        joins = {}
        for i in range(B * NTT):
            b, tt = divmod(i, NTT)
            u = i % 2
            war = joins.get(i - 2, list(deps) + pre)
            t0 = tt * TT
            tl = _shift_load(k, stg[:, u, :], src, b, t0, war)
            tcst = k.do("v", "tensor_copy", hbt[:], stg[:, u, :],
                        deps=war + tl)
            tsqr = k.do("s", "activation", sq[:], hbt[:], AF.Square,
                        deps=war + [tcst])
            for c in range(NC):
                ts0 = k.mm(ps[0:1, 4 * u, :], ones[:],
                           hbt[:, c * TT:(c + 1) * TT],
                           start=(c == 0), stop=(c == NC - 1),
                           deps=war + [tcst, ton])
            for c in range(NC):
                ts1 = k.mm(ps[0:1, 4 * u + 1, :], ones[:],
                           sq[:, c * TT:(c + 1) * TT],
                           start=(c == 0), stop=(c == NC - 1), deps=[tsqr])
            R = rows[:, u, :, :]
            tm = k.do("v", "tensor_scalar_mul", R[:, 0, :], ps[0:1, 4 * u, :],
                      1.0 / D, deps=war + [ts0])
            te = k.do("v", "tensor_scalar_mul", R[:, 1, :],
                      ps[0:1, 4 * u + 1, :], 1.0 / D, deps=[ts1])
            tmm = k.do("v", "tensor_mul", R[:, 2, :], R[:, 0, :], R[:, 0, :],
                       deps=[tm])
            tv = k.do("v", "tensor_sub", R[:, 3, :], R[:, 1, :], R[:, 2, :],
                      deps=[te, tmm])
            tsd = k.do("s", "activation", R[:, 4, :], R[:, 3, :], AF.Sqrt,
                       bias=eps[:], deps=war + [tv, tep])
            tr_ = k.do("v", "reciprocal", R[:, 5, :], R[:, 4, :], deps=[tsd])
            tra = k.do("v", "tensor_copy", rh[0:1, u, 0, :], R[:, 5, :],
                       deps=[tr_])
            tnm = k.do("v", "tensor_mul", R[:, 2, :], R[:, 0, :], R[:, 5, :],
                       deps=[tr_])
            tnm2 = k.do("v", "tensor_scalar_mul", rh[0:1, u, 1, :], R[:, 2, :],
                        -1.0, deps=[tnm])
            hn_toks = []
            for c in range(NC):
                ta = k.mm(ps[:, 4 * u + 2, :], glb[:, c * 128:(c + 1) * 128],
                          rh[:, u, 0, :], True, True,
                          deps=war + [tra, trh2, tg2] +
                          (hn_toks[-1:] if hn_toks else []))
                tb_ = k.mm(ps[:, 4 * u + 3, :], glb[:, c * 128:(c + 1) * 128],
                           rh[:, u, 1, :], True, True, deps=[tnm2])
                th1 = k.do("v", "tensor_mul", hnf[:],
                           stg[:, u, c * TT:(c + 1) * TT], ps[:, 4 * u + 2, :],
                           deps=war + [ta] + tl)
                th2_ = k.do("v", "tensor_add", hnf[:], hnf[:],
                            ps[:, 4 * u + 3, :], deps=[tb_, th1])
                th3 = k.do("v", "tensor_copy", hnb[:, u, c * TT:(c + 1) * TT],
                           hnf[:], deps=[th2_])
                hn_toks.append(th3)
            gm_prev = {}
            for j in range(NHC):
                jp = j % 2
                ba = 4 * u + jp
                bg = 4 * u + 2 + jp
                wdeps = war + [tw1, hn_toks[-1]]
                prev = gm_prev.get(jp)
                if prev is not None:
                    wdeps = wdeps + [prev[0]]
                tma = tmg = None
                for c in range(NC):
                    tma = k.mm(ps[:, ba, :],
                               w1[:, c * 2 * HIDP + j * 128:
                                  c * 2 * HIDP + j * 128 + 128],
                               hnb[:, u, c * TT:(c + 1) * TT],
                               start=(c == 0), stop=(c == NC - 1), deps=wdeps)
                for c in range(NC):
                    tmg = k.mm(ps[:, bg, :],
                               w1[:, c * 2 * HIDP + HIDP + j * 128:
                                  c * 2 * HIDP + HIDP + j * 128 + 128],
                               hnb[:, u, c * TT:(c + 1) * TT],
                               start=(c == 0), stop=(c == NC - 1), deps=wdeps)
                tgl = k.do("s", "activation", gl[:, u, jp, :], ps[:, bg, :],
                           AF.Gelu, bias=b1[:, NHC + j:NHC + j + 1],
                           deps=war + [tmg, tb1] +
                           ([prev[0]] if prev else []))
                tab = k.do("v", "tensor_scalar_add", ab[:, u, jp, :],
                           ps[:, ba, :], b1[:, j:j + 1],
                           deps=war + [tma, tb1] +
                           ([prev[0]] if prev else []))
                tm_ = k.do("v", "tensor_mul", mo[:, u, jp, :],
                           ab[:, u, jp, :], gl[:, u, jp, :],
                           deps=[tgl, tab] + ([prev[1]] if prev else []))
                td = k.dma("p", mgl[b, j, :, t0:t0 + TT], mo[:, u, jp, :],
                           deps=[tm_])
                gm_prev[jp] = (tm_, td)
            joins[i] = k.tip()
    return k.tip()


def emit_l1a_qkv(k, ap, deps):
    nc = k.nc
    xn2, q2aug, k2aug, vt2 = ap["xn2"], ap["q2aug"], ap["k2aug"], ap["vt2"]
    with ExitStack() as ctx:
        wqs = ctx.enter_context(nc.sbuf_tensor(k.uid + "lq_wq", [128, NC * 1024], BF16))
        wks = ctx.enter_context(nc.sbuf_tensor(k.uid + "lq_wk", [128, NC * 1024], BF16))
        wvs = ctx.enter_context(nc.sbuf_tensor(k.uid + "lq_wv", [128, NC * 1024], BF16))
        stg = ctx.enter_context(nc.sbuf_tensor(k.uid + "lq_st", [128, 2048], F32))
        bqs = ctx.enter_context(nc.sbuf_tensor(k.uid + "lq_bq", [128, NC], F32))
        bks = ctx.enter_context(nc.sbuf_tensor(k.uid + "lq_bk", [128, NC], F32))
        bvr = ctx.enter_context(nc.sbuf_tensor(k.uid + "lq_bv", [1, 1024], BF16))
        bvf = ctx.enter_context(nc.sbuf_tensor(k.uid + "lq_bvf", [1, 1024], F32))
        wqr = ctx.enter_context(nc.sbuf_tensor(k.uid + "lq_wqr", [128, 16], BF16))
        wkr = ctx.enter_context(nc.sbuf_tensor(k.uid + "lq_wkr", [128, 16], BF16))
        wrf = ctx.enter_context(nc.sbuf_tensor(k.uid + "lq_wrf", [128, 32], F32))
        onesc = ctx.enter_context(nc.sbuf_tensor(k.uid + "lq_on", [128, 1], BF16))
        onesr = ctx.enter_context(nc.sbuf_tensor(k.uid + "lq_or", [1, T], BF16))
        xnb = ctx.enter_context(nc.sbuf_tensor(k.uid + "lq_xnb", [128, 2, NC * TT], BF16))
        qk = ctx.enter_context(nc.sbuf_tensor(k.uid + "lq_qk", [128, 2, TT], BF16))
        sqr = ctx.enter_context(nc.sbuf_tensor(k.uid + "lq_sqr", [16, 2, TT], BF16))
        rr = ctx.enter_context(nc.sbuf_tensor(k.uid + "lq_rr", [16, 2, TT], BF16))
        nr = ctx.enter_context(nc.sbuf_tensor(k.uid + "lq_nr", [1, 2, TT], F32))
        crw = ctx.enter_context(nc.sbuf_tensor(k.uid + "lq_crw", [1, 2, TT], BF16))
        mks = ctx.enter_context(nc.sbuf_tensor(k.uid + "lq_mks", [1, 16, 6], F32))
        mkv = ctx.enter_context(nc.sbuf_tensor(k.uid + "lq_mkv", [1, 16], F32))
        vb = ctx.enter_context(nc.sbuf_tensor(k.uid + "lq_vb", [128, 2, TT], BF16))
        ps = ctx.enter_context(nc.psum_tensor(k.uid + "lq_ps", [128, 8, TT], F32))
        tw1 = cast_weights(k, wqs[:], ap["l1a_wq"].rearrange("c p n -> p c n"),
                           deps, stage_f32=stg[:])
        tw2 = cast_weights(k, wks[:], ap["l1a_wk"].rearrange("c p n -> p c n"),
                           [tw1], stage_f32=stg[:])
        tw3 = cast_weights(k, wvs[:], ap["l1a_wv"].rearrange("c p n -> p c n"),
                           [tw2], stage_f32=stg[:])
        tb1 = k.dma("q", bqs[:], ap["l1a_bq"][:, :], deps=deps)
        tb2 = k.dma("q", bks[:], ap["l1a_bk"][:, :], deps=deps)
        tb3 = k.dma("q", bvf[:], ap["l1a_bvrow"][:, :], deps=deps)
        tb4 = k.do("v", "tensor_copy", bvr[:], bvf[:], deps=[tb3])
        tw4a = k.dma("q", wrf[0:64, 0:16], ap["l1a_wqr"][:, :], deps=deps)
        tw4b = k.dma("q", wrf[64:128, 0:16], ap["l1a_wqr"][:, :], deps=deps)
        tw5a = k.dma("q", wrf[0:64, 16:32], ap["l1a_wkr"][:, :], deps=deps)
        tw5b = k.dma("q", wrf[64:128, 16:32], ap["l1a_wkr"][:, :], deps=deps)
        tw6 = k.do("v", "tensor_copy", wqr[:], wrf[:, 0:16],
                   deps=[tw4a, tw4b])
        tw7 = k.do("v", "tensor_copy", wkr[:], wrf[:, 16:32],
                   deps=[tw5a, tw5b])
        to1 = k.do("v", "memset", onesc[:], 1.0, deps=deps)
        to2 = k.do("v", "memset", onesr[:], 1.0, deps=deps)
        pre = [tw3, tb1, tb2, tb4, tw6, tw7, to1, to2]
        bj = list(deps) + pre

        def qk_phase(wsb, bsb, wr, dst, is_q, war0):
            ph = []
            rprev = {}
            nprev = {}
            shr = {}
            for tt in range(NTT):
                u = tt % 2
                war = ph[-2] if tt >= 2 else war0
                tl = k.dma("q", xnb[:, u, :].rearrange("p (c t) -> p c t", c=NC),
                           _fmslice(xn2, b, tt * TT), deps=war)
                toks = [tl]
                eprev = {}
                for j in range(NC):
                    bqk = (j % 2)
                    tpm = None
                    for c in range(NC):
                        tpm = k.mm(ps[:, bqk, :],
                                   wsb[:, c * 1024 + j * 128:
                                       c * 1024 + j * 128 + 128],
                                   xnb[:, u, c * TT:(c + 1) * TT],
                                   start=(c == 0), stop=(c == NC - 1),
                                   deps=list(war) + [tl] +
                                   ([eprev[bqk]] if bqk in eprev else []) +
                                   (rprev.get((bqk, 0), []) +
                                    rprev.get((bqk, 1), [])))
                    tev = k.do("s", "activation", qk[:, u, :], ps[:, bqk, :],
                               AF.Identity, bias=bsb[:, j:j + 1],
                               deps=list(war) + [tpm] +
                               ([eprev[bqk]] if bqk in eprev else []))
                    eprev[bqk] = tev
                    for hh in range(2):
                        if INNER14 < 1:
                            break
                        h = 2 * j + hh
                        rb = 2 + 2 * (j % 2) + hh
                        nb = 6 + hh
                        rpv = rprev.get((j % 2, hh), [])
                        npv = nprev.get(hh, [])
                        trm = k.mm(ps[0:16, rb, :],
                                   wr[hh * 64:hh * 64 + 64, :],
                                   qk[hh * 64:hh * 64 + 64, u, :], True, True,
                                   deps=[tev] + rpv)
                        if INNER14 >= 2:
                            tsq = k.do("s", "activation", sqr[:, u, :],
                                       ps[0:16, rb, :], AF.Square,
                                       deps=[trm, shr.get(("sq", u))] + npv)
                            tnm = tsq
                        if INNER14 >= 3:
                            tnm = k.mm(ps[0:1, nb, :], onesc[0:16, :],
                                       sqr[:, u, :], True, True,
                                       deps=[tsq, to1] + npv)
                        tcv = k.do("v", "tensor_copy", rr[:, u, :],
                                   ps[0:16, rb, :],
                                   deps=[trm, tsq if INNER14 >= 2 else None,
                                         shr.get(("rr", u))] + rpv)
                        td1 = k.dma("p", dst[b, h, 0:16, tt * TT:(tt + 1) * TT],
                                    rr[:, u, :], deps=[tcv])
                        shr[("rr", u)] = td1
                        if INNER14 >= 3:
                            shr[("sq", u)] = tnm
                        elif INNER14 >= 2:
                            shr[("sq", u)] = tsq
                            rprev[(j % 2, hh)] = [tsq, tcv, td1]
                        else:
                            rprev[(j % 2, hh)] = [tcv, td1]
                        if is_q and INNER14 >= 2:
                            tsr = k.do("s", "activation", nr[:, u, :],
                                       ps[0:1, nb, :], AF.Sqrt,
                                       deps=[tnm, shr.get(("nr", u))])
                            tcr = k.do("v", "tensor_scalar_mul", crw[:, u, :],
                                       nr[:, u, :], mkv[0:1, h:h + 1],
                                       deps=[tsr, mkr[0],
                                             shr.get(("cr", u))])
                            td2 = k.dma("p", dst[b, h, 16:17,
                                             tt * TT:(tt + 1) * TT],
                                        crw[:, u, :], deps=[tcr])
                            shr[("nr", u)] = tcr
                            shr[("cr", u)] = td2
                            nprev[hh] = [tsr]
                            toks += [tcr, td1, td2]
                        elif INNER14 >= 4:
                            tmx = k.do("v", "reduce_max",
                                       mks[0:1, h, tt:tt + 1],
                                       ps[0:1, nb, :], axis=AX.X,
                                       deps=[tnm])
                            if DBG14 and b == 0 and h == 0 and tt == 0:
                                tdg = k.do("v", "tensor_copy",
                                           rr[:, u, :], ps[0:16, rb, :],
                                           deps=[tnm, tmx, tcv])
                                k.dma("p", ap["dbg14"][1:2, 0:512],
                                      sqr[0:1, u, :], deps=[tnm, tsq])
                                tdh = k.do("v", "tensor_copy",
                                           nr[0:1, 0, :], ps[0:1, nb, :],
                                           deps=[tnm, tmx])
                                k.dma("p", ap["dbg14"][2:3, 0:512],
                                      nr[0:1, 0, :], deps=[tdh])
                                shr[("rr", u)] = tdg
                            nprev[hh] = [tmx]
                            toks += [tmx, td1]
                        elif INNER14 >= 2:
                            nprev[hh] = [tnm]
                            toks += [td1]
                        else:
                            toks += [td1]
                ph.append(toks)
            return [x for t_ in ph for x in t_]

        mkr = [None]
        for b in range(B):
            kj = qk_phase(wks, bks, wkr, k2aug, False, bj)
            if PH14 < 2:
                bj = k.tip()
                continue
            for h in range(16):
                k.dma("p", k2aug[b, h, 16:17, :], onesr[:], deps=[to2])
            mt = []
            for h in range(16):
                t1_ = k.do("v", "reduce_max", mks[0:1, h, 4:5],
                           mks[0:1, h, 0:NTT], axis=AX.X, deps=kj)
                t2_ = k.do("s", "activation", mks[0:1, h, 5:6],
                           mks[0:1, h, 4:5], AF.Sqrt, deps=[t1_])
                t3_ = k.do("v", "tensor_scalar_mul", mkv[0:1, h:h + 1],
                           mks[0:1, h, 5:6], -1.0, deps=[t2_])
                mt.append(t3_)
            mkr[0] = mt[-1]
            if b == 0:
                k.dma("p", ap["dbg14"][0:1, 0:96], mks[0:1, :, :], deps=mt)
                k.dma("p", ap["dbg14"][0:1, 96:112], mkv[:], deps=mt)
            kend = k.tip()
            if PH14 < 3:
                bj = k.tip()
                continue
            vph = []
            vprev = {}
            for tt in range(NTT):
                u = tt % 2
                war = vph[-2] if tt >= 2 else kend
                tl = k.dma("q", xnb[:, u, :].rearrange("p (c t) -> p c t", c=NC),
                           _fmslice(xn2, b, tt * TT), deps=war)
                toks = [tl]
                for tc4 in range(4):
                    for vc in range(2):
                        iv = tc4 * 2 + vc
                        uv = iv % 2
                        bv_ = 6 + uv
                        pv2 = vprev.get(uv, [])
                        tvm = None
                        for c in range(NC):
                            tvm = k.mm(ps[:, bv_, :],
                                       xnb[:, u, c * TT + tc4 * 128:
                                           c * TT + tc4 * 128 + 128],
                                       wvs[:, c * 1024 + vc * 512:
                                           c * 1024 + vc * 512 + 512],
                                       start=(c == 0), stop=False,
                                       deps=list(war) + [tl] + pv2[:1],
                                       mark=False)
                        tvm = k.mm(ps[:, bv_, :], onesr[0:1, 0:128],
                                   bvr[:, vc * 512:(vc + 1) * 512],
                                   start=False, stop=True,
                                   deps=[to2, tb4])
                        tvc = k.do("v", "tensor_copy", vb[:, uv, :],
                                   ps[:, bv_, :],
                                   deps=list(war) + [tvm] + pv2)
                        tds = [k.dma("p",
                                     vt2[b, vc * 8 + h2,
                                         tt * TT + tc4 * 128:
                                         tt * TT + tc4 * 128 + 128, :],
                                     vb[:, uv, h2 * 64:(h2 + 1) * 64],
                                     deps=[tvc]) for h2 in range(8)]
                        vprev[uv] = [tvc] + tds
                        toks += [tvc] + tds
                vph.append(toks)
            vend = k.tip()
            if PH14 >= 4:
                qj = qk_phase(wqs, bqs, wqr, q2aug, True, vend)
            bj = k.tip()
    return k.tip()


def emit_gate(k, ap, deps):
    nc = k.nc
    xn2, m8, mbeta = ap["xn2"], ap["m8"], ap["mbeta"]
    with ExitStack() as ctx:
        gw = ctx.enter_context(nc.sbuf_tensor(k.uid + "gt_w", [128, NC * 8], BF16))
        stg = ctx.enter_context(nc.sbuf_tensor(k.uid + "gt_st", [128, NC * 8], F32))
        gb8 = ctx.enter_context(nc.sbuf_tensor(k.uid + "gt_b", [8, 1], F32))
        idn = ctx.enter_context(nc.sbuf_tensor(k.uid + "gt_id", [128, 128], BF16))
        xnb = ctx.enter_context(nc.sbuf_tensor(k.uid + "gt_x", [128, 2, NC * TT], BF16))
        lgb = ctx.enter_context(nc.sbuf_tensor(k.uid + "gt_lb", [8, 2, TT], BF16))
        lgt = ctx.enter_context(nc.sbuf_tensor(k.uid + "gt_lt", [128, 2, 8], F32))
        mxt = ctx.enter_context(nc.sbuf_tensor(k.uid + "gt_mx", [128, 2, 1], F32))
        mskb = ctx.enter_context(nc.sbuf_tensor(k.uid + "gt_mb", [128, 2, 8], BF16))
        mk8 = ctx.enter_context(nc.sbuf_tensor(k.uid + "gt_m8", [8, 2, 128], F32))
        mbt = ctx.enter_context(nc.sbuf_tensor(k.uid + "gt_bt", [8, 2, 128], F32))
        psg = ctx.enter_context(nc.psum_tensor(k.uid + "gt_ps", [128, 2, TT], F32))
        psa = ctx.enter_context(nc.psum_tensor(k.uid + "gt_pa", [128, 2, 8], BF16))
        psb = ctx.enter_context(nc.psum_tensor(k.uid + "gt_pb", [8, 2, 128], BF16))
        tw = cast_weights(k, gw[:], ap["l1f_gw"].rearrange("c p n -> p c n"),
                          deps, stage_f32=stg[:])
        tb = k.dma("q", gb8[:], ap["l1f_gbrow"][:, :], deps=deps)
        tid = k.dma("q", idn[:], ap["c_id"][:, :], deps=deps)
        pre = [tw, tb, tid]
        joins = {}
        tprev = {}
        for i in range(B * NTT):
            b, tt = divmod(i, NTT)
            u = i % 2
            war = joins.get(i - 2, list(deps) + pre)
            t0g = b * T + tt * TT
            tl = k.dma("q", xnb[:, u, :].rearrange("p (c t) -> p c t", c=NC),
                       _fmslice(xn2, b, tt * TT), deps=war)
            tg = None
            for c in range(NC):
                tg = k.mm(psg[0:8, u, :], gw[:, c * 8:(c + 1) * 8],
                          xnb[:, u, c * TT:(c + 1) * TT],
                          start=(c == 0), stop=(c == NC - 1),
                          deps=list(war) + [tl, tw])
            tev = k.do("s", "activation", lgb[:, u, :], psg[0:8, u, :],
                       AF.Identity, bias=gb8[:], deps=list(war) + [tg, tb])
            for tc4 in range(4):
                gc = i * 4 + tc4
                u4 = gc % 2
                pv = tprev.get(gc - 2, {})
                wv = list(war) if tc4 < 2 else []
                ttr = k.do("t", "transpose", psa[:, u4, :],
                           lgb[0:8, u, tc4 * 128:(tc4 + 1) * 128],
                           idn[0:8, 0:8],
                           deps=[tev, tid, pv.get("tcp")] + wv)
                tcp = k.do("v", "tensor_copy", lgt[:, u4, :], psa[:, u4, :],
                           deps=[ttr, pv.get("teq")] + wv)
                tmx = k.do("v", "reduce_max", mxt[:, u4, :], lgt[:, u4, :],
                           axis=AX.X, deps=[tcp, pv.get("teq")])
                teq = k.do("v", "tensor_scalar", mskb[:, u4, :],
                           lgt[:, u4, :], mxt[:, u4, :], None, ALU.is_equal,
                           deps=[tmx, pv.get("ttr2")])
                ttr2 = k.do("t", "transpose", psb[:, u4, :], mskb[:, u4, :],
                            idn[:], deps=[teq, pv.get("tm8")] + wv)
                tm8 = k.do("v", "tensor_copy", mk8[:, u4, :], psb[:, u4, :],
                           deps=[ttr2, pv.get("d1"), pv.get("tbt")])
                tbt = k.do("v", "tensor_scalar", mbt[:, u4, :], mk8[:, u4, :],
                           1.0, 1.0e9, ALU.subtract, ALU.mult,
                           deps=[tm8, pv.get("d2")])
                d1 = k.dma("p", m8[:, t0g + tc4 * 128:t0g + tc4 * 128 + 128],
                           mk8[:, u4, :], deps=[tm8])
                d2 = k.dma("p", mbeta[:, t0g + tc4 * 128:t0g + tc4 * 128 + 128],
                           mbt[:, u4, :], deps=[tbt])
                tprev[gc] = dict(ttr=ttr, tcp=tcp, tmx=tmx, teq=teq,
                                 ttr2=ttr2, tm8=tm8, tbt=tbt, d1=d1, d2=d2)
            joins[i] = k.tip()
    return k.tip()


def emit_moe(k, ap, deps):
    nc = k.nc
    xn2, m8, mbeta, out = ap["xn2"], ap["m8"], ap["mbeta"], ap["out"]
    acc_in = RD(ap, "out")
    NTU = B * NTT
    with ExitStack() as ctx:
        w1e = ctx.enter_context(nc.sbuf_tensor(k.uid + "mo_w1", [128, NC * 4096], BF16))
        w2e = ctx.enter_context(nc.sbuf_tensor(k.uid + "mo_w2", [128, 32 * 1024], BF16))
        stg = ctx.enter_context(nc.sbuf_tensor(k.uid + "mo_st", [128, 2048], F32))
        b1s = ctx.enter_context(nc.sbuf_tensor(k.uid + "mo_b1", [128, 32], F32))
        b2m = ctx.enter_context(nc.sbuf_tensor(k.uid + "mo_b2", [8, NC * 128], BF16))
        b2f = ctx.enter_context(nc.sbuf_tensor(k.uid + "mo_b2f", [8, NC * 128], F32))
        onesb = ctx.enter_context(nc.sbuf_tensor(k.uid + "mo_on", [1, 128], BF16))
        xg = ctx.enter_context(nc.sbuf_tensor(k.uid + "mo_x", [128, 2, NC * TT], BF16))
        m8f = ctx.enter_context(nc.sbuf_tensor(k.uid + "mo_m8f", [8, 2, TT], F32))
        m8s = ctx.enter_context(nc.sbuf_tensor(k.uid + "mo_m8s", [8, 2, TT], BF16))
        mbf = ctx.enter_context(nc.sbuf_tensor(k.uid + "mo_mbf", [1, 2, TT], F32))
        mbs = ctx.enter_context(nc.sbuf_tensor(k.uid + "mo_mbs", [1, 2, TT], BF16))
        hsb = ctx.enter_context(nc.sbuf_tensor(k.uid + "mo_h", [128, 32 * TT], BF16))
        acct = ctx.enter_context(nc.sbuf_tensor(k.uid + "mo_a", [128, 2, TT], F32))
        ps = ctx.enter_context(nc.psum_tensor(k.uid + "mo_ps", [128, 8, TT], F32))
        ton = k.do("v", "memset", onesb[:], 1.0, deps=deps)
        tb2l = k.dma("q", b2f[:].rearrange("p (c n) -> p c n", c=NC),
                     ap["l1f_b2m"].rearrange("c p n -> p c n"), deps=deps)
        tb2 = k.do("v", "tensor_copy", b2m[:], b2f[:], deps=[tb2l])
        pre = [ton, tb2]
        ej = list(deps) + pre
        region = {}
        for e in range(8):
            tw1 = cast_weights(k, w1e[:],
                               ap["l1f_w1t"][e].rearrange("c p n -> p c n"),
                               ej, stage_f32=stg[:])
            tw2 = cast_weights(k, w2e[:],
                               ap["l1f_w2t"][e].rearrange("c p n -> p c n"),
                               [tw1], stage_f32=stg[:])
            tb1l = k.dma("q", b1s[:], ap["l1f_b1"][e], deps=ej)
            epre = [tw2, tb1l]
            uprev = {}
            for ttu in range(NTU):
                b_, tt = divmod(ttu, NTT)
                t0 = tt * TT
                tg0 = b_ * T + t0
                ux = ttu % 2
                tlx = k.dma("q",
                            xg[:, ux, :].rearrange("p (c t) -> p c t", c=NC),
                            _fmslice(xn2, b_, t0),
                            deps=ej + epre + [uprev.get(("x", ux))])
                tmbf = k.dma("q", mbf[:, ux, :], mbeta[e:e + 1, tg0:tg0 + TT],
                             deps=ej + [uprev.get(("mbf", ux))])
                tmbs = k.do("v", "tensor_copy", mbs[:, ux, :], mbf[:, ux, :],
                            deps=[tmbf, uprev.get(("mbs", ux))])
                if e == 7:
                    tl8f = k.dma("q", m8f[:, ux, :], m8[:, tg0:tg0 + TT],
                                 deps=ej + [uprev.get(("m8f", ux))])
                    tl8 = k.do("v", "tensor_copy", m8s[:, ux, :],
                               m8f[:, ux, :],
                               deps=[tl8f, uprev.get(("m8s", ux))])
                else:
                    tl8 = None
                relu_toks = []
                for hc in range(32):
                    bh = 6 + (hc % 2)
                    wdeps = [tw1, tlx, tmbs, ton]
                    if hc >= 2:
                        wdeps.append(relu_toks[-2])
                    elif ("pb", 6 + hc) in uprev:
                        wdeps.append(uprev[("pb", 6 + hc)])
                    tph = None
                    for c in range(NC):
                        tph = k.mm(ps[:, bh, :],
                                   w1e[:, c * 4096 + hc * 128:
                                       c * 4096 + hc * 128 + 128],
                                   xg[:, ux, c * TT:(c + 1) * TT],
                                   start=(c == 0), stop=False, deps=wdeps,
                                   mark=False)
                    tph = k.mm(ps[:, bh, :], onesb[:],
                               mbs[:, ux, :], start=False, stop=True,
                               deps=wdeps + [tmbs])
                    trl = k.do("s", "activation", hsb[:, hc * TT:(hc + 1) * TT],
                               ps[:, bh, :], AF.Relu, bias=b1s[:, hc:hc + 1],
                               deps=[tph, tb1l] +
                               ([relu_toks[-2]] if hc >= 2 else
                                [uprev.get(("h", hc % 2))]))
                    relu_toks.append(trl)
                uprev[("x", ux)] = relu_toks[-1]
                uprev[("mbf", ux)] = tmbs
                uprev[("mbs", ux)] = relu_toks[-1]
                if e == 7:
                    uprev[("m8f", ux)] = tl8
                uprev[("h", 0)] = relu_toks[-2]
                uprev[("h", 1)] = relu_toks[-1]
                for dc in range(NC):
                    ua = dc % 2
                    src_acc = acc_in if e == 0 else out
                    tla = k.dma("q", acct[:, ua, :],
                                src_acc[b_, dc, :, t0:t0 + TT],
                                deps=[region.get((ttu, dc)),
                                      uprev.get(("a", ua))])
                    tpy = None
                    wdeps2 = [tw2, relu_toks[-1]]
                    if ("pb", dc) in uprev:
                        wdeps2.append(uprev[("pb", dc)])
                    for hc in range(32):
                        last = (hc == 31 and e != 7)
                        tpy = k.mm(ps[:, dc, :],
                                   w2e[:, hc * 1024 + dc * 128:
                                       hc * 1024 + dc * 128 + 128],
                                   hsb[:, hc * TT:(hc + 1) * TT],
                                   start=(hc == 0), stop=last,
                                   deps=wdeps2 + [relu_toks[hc]], mark=last)
                    if e == 7:
                        tpy = k.mm(ps[:, dc, :],
                                   b2m[:, dc * 128:(dc + 1) * 128],
                                   m8s[:, ux, :], start=False,
                                   stop=True, deps=[tb2, tl8])
                        uprev[("m8s", ux)] = tpy
                    tac = k.do("v", "tensor_add", acct[:, ua, :],
                               acct[:, ua, :], ps[:, dc, :],
                               deps=[tpy, tla])
                    tst = k.dma("p", out[b_, dc, :, t0:t0 + TT],
                                acct[:, ua, :], deps=[tac])
                    region[(ttu, dc)] = tst
                    uprev[("a", ua)] = tst
                    uprev[("pb", dc)] = tac
            ej = k.tip()
    return k.tip()



def _l0a_qkv_reads():
    return ["xn1", "l0a_wqa", "l0a_wqb", "l0a_wkva", "l0a_wkvbK", "l0a_wkvbV",
            "c_cd", "c_sd", "c_p128"]


def RD(ap, name):
    return ap.get(name + "__in", ap[name])


STAGES = {
    "s0_ln1": {
        "fn": lambda k, ap, d: emit_ln(k, ap, ap["xin"], ap["xn1"], ap["xfa"], d),
        "reads": ["xin"], "writes": ["xn1", "xfa"]},
    "s1_l0aqkv": {
        "fn": lambda k, ap, d: emit_l0a_qkv(k, ap, d),
        "reads": _l0a_qkv_reads(), "writes": ["qaug", "kaug", "vt0"]},
    "s2_l0aatt": {
        "fn": lambda k, ap, d: emit_l0a_att(k, ap, d),
        "reads": ["qaug", "kaug", "vt0", "c_cm"], "writes": ["o0"]},
    "s3_l0aout": {
        "fn": lambda k, ap, d: emit_matproj_add(
            k, ap, ap["o0"], ap["l0a_wout"], ap["l0a_bout"], ap["xin"],
            ap["x1"], d, tag="s3"),
        "reads": ["o0", "xin", "l0a_wout", "l0a_bout"], "writes": ["x1"]},
}

for _l in (0, 1):
    _src = "xfa" if _l == 0 else "xfb"
    _dst = "xfb" if _l == 0 else "xfa"
    STAGES[f"s4_l{_l}"] = {
        "fn": (lambda l: lambda k, ap, d: emit_l0f_qkv(k, ap, l, d))(_l),
        "reads": [_src, "l0f_wqkv", "l0f_qs", "l0f_ks", "c_xqc", "c_xqs",
                  "c_xkc", "c_xks", "c_prot"],
        "writes": ["qf", "kf", "vt1"]}
    STAGES[f"s5_l{_l}"] = {
        "fn": (lambda l: lambda k, ap, d: emit_l0f_att(k, ap, l, d))(_l),
        "reads": ["qf", "kf", "vt1", "c_mtm", "c_mkm", "c_id"],
        "writes": ["of"]}
    STAGES[f"s6_l{_l}"] = {
        "fn": (lambda l, s: lambda k, ap, d: emit_matproj_add(
            k, ap, ap["of"], ap["l0f_wo"][l], None, RD(ap, s), ap[s], d,
            nin=4, tag=f"s6_{l}"))(_l, _src),
        "reads": ["of", "l0f_wo", _src], "writes": [_src]}
    STAGES[f"s7a_l{_l}"] = {
        "fn": (lambda l, s: lambda k, ap, d: emit_l0f_w1(k, ap, l, s, d))(
            _l, _src),
        "reads": [_src, "l0f_w1", "l0f_b1", "l0f_lng", "l0f_lnb"],
        "writes": ["mgl"]}
    STAGES[f"s7b_l{_l}"] = {
        "fn": (lambda l, s, t: lambda k, ap, d: emit_matproj_add(
            k, ap, ap["mgl"], ap["l0f_w2"][l], ap["l0f_b2"][l], RD(ap, s),
            ap[t], d, nin=NHC, tag=f"s7b_{l}"))(_l, _src, _dst),
        "reads": ["mgl", "l0f_w2", "l0f_b2", _src], "writes": [_dst]}

STAGES["s12_add"] = {
    "fn": lambda k, ap, d: emit_add(k, ap, ap["x1"], ap["xfa"], ap["x2"], d),
    "reads": ["x1", "xfa"], "writes": ["x2"]}
STAGES["s13_ln2"] = {
    "fn": lambda k, ap, d: emit_ln(k, ap, ap["x2"], ap["xn2"], None, d),
    "reads": ["x2"], "writes": ["xn2"]}
STAGES["s14_l1aqkv"] = {
    "fn": lambda k, ap, d: emit_l1a_qkv(k, ap, d),
    "reads": ["xn2", "l1a_wq", "l1a_wk", "l1a_wv", "l1a_bq", "l1a_bk",
              "l1a_bvrow", "l1a_wqr", "l1a_wkr"],
    "writes": ["q2aug", "k2aug", "vt2", "dbg14"]}
STAGES["s15_l1aatt"] = {
    "fn": lambda k, ap, d: emit_l1a_att(k, ap, d),
    "reads": ["q2aug", "k2aug", "vt2"], "writes": ["o2"]}
STAGES["s16_l1aout"] = {
    "fn": lambda k, ap, d: emit_matproj_add(
        k, ap, ap["o2"], ap["l1a_wo"], ap["l1a_bo"], ap["x2"], ap["out"], d,
        tag="s16"),
    "reads": ["o2", "x2", "l1a_wo", "l1a_bo"], "writes": ["out"]}
STAGES["s17_gate"] = {
    "fn": lambda k, ap, d: emit_gate(k, ap, d),
    "reads": ["xn2", "l1f_gw", "l1f_gbrow", "c_id"],
    "writes": ["m8", "mbeta"]}
STAGES["s18_moe"] = {
    "fn": lambda k, ap, d: emit_moe(k, ap, d),
    "reads": ["xn2", "m8", "mbeta", "l1f_w1t", "l1f_b1", "l1f_w2t",
              "l1f_b2m", "out"],
    "writes": ["out"]}

STAGE_ORDER = ["s0_ln1", "s1_l0aqkv", "s2_l0aatt", "s3_l0aout",
               "s4_l0", "s5_l0", "s6_l0", "s7a_l0", "s7b_l0",
               "s4_l1", "s5_l1", "s6_l1", "s7a_l1", "s7b_l1",
               "s12_add", "s13_ln2", "s14_l1aqkv", "s15_l1aatt",
               "s16_l1aout", "s17_gate", "s18_moe"]

GROUPS = [list(STAGE_ORDER)]


def _spec(name):
    if name in WEIGHTS:
        return WEIGHTS[name], F32
    if name in CONSTS:
        return CONSTS[name]
    if name in SCRATCH:
        return SCRATCH[name]
    if name in INPUTS:
        return INPUTS[name]
    if name in OUTPUTS:
        return OUTPUTS[name]
    raise KeyError(name)


def build_group(group, fused):
    nc = bass.Bass("TRN2", target_bir_lowering=False, debug=False)
    reads, writes = [], []
    for s in group:
        reads += [r for r in STAGES[s]["reads"] if r not in reads]
        writes += [w for w in STAGES[s]["writes"] if w not in writes]
    first = {}
    for s in group:
        for r in STAGES[s]["reads"]:
            first.setdefault(r, "r")
        for w in STAGES[s]["writes"]:
            first.setdefault(w, "w")
    ap = {}
    innames, outnames = [], []
    names = list(dict.fromkeys(reads + writes))
    for n in names:
        shape, dt = _spec(n)
        is_scratch = n in SCRATCH or n in OUTPUTS
        written = n in writes
        if not is_scratch or (fused and n not in OUTPUTS and n not in INPUTS):
            if not is_scratch:
                ap[n] = nc.dram_tensor(n, shape, dt, kind="ExternalInput").ap()
                innames.append(n)
            else:
                ap[n] = nc.dram_tensor(n, shape, dt, kind="Internal").ap()
        else:
            if written:
                if first.get(n) == "r" and not fused:
                    ap[n + "__in"] = nc.dram_tensor(
                        n + "__in", shape, dt, kind="ExternalInput").ap()
                    innames.append(n + "__in")
                ap[n] = nc.dram_tensor(n, shape, dt, kind="ExternalOutput").ap()
                outnames.append(n)
            else:
                ap[n] = nc.dram_tensor(n, shape, dt, kind="ExternalInput").ap()
                innames.append(n)
    if fused:
        pass
    k = K(nc)
    deps = []
    for si, s in enumerate(group):
        k.uid = f"g{si}_"
        deps = STAGES[s]["fn"](k, ap, list(deps))
    k.fence_all(k.tip())
    return nc, innames, outnames


_GROUP_CACHE = {}


def kernel(**inputs):
    w = host_weights(inputs)
    c = host_consts()
    xs = [host_core_inputs(inputs, i) for i in range(NCORE)]
    state = [dict() for _ in range(NCORE)]
    fused = len(GROUPS) == 1
    for gi, group in enumerate(GROUPS):
        key = (tuple(group), fused)
        if key not in _GROUP_CACHE:
            _GROUP_CACHE[key] = build_group(group, fused)
        nc, innames, outnames = _GROUP_CACHE[key]
        in_maps = []
        for core in range(NCORE):
            m = {}
            for n in innames:
                base = n[:-4] if n.endswith("__in") else n
                if base in w:
                    m[n] = w[base]
                elif base in c:
                    m[n] = c[base]
                elif base in INPUTS:
                    m[n] = xs[core][base]
                else:
                    arr = state[core].get(base)
                    if arr is None:
                        shape, dt = _spec(base)
                        arr = np.zeros(shape, mybir.dt.np(dt))
                    m[n] = arr
            in_maps.append(m)
        res = run_bass_kernel_spmd(nc, in_maps, core_ids=list(range(NCORE)))
        for core in range(NCORE):
            for n in outnames:
                state[core][n] = res.results[core][n]
    outs = []
    for core in range(NCORE):
        o = state[core].get("out")
        if o is None:
            o = np.zeros(OUTPUTS["out"][0], np.float32)
        outs.append(o.transpose(0, 3, 1, 2).reshape(B, T, D))
    return np.concatenate(outs, axis=0)
